# version 14
# baseline (speedup 1.0000x reference)
import math
import numpy as np
import concourse.bass as bass
from concourse import dve_ops as _dve_ops
import concourse.mybir as mybir
from concourse.alu_op_type import AluOpType as ALU
from concourse.bass_utils import run_bass_kernel_spmd

F32 = mybir.dt.float32
BF16 = mybir.dt.bfloat16
AF = mybir.ActivationFunctionType
AX = mybir.AxisListType

ENGINES = ("tensor", "vector", "scalar", "pool", "sync")

STRICT_SYNC = False
LAST_PROG = None
EXEMPT_ENGINES = frozenset(("tensor", "scalar", "pool", "sync"))
VEC_DIST = 1


NDMASEM = 8


class Prog:

    def __init__(self, nc):
        self.nc = nc
        self.ops = []
        self.eng_pos = {e: [] for e in ENGINES}
        self.last_w = {}
        self.readers = {}
        self.consumers = {}
        self.batches = {e: [] for e in ENGINES}
        self.slot_count = {e: [0] * NDMASEM for e in ENGINES}
        self.implicit_deps = []
        self.fence_pending = {}

    def _regions(self, aps):
        out = []
        for a in aps:
            if a is None:
                continue
            if isinstance(a, str):
                out.append(a)
            else:
                out.append(a.tensor.name)
        return out

    def emit(self, engine, fn, ins=(), outs=(), dma=False, tag="", extra_deps=()):
        idx = len(self.ops)
        rd = self._regions(ins)
        wr = self._regions(outs)
        deps = set(extra_deps)
        deps.update(self.implicit_deps)
        deps.update(self.fence_pending.pop(engine, ()))
        for r in rd:
            if r in self.last_w:
                deps.add(self.last_w[r])
        for r in wr:
            if r in self.last_w:
                deps.add(self.last_w[r])
            for ridx in self.readers.get(r, ()):
                deps.add(ridx)
        deps.discard(idx)
        op = dict(engine=engine, fn=fn, deps=deps, dma=dma, tag=tag,
                  idx=idx, batch=None, outr=set(wr))
        self.eng_pos[engine].append(idx)
        self.ops.append(op)
        for d in deps:
            self.consumers.setdefault(d, []).append(idx)
        for r in rd:
            self.readers.setdefault(r, []).append(idx)
        for r in wr:
            self.last_w[r] = idx
            self.readers[r] = []
        return idx

    def _open_batch(self, queue):
        bs = self.batches[queue]
        b = dict(slot=None, ops=[], value=None, bidx=len(bs))
        bs.append(b)
        return b

    def dma(self, engine, out, in_, ins_extra=(), tag="", batch=None, **kw):
        b = batch if batch is not None else self._open_batch(engine)
        idx = self.emit(engine,
                        lambda e, out=out, in_=in_, kw=kw: e.dma_start(out=out, in_=in_, **kw),
                        ins=[in_] + list(ins_extra), outs=[out], dma=True, tag=tag)
        op = self.ops[idx]
        inbatch = set(b["ops"]) & op["deps"]
        for d in inbatch:
            op["deps"].discard(d)
            self.consumers[d].remove(idx)
        op["batch"] = b
        b["ops"].append(idx)
        return idx

    def open_batch(self, queue):
        return self._open_batch(queue)

    def fence(self):
        targets = set()
        if not hasattr(self, "_fence_mark"):
            self._fence_mark = {q: 0 for q in ENGINES}
        for e in ENGINES:
            if self.eng_pos[e]:
                targets.add(self.eng_pos[e][-1])
            for b in self.batches[e][self._fence_mark[e]:]:
                if b["ops"]:
                    targets.add(b["ops"][-1])
            self._fence_mark[e] = len(self.batches[e])
        for e in ENGINES:
            self.emit(e, lambda eng: eng.nop(), extra_deps=targets,
                      tag=f"fence_{e}_{len(self.ops)}")

    def finalize_and_emit(self):
        nc = self.nc
        n = len(self.ops)
        for q in ENGINES:
            slot_last = [None] * NDMASEM
            slot_cum = [0] * NDMASEM
            bi = 0
            for b in self.batches[q]:
                if not b["ops"]:
                    continue
                first = b["ops"][0]
                cons = []
                for oi in b["ops"]:
                    cons += self.consumers.get(oi, [])
                if cons:
                    assert max(b["ops"]) < min(cons), \
                        f"dma batch on {q} consumed before fully recorded"
                sl = bi % NDMASEM
                bi += 1
                prev = slot_last[sl]
                b["slot"] = sl
                slot_last[sl] = b
                slot_cum[sl] += 16 * len(b["ops"])
                b["value"] = slot_cum[sl]
                if prev is not None:
                    x = prev["ops"][-1]
                    if x not in self.ops[first]["deps"]:
                        self.ops[first]["deps"].add(x)
                        self.consumers.setdefault(x, []).append(first)
        tpos = {}
        for e2 in ENGINES:
            for k2, i2 in enumerate(self.eng_pos[e2]):
                tpos[i2] = k2

        PE_PIPELINE = getattr(self, "pe_pipeline", False)

        def pe_chain(dep, op):
            return (PE_PIPELINE
                    and dep["engine"] == "tensor" and op["engine"] == "tensor"
                    and dep["outr"] & op["outr"]
                    and tpos[dep["idx"]] == tpos[op["idx"]] - 1)
        strict = STRICT_SYNC
        need_inc = [False] * n

        def same_ok(dep, op):
            if dep["engine"] != op["engine"] or dep["dma"]:
                return False
            if dep["engine"] in EXEMPT_ENGINES:
                return True
            if dep["engine"] == "vector":
                return tpos[op["idx"]] - tpos[dep["idx"]] > VEC_DIST
            return False
        for op in self.ops:
            for d in op["deps"]:
                dep = self.ops[d]
                if dep["dma"]:
                    continue
                if same_ok(dep, op) and not strict:
                    continue
                need_inc[d] = True
        inc_count = [0] * n
        eng_counter = {e: 0 for e in ENGINES}
        for op in self.ops:
            i = op["idx"]
            if op["dma"]:
                continue
            if need_inc[i]:
                eng_counter[op["engine"]] += 1
                inc_count[i] = eng_counter[op["engine"]]

        def dep_sem_val(d):
            dep = self.ops[d]
            if dep["dma"]:
                b = dep["batch"]
                return (f"q_{dep['engine']}_{b['slot']}", b["value"])
            return ("p_" + dep["engine"], inc_count[d])

        knowledge = {e: {} for e in ENGINES}
        op_know = {}
        waits = [[] for _ in range(n)]
        cur_inc = {e: 0 for e in ENGINES}
        for op in self.ops:
            i, e = op["idx"], op["engine"]
            myknow = knowledge[e]
            needed = {}
            merges = []
            for d in sorted(op["deps"]):
                dep = self.ops[d]
                if (not strict) and same_ok(dep, op):
                    merges.append(d)
                    continue
                sem, val = dep_sem_val(d)
                if myknow.get(sem, 0) >= val:
                    merges.append(d)
                    continue
                if needed.get(sem, 0) < val:
                    needed[sem] = val
                merges.append(d)
            for sem, val in needed.items():
                waits[i].append((sem, val))
                myknow[sem] = val
            for d in merges:
                for s2, v2 in op_know.get(d, {}).items():
                    if myknow.get(s2, 0) < v2:
                        myknow[s2] = v2
            if op["dma"]:
                op_know[i] = dict(myknow)
            elif need_inc[i]:
                cur_inc[e] = inc_count[i]
                kn = dict(myknow)
                kn["p_" + e] = inc_count[i]
                op_know[i] = kn
            else:
                op_know[i] = dict(myknow)
        self._waits = waits
        sem_handles = {}
        sem_ctxs = []

        def mksem(name):
            ctx = nc.semaphore(name)
            sem_handles[name] = ctx.__enter__()
            sem_ctxs.append(ctx)
        for e in ENGINES:
            mksem("p_" + e)
            nslots = min(NDMASEM, len(self.batches[e])) if self.batches[e] else 0
            for sidx in range(nslots):
                mksem(f"q_{e}_{sidx}")
        blk_ctx = nc.Block()
        blk = blk_ctx.__enter__()
        ops = self.ops
        final_q = {}
        for q in ENGINES:
            slot_cum = {}
            for b in self.batches[q]:
                slot_cum[b["slot"]] = b["value"]
            final_q[q] = slot_cum

        def mk_section(engine):
            def body(eng):
                for i in self.eng_pos[engine]:
                    op = ops[i]
                    for sem, val in waits[i]:
                        eng.wait_ge(sem_handles[sem], val)
                    inst = op["fn"](eng)
                    if op["dma"]:
                        b = op["batch"]
                        inst.then_inc(sem_handles[f"q_{engine}_{b['slot']}"], 16)
                    elif need_inc[i]:
                        if inst is None:
                            raise RuntimeError(f"op {op['tag']} returned no inst")
                        inst.then_inc(sem_handles["p_" + engine], 1)
                for sidx, val in final_q[engine].items():
                    eng.wait_ge(sem_handles[f"q_{engine}_{sidx}"], val)
            return body

        for e in ENGINES:
            sect = getattr(blk, "gpsimd" if e == "pool" else e)
            sect(mk_section(e))
        global LAST_PROG
        LAST_PROG = self
        blk_ctx.__exit__(None, None, None)
        for ctx in sem_ctxs:
            ctx.__exit__(None, None, None)


D = 1024
T = 4096
H = 16
DM = 64
NDC = D // 128
NTB = T // 512
NSB = T // 128


def host_consts():
    import ml_dtypes
    c = {}
    c["c_vals"] = np.tile(np.array([CONST_VALS], np.float32), (128, 1))
    c["c_ones"] = np.ones((128, 1), np.float32)
    c["c_onesr"] = np.ones((1, 128), np.float32)
    eye = np.eye(128, dtype=np.float32)
    c["c_eye32"] = eye
    import ml_dtypes
    c["c_eyebf"] = eye.astype(ml_dtypes.bfloat16)
    hb = np.zeros((16, 1024), np.float32)
    for pc in range(8):
        for j in range(128):
            hb[(pc * 128 + j) // 64, pc * 128 + j] = 1.0
    c["c_hb"] = hb
    import ml_dtypes
    c["c_hbbf"] = hb.astype(ml_dtypes.bfloat16)
    hs2 = np.zeros((128, 2), np.float32)
    hs2[np.arange(128), np.arange(128) // 64] = 1.0
    c["c_hs2"] = hs2
    c["c_hb2"] = hs2.T.copy()
    hsg = np.zeros((128, 8 * 16), np.float32)
    for pc in range(8):
        for j in range(128):
            hsg[j, 16 * pc + (pc * 128 + j) // 64] = 1.0
    c["c_hsg"] = hsg
    c["c_hsgbf"] = hsg.astype(ml_dtypes.bfloat16)
    m = np.zeros((128, 4 * 512), np.float32)
    dg = np.zeros((128, 4 * 512), np.float32)
    for r in range(4):
        s = np.arange(128)[:, None]
        t = np.arange(512)[None, :]
        m[:, r * 512:(r + 1) * 512] = (128 * r + s >= t)
        dg[:, r * 512:(r + 1) * 512] = (128 * r + s == t)
    c["c_l0mask"] = m.astype(ml_dtypes.bfloat16)
    c["c_l0diag"] = dg.astype(ml_dtypes.bfloat16)
    ml = np.where(np.arange(128)[:, None] <= np.arange(128)[None, :], 0.0, -30000.0)
    c["c_masklehb"] = np.tile(ml, (1, 16)).astype(ml_dtypes.bfloat16)
    e64 = np.eye(64, dtype=np.float32)
    c["c_eye2"] = np.vstack([e64, e64])
    dup = np.zeros((64, 128), np.float32)
    dup[np.arange(128) % 64, np.arange(128)] = 1.0
    c["c_dup"] = dup
    d0 = np.zeros((64, 128), np.float32); d0[np.arange(64), np.arange(64)] = 1.0
    d1 = np.zeros((64, 128), np.float32); d1[np.arange(64), 64 + np.arange(64)] = 1.0
    c["c_dup0"] = d0
    c["c_dup1"] = d1
    he = np.zeros((16, 2048), np.float32)
    for h in range(16):
        he[h, h * 128:(h + 1) * 128] = 1.0
    c["c_heye"] = he
    return c


class S:

    def __init__(self, nc, stack):
        self.nc = nc
        self.stack = stack
        self.n = 0

    def sb(self, name, shape, dt=F32):
        return self.stack.enter_context(self.nc.sbuf_tensor(name, shape, dt))

    def ps(self, name, shape, dt=F32):
        return self.stack.enter_context(self.nc.psum_tensor(name, shape, dt))

    def dram(self, name, shape, dt=F32, kind="Internal"):
        return self.nc.dram_tensor(name, shape, dt, kind=kind).ap()


CONST_VALS = (1e-5, 1e-6, 1e-12, 1e-8, 0.25, -0.5, 0.5, 2.0, -1.0, 1.0, 0.0)


def load_consts(p, s, nc, names=("c_ones", "c_onesr", "c_eye32", "c_eyebf",
                                "c_hb", "c_hs2", "c_hb2", "c_hsg",
                                "c_l0mask", "c_l0diag")):
    shapes = {"c_ones": ([128, 1], F32), "c_onesr": ([1, 128], F32),
              "c_eye32": ([128, 128], F32), "c_eyebf": ([128, 128], BF16),
              "c_hb": ([16, 1024], F32), "c_hbbf": ([16, 1024], BF16),
              "c_hs2": ([128, 2], F32),
              "c_hb2": ([2, 128], F32), "c_hsg": ([128, 128], F32),
              "c_hsgbf": ([128, 128], BF16),
              "c_l0mask": ([128, 2048], BF16), "c_l0diag": ([128, 2048], BF16),
              "c_masklehb": ([128, 2048], BF16), "c_heye": ([16, 2048], F32),
              "c_eye2": ([128, 64], F32), "c_dup": ([64, 128], F32),
              "c_dup0": ([64, 128], F32), "c_dup1": ([64, 128], F32)}
    tiles = {}
    b = p.open_batch("sync")
    for nm in names:
        shp, dt = shapes[nm]
        dr = nc.dram_tensor(nm, shp, dt, kind="ExternalInput").ap()
        tl = s.sb(nm + "_t", shp, dt)
        p.dma("sync", tl[:], dr, tag="ld_" + nm, batch=b)
        tiles[nm] = tl
    cv = nc.dram_tensor("c_vals", [128, len(CONST_VALS)], F32, kind="ExternalInput").ap()
    cvt = s.sb("c_vals_t", [128, len(CONST_VALS)], F32)
    cv_idx = p.dma("sync", cvt[:], cv, tag="ld_cvals", batch=b)
    p.implicit_deps.append(cv_idx)
    for i, v in enumerate(CONST_VALS):
        nc.const_aps.register(F32, v, cvt[:, i:i + 1])
    tiles["c_vals"] = cvt
    return tiles


def reload_const(p, s_local, C, nm, dram_ap, shp, dt):
    tl = s_local.sb(nm + "_t" + str(len(p.ops)), shp, dt)
    p.dma("sync", tl[:], dram_ap, tag="rlc_" + nm)
    C[nm] = tl
    return tl


def load_extra_consts(p, s, nc, shapes):
    tiles = {}
    b = p.open_batch("sync")
    for nm, (shp, dt) in shapes.items():
        dr = nc.dram_tensor(nm, shp, dt, kind="ExternalInput").ap()
        tl = s.sb(nm + "_t", shp, dt)
        p.dma("sync", tl[:], dr, tag="ld_" + nm, batch=b)
        tiles[nm] = tl
    return tiles


def ln_fm(p, s, C, x_dram, xn_out, xn_lo=None, tag="ln"):
    xbuf = [s.sb(f"{tag}_xb{c}", [128, 512]) for c in range(NDC)]
    sq = s.sb(f"{tag}_sq", [128, 512])
    ps_s = s.ps(f"{tag}_ps_s", [1, 512])
    ps_q = s.ps(f"{tag}_ps_q", [1, 512])
    stat = s.sb(f"{tag}_stat", [1, 2048])
    ps_ab = s.ps(f"{tag}_ps_ab", [128, 1024])
    xn32 = s.sb(f"{tag}_xn32", [128, 512])
    ones = C["c_ones"]
    onesr = C["c_onesr"]
    for j in range(NTB):
        cols = slice(j * 512, (j + 1) * 512)
        bb = p.open_batch("sync")
        for c in range(NDC):
            p.dma("sync", xbuf[c][:], x_dram[c * 128:(c + 1) * 128, cols],
                  tag=f"{tag}_ldx{c}_{j}", batch=bb)
        for c in range(NDC):
            p.emit("tensor", lambda e, c=c: e.matmul(ps_s[:], ones[:, 0:1], xbuf[c][:],
                                                     start=(c == 0), stop=(c == NDC - 1)),
                   ins=[ones[:], xbuf[c][:]], outs=[ps_s[:]], tag=f"{tag}_mmS{c}_{j}")
        for c in range(NDC):
            p.emit("scalar", lambda e, c=c: e.activation(sq[:], xbuf[c][:], AF.Square),
                   ins=[xbuf[c][:]], outs=[sq[:]], tag=f"{tag}_sq{c}_{j}")
            p.emit("tensor", lambda e, c=c: e.matmul(ps_q[:], ones[:, 0:1], sq[:],
                                                     start=(c == 0), stop=(c == NDC - 1)),
                   ins=[ones[:], sq[:]], outs=[ps_q[:]], tag=f"{tag}_mmQ{c}_{j}")
        p.emit("vector", lambda e: e.tensor_scalar(stat[:, 0:512], ps_s[:], 1.0 / D, None,
                                                   ALU.mult),
               ins=[ps_s[:]], outs=["stat_m"], tag=f"{tag}_m_{j}")
        p.emit("vector", lambda e: e.tensor_scalar(stat[:, 512:1024], ps_q[:], 1.0 / D, None,
                                                   ALU.mult),
               ins=[ps_q[:]], outs=["stat_v"], tag=f"{tag}_q_{j}")
        p.emit("vector", lambda e: e.tensor_mul(sq[0:1, :], stat[:, 0:512], stat[:, 0:512]),
               ins=["stat_m"], outs=["stat_m2"], tag=f"{tag}_m2_{j}")
        p.emit("vector", lambda e: e.tensor_sub(stat[:, 512:1024], stat[:, 512:1024], sq[0:1, :]),
               ins=["stat_v", "stat_m2"], outs=["stat_v"], tag=f"{tag}_v2_{j}")
        p.emit("scalar", lambda e: e.activation(stat[:, 1024:1536], stat[:, 512:1024], AF.Sqrt,
                                                bias=1e-5),
               ins=["stat_v"], outs=["stat_sd"], tag=f"{tag}_sd_{j}")
        p.emit("vector", lambda e: e.reciprocal(stat[:, 1024:1536], stat[:, 1024:1536]),
               ins=["stat_sd"], outs=["stat_r"], tag=f"{tag}_r_{j}")
        p.emit("vector", lambda e: e.scalar_tensor_tensor(stat[:, 1536:2048], stat[:, 0:512], -1.0,
                                                          stat[:, 1024:1536], ALU.mult, ALU.mult),
               ins=["stat_m", "stat_r"], outs=["stat_b"], tag=f"{tag}_b_{j}")
        p.emit("tensor", lambda e: e.matmul(ps_ab[:, 0:512], onesr[0:1, :], stat[:, 1024:1536],
                                            start=True, stop=True),
               ins=[onesr[:], "stat_r"], outs=["ps_a"], tag=f"{tag}_bA_{j}")
        p.emit("tensor", lambda e: e.matmul(ps_ab[:, 512:1024], onesr[0:1, :], stat[:, 1536:2048],
                                            start=True, stop=True),
               ins=[onesr[:], "stat_b"], outs=["ps_b"], tag=f"{tag}_bB_{j}")
        for c in range(NDC):
            p.emit("vector", lambda e, c=c: e.tensor_mul(xn32[:], xbuf[c][:],
                                                         ps_ab[:, 0:512]),
                   ins=[xbuf[c][:], "ps_a"], outs=[xn32[:]], tag=f"{tag}_sc{c}_{j}")
            if xn_lo is None:
                p.emit("vector", lambda e, c=c, cols=cols: e.tensor_add(
                    xn_out[c][:, cols], xn32[:], ps_ab[:, 512:1024]),
                    ins=[xn32[:], "ps_b"], outs=[f"{xn_out[c].name}@{j}"],
                    tag=f"{tag}_o{c}_{j}")
            else:
                p.emit("vector", lambda e, c=c: e.tensor_add(xn32[:], xn32[:],
                                                             ps_ab[:, 512:1024]),
                       ins=[xn32[:], "ps_b"], outs=[xn32[:]], tag=f"{tag}_o32{c}_{j}")
                p.emit("scalar", lambda e, c=c, cols=cols: e.copy(xn_out[c][:, cols], xn32[:]),
                       ins=[xn32[:]], outs=[f"{xn_out[c].name}@{j}"], tag=f"{tag}_hi{c}_{j}")
                p.emit("vector", lambda e, c=c, cols=cols: e.tensor_sub(
                    xn_lo[c][:, cols], xn32[:], xn_out[c][:, cols]),
                    ins=[xn32[:], f"{xn_out[c].name}@{j}"],
                    outs=[f"{xn_lo[c].name}@{j}"], tag=f"{tag}_lo{c}_{j}")


def load_w_bf16(p, s, C, w_dram, name, cols, stage, batch_q="sync", tiles=None):
    if tiles is None:
        tiles = [s.sb(f"{name}_b{c}", [128, cols], BF16) for c in range(NDC)]
    for c in range(NDC):
        p.dma(batch_q, stage[:], w_dram[c * 128:(c + 1) * 128, :], tag=f"ldw_{name}{c}")
        p.emit("vector", lambda e, c=c: e.tensor_copy(tiles[c][:], stage[:]),
               ins=[stage[:]], outs=[tiles[c][:]], tag=f"cw_{name}{c}")
    return tiles


def l0_attn(p, s, C, nc, x_dram, wk_d, wv_d, wq_d, wbeta_d, wout_d, xout_d,
            q_dr, k_dr, v_dr, cl_dr, bt_dr):
    from contextlib import ExitStack
    ones = C["c_ones"]; onesr = C["c_onesr"]; eye32 = C["c_eye32"]
    hs2 = C["c_hs2"]; hb2 = C["c_hb2"]
    clt_tok = s.sb("clt_tok", [128, NSB * 16])
    bt_tok = s.sb("bt_tok", [128, NSB * 16])
    oT = [s.sb(f"oT{pc}", [128, T], BF16) for pc in range(8)]

    stA_outer = ExitStack()
    sAo = S(nc, stA_outer)
    xn = [sAo.sb(f"xn{c}", [128, T], BF16) for c in range(NDC)]
    with ExitStack() as stLN:
        ln_fm(p, S(nc, stLN), C, x_dram, xn, tag="ln0a")
    p.fence()
    with ExitStack() as stA:
        sA = S(nc, stA)
        wbeta_st = sA.sb("wbeta_st", [128, H])
        wbeta_b = load_w_bf16(p, sA, C, wbeta_d, "wbeta", H, wbeta_st)
        ps_be = sA.ps("ps_be", [16, 512])
        betaT = sA.sb("betaT", [16, T])
        clT = sA.sb("clT", [16, T])
        lrT = clT
        for j in range(NTB):
            cols = slice(j * 512, (j + 1) * 512)
            for c in range(NDC):
                p.emit("tensor", lambda e, c=c, cols=cols: e.matmul(
                    ps_be[:], wbeta_b[c][:], xn[c][:, cols],
                    start=(c == 0), stop=(c == NDC - 1)),
                    ins=[wbeta_b[c][:], f"xn{c}@{j}"], outs=[ps_be[:]], tag=f"bmm{c}_{j}")
            p.emit("scalar", lambda e, cols=cols: e.activation(betaT[:, cols], ps_be[:],
                                                               AF.Sigmoid),
                   ins=[ps_be[:]], outs=[f"betaT@{j}"], tag=f"bsig_{j}")
            p.emit("vector", lambda e, cols=cols: e.tensor_scalar(
                lrT[:, cols], betaT[:, cols], -1.0, 1.0 + 1e-8, ALU.mult, ALU.add),
                ins=[f"betaT@{j}"], outs=[f"lrT@{j}"], tag=f"blr_{j}")
            p.emit("scalar", lambda e, cols=cols: e.activation(lrT[:, cols], lrT[:, cols],
                                                               AF.Ln),
                   ins=[f"lrT@{j}"], outs=[f"lrT@{j}"], tag=f"bln_{j}")
        p.emit("vector", lambda e: e.tensor_tensor_scan(clT[:], lrT[:], lrT[:], 0.0,
                                                        ALU.add, ALU.bypass),
               ins=[f"lrT@{j}" for j in range(NTB)], outs=[clT[:]], tag="bscan")
        p.dma("pool", cl_dr, clT[:], tag="st_cl")
        bb = p.open_batch("pool")
        for j in range(NTB):
            p.dma("pool", bt_dr[:, j * 512:(j + 1) * 512],
                  betaT[:, j * 512:(j + 1) * 512],
                  ins_extra=[f"betaT@{j}"], tag=f"st_bt{j}", batch=bb)
        ps_tr = sA.ps("ps_tr", [128, 16])
        for sb in range(NSB):
            sl = slice(sb * 128, (sb + 1) * 128)
            dst = slice(sb * 16, (sb + 1) * 16)
            p.emit("tensor", lambda e, sl=sl: e.transpose(ps_tr[:], clT[:, sl],
                                                          eye32[:][0:16, 0:16]),
                   ins=[clT[:], eye32[:]], outs=[ps_tr[:]], tag=f"trc{sb}")
            p.emit("vector", lambda e, dst=dst: e.tensor_copy(clt_tok[:, dst], ps_tr[:]),
                   ins=[ps_tr[:]], outs=[f"clt_tok@{sb}"], tag=f"trce{sb}")
            p.emit("tensor", lambda e, sl=sl: e.transpose(ps_tr[:], betaT[:, sl],
                                                          eye32[:][0:16, 0:16]),
                   ins=[f"betaT@{sb // 4}", eye32[:]], outs=[ps_tr[:]], tag=f"trb{sb}")
            p.emit("vector", lambda e, dst=dst: e.tensor_copy(bt_tok[:, dst], ps_tr[:]),
                   ins=[ps_tr[:]], outs=[f"bt_tok@{sb}"], tag=f"trbe{sb}")

    p.fence()
    with ExitStack() as stA:
        sA = S(nc, stA)
        wv_st = sA.sb("wv_st", [128, D])
        wv_b = load_w_bf16(p, sA, C, wv_d, "wv", D, wv_st)
        ps_v = [sA.ps(f"ps_v{i}", [128, 512]) for i in range(2)]
        vrow = sA.sb("vrow", [128, D], BF16)
        for sb in range(NSB):
            j = sb // 4
            tsl = slice(sb * 128, (sb + 1) * 128)
            for half in range(2):
                esl = slice(half * 512, (half + 1) * 512)
                for c in range(NDC):
                    p.emit("tensor", lambda e, c=c, tsl=tsl, esl=esl, half=half: e.matmul(
                        ps_v[half][:], xn[c][:, tsl], wv_b[c][:, esl],
                        start=(c == 0), stop=(c == NDC - 1)),
                        ins=[f"xn{c}@{j}", wv_b[c][:]], outs=[ps_v[half][:]],
                        tag=f"vmm{sb}_{half}_{c}")
                p.emit("vector", lambda e, esl=esl, half=half: e.tensor_copy(
                    vrow[:, esl], ps_v[half][:]),
                    ins=[ps_v[half][:]], outs=[f"vrow@{half}"], tag=f"vev{sb}_{half}")
            p.dma("pool", v_dr[tsl, :], vrow[:],
                  ins_extra=["vrow@0", "vrow@1"], tag=f"st_v{sb}")

    p.fence()
    with ExitStack() as stA:
        sA = S(nc, stA)
        w_st = sA.sb("w_st", [128, 128])
        wp_b = sA.sb("wp_b", [128, 2048], BF16)
        qk_t = sA.sb("qk_t", [128, T], BF16)
        qf = sA.sb("qf", [128, 512])
        qsq = sA.sb("qsq", [128, 512])
        ps_p = sA.ps("ps_p", [128, 512])
        ps_ss = sA.ps("ps_ss", [2, 512])
        ps_bb = sA.ps("ps_bb", [128, 512])
        rst = sA.sb("rst", [2, 512])
        for pc in range(8):
            csl = slice(pc * 128, (pc + 1) * 128)
            for wi, (which, wd, dr) in enumerate((("q", wq_d, q_dr), ("k", wk_d, k_dr))):
                for c in range(NDC):
                    p.dma("sync", w_st[:], wd[c * 128:(c + 1) * 128, csl],
                          tag=f"ld{which}{pc}_{c}")
                    p.emit("vector",
                           lambda e, c=c, wi=wi: e.tensor_copy(
                               wp_b[:, wi * 1024 + c * 128: wi * 1024 + (c + 1) * 128],
                               w_st[:]),
                           ins=[w_st[:]], outs=[f"wp_{wi}_{c}"], tag=f"cw{which}{pc}_{c}")
                for j in range(NTB):
                    cols = slice(j * 512, (j + 1) * 512)
                    for c in range(NDC):
                        p.emit("tensor", lambda e, c=c, cols=cols, wi=wi: e.matmul(
                            ps_p[:],
                            wp_b[:, wi * 1024 + c * 128: wi * 1024 + (c + 1) * 128],
                            xn[c][:, cols], start=(c == 0), stop=(c == NDC - 1)),
                            ins=[f"wp_{wi}_{c}", f"xn{c}@{j}"], outs=[ps_p[:]],
                            tag=f"p{which}{pc}_{j}_{c}")
                    p.emit("vector", lambda e: e.tensor_copy(qf[:], ps_p[:]),
                           ins=[ps_p[:]], outs=[qf[:]], tag=f"pe{which}{pc}_{j}")
                    p.emit("scalar", lambda e: e.activation(qsq[:], qf[:], AF.Square),
                           ins=[qf[:]], outs=[qsq[:]], tag=f"psq{which}{pc}_{j}")
                    p.emit("tensor", lambda e: e.matmul(ps_ss[:], hs2[:], qsq[:],
                                                        start=True, stop=True),
                           ins=[hs2[:], qsq[:]], outs=[ps_ss[:]], tag=f"pss{which}{pc}_{j}")
                    p.emit("scalar", lambda e: e.activation(rst[:], ps_ss[:], AF.Sqrt),
                           ins=[ps_ss[:]], outs=[rst[:]], tag=f"psd{which}{pc}_{j}")
                    p.emit("vector", lambda e: e.tensor_scalar(rst[:], rst[:], 1e-12,
                                                               None, ALU.max),
                           ins=[rst[:]], outs=[rst[:]], tag=f"pmx{which}{pc}_{j}")
                    p.emit("vector", lambda e: e.reciprocal(rst[:], rst[:]),
                           ins=[rst[:]], outs=[rst[:]], tag=f"prc{which}{pc}_{j}")
                    if which == "q":
                        p.emit("vector", lambda e: e.tensor_scalar(rst[:], rst[:], 0.125,
                                                                   None, ALU.mult),
                               ins=[rst[:]], outs=[rst[:]], tag=f"pq8{pc}_{j}")
                    p.emit("tensor", lambda e: e.matmul(ps_bb[:], hb2[:], rst[:],
                                                        start=True, stop=True),
                           ins=[hb2[:], rst[:]], outs=[ps_bb[:]], tag=f"pbb{which}{pc}_{j}")
                    p.emit("vector", lambda e, cols=cols: e.tensor_mul(
                        qk_t[:, cols], qf[:], ps_bb[:]),
                        ins=[qf[:], ps_bb[:]], outs=[f"qk_t@{j}"], tag=f"pn{which}{pc}_{j}")
                bq = p.open_batch("pool")
                for j in range(NTB):
                    p.dma("pool", dr[csl, j * 512:(j + 1) * 512],
                          qk_t[:, j * 512:(j + 1) * 512], ins_extra=[f"qk_t@{j}"],
                          tag=f"st{which}{pc}_{j}", batch=bq)
    stA_outer.close()
    p.fence()

    l0mask = C["c_l0mask"]; l0diag = C["c_l0diag"]
    with ExitStack() as stB:
        sB = S(nc, stB)
        kp = sB.sb("kp", [128, T], BF16)
        qp = sB.sb("qp", [128, T], BF16)
        vp = sB.sb("vp", [128, T], BF16)
        clr = [sB.sb(f"clr{h}", [1, 512]) for h in range(2)]
        clb = [sB.sb(f"clb{h}", [128, 512]) for h in range(2)]
        Et = [sB.sb(f"Et{h}", [128, 512]) for h in range(2)]
        Wt = [sB.sb(f"Wt{h}", [128, 512], BF16) for h in range(2)]
        ps_S = [sB.ps(f"ps_S{h}", [128, 512]) for h in range(2)]
        ps_o = [sB.ps(f"ps_o{h}", [128, 512]) for h in range(2)]
        ps_cb = sB.ps("ps_cb", [128, 512])
        for pc in range(8):
            rsl = slice(pc * 128, (pc + 1) * 128)
            bqk = p.open_batch("sync")
            p.dma("sync", kp[:], k_dr[rsl, :], tag=f"B_ldk{pc}", batch=bqk)
            p.dma("sync", qp[:], q_dr[rsl, :], tag=f"B_ldq{pc}", batch=bqk)
            p.dma("sync", vp[:].rearrange("p (sb j) -> p sb j", j=128),
                  v_dr[:, rsl].rearrange("(sb p) j -> p sb j", p=128),
                  tag=f"B_ldv{pc}", batch=bqk)
            for tb in range(NTB):
                tsl = slice(tb * 512, (tb + 1) * 512)
                for h in range(2):
                    p.dma("sync", clr[h][:], cl_dr[2 * pc + h: 2 * pc + h + 1, tsl],
                          tag=f"B_ldcl{pc}_{tb}_{h}")
                    p.emit("tensor", lambda e, h=h: e.matmul(
                        ps_cb[:], onesr[0:1, :], clr[h][:], start=True, stop=True),
                        ins=[onesr[:], clr[h][:]], outs=[ps_cb[:]], tag=f"B_cb{pc}_{tb}_{h}")
                    p.emit("vector", lambda e, h=h: e.tensor_copy(clb[h][:], ps_cb[:]),
                           ins=[ps_cb[:]], outs=[clb[h][:]], tag=f"B_cbe{pc}_{tb}_{h}")
                for sb in range(4 * tb, NSB):
                    r = sb - 4 * tb
                    for h in range(2):
                        hof = 64 * h
                        gh = 2 * pc + h
                        p.emit("tensor", lambda e, h=h, hof=hof, sb=sb, tsl=tsl: e.matmul(
                            ps_S[h][:], kp[hof:hof + 64, sb * 128:(sb + 1) * 128],
                            qp[hof:hof + 64, tsl], start=True, stop=True,
                            tile_position=(hof, 0)),
                            ins=[kp[:], qp[:]], outs=[ps_S[h][:]],
                            tag=f"B_S{pc}_{tb}_{sb}_{h}")
                        p.emit("vector", lambda e, h=h, sb=sb, gh=gh: e.tensor_scalar(
                            Et[h][:], clb[h][:],
                            clt_tok[:, sb * 16 + gh: sb * 16 + gh + 1], 0.0,
                            ALU.subtract, ALU.min),
                            ins=[clb[h][:], f"clt_tok@{sb}"], outs=[Et[h][:]],
                            tag=f"B_E{pc}_{tb}_{sb}_{h}")
                        p.emit("scalar", lambda e, h=h: e.activation(Et[h][:], Et[h][:],
                                                                     AF.Exp),
                               ins=[Et[h][:]], outs=[Et[h][:]], tag=f"B_X{pc}_{tb}_{sb}_{h}")
                        if r < 4:
                            p.emit("vector", lambda e, h=h, r=r: e.tensor_mul(
                                Et[h][:], Et[h][:], l0mask[:][:, r * 512:(r + 1) * 512]),
                                ins=[Et[h][:], l0mask[:]], outs=[Et[h][:]],
                                tag=f"B_M{pc}_{tb}_{sb}_{h}")
                        p.emit("vector", lambda e, h=h: e.tensor_mul(
                            Wt[h][:], ps_S[h][:], Et[h][:]),
                            ins=[ps_S[h][:], Et[h][:]], outs=[Wt[h][:]],
                            tag=f"B_W{pc}_{tb}_{sb}_{h}")
                        if r < 4:
                            p.emit("vector", lambda e, h=h, r=r, sb=sb, gh=gh:
                                   e.scalar_tensor_tensor(
                                Wt[h][:], l0diag[:][:, r * 512:(r + 1) * 512],
                                bt_tok[:, sb * 16 + gh: sb * 16 + gh + 1], Wt[h][:],
                                ALU.mult, ALU.add),
                                ins=[l0diag[:], f"bt_tok@{sb}", Wt[h][:]],
                                outs=[Wt[h][:]], tag=f"B_D{pc}_{tb}_{sb}_{h}")
                        p.emit("tensor", lambda e, h=h, hof=hof, sb=sb, tb=tb: e.matmul(
                            ps_o[h][hof:hof + 64, :],
                            vp[:, sb * 128 + hof: sb * 128 + hof + 64], Wt[h][:],
                            start=(sb == 4 * tb), stop=(sb == NSB - 1),
                            tile_position=(0, hof)),
                            ins=[vp[:], Wt[h][:]], outs=[ps_o[h][:]],
                            tag=f"B_O{pc}_{tb}_{sb}_{h}")
                for h in range(2):
                    hof = 64 * h
                    p.emit("vector", lambda e, h=h, hof=hof, tsl=tsl, pc=pc:
                           e.tensor_copy(oT[pc][hof:hof + 64, tsl],
                                         ps_o[h][hof:hof + 64, :]),
                        ins=[ps_o[h][:]], outs=[f"oT{pc}@{tb}_{h}"],
                        tag=f"B_oe{pc}_{tb}_{h}")
    p.fence()

    with ExitStack() as stC:
        sC = S(nc, stC)
        wo_st = sC.sb("wo_st", [128, D])
        wo_b = load_w_bf16(p, sC, C, wout_d, "wo", D, wo_st)
        xbuf = sC.sb("C_xb", [128, 512])
        xnew = sC.sb("C_xn", [128, 512])
        ps_y = [sC.ps(f"ps_y{i}", [128, 512]) for i in range(2)]
        for tb in range(NTB):
            tsl = slice(tb * 512, (tb + 1) * 512)
            for dc in range(NDC):
                pb = ps_y[dc % 2]
                for pc in range(8):
                    p.emit("tensor", lambda e, pc=pc, dc=dc, tsl=tsl, pb=pb: e.matmul(
                        pb[:], wo_b[pc][:, dc * 128:(dc + 1) * 128], oT[pc][:, tsl],
                        start=(pc == 0), stop=(pc == 7)),
                        ins=[wo_b[pc][:]] + [f"oT{pc}@{tb}_{h}" for h in range(2)],
                        outs=[pb[:]], tag=f"C_mm{tb}_{dc}_{pc}")
                p.dma("sync", xbuf[:], x_dram[dc * 128:(dc + 1) * 128, tsl],
                      tag=f"C_ldx{tb}_{dc}")
                p.emit("vector", lambda e, pb=pb: e.tensor_add(xnew[:], xbuf[:], pb[:]),
                       ins=[xbuf[:], pb[:]], outs=[xnew[:]], tag=f"C_add{tb}_{dc}")
                p.dma("pool", xout_d[dc * 128:(dc + 1) * 128, tsl], xnew[:],
                      ins_extra=[xnew[:]], tag=f"C_st{tb}_{dc}")
    p.fence()


def host_consts_ffn():
    c = {}
    jg = np.zeros((128, 64), np.float32)
    for jj in range(8):
        jg[:, jj * 8:(jj + 1) * 8] = (np.arange(8)[None, :] > jj)
    c["c_jgt"] = jg
    return c


def l0_ffn(p, s, C, nc, x_dram, rw_d, w13_d, w2_d, xout_d, sc_dr):
    from contextlib import ExitStack
    ones = C["c_ones"]; onesr = C["c_onesr"]; eye32 = C["c_eye32"]
    jgt = C["c_jgt"]
    xn = [s.sb(f"fxn{c}", [128, T], BF16) for c in range(NDC)]
    scale_dr = sc_dr

    with ExitStack() as st1:
        s1 = S(nc, st1)
        xlo = [s1.sb(f"fxl{c}", [128, T], BF16) for c in range(NDC)]
        with ExitStack() as stLN:
            ln_fm(p, S(nc, stLN), C, x_dram, xn, xn_lo=xlo, tag="ln0f")
        p.fence()
        with ExitStack() as st2:
            s2 = S(nc, st2)
            rw_st = s2.sb("rw_st", [8, D])
            p.dma("sync", rw_st[:], rw_d, tag="ld_rw")
            ps_rt = s2.ps("ps_rt", [128, 8])
            rw_hi = s2.sb("rw_hi", [128, 8 * NDC], BF16)
            rw_lo = s2.sb("rw_lo", [128, 8 * NDC], BF16)
            rw_f = s2.sb("rw_f", [128, 8])
            for c in range(NDC):
                dst = slice(c * 8, (c + 1) * 8)
                p.emit("tensor", lambda e, c=c: e.transpose(
                    ps_rt[:], rw_st[:, c * 128:(c + 1) * 128], eye32[:][0:8, 0:8]),
                    ins=[rw_st[:], eye32[:]], outs=[ps_rt[:]], tag=f"rtr{c}")
                p.emit("vector", lambda e: e.tensor_copy(rw_f[:], ps_rt[:]),
                       ins=[ps_rt[:]], outs=[rw_f[:]], tag=f"rtre{c}")
                p.emit("scalar", lambda e, dst=dst: e.copy(rw_hi[:, dst], rw_f[:]),
                       ins=[rw_f[:]], outs=[f"rw_hi@{c}"], tag=f"rthi{c}")
                p.emit("vector", lambda e, dst=dst: e.tensor_sub(
                    rw_lo[:, dst], rw_f[:], rw_hi[:, dst]),
                    ins=[rw_f[:], f"rw_hi@{c}"], outs=[f"rw_lo@{c}"], tag=f"rtlo{c}")
            ps_lg = s2.ps("ps_lg", [8, 512])
            lgT = s2.sb("lgT", [8, T])
            for j in range(NTB):
                cols = slice(j * 512, (j + 1) * 512)
                nmm = 3 * NDC
                i = 0
                for c in range(NDC):
                    for (wa, xb) in ((rw_hi, xn), (rw_hi, xlo), (rw_lo, xn)):
                        wreg = "rw_hi" if wa is rw_hi else "rw_lo"
                        xreg = "fxn" if xb is xn else "fxl"
                        p.emit("tensor", lambda e, c=c, cols=cols, wa=wa, xb=xb,
                               i=i, nmm=nmm: e.matmul(
                            ps_lg[:], wa[:, c * 8:(c + 1) * 8], xb[c][:, cols],
                            start=(i == 0), stop=(i == nmm - 1)),
                            ins=[f"{wreg}@{c}", f"{xreg}{c}@{j}"], outs=[ps_lg[:]],
                            tag=f"rl{j}_{c}_{i}")
                        i += 1
                p.emit("vector", lambda e, cols=cols: e.tensor_copy(lgT[:, cols],
                                                                    ps_lg[:]),
                       ins=[ps_lg[:]], outs=[f"lgT@{j}"], tag=f"rle{j}")
            ps_tt = s2.ps("ps_tt", [128, 8])
            lg = s2.sb("lg", [128, 8])
            rmx = s2.sb("rmx", [128, 2])
            es = s2.sb("es", [128, 8])
            rank = s2.sb("rank", [128, 8])
            eqt = s2.sb("eqt", [128, 8])
            sel = s2.sb("sel", [128, 8])
            scl = s2.sb("scl", [128, 8])
            sct = s2.sb("sct", [8, 128])
            ps_sc = s2.ps("ps_sc", [8, 128])
            for sb in range(NSB):
                sl = slice(sb * 128, (sb + 1) * 128)
                p.emit("tensor", lambda e, sl=sl: e.transpose(
                    ps_tt[:], lgT[:, sl], eye32[:][0:8, 0:8]),
                    ins=[f"lgT@{sb // 4}", eye32[:]], outs=[ps_tt[:]], tag=f"qt{sb}")
                p.emit("vector", lambda e: e.tensor_copy(lg[:], ps_tt[:]),
                       ins=[ps_tt[:]], outs=[lg[:]], tag=f"qte{sb}")
                p.emit("vector", lambda e: e.tensor_reduce(rmx[:, 0:1], lg[:], AX.X,
                                                           ALU.max),
                       ins=[lg[:]], outs=["rmx_m"], tag=f"qmx{sb}")
                p.emit("vector", lambda e: e.tensor_scalar(rmx[:, 1:2], rmx[:, 0:1],
                                                           -1.0, None, ALU.mult),
                       ins=["rmx_m"], outs=["rmx_n"], tag=f"qng{sb}")
                p.emit("scalar", lambda e: e.activation(es[:], lg[:], AF.Exp,
                                                        bias=rmx[:, 1:2]),
                       ins=[lg[:], "rmx_n"], outs=[es[:]], tag=f"qex{sb}")
                p.emit("vector", lambda e: e.tensor_reduce(rmx[:, 0:1], es[:], AX.X,
                                                           ALU.add),
                       ins=[es[:]], outs=["rmx_s"], tag=f"qsm{sb}")
                p.emit("vector", lambda e: e.reciprocal(rmx[:, 0:1], rmx[:, 0:1]),
                       ins=["rmx_s"], outs=["rmx_r"], tag=f"qrc{sb}")
                p.emit("vector", lambda e: e.tensor_scalar(es[:], es[:], rmx[:, 0:1],
                                                           None, ALU.mult),
                       ins=[es[:], "rmx_r"], outs=[es[:]], tag=f"qpr{sb}")
                p.emit("pool", lambda e: e.memset(rank[:], 0.0),
                       outs=[rank[:]], tag=f"qr0{sb}")
                for jj in range(8):
                    p.emit("vector", lambda e, jj=jj: e.scalar_tensor_tensor(
                        rank[:], es[:], es[:, jj:jj + 1], rank[:], ALU.is_lt, ALU.add),
                        ins=[es[:], rank[:]], outs=[rank[:]], tag=f"qlt{sb}_{jj}")
                    p.emit("vector", lambda e, jj=jj: e.tensor_scalar(
                        eqt[:], es[:], es[:, jj:jj + 1], None, ALU.is_equal),
                        ins=[es[:]], outs=[eqt[:]], tag=f"qeq{sb}_{jj}")
                    p.emit("pool", lambda e, jj=jj: e.tensor_mul(
                        eqt[:], eqt[:], jgt[:][:, jj * 8:(jj + 1) * 8]),
                        ins=[eqt[:], jgt[:]], outs=[eqt[:]], tag=f"qm2{sb}_{jj}")
                    p.emit("vector", lambda e: e.tensor_add(rank[:], rank[:], eqt[:]),
                           ins=[rank[:], eqt[:]], outs=[rank[:]], tag=f"qad{sb}_{jj}")
                p.emit("vector", lambda e: e.tensor_scalar(sel[:], rank[:], 1.5, None,
                                                           ALU.is_le),
                       ins=[rank[:]], outs=[sel[:]], tag=f"qsl{sb}")
                p.emit("vector", lambda e: e.tensor_mul(sel[:], sel[:], es[:]),
                       ins=[sel[:], es[:]], outs=[sel[:]], tag=f"qw{sb}")
                p.emit("vector", lambda e: e.tensor_reduce(rmx[:, 0:1], sel[:], AX.X,
                                                           ALU.add),
                       ins=[sel[:]], outs=["rmx_s2"], tag=f"qws{sb}")
                p.emit("vector", lambda e: e.tensor_scalar(rmx[:, 0:1], rmx[:, 0:1],
                                                           1e-6, None, ALU.add),
                       ins=["rmx_s2"], outs=["rmx_s2"], tag=f"qwe{sb}")
                p.emit("vector", lambda e: e.reciprocal(rmx[:, 0:1], rmx[:, 0:1]),
                       ins=["rmx_s2"], outs=["rmx_r2"], tag=f"qwr{sb}")
                p.emit("vector", lambda e: e.tensor_scalar(scl[:], sel[:], rmx[:, 0:1],
                                                           None, ALU.mult),
                       ins=[sel[:], "rmx_r2"], outs=[scl[:]], tag=f"qsc{sb}")
                p.emit("tensor", lambda e: e.transpose(ps_sc[:], scl[:],
                                                       eye32[:]),
                       ins=[scl[:], eye32[:]], outs=[ps_sc[:]], tag=f"qtr{sb}")
                p.emit("vector", lambda e: e.tensor_copy(sct[:], ps_sc[0:8, :]),
                       ins=[ps_sc[:]], outs=[sct[:]], tag=f"qse{sb}")
                p.dma("pool", scale_dr[:, sb * 128:(sb + 1) * 128], sct[:],
                      tag=f"qst{sb}")
    p.fence()
    return xn


def l0_ffn_experts(p, s, C, nc, xn, w13_d, w2_d, x_dram, xout_d, sc_dr):
    from contextlib import ExitStack
    onesr = C["c_onesr"]
    with ExitStack() as st:
        s3 = S(nc, st)
        w13s = s3.sb("w13s", [128, 1024])
        w13b = [s3.sb(f"w13b{c}", [128, 4096], BF16) for c in range(NDC)]
        w2s = s3.sb("w2s", [128, D])
        w2b = [s3.sb(f"w2b{i}", [128, D], BF16) for i in range(16)]
        xnu = [s3.sb(f"xnu{c}", [128, 512], BF16) for c in range(NDC)]
        scr = s3.sb("scr", [1, 512])
        hsil = s3.sb("hsil", [128, 512])
        hbuf = [s3.sb(f"hbuf{i}", [128, 512], BF16) for i in range(16)]
        xacc = s3.sb("xacc", [128, 512])
        ps_g = s3.ps("ps_g", [128, 512])
        ps_u = s3.ps("ps_u", [128, 512])
        ps_sb = s3.ps("ps_sb", [128, 512])
        ps_y = [s3.ps(f"ps_yy{i}", [128, 512]) for i in range(2)]
        for ei in range(8):
            for c in range(NDC):
                for hf in range(4):
                    p.dma("sync", w13s[:],
                          w13_d[ei, c * 128:(c + 1) * 128,
                                hf * 1024:(hf + 1) * 1024],
                          tag=f"E{ei}_ldw13_{c}_{hf}")
                    p.emit("vector", lambda e, c=c, hf=hf: e.tensor_copy(
                        w13b[c][:, hf * 1024:(hf + 1) * 1024], w13s[:]),
                        ins=[w13s[:]], outs=[f"w13b{c}@{hf // 2}"],
                        tag=f"E{ei}_cw13_{c}_{hf}")
            for i in range(16):
                p.dma("sync", w2s[:], w2_d[ei, i * 128:(i + 1) * 128, :],
                      tag=f"E{ei}_ldw2_{i}")
                p.emit("scalar", lambda e, i=i: e.copy(w2b[i][:], w2s[:]),
                       ins=[w2s[:]], outs=[w2b[i][:]], tag=f"E{ei}_cw2_{i}")
            for j in range(NTB):
                cols = slice(j * 512, (j + 1) * 512)
                p.dma("sync", scr[:], sc_dr[ei:ei + 1, cols], tag=f"E{ei}_ldsc_{j}")
                p.emit("tensor", lambda e: e.matmul(ps_sb[:], onesr[0:1, :], scr[:],
                                                    start=True, stop=True),
                       ins=[onesr[:], scr[:]], outs=[ps_sb[:]], tag=f"E{ei}_scb_{j}")
                for c in range(NDC):
                    p.emit("vector", lambda e, c=c, cols=cols: e.tensor_mul(
                        xnu[c][:], xn[c][:, cols], ps_sb[:]),
                        ins=[f"fxn{c}@{j}", ps_sb[:]], outs=[xnu[c][:]],
                        tag=f"E{ei}_xu_{j}_{c}")
                for ip in range(16):
                    for c in range(NDC):
                        p.emit("tensor", lambda e, c=c, ip=ip, cols=cols: e.matmul(
                            ps_g[:], w13b[c][:, ip * 128:(ip + 1) * 128],
                            xn[c][:, cols], start=(c == 0), stop=(c == NDC - 1)),
                            ins=[f"w13b{c}@0", f"fxn{c}@{j}"], outs=[ps_g[:]],
                            tag=f"E{ei}_g_{j}_{ip}_{c}")
                    for c in range(NDC):
                        p.emit("tensor", lambda e, c=c, ip=ip: e.matmul(
                            ps_u[:], w13b[c][:, 2048 + ip * 128: 2048 + (ip + 1) * 128],
                            xnu[c][:], start=(c == 0), stop=(c == NDC - 1)),
                            ins=[f"w13b{c}@1", xnu[c][:]], outs=[ps_u[:]],
                            tag=f"E{ei}_u_{j}_{ip}_{c}")
                    p.emit("scalar", lambda e: e.activation(hsil[:], ps_g[:], AF.Silu),
                           ins=[ps_g[:]], outs=[hsil[:]], tag=f"E{ei}_sil_{j}_{ip}")
                    p.emit("vector", lambda e, ip=ip: e.tensor_mul(hbuf[ip][:], hsil[:],
                                                                   ps_u[:]),
                           ins=[hsil[:], ps_u[:]], outs=[hbuf[ip][:]],
                           tag=f"E{ei}_h_{j}_{ip}")
                for hc in range(NDC):
                    pb = ps_y[hc % 2]
                    for i in range(16):
                        p.emit("tensor", lambda e, i=i, hc=hc, pb=pb: e.matmul(
                            pb[:], w2b[i][:, hc * 128:(hc + 1) * 128], hbuf[i][:],
                            start=(i == 0), stop=(i == 15)),
                            ins=[w2b[i][:], hbuf[i][:]], outs=[pb[:]],
                            tag=f"E{ei}_d_{j}_{hc}_{i}")
                    src_d = x_dram if ei == 0 else xout_d
                    p.dma("sync", xacc[:], src_d[hc * 128:(hc + 1) * 128, cols],
                          tag=f"E{ei}_lda_{j}_{hc}")
                    p.emit("vector", lambda e, pb=pb: e.tensor_add(xacc[:], xacc[:],
                                                                   pb[:]),
                           ins=[xacc[:], pb[:]], outs=[xacc[:]], tag=f"E{ei}_acc_{j}_{hc}")
                    p.dma("pool", xout_d[hc * 128:(hc + 1) * 128, cols], xacc[:],
                          tag=f"E{ei}_st_{j}_{hc}")
    p.fence()


def l1_ffn_block(p, s, C, nc, x_dram, FD, xout_d, q_dr, k_dr, v_dr):
    from contextlib import ExitStack
    with ExitStack() as st:
        sF = S(nc, st)
        xn = l1_ffn(p, sF, C, nc, x_dram, FD, xout_d, q_dr, k_dr, v_dr)
        selT = _SELT
        l1_ffn_fff(p, sF, C, nc, xn, selT["q"][:], "q", FD, q_dr, False, 1.0 / 32.0)
        l1_ffn_fff(p, sF, C, nc, xn, selT["k"][:], "k", FD, k_dr, False, 1.0)
        l1_ffn_fff(p, sF, C, nc, xn, selT["v"][:], "v", FD, v_dr, True, 1.0)
    p.fence()
    l1_ffn_attn(p, s, C, nc, x_dram, xout_d, q_dr, k_dr, v_dr)


def l1_ffn(p, s, C, nc, x_dram, FD, xout_d, q_dr, k_dr, v_dr):
    from contextlib import ExitStack
    ones = C["c_ones"]; onesr = C["c_onesr"]; eye32 = C["c_eye32"]
    eyebf = C["c_eyebf"]; hbbf = C["c_hbbf"]
    xn = [s.sb(f"uxn{c}", [128, T], BF16) for c in range(NDC)]
    selT = {f: s.sb(f"selT_{f}", [16, T], BF16) for f in "qkv"}
    global _SELT
    _SELT = selT

    with ExitStack() as st1:
        s1 = S(nc, st1)
        xlo = [s1.sb(f"uxl{c}", [128, T], BF16) for c in range(NDC)]
        with ExitStack() as stLN:
            ln_fm(p, S(nc, stLN), C, x_dram, xn, xn_lo=xlo, tag="ln1f")
        p.fence()
        with ExitStack() as st2:
            s2 = S(nc, st2)
            nw_st = s2.sb("nw_st", [15, D])
            nb_st = s2.sb("nb_st", [15, 1])
            ps_nt = s2.ps("ps_nt", [128, 15])
            nw_hi = s2.sb("nw_hi", [128, 15 * NDC], BF16)
            nw_lo = s2.sb("nw_lo", [128, 15 * NDC], BF16)
            nw_f = s2.sb("nw_f", [128, 15])
            ps_ns = s2.ps("ps_ns", [15, 512])
            nsT = s2.sb("nsT", [15, T])
            ps_st = s2.ps("ps_st", [128, 15])
            sco = s2.sb("sco", [128, 15])
            pos = s2.sb("pos", [128, 15])
            neg = s2.sb("neg", [128, 15])
            Qb = s2.sb("Qb", [128, 16])
            Qc = s2.sb("Qc", [128, 16])
            ps_sel = s2.ps("ps_sel", [16, 128])
            for f in "qkv":
                b = p.open_batch("sync")
                p.dma("sync", nw_st[:], FD[f + "_node_w"], tag=f"N{f}_ldw", batch=b)
                p.dma("sync", nb_st[:], FD[f + "_node_b"], tag=f"N{f}_ldb", batch=b)
                for c in range(NDC):
                    dst = slice(c * 15, (c + 1) * 15)
                    p.emit("tensor", lambda e, c=c: e.transpose(
                        ps_nt[:], nw_st[:, c * 128:(c + 1) * 128], eye32[:][0:15, 0:15]),
                        ins=[nw_st[:], eye32[:]], outs=[ps_nt[:]], tag=f"N{f}tr{c}")
                    p.emit("vector", lambda e: e.tensor_copy(nw_f[:], ps_nt[:]),
                           ins=[ps_nt[:]], outs=[nw_f[:]], tag=f"N{f}te{c}")
                    p.emit("scalar", lambda e, dst=dst: e.copy(nw_hi[:, dst], nw_f[:]),
                           ins=[nw_f[:]], outs=[f"nw_hi@{c}"], tag=f"N{f}hi{c}")
                    p.emit("vector", lambda e, dst=dst: e.tensor_sub(
                        nw_lo[:, dst], nw_f[:], nw_hi[:, dst]),
                        ins=[nw_f[:], f"nw_hi@{c}"], outs=[f"nw_lo@{c}"], tag=f"N{f}lo{c}")
                for j in range(NTB):
                    cols = slice(j * 512, (j + 1) * 512)
                    i = 0
                    for c in range(NDC):
                        for (wa, xb, wreg, xreg) in ((nw_hi, xn, "nw_hi", "uxn"),
                                                     (nw_hi, xlo, "nw_hi", "uxl"),
                                                     (nw_lo, xn, "nw_lo", "uxn")):
                            p.emit("tensor", lambda e, c=c, cols=cols, wa=wa, xb=xb,
                                   i=i: e.matmul(
                                ps_ns[:], wa[:, c * 15:(c + 1) * 15], xb[c][:, cols],
                                start=(i == 0), stop=(i == 3 * NDC - 1)),
                                ins=[f"{wreg}@{c}", f"{xreg}{c}@{j}"], outs=[ps_ns[:]],
                                tag=f"N{f}mm{j}_{c}_{i}")
                            i += 1
                    p.emit("vector", lambda e, cols=cols: e.tensor_scalar(
                        nsT[:, cols], ps_ns[:], nb_st[:], None, ALU.add),
                        ins=[ps_ns[:], nb_st[:]], outs=[f"nsT@{j}"], tag=f"N{f}b{j}")
                for sb in range(NSB):
                    sl = slice(sb * 128, (sb + 1) * 128)
                    p.emit("tensor", lambda e, sl=sl: e.transpose(
                        ps_st[:], nsT[:, sl], eye32[:][0:15, 0:15]),
                        ins=[f"nsT@{sb // 4}", eye32[:]], outs=[ps_st[:]],
                        tag=f"N{f}tt{sb}")
                    p.emit("vector", lambda e: e.tensor_copy(sco[:], ps_st[:]),
                           ins=[ps_st[:]], outs=[sco[:]], tag=f"N{f}se{sb}")
                    p.emit("vector", lambda e: e.tensor_scalar(pos[:], sco[:], 0.0,
                                                               None, ALU.is_ge),
                           ins=[sco[:]], outs=[pos[:]], tag=f"N{f}pg{sb}")
                    p.emit("vector", lambda e: e.tensor_scalar(neg[:], sco[:], 0.0,
                                                               None, ALU.is_lt),
                           ins=[sco[:]], outs=[neg[:]], tag=f"N{f}ng{sb}")
                    p.emit("vector", lambda e: e.tensor_copy(Qb[:, 0:1], neg[:, 0:1]),
                           ins=[neg[:]], outs=["Qb"], tag=f"N{f}q0{sb}")
                    p.emit("vector", lambda e: e.tensor_copy(Qb[:, 1:2], pos[:, 0:1]),
                           ins=[pos[:], "Qb"], outs=["Qb"], tag=f"N{f}q1{sb}")
                    cur, nxt = Qb, Qc
                    base = 1
                    for lev in range(1, 4):
                        n = 2 ** lev
                        nodes = slice(base, base + n)
                        p.emit("vector", lambda e, cur=cur, nxt=nxt, n=n, nodes=nodes:
                               e.tensor_mul(
                            nxt[:].rearrange("p (a b) -> p a b", b=2)[:, 0:n, 0:1],
                            cur[:, 0:n].unsqueeze(2), neg[:, nodes].unsqueeze(2)),
                            ins=["Qb" if cur is Qb else "Qc", neg[:]],
                            outs=["halfA"], tag=f"N{f}e{sb}_{lev}")
                        p.emit("vector", lambda e, cur=cur, nxt=nxt, n=n, nodes=nodes:
                               e.tensor_mul(
                            nxt[:].rearrange("p (a b) -> p a b", b=2)[:, 0:n, 1:2],
                            cur[:, 0:n].unsqueeze(2), pos[:, nodes].unsqueeze(2)),
                            ins=["Qb" if cur is Qb else "Qc", pos[:], "halfA"],
                            outs=["Qc" if nxt is Qc else "Qb"], tag=f"N{f}o{sb}_{lev}")
                        cur, nxt = nxt, cur
                        base += n
                    p.emit("tensor", lambda e, cur=cur: e.transpose(
                        ps_sel[:], cur[:, 0:16].bitcast(F32) if False else cur[:, 0:16],
                        eye32[:]),
                        ins=["Qc" if cur is Qc else "Qb", eye32[:]], outs=[ps_sel[:]],
                        tag=f"N{f}st{sb}")
                    p.emit("vector", lambda e, f=f, sb=sb: e.tensor_copy(
                        selT[f][:, sb * 128:(sb + 1) * 128], ps_sel[:]),
                        ins=[ps_sel[:]], outs=[f"selT_{f}@{sb}"], tag=f"N{f}sv{sb}")
    p.fence()
    return xn


def l1_ffn_fff(p, s, C, nc, xn, selT, f, FD, out_dr, token_major, qscale):
    from contextlib import ExitStack
    hbbf = C["c_hbbf"]
    with ExitStack() as st:
        s3 = S(nc, st)
        w1s = s3.sb(f"{f}_w1s", [128, 2048])
        w1b = [s3.sb(f"{f}_w1b{c}", [128, D], BF16) for c in range(NDC)]
        w2b = [s3.sb(f"{f}_w2b{c}", [128, D], BF16) for c in range(NDC)]
        b1t = s3.sb(f"{f}_b1t", [128, NDC])
        b2b = s3.sb(f"{f}_b2b", [16, D], BF16)
        b2s = s3.sb(f"{f}_b2s", [16, D])
        hrel = s3.sb(f"{f}_hrel", [128, 512])
        hm = [s3.sb(f"{f}_hm{cc}", [128, 512], BF16) for cc in range(NDC)]
        orow = s3.sb(f"{f}_orow", [128, 512], BF16)
        ps_h = s3.ps(f"{f}_ps_h", [128, 512])
        ps_sb2 = s3.ps(f"{f}_ps_sb2", [128, 512])
        ps_d = [s3.ps(f"{f}_ps_d{i}", [128, 512]) for i in range(2)]
        for c in range(NDC):
            p.dma("sync", w1s[:, 0:1024], FD[f + "_w1cat"][c * 128:(c + 1) * 128, :],
                  tag=f"F{f}_lw1_{c}")
            p.emit("vector", lambda e, c=c: e.tensor_copy(w1b[c][:], w1s[:, 0:1024]),
                   ins=[w1s[:]], outs=[w1b[c][:]], tag=f"F{f}_cw1_{c}")
            p.dma("sync", w1s[:, 1024:2048], FD[f + "_w2cat"][c * 128:(c + 1) * 128, :],
                  tag=f"F{f}_lw2_{c}")
            p.emit("scalar", lambda e, c=c: e.copy(w2b[c][:], w1s[:, 1024:2048]),
                   ins=[w1s[:]], outs=[w2b[c][:]], tag=f"F{f}_cw2_{c}")
        b0 = p.open_batch("sync")
        p.dma("sync", b1t[:], FD[f + "_b1cat"], tag=f"F{f}_lb1", batch=b0)
        p.dma("sync", b2s[:], FD[f + "_b2"], tag=f"F{f}_lb2", batch=b0)
        p.emit("vector", lambda e: e.tensor_copy(b2b[:], b2s[:]),
               ins=[b2s[:]], outs=[b2b[:]], tag=f"F{f}_cb2")
        for j in range(NTB):
            cols = slice(j * 512, (j + 1) * 512)
            for cc in range(NDC):
                for c in range(NDC):
                    p.emit("tensor", lambda e, c=c, cc=cc, cols=cols: e.matmul(
                        ps_h[:], w1b[c][:, cc * 128:(cc + 1) * 128], xn[c][:, cols],
                        start=(c == 0), stop=(c == NDC - 1)),
                        ins=[w1b[c][:], f"uxn{c}@{j}"], outs=[ps_h[:]],
                        tag=f"F{f}_h_{j}_{cc}_{c}")
                p.emit("scalar", lambda e, cc=cc: e.activation(
                    hrel[:], ps_h[:], AF.Relu, bias=b1t[:, cc:cc + 1]),
                    ins=[ps_h[:], b1t[:]], outs=[hrel[:]], tag=f"F{f}_r_{j}_{cc}")
                p.emit("tensor", lambda e, cc=cc, cols=cols: e.matmul(
                    ps_sb2[:], hbbf[:][:, cc * 128:(cc + 1) * 128], selT[:, cols],
                    start=True, stop=True),
                    ins=[hbbf[:]] + [f"selT_{f}@{sb}" for sb in range(4 * j, 4 * j + 4)],
                    outs=[ps_sb2[:]], tag=f"F{f}_sb_{j}_{cc}")
                p.emit("vector", lambda e, cc=cc: e.tensor_mul(hm[cc][:], hrel[:],
                                                               ps_sb2[:]),
                       ins=[hrel[:], ps_sb2[:]], outs=[hm[cc][:]],
                       tag=f"F{f}_m_{j}_{cc}")
            if not token_major:
                for oc in range(NDC):
                    pb = ps_d[oc % 2]
                    for cc in range(NDC):
                        p.emit("tensor", lambda e, cc=cc, oc=oc, pb=pb: e.matmul(
                            pb[:], w2b[cc][:, oc * 128:(oc + 1) * 128], hm[cc][:],
                            start=(cc == 0), stop=False),
                            ins=[w2b[cc][:], hm[cc][:]], outs=[pb[:]],
                            tag=f"F{f}_d_{j}_{oc}_{cc}")
                    p.emit("tensor", lambda e, oc=oc, cols=cols, pb=pb: e.matmul(
                        pb[:], b2b[:, oc * 128:(oc + 1) * 128], selT[:, cols],
                        start=False, stop=True),
                        ins=[b2b[:]] + [f"selT_{f}@{sb}" for sb in range(4 * j, 4 * j + 4)],
                        outs=[pb[:]], tag=f"F{f}_db_{j}_{oc}")
                    p.emit("vector", lambda e, pb=pb: e.tensor_scalar(
                        orow[:], pb[:], qscale, None, ALU.mult),
                        ins=[pb[:]], outs=[orow[:]], tag=f"F{f}_oe_{j}_{oc}")
                    p.dma("pool", out_dr[oc * 128:(oc + 1) * 128, cols], orow[:],
                          tag=f"F{f}_os_{j}_{oc}")
            else:
                for tq in range(4):
                    trow = j * 4 + tq
                    tsl = slice(tq * 128, (tq + 1) * 128)
                    for oc in range(2):
                        pb = ps_d[oc]
                        osl = slice(oc * 512, (oc + 1) * 512)
                        for cc in range(NDC):
                            p.emit("tensor", lambda e, cc=cc, tsl=tsl, osl=osl, pb=pb:
                                   e.matmul(
                                pb[:], hm[cc][:, tsl], w2b[cc][:, osl],
                                start=(cc == 0), stop=False),
                                ins=[w2b[cc][:], hm[cc][:]], outs=[pb[:]],
                                tag=f"F{f}_vd_{trow}_{oc}_{cc}")
                        p.emit("tensor", lambda e, trow=trow, osl=osl, pb=pb: e.matmul(
                            pb[:], selT[:, trow * 128:(trow + 1) * 128], b2b[:, osl],
                            start=False, stop=True),
                            ins=[b2b[:], f"selT_{f}@{trow}"], outs=[pb[:]],
                            tag=f"F{f}_vb_{trow}_{oc}")
                        p.emit("vector", lambda e, pb=pb: e.tensor_copy(orow[:], pb[:]),
                               ins=[pb[:]], outs=[orow[:]], tag=f"F{f}_ve_{trow}_{oc}")
                        p.dma("pool", out_dr[trow * 128:(trow + 1) * 128, osl], orow[:],
                              tag=f"F{f}_vs_{trow}_{oc}")
    p.fence()


def l1_ffn_attn(p, s, C, nc, x_dram, xout_d, q_dr, k_dr, v_dr):
    from contextlib import ExitStack
    eyebf = C["c_eyebf"]
    with ExitStack() as st:
        s4 = S(nc, st)
        kT = [s4.sb(f"akT{c}", [128, T], BF16) for c in range(NDC)]
        vtk = s4.sb("avtk", [128, NSB * 1024], BF16)
        qb = s4.sb("aqb", [128, NDC * 128], BF16)
        Ab = s4.sb("aAb", [128, T])
        At = s4.sb("aAt", [128, T], BF16)
        At2 = s4.sb("aAt2", [128, T], BF16)
        rmx = s4.sb("armx", [128, 2])
        xb2 = s4.sb("axb", [128, 128])
        yout = s4.sb("ayout", [128, 128])
        ps_A = s4.ps("ps_A", [128, 512])
        ps_T = s4.ps("ps_T", [128, 512], BF16)
        ps_O = [s4.ps(f"ps_O{i}", [128, 128]) for i in range(2)]
        bk = p.open_batch("sync")
        for c in range(NDC):
            p.dma("sync", kT[c][:], k_dr[c * 128:(c + 1) * 128, :], tag=f"A_lk{c}",
                  batch=bk)
        p.dma("sync", vtk[:].rearrange("p (sb o) -> p sb o", o=1024),
              v_dr.rearrange("(sb p) o -> p sb o", p=128), tag="A_lv", batch=bk)
        for qbi in range(NSB):
            p.dma("sync", qb[:].rearrange("p (c q) -> p c q", q=128),
                  q_dr[:, qbi * 128:(qbi + 1) * 128].rearrange(
                      "(c p) q -> p c q", p=128), tag=f"A_lq{qbi}")
            for sc in range(NTB):
                ssl = slice(sc * 512, (sc + 1) * 512)
                for c in range(NDC):
                    p.emit("tensor", lambda e, c=c, ssl=ssl: e.matmul(
                        ps_A[:], qb[:, c * 128:(c + 1) * 128], kT[c][:, ssl],
                        start=(c == 0), stop=(c == NDC - 1)),
                        ins=[qb[:], kT[c][:]], outs=[ps_A[:]], tag=f"A_s{qbi}_{sc}_{c}")
                p.emit("vector", lambda e, ssl=ssl: e.tensor_copy(Ab[:, ssl], ps_A[:]),
                       ins=[ps_A[:]], outs=[f"Ab@{sc}"], tag=f"A_se{qbi}_{sc}")
            p.emit("vector", lambda e: e.tensor_reduce(rmx[:, 0:1], Ab[:], AX.X,
                                                       ALU.max),
                   ins=[f"Ab@{sc}" for sc in range(NTB)], outs=["armx_m"],
                   tag=f"A_mx{qbi}")
            p.emit("vector", lambda e: e.tensor_scalar(rmx[:, 1:2], rmx[:, 0:1], -1.0,
                                                       None, ALU.mult),
                   ins=["armx_m"], outs=["armx_n"], tag=f"A_ng{qbi}")
            p.emit("scalar", lambda e: e.activation(Ab[:], Ab[:], AF.Exp,
                                                    bias=rmx[:, 1:2],
                                                    accum_out=rmx[:, 0:1]),
                   ins=["armx_n"] + [f"Ab@{sc}" for sc in range(NTB)],
                   outs=["armx_s"] + [f"Ab@{sc}" for sc in range(NTB)],
                   tag=f"A_ex{qbi}")
            p.emit("vector", lambda e: e.reciprocal(rmx[:, 0:1], rmx[:, 0:1]),
                   ins=["armx_s"], outs=["armx_r"], tag=f"A_rc{qbi}")
            p.emit("vector", lambda e: e.tensor_scalar(At[:], Ab[:], rmx[:, 0:1],
                                                       None, ALU.mult),
                   ins=["armx_r"] + [f"Ab@{sc}" for sc in range(NTB)],
                   outs=["At_pre"], tag=f"A_pr{qbi}")
            for g in range(NDC):
                for t4 in range(4):
                    sb = g * 4 + t4
                    p.emit("tensor", lambda e, sb=sb, t4=t4: e.transpose(
                        ps_T[:, t4 * 128:(t4 + 1) * 128],
                        At[:, sb * 128:(sb + 1) * 128], eyebf[:]),
                        ins=["At_pre", eyebf[:]], outs=[f"ps_T@{t4}"],
                        tag=f"A_t{qbi}_{sb}")
                p.emit("vector", lambda e, g=g: e.tensor_copy(
                    At2[:, g * 512:(g + 1) * 512], ps_T[:]),
                    ins=[f"ps_T@{t4}" for t4 in range(4)], outs=[f"AtT@{g}"],
                    tag=f"A_te{qbi}_{g}")
            for oc in range(NDC):
                pb = ps_O[oc % 2]
                for sb in range(NSB):
                    p.emit("tensor", lambda e, sb=sb, oc=oc, pb=pb: e.matmul(
                        pb[:], vtk[:, sb * 1024 + oc * 128: sb * 1024 + (oc + 1) * 128],
                        At2[:, sb * 128:(sb + 1) * 128],
                        start=(sb == 0), stop=(sb == NSB - 1)),
                        ins=[vtk[:], f"AtT@{sb // 4}"], outs=[pb[:]],
                        tag=f"A_o{qbi}_{oc}_{sb}")
                p.dma("sync", xb2[:], x_dram[oc * 128:(oc + 1) * 128,
                                             qbi * 128:(qbi + 1) * 128],
                      tag=f"A_lx{qbi}_{oc}")
                p.emit("vector", lambda e, pb=pb: e.tensor_add(yout[:], xb2[:], pb[:]),
                       ins=[xb2[:], pb[:]], outs=[yout[:]], tag=f"A_ad{qbi}_{oc}")
                p.dma("pool", xout_d[oc * 128:(oc + 1) * 128,
                                     qbi * 128:(qbi + 1) * 128], yout[:],
                      tag=f"A_st{qbi}_{oc}")
    p.fence()


NCH = 32
CG_ITERS = 30


def l1_attn(p, s, C, nc, x_dram, WD, xout_d, cft_dr, cf2_dr):
    from contextlib import ExitStack
    ones = C["c_ones"]; onesr = C["c_onesr"]; eye32 = C["c_eye32"]
    hb = C["c_hb"]; hsg = C["c_hsg"]; hs2 = C["c_hs2"]; hb2 = C["c_hb2"]
    mle = C["c_masklehb"]

    cf_tok = s.sb("d_cftok", [128, (T // 128) * 16])
    bt_tok = s.sb("d_bttok", [128, (T // 128) * 16])
    g1T = s.sb("d_g1T", [64, T], BF16)
    lamt = s.sb("d_lamt", [64, 16])
    lamt2 = s.sb("d_lamt2", [128, 16])
    lamt3 = s.sb("d_lamt3", [128, 8])
    lamb_b = s.sb("d_lambb", [128, 1024])
    dlam = s.sb("d_dlam", [128, 16 * 64], BF16)
    nwl = s.sb("d_nwl", [128, 8])
    Hkk = Hkv = Hkkb = Hkvb = None
    gw2t = s.sb("d_gw2t", [64, D], BF16)
    wo_b = [s.sb(f"dwo_b{c}", [128, D], BF16) for c in range(NDC)]
    stQ = ExitStack()
    sQ = S(nc, stQ)
    xn = [sQ.sb(f"dxn{c}", [128, T], BF16) for c in range(NDC)]
    with ExitStack() as stLN:
        ln_fm(p, S(nc, stLN), C, x_dram, xn, tag="ln1a")
    p.fence()

    wq_b = [sQ.sb(f"dwq_b{c}", [128, D], BF16) for c in range(NDC)]
    wk_b = [sQ.sb(f"dwk_b{c}", [128, D], BF16) for c in range(NDC)]
    wv_b = [sQ.sb(f"dwv_b{c}", [128, D], BF16) for c in range(NDC)]
    with ExitStack() as stw:
        wst = S(nc, stw).sb("d_wst", [128, D])
        load_w_bf16(p, s, C, WD["wq"], "dwq", D, wst, tiles=wq_b)
        load_w_bf16(p, s, C, WD["wk"], "dwk", D, wst, tiles=wk_b)
        load_w_bf16(p, s, C, WD["wv"], "dwv", D, wst, tiles=wv_b)
    p.fence()

    with ExitStack() as st2:
        s2 = S(nc, st2)
        lam_s = s2.sb("d_lams", [64, 16])
        p.dma("sync", lam_s[:], WD["lamb"], tag="d_ldlam")
        t1 = s2.sb("d_t1", [64, 16])
        p.emit("scalar", lambda e: e.activation(t1[:], lam_s[:], AF.Abs),
               ins=[lam_s[:]], outs=[t1[:]], tag="d_labs")
        p.emit("scalar", lambda e: e.activation(t1[:], t1[:], AF.Exp, scale=-1.0),
               ins=[t1[:]], outs=[t1[:]], tag="d_lexp")
        p.emit("scalar", lambda e: e.activation(t1[:], t1[:], AF.Ln, bias=1.0),
               ins=[t1[:]], outs=[t1[:]], tag="d_lln")
        p.emit("vector", lambda e: e.tensor_scalar(lamt[:], lam_s[:], 0.0, None,
                                                   ALU.max),
               ins=[lam_s[:]], outs=[lamt[:]], tag="d_lmax")
        p.emit("vector", lambda e: e.tensor_add(lamt[:], lamt[:], t1[:]),
               ins=[lamt[:], t1[:]], outs=[lamt[:]], tag="d_ladd")
        p.emit("vector", lambda e: e.tensor_scalar(lamt[:], lamt[:], 0.25, None,
                                                   ALU.add),
               ins=[lamt[:]], outs=[lamt[:]], tag="d_l25")
        ps_l2 = s2.ps("d_psl2", [128, 16])
        p.emit("tensor", lambda e: e.matmul(ps_l2[:], C["c_dup"][:], lamt[:],
                                            start=True, stop=True),
               ins=[C["c_dup"][:], lamt[:]], outs=[ps_l2[:]], tag="d_l2m")
        p.emit("vector", lambda e: e.tensor_copy(lamt2[:], ps_l2[:]),
               ins=[ps_l2[:]], outs=[lamt2[:]], tag="d_l2e")
        for h in range(16):
            p.emit("vector", lambda e, h=h: e.tensor_scalar(
                dlam[:, h * 64:(h + 1) * 64], C["c_eye2"][:],
                lamt2[:, h:h + 1], None, ALU.mult),
                ins=[C["c_eye2"][:], lamt2[:]], outs=[f"dlam@{h}"], tag=f"d_dg{h}")
        p.emit("tensor", lambda e: e.matmul(ps_l2[:, 0:8], C["c_dup0"][:],
                                            lamt[:, 0:16:2], start=True, stop=False),
               ins=[C["c_dup0"][:], lamt[:]], outs=[ps_l2[:]], tag="d_l3a")
        p.emit("tensor", lambda e: e.matmul(ps_l2[:, 0:8], C["c_dup1"][:],
                                            lamt[:, 1:16:2], start=False, stop=True),
               ins=[C["c_dup1"][:], lamt[:]], outs=[ps_l2[:]], tag="d_l3b")
        p.emit("vector", lambda e: e.tensor_copy(lamt3[:], ps_l2[:, 0:8]),
               ins=[ps_l2[:]], outs=[lamt3[:]], tag="d_l3e")
        p.emit("pool", lambda e: e.memset(lamb_b[:], 1.0), outs=["lambb_init"],
               tag="d_lbz")
        for pc in range(8):
            p.emit("scalar", lambda e, pc=pc: e.mul(
                lamb_b[:, pc * 128:(pc + 1) * 128],
                lamb_b[:, pc * 128:(pc + 1) * 128], lamt3[:, pc:pc + 1]),
                ins=["lambb_init", lamt3[:]], outs=[f"lambb@{pc}"], tag=f"d_lb{pc}")
        p.dma("sync", nwl[:], WD["nw"], tag="d_ldnw")
        dlt = s2.sb("d_dlt", [16, 1])
        p.dma("sync", dlt[:], WD["delta"], tag="d_lddlt")
        st2b = ExitStack()
        s2b = S(nc, st2b)
        cfT = s2b.sb("d_cfT", [16, T])
        btt = s2b.sb("d_btt", [16, 512])
        wf_st = s2b.sb("d_wfst", [128, 16])
        wf_b = load_w_bf16(p, s2b, C, WD["wf"], "dwf", 16, wf_st)
        wb_b = load_w_bf16(p, s2b, C, WD["wbet"], "dwb", 16, wf_st)
        ps_f = s2b.ps("d_psf", [16, 512])
        ps_bt = s2b.ps("d_psbt", [128, 16])
        zf = s2b.sb("d_zf", [16, 512])
        za = s2b.sb("d_za", [16, 512])
        for j in range(NTB):
            cols = slice(j * 512, (j + 1) * 512)
            for c in range(NDC):
                p.emit("tensor", lambda e, c=c, cols=cols: e.matmul(
                    ps_f[:], wf_b[c][:], xn[c][:, cols],
                    start=(c == 0), stop=(c == NDC - 1)),
                    ins=[wf_b[c][:], f"dxn{c}@{j}"], outs=[ps_f[:]], tag=f"d_f{j}_{c}")
            p.emit("vector", lambda e: e.tensor_scalar(zf[:], ps_f[:], dlt[:], None,
                                                       ALU.add),
                   ins=[ps_f[:], dlt[:]], outs=[zf[:]], tag=f"d_fd{j}")
            p.emit("scalar", lambda e: e.activation(za[:], zf[:], AF.Abs),
                   ins=[zf[:]], outs=[za[:]], tag=f"d_fa{j}")
            p.emit("scalar", lambda e: e.activation(za[:], za[:], AF.Exp, scale=-1.0),
                   ins=[za[:]], outs=[za[:]], tag=f"d_fe{j}")
            p.emit("scalar", lambda e: e.activation(za[:], za[:], AF.Ln, bias=1.0),
                   ins=[za[:]], outs=[za[:]], tag=f"d_fl{j}")
            p.emit("vector", lambda e, cols=cols: e.tensor_scalar(
                cfT[:, cols], zf[:], 0.0, None, ALU.min),
                ins=[zf[:]], outs=[f"d_cfT@{j}"], tag=f"d_fm{j}")
            p.emit("vector", lambda e, cols=cols: e.tensor_sub(
                cfT[:, cols], cfT[:, cols], za[:]),
                ins=[f"d_cfT@{j}", za[:]], outs=[f"d_cfT@{j}"], tag=f"d_fs{j}")
            for c in range(NDC):
                p.emit("tensor", lambda e, c=c, cols=cols: e.matmul(
                    ps_f[:], wb_b[c][:], xn[c][:, cols],
                    start=(c == 0), stop=(c == NDC - 1)),
                    ins=[wb_b[c][:], f"dxn{c}@{j}"], outs=[ps_f[:]], tag=f"d_b{j}_{c}")
            p.emit("scalar", lambda e: e.activation(btt[:], ps_f[:], AF.Sigmoid),
                   ins=[ps_f[:]], outs=[btt[:]], tag=f"d_bs{j}")
            for q4 in range(4):
                sb = 4 * j + q4
                p.emit("tensor", lambda e, q4=q4: e.transpose(
                    ps_f[:, 0:16] if False else ps_bt[:],
                    btt[:, q4 * 128:(q4 + 1) * 128], eye32[:][0:16, 0:16]),
                    ins=[btt[:], eye32[:]], outs=[ps_bt[:]], tag=f"d_tb{sb}")
                p.emit("vector", lambda e, sb=sb: e.tensor_copy(
                    bt_tok[:, sb * 16:(sb + 1) * 16], ps_bt[:]),
                    ins=[ps_bt[:]], outs=[f"d_bttok@{sb}"], tag=f"d_tbe{sb}")
        p.emit("vector", lambda e: e.tensor_tensor_scan(cfT[:], cfT[:], cfT[:], 0.0,
                                                        ALU.add, ALU.bypass),
               ins=[f"d_cfT@{j}" for j in range(NTB)], outs=["d_cf_done"], tag="d_scan")
        p.dma("pool", cf2_dr, cfT[:], ins_extra=["d_cf_done"], tag="d_stcf2")
        ps_tr2 = s2b.ps("d_pstr", [128, 16])
        for sb in range(T // 128):
            sl = slice(sb * 128, (sb + 1) * 128)
            dst = slice(sb * 16, (sb + 1) * 16)
            p.emit("tensor", lambda e, sl=sl: e.transpose(ps_tr2[:], cfT[:, sl],
                                                          eye32[:][0:16, 0:16]),
                   ins=["d_cf_done", eye32[:]], outs=[ps_tr2[:]], tag=f"d_tc{sb}")
            p.emit("vector", lambda e, dst=dst: e.tensor_copy(cf_tok[:, dst], ps_tr2[:]),
                   ins=[ps_tr2[:]], outs=[f"d_cftok@{sb}"], tag=f"d_tce{sb}")
            p.dma("pool", cft_dr[sl, :], cf_tok[:, dst],
                  ins_extra=[f"d_cftok@{sb}"], tag=f"d_tcs{sb}")

        st2b.close()
        p.fence()
        gw1_st = s2.sb("d_g1st", [128, 64])
        gw1_b = load_w_bf16(p, s2, C, WD["gw1"], "dg1", 64, gw1_st)
        ps_g1 = s2.ps("d_psg1", [64, 512])
        for j in range(NTB):
            cols = slice(j * 512, (j + 1) * 512)
            for c in range(NDC):
                p.emit("tensor", lambda e, c=c, cols=cols: e.matmul(
                    ps_g1[:], gw1_b[c][:], xn[c][:, cols],
                    start=(c == 0), stop=(c == NDC - 1)),
                    ins=[gw1_b[c][:], f"dxn{c}@{j}"], outs=[ps_g1[:]], tag=f"d_g1{j}_{c}")
            p.emit("vector", lambda e, cols=cols: e.tensor_copy(g1T[:, cols], ps_g1[:]),
                   ins=[ps_g1[:]], outs=[f"d_g1T@{j}"], tag=f"d_g1e{j}")
        gw2s = s2.sb("d_g2s", [64, D])
        p.dma("sync", gw2s[:], WD["gw2"], tag="d_ldg2")
        p.emit("vector", lambda e: e.tensor_copy(gw2t[:], gw2s[:]),
               ins=[gw2s[:]], outs=[gw2t[:]], tag="d_cg2")
    p.fence()
    ql_dr, kl_dr, kt_dr, vt_dr = WD["ql_dr"], WD["kl_dr"], WD["kt_dr"], WD["vt_dr"]
    with ExitStack() as stP:
        sP = S(nc, stP)
        hs2 = C["c_hs2"]; hb2 = C["c_hb2"]
        ps_p = sP.ps("y_psp", [128, 512])
        ps_s2 = sP.ps("y_pss", [128, 512])
        qf = sP.sb("y_qf", [128, 512])
        qq = sP.sb("y_qq", [128, 512])
        rst = sP.sb("y_rst", [2, 512])
        qrow = sP.sb("y_qrow", [128, T], BF16)
        vrow = sP.sb("y_vrow", [128, 1024], BF16)
        wexp = sP.sb("y_wex", [128, 16])
        for (wb, dr, nm) in ((wq_b, ql_dr, "q"), (wk_b, kl_dr, "k")):
            for pc in range(8):
                csl = slice(pc * 128, (pc + 1) * 128)
                for j in range(NTB):
                    cols = slice(j * 512, (j + 1) * 512)
                    for c in range(NDC):
                        p.emit("tensor", lambda e, c=c, csl=csl, wb=wb, cols=cols:
                               e.matmul(ps_p[:], wb[c][:, csl], xn[c][:, cols],
                                        start=(c == 0), stop=(c == NDC - 1)),
                               ins=[wb[c][:], f"dxn{c}@{j}"], outs=[ps_p[:]],
                               tag=f"y{nm}{pc}_{j}_mm{c}")
                    p.emit("vector", lambda e: e.tensor_copy(qf[:], ps_p[:]),
                           ins=[ps_p[:]], outs=[qf[:]], tag=f"y{nm}{pc}_{j}_e")
                    p.emit("scalar", lambda e: e.activation(qq[:], qf[:], AF.Square),
                           ins=[qf[:]], outs=[qq[:]], tag=f"y{nm}{pc}_{j}_q")
                    p.emit("tensor", lambda e: e.matmul(ps_s2[0:2, 0:512], hs2[:],
                                                        qq[:], start=True, stop=True),
                           ins=[hs2[:], qq[:]], outs=["y_pss"], tag=f"y{nm}{pc}_{j}_s")
                    p.emit("scalar", lambda e: e.activation(rst[:], ps_s2[0:2, 0:512],
                                                            AF.Sqrt, bias=1e-6),
                           ins=["y_pss"], outs=[rst[:]], tag=f"y{nm}{pc}_{j}_r")
                    p.emit("vector", lambda e: e.reciprocal(rst[:], rst[:]),
                           ins=[rst[:]], outs=[rst[:]], tag=f"y{nm}{pc}_{j}_i")
                    p.emit("tensor", lambda e: e.matmul(ps_s2[:, 0:512], hb2[:],
                                                        rst[:], start=True, stop=True),
                           ins=[hb2[:], rst[:]], outs=["y_pss"], tag=f"y{nm}{pc}_{j}_b")
                    p.emit("vector", lambda e, cols=cols: e.tensor_mul(
                        qrow[:, cols], qf[:], ps_s2[:, 0:512]),
                        ins=[qf[:], "y_pss"], outs=[f"y_qrow@{j}"],
                        tag=f"y{nm}{pc}_{j}_n")
                bq = p.open_batch("pool")
                for j in range(NTB):
                    p.dma("pool", dr[csl, j * 512:(j + 1) * 512],
                          qrow[:, j * 512:(j + 1) * 512],
                          ins_extra=[f"y_qrow@{j}"], tag=f"y{nm}{pc}_st{j}", batch=bq)
        for sb in range(NCH):
            tsl2 = slice(sb * 128, (sb + 1) * 128)
            jj = sb // 4
            for (wb, dr, nm) in ((wk_b, kt_dr, "K"), (wv_b, vt_dr, "V")):
                for half in range(2):
                    esl = slice(half * 512, (half + 1) * 512)
                    for c in range(NDC):
                        p.emit("tensor", lambda e, c=c, esl=esl, wb=wb, tsl2=tsl2:
                               e.matmul(ps_p[:], xn[c][:, tsl2], wb[c][:, esl],
                                        start=(c == 0), stop=(c == NDC - 1)),
                               ins=[wb[c][:], f"dxn{c}@{jj}"], outs=[ps_p[:]],
                               tag=f"yt{nm}{sb}_{half}_mm{c}")
                    p.emit("vector", lambda e, esl=esl: e.tensor_copy(vrow[:, esl],
                                                                      ps_p[:]),
                           ins=[ps_p[:]], outs=[f"y_vrow@{half}"],
                           tag=f"yt{nm}{sb}_{half}_e")
                if nm == "K":
                    p.emit("scalar", lambda e: e.activation(qf[:], vrow[:, 0:512],
                                                            AF.Square,
                                                            accum_out=None),
                           ins=["y_vrow@0"], outs=[qf[:]], tag=f"ytK{sb}_sqa")
                    p.emit("vector", lambda e: e.tensor_reduce(
                        wexp[:, 0:8], qf[:].rearrange("p (h j) -> p h j", j=64),
                        AX.X, ALU.add),
                        ins=[qf[:]], outs=["y_wexp@a"], tag=f"ytK{sb}_ra")
                    p.emit("scalar", lambda e: e.activation(qf[:], vrow[:, 512:1024],
                                                            AF.Square),
                           ins=["y_vrow@1"], outs=[qf[:]], tag=f"ytK{sb}_sqb")
                    p.emit("vector", lambda e: e.tensor_reduce(
                        wexp[:, 8:16], qf[:].rearrange("p (h j) -> p h j", j=64),
                        AX.X, ALU.add),
                        ins=[qf[:]], outs=["y_wexp@b"], tag=f"ytK{sb}_rb")
                    p.emit("scalar", lambda e: e.activation(wexp[:], wexp[:], AF.Sqrt,
                                                            bias=1e-6),
                           ins=["y_wexp@a", "y_wexp@b"], outs=["y_wexp@s"],
                           tag=f"ytK{sb}_sq")
                    p.emit("vector", lambda e: e.reciprocal(wexp[:], wexp[:]),
                           ins=["y_wexp@s"], outs=["y_wexp@r"], tag=f"ytK{sb}_rc")
                    p.emit("vector", lambda e: e.tensor_tensor(
                        vrow[:].rearrange("p (h j) -> p h j", j=64),
                        vrow[:].rearrange("p (h j) -> p h j", j=64),
                        wexp[:].unsqueeze(2).broadcast_to([128, 16, 64]), ALU.mult),
                        ins=["y_vrow@0", "y_vrow@1", "y_wexp@r"],
                        outs=["y_vrow@0", "y_vrow@1"], tag=f"ytK{sb}_n")
                    p.dma("pool", dr[tsl2, :], vrow[:],
                          ins_extra=["y_vrow@0", "y_vrow@1"], tag=f"yt{nm}{sb}_st")
                else:
                    p.dma("pool", dr[tsl2, :], vrow[:],
                          ins_extra=["y_vrow@0", "y_vrow@1"], tag=f"yt{nm}{sb}_st")
    stQ.close()
    p.fence()
    with ExitStack() as stw2:
        wst2 = S(nc, stw2).sb("d_wst2", [128, D])
        load_w_bf16(p, s, C, WD["wo"], "dwo", D, wst2, tiles=wo_b)
    p.fence()
    global _L1W
    _L1W = dict(wq_b=wq_b, wk_b=wk_b, wv_b=wv_b, wo_b=wo_b, cfT=None, btT=None,
                cf_tok=cf_tok, bt_tok=bt_tok, g1T=g1T, dlam=dlam, nwl=nwl,
                Hkk=Hkk, Hkv=Hkv, Hkkb=Hkkb, Hkvb=Hkvb, gw2t=gw2t,
                lamb_b=lamb_b)
    return xn




DBG = False


def l1_attn_chunk_loop(p, s, C, nc, QD, tens, x_dram, xout_d, cft_dr, cf2_dr):
    from contextlib import ExitStack
    if DBG:
        dbg_dt = nc.dram_tensor("dbg_Dt", [128, 2048], BF16, kind="Internal").ap()
        dbg_db = nc.dram_tensor("dbg_dbl", [128, 1024], BF16, kind="Internal").ap()
        dbg_qs = nc.dram_tensor("dbg_qs", [128, 1024], F32, kind="Internal").ap()
        dbg_o = nc.dram_tensor("dbg_o", [128, 1024], F32, kind="Internal").ap()
        dbg_ap = nc.dram_tensor("dbg_ap", [128, 1024], F32, kind="Internal").ap()
    ones = C["c_ones"]; onesr = C["c_onesr"]; eye32 = C["c_eye32"]
    hb = C["c_hb"]; hsg = C["c_hsg"]; mle = C["c_masklehb"]
    hs2 = C["c_hs2"]; hb2 = C["c_hb2"]; heye = C["c_heye"]
    (wq_b, wk_b, wv_b, wo_b, cfT, btT, cf_tok, bt_tok, g1T, dlam, nwl,
     Hkk, Hkv, Hkkb, Hkvb, gw2t) = tens
    HSEG = lambda t: t.rearrange("p (h j) -> p h j", j=64)
    with ExitStack() as st:
        s3 = S(nc, st)
        ps_big = s3.ps("e_psbig", [128, 2048])
        ps_Ap = s3.ps("e_psAp", [128, 1024])
        ps_sm = s3.ps("e_pssm", [128, 512])
        q_l = s3.sb("e_ql", [128, 1024], BF16)
        k_l = s3.sb("e_kl", [128, 1024], BF16)
        Ktok = s3.sb("e_Ktok", [128, 1024], BF16)
        Kbe = s3.sb("e_Kbe", [128, 1024], BF16)
        Ksc = s3.sb("e_Ksc", [128, 1024], BF16)
        Ksc2 = s3.sb("e_Ksc2", [128, 2048], BF16)
        Vtok = s3.sb("e_Vtok", [128, 1024], BF16)
        Vbe = s3.sb("e_Vbe", [128, 1024], BF16)
        Dt = s3.sb("e_Dt", [128, 2048], BF16)
        dbl = s3.sb("e_dbl", [128, 1024], BF16)
        Wt = s3.sb("e_Wt", [128, 2048], BF16)
        xcg = s3.sb("e_xq", [128, 1024])
        rcg = s3.sb("e_rr", [128, 1024])
        pcg = s3.sb("e_pp", [128, 1024])
        p_bf = s3.sb("e_pbf", [128, 1024], BF16)
        dp_bf = s3.sb("e_dpbf", [128, 1024], BF16)
        tmp = s3.sb("e_tmp", [128, 1024])
        tsq = s3.sb("e_tsq", [128, 1024])
        rs_t = [s3.sb(f"e_rs{i}", [16, 128]) for i in range(2)]
        alf = s3.sb("e_alf", [16, 256])
        gam = s3.sb("e_gam", [16, 128])
        sca = s3.sb("e_sca", [16, 512])
        wex = s3.sb("e_wex", [128, 16])
        cfr = s3.sb("e_cfr", [1, 32])
        nst = s3.sb("e_nst", [16, 128])
        cfc = s3.sb("e_cfc", [16, 129])
        ncb = s3.sb("e_ncb", [128, 1])
        dec64 = s3.sb("e_dec64", [128, 16])
        cfl = s3.sb("e_cfl", [1, 512])
        ogn = s3.sb("e_ogn", [128, 1024], BF16)
        osb = s3.sb("e_osb", [128, 1024])
        xac = s3.sb("e_xac", [128, 128])

        def seg_bcast_mul(out_t, in_t, col_ap, ins, outs, tag):
            p.emit("vector", lambda e: e.tensor_tensor(
                HSEG(out_t[:]), HSEG(in_t[:]),
                col_ap.unsqueeze(2).broadcast_to([128, 16, 64]), ALU.mult),
                ins=ins, outs=outs, tag=tag)

        for ch in range(NCH):
            tsl = slice(ch * 128, (ch + 1) * 128)
            j512 = ch // 4
            bt5 = p.open_batch("sync")
            p.dma("sync", q_l[:].rearrange("p (pc t) -> p pc t", t=128),
                  QD["ql"][:, tsl].rearrange("(pc p) t -> p pc t", p=128),
                  tag=f"e{ch}_lq", batch=bt5)
            p.dma("sync", k_l[:].rearrange("p (pc t) -> p pc t", t=128),
                  QD["kl"][:, tsl].rearrange("(pc p) t -> p pc t", p=128),
                  tag=f"e{ch}_lk", batch=bt5)
            p.dma("sync", Ktok[:], QD["kt"][tsl, :], tag=f"e{ch}_lKt", batch=bt5)
            p.dma("sync", Vtok[:], QD["vt"][tsl, :], tag=f"e{ch}_lVt", batch=bt5)
            seg_bcast_mul(Kbe, Ktok, bt_tok[:, ch * 16:(ch + 1) * 16],
                          [Ktok[:], f"d_bttok@{ch}"], [Kbe[:]], f"e{ch}_Kb")
            seg_bcast_mul(Vbe, Vtok, bt_tok[:, ch * 16:(ch + 1) * 16],
                          [Vtok[:], f"d_bttok@{ch}"], ["e_Vbeta"],
                          f"e{ch}_Vb")
            p.dma("sync", cfr[:, 0:16], cft_dr[ch * 128 + 127: ch * 128 + 128, :],
                  tag=f"e{ch}_lcfe")
            if ch > 0:
                p.dma("sync", cfr[:, 16:32], cft_dr[ch * 128 - 1: ch * 128, :],
                      tag=f"e{ch}_lcfs")
            else:
                p.emit("pool", lambda e: e.memset(cfr[:, 16:32], 0.0),
                       outs=[cfr[:]], tag=f"e{ch}_zcfs")
            p.emit("tensor", lambda e: e.matmul(ps_sm[:, 0:16], onesr[0:1, :],
                                                cfr[:, 0:16], start=True, stop=True),
                   ins=[onesr[:], cfr[:]], outs=["e_pssm"], tag=f"e{ch}_cfeb")
            p.emit("vector", lambda e, ch=ch: e.tensor_sub(
                wex[:], ps_sm[:, 0:16], cf_tok[:, ch * 16:(ch + 1) * 16]),
                ins=["e_pssm", f"d_cftok@{ch}"], outs=["e_wex@d"], tag=f"e{ch}_wxd")
            p.emit("scalar", lambda e: e.activation(wex[:], wex[:], AF.Exp),
                   ins=["e_wex@d"], outs=["e_wex@e"], tag=f"e{ch}_wxe")
            seg_bcast_mul(Ksc, Kbe, wex[:], [Kbe[:], "e_wex@e"], [Ksc[:]],
                          f"e{ch}_Kw")
            p.emit("vector", lambda e: e.tensor_sub(cfr[:, 0:16], cfr[:, 0:16],
                                                    cfr[:, 16:32]),
                   ins=[cfr[:]], outs=[cfr[:]], tag=f"e{ch}_dcd")
            p.emit("scalar", lambda e: e.activation(cfr[:, 0:16], cfr[:, 0:16],
                                                    AF.Exp),
                   ins=[cfr[:]], outs=[cfr[:]], tag=f"e{ch}_dce")
            p.emit("tensor", lambda e: e.matmul(ps_sm[:, 16:32], onesr[0:1, :],
                                                cfr[:, 0:16], start=True, stop=True),
                   ins=[onesr[:], cfr[:]], outs=["e_pssm"], tag=f"e{ch}_dcb")
            p.emit("vector", lambda e: e.tensor_copy(dec64[:], ps_sm[:, 16:32]),
                   ins=["e_pssm"], outs=[dec64[:]], tag=f"e{ch}_dce2")
            p.emit("vector", lambda e: e.tensor_copy(
                Ksc2[:].rearrange("p (h d j) -> p h d j", d=2, j=64),
                Ksc[:].rearrange("p (h j) -> p h j", j=64).unsqueeze(2)
                .broadcast_to([128, 16, 2, 64])),
                ins=[Ksc[:]], outs=[Ksc2[:]], tag=f"e{ch}_K2")
            if ch > 0:
                p.dma("sync", cfc[:], cf2_dr[:, ch * 128 - 1: ch * 128 + 128],
                      tag=f"e{ch}_lcfc")
            else:
                p.emit("pool", lambda e: e.memset(cfc[:, 0:1], 0.0),
                       outs=["e_cfc0"], tag=f"e{ch}_zcfc")
                p.dma("sync", cfc[:, 1:129], cf2_dr[:, 0:128],
                      ins_extra=["e_cfc0"], tag=f"e{ch}_lcfc")
            p.emit("vector", lambda e: e.tensor_scalar(
                nst[:], cfc[:, 1:129], -1.0, None, ALU.mult),
                ins=[cfc[:]], outs=[nst[:]], tag=f"e{ch}_nst")
            for q4 in range(4):
                qsl = slice(q4 * 512, (q4 + 1) * 512)
                p.dma("sync", cfl[:].rearrange("p (a b) -> p a b", b=128),
                      cf2_dr[4 * q4:4 * q4 + 4,
                             ch * 128:(ch + 1) * 128].unsqueeze(0),
                      tag=f"e{ch}_lcfl{q4}")
                p.emit("tensor", lambda e, qsl=qsl: e.matmul(
                    ps_big[:, qsl], onesr[0:1, :], cfl[:],
                    start=True, stop=False),
                    ins=[onesr[:], cfl[:]], outs=[f"e_psbig@{q4}"], tag=f"e{ch}_A{q4}")
                p.emit("tensor", lambda e, qsl=qsl: e.matmul(
                    ps_big[:, qsl], nst[:], heye[:][:, qsl],
                    start=False, stop=True),
                    ins=[nst[:], heye[:]], outs=[f"e_psbig@{q4}"], tag=f"e{ch}_B{q4}")
            p.emit("vector", lambda e: e.scalar_tensor_tensor(
                ps_big[:], ps_big[:], 0.0, mle[:], ALU.min, ALU.add),
                ins=[mle[:]] + [f"e_psbig@{q4}" for q4 in range(4)],
                outs=[f"e_psbig@{q4}" for q4 in range(4)], tag=f"e{ch}_Dmn")
            p.emit("scalar", lambda e: e.activation(Dt[:], ps_big[:], AF.Exp),
                   ins=[f"e_psbig@{q4}" for q4 in range(4)], outs=[Dt[:]],
                   tag=f"e{ch}_Dex")
            for pc in range(8):
                p.emit("tensor", lambda e, pc=pc: e.matmul(
                    ps_sm[:, 128:256], hb[:][:, pc * 128:(pc + 1) * 128],
                    cfc[:, 1:129], start=True, stop=True),
                    ins=[hb[:], cfc[:]], outs=["e_pssm"], tag=f"e{ch}_db{pc}")
                p.emit("tensor", lambda e, pc=pc: e.matmul(
                    ps_sm[:, 384:385], hb[:][:, pc * 128:(pc + 1) * 128],
                    cfc[:, 0:1], start=True, stop=True),
                    ins=[hb[:], cfc[:]], outs=["e_pssm"],
                    tag=f"e{ch}_ns{pc}")
                p.emit("vector", lambda e: e.tensor_scalar(
                    ncb[:], ps_sm[:, 384:385], -1.0, None, ALU.mult),
                    ins=["e_pssm"], outs=["e_ncb"], tag=f"e{ch}_nsn{pc}")
                p.emit("scalar", lambda e, pc=pc: e.activation(
                    dbl[:, pc * 128:(pc + 1) * 128], ps_sm[:, 128:256], AF.Exp,
                    bias=ncb[:]),
                    ins=["e_pssm", "e_ncb"], outs=[f"e_dbl@{pc}"],
                    tag=f"e{ch}_dbe{pc}")
            if DBG and ch == 0:
                p.dma("pool", dbg_dt, Dt[:], ins_extra=[Dt[:]], tag="dbgdt")
                p.dma("pool", dbg_db, dbl[:],
                      ins_extra=[f"e_dbl@{pc}" for pc in range(8)], tag="dbgdb")
            p.emit("pool", lambda e: e.memset(xcg[:], 0.0), outs=[xcg[:]],
                   tag=f"e{ch}_cgx0")
            qreg = [q_l[:]]
            p.emit("vector", lambda e: e.tensor_copy(rcg[:], q_l[:]),
                   ins=qreg, outs=[rcg[:]], tag=f"e{ch}_cgr0")
            p.emit("vector", lambda e: e.tensor_copy(pcg[:], q_l[:]),
                   ins=qreg, outs=[pcg[:]], tag=f"e{ch}_cgp0")
            p.emit("scalar", lambda e: e.activation(tsq[:], rcg[:], AF.Square),
                   ins=[rcg[:]], outs=["e_tsq"], tag=f"e{ch}_cgsq0")
            for pc in range(8):
                p.emit("tensor", lambda e, pc=pc: e.matmul(
                    ps_sm[0:16, 0:128], hsg[:][:, pc * 16:(pc + 1) * 16],
                    tsq[:, pc * 128:(pc + 1) * 128],
                    start=(pc == 0), stop=(pc == 7)),
                    ins=[hsg[:], "e_tsq"], outs=["e_pssm"], tag=f"e{ch}_cgr0m{pc}")
            p.emit("vector", lambda e: e.tensor_copy(rs_t[0][:], ps_sm[0:16, 0:128]),
                   ins=["e_pssm"], outs=[rs_t[0][:]], tag=f"e{ch}_cgrs0")
            for it in range(CG_ITERS):
                rs_old = rs_t[it % 2]
                rs_new = rs_t[(it + 1) % 2]
                E = f"e{ch}i{it}"
                p.emit("vector", lambda e: e.tensor_copy(p_bf[:], pcg[:]),
                       ins=[pcg[:]], outs=[p_bf[:]], tag=f"{E}_pbf")
                p.emit("vector", lambda e: e.tensor_mul(dp_bf[:], p_bf[:], dbl[:]),
                       ins=[p_bf[:]] + [f"e_dbl@{pc}" for pc in range(8)],
                       outs=[dp_bf[:]], tag=f"{E}_dp")
                for h in range(16):
                    pc, hl = h // 2, h % 2
                    p.emit("tensor", lambda e, pc=pc, hl=hl, h=h: e.matmul(
                        ps_big[:, h * 128:(h + 1) * 128],
                        k_l[64 * hl:64 * hl + 64, pc * 128:(pc + 1) * 128],
                        p_bf[64 * hl:64 * hl + 64, pc * 128:(pc + 1) * 128],
                        start=True, stop=True, tile_position=(64 * hl, 0)),
                        ins=[k_l[:], p_bf[:]], outs=[f"e_psbig@{h // 4}"],
                        tag=f"{E}_S{h}")
                p.emit("vector", lambda e: e.tensor_mul(Wt[:], ps_big[:], Dt[:]),
                       ins=[Dt[:]] + [f"e_psbig@{q}" for q in range(4)], outs=[Wt[:]],
                       tag=f"{E}_W")
                for h in range(16):
                    pc, hl = h // 2, h % 2
                    p.emit("tensor", lambda e, pc=pc, hl=hl, h=h: e.matmul(
                        ps_Ap[64 * hl:64 * hl + 64, pc * 128:(pc + 1) * 128],
                        Kbe[:, h * 64:(h + 1) * 64], Wt[:, h * 128:(h + 1) * 128],
                        start=True, stop=False, tile_position=(0, 64 * hl)),
                        ins=[Kbe[:], Wt[:]], outs=[f"e_psAp@{h // 8}"], tag=f"{E}_I{h}")
                    p.emit("tensor", lambda e, pc=pc, hl=hl, h=h: e.matmul(
                        ps_Ap[64 * hl:64 * hl + 64, pc * 128:(pc + 1) * 128],
                        Hkkb[64 * hl:64 * hl + 64, h * 64:(h + 1) * 64],
                        dp_bf[64 * hl:64 * hl + 64, pc * 128:(pc + 1) * 128],
                        start=False, stop=False, tile_position=(64 * hl, 64 * hl)),
                        ins=[f"Hb0@{h}", dp_bf[:]], outs=[f"e_psAp@{h // 8}"],
                        tag=f"{E}_H{h}")
                    p.emit("tensor", lambda e, pc=pc, hl=hl, h=h: e.matmul(
                        ps_Ap[64 * hl:64 * hl + 64, pc * 128:(pc + 1) * 128],
                        dlam[64 * hl:64 * hl + 64, h * 64:(h + 1) * 64],
                        p_bf[64 * hl:64 * hl + 64, pc * 128:(pc + 1) * 128],
                        start=False, stop=True, tile_position=(64 * hl, 64 * hl)),
                        ins=[f"dlam@{h}", p_bf[:]], outs=[f"e_psAp@{h // 8}"],
                        tag=f"{E}_L{h}")
                p.emit("vector", lambda e: e.tensor_copy(tsq[:], ps_Ap[:]),
                       ins=[f"e_psAp@{b}" for b in range(2)], outs=["e_tsq"],
                       tag=f"{E}_Ae")
                p.emit("vector", lambda e: e.tensor_mul(tmp[:], pcg[:], tsq[:]),
                       ins=[pcg[:], "e_tsq"], outs=["e_tmp"], tag=f"{E}_pa")
                for pc in range(8):
                    p.emit("tensor", lambda e, pc=pc: e.matmul(
                        ps_sm[0:16, 0:128], hsg[:][:, pc * 16:(pc + 1) * 16],
                        tmp[:, pc * 128:(pc + 1) * 128],
                        start=(pc == 0), stop=(pc == 7)),
                        ins=[hsg[:], "e_tmp"], outs=["e_pssm"],
                        tag=f"{E}_pm{pc}")
                p.emit("vector", lambda e: e.tensor_scalar(
                    sca[:, 0:128], ps_sm[0:16, 0:128], 1e-12, None, ALU.add),
                    ins=["e_pssm"], outs=["e_sca@pap"], tag=f"{E}_ae")
                p.emit("vector", lambda e: e.reciprocal(sca[:, 0:128], sca[:, 0:128]),
                       ins=["e_sca@pap"], outs=["e_sca@pap"], tag=f"{E}_ar")
                p.emit("vector", lambda e, rs_old=rs_old: e.tensor_mul(
                    alf[:, 0:128], rs_old[:], sca[:, 0:128]),
                    ins=[rs_old[:], "e_sca@pap"], outs=["e_alf@a"], tag=f"{E}_am")
                p.emit("vector", lambda e: e.tensor_scalar(
                    alf[:, 128:256], alf[:, 0:128], -1.0, None, ALU.mult),
                    ins=["e_alf@a"], outs=["e_alf@n"], tag=f"{E}_an")
                for pc in range(8):
                    p.emit("tensor", lambda e, pc=pc: e.matmul(
                        ps_big[:, pc * 256:pc * 256 + 256],
                        hb[:][:, pc * 128:(pc + 1) * 128], alf[:],
                        start=True, stop=True),
                        ins=[hb[:], "e_alf@a", "e_alf@n"], outs=[f"e_psbig@{pc // 2}"],
                        tag=f"{E}_ab{pc}")
                for pc in range(8):
                    p.emit("vector", lambda e, pc=pc: e.tensor_mul(
                        tmp[:, pc * 128:(pc + 1) * 128],
                        pcg[:, pc * 128:(pc + 1) * 128],
                        ps_big[:, pc * 256:pc * 256 + 128]),
                        ins=[pcg[:], f"e_psbig@{pc // 2}"], outs=["e_tmp"],
                        tag=f"{E}_xm{pc}")
                    p.emit("vector", lambda e, pc=pc: e.tensor_mul(
                        tsq[:, pc * 128:(pc + 1) * 128],
                        tsq[:, pc * 128:(pc + 1) * 128],
                        ps_big[:, pc * 256 + 128:pc * 256 + 256]),
                        ins=["e_tsq", f"e_psbig@{pc // 2}"], outs=["e_tsq"],
                        tag=f"{E}_rm{pc}")
                p.emit("vector", lambda e: e.tensor_add(xcg[:], xcg[:], tmp[:]),
                       ins=[xcg[:], "e_tmp"],
                       outs=[xcg[:]], tag=f"{E}_xa")
                p.emit("vector", lambda e: e.tensor_add(rcg[:], rcg[:], tsq[:]),
                       ins=[rcg[:], "e_tsq"],
                       outs=[rcg[:]], tag=f"{E}_ra")
                p.emit("scalar", lambda e: e.activation(tsq[:], rcg[:], AF.Square),
                       ins=[rcg[:]], outs=["e_tsq"], tag=f"{E}_rsq")
                for pc in range(8):
                    p.emit("tensor", lambda e, pc=pc: e.matmul(
                        ps_sm[0:16, 128:256], hsg[:][:, pc * 16:(pc + 1) * 16],
                        tsq[:, pc * 128:(pc + 1) * 128],
                        start=(pc == 0), stop=(pc == 7)),
                        ins=[hsg[:], "e_tsq"], outs=["e_pssm"],
                        tag=f"{E}_rn{pc}")
                p.emit("vector", lambda e, rs_new=rs_new: e.tensor_copy(
                    rs_new[:], ps_sm[0:16, 128:256]),
                    ins=["e_pssm"], outs=[rs_new[:]], tag=f"{E}_rne")
                p.emit("vector", lambda e, rs_old=rs_old: e.tensor_scalar(
                    sca[:, 128:256], rs_old[:], 1e-12, None, ALU.add),
                    ins=[rs_old[:]], outs=["e_sca@g"], tag=f"{E}_ge")
                p.emit("vector", lambda e: e.reciprocal(sca[:, 128:256],
                                                        sca[:, 128:256]),
                       ins=["e_sca@g"], outs=["e_sca@g"], tag=f"{E}_gr")
                p.emit("vector", lambda e, rs_new=rs_new: e.tensor_mul(
                    gam[:], rs_new[:], sca[:, 128:256]),
                    ins=[rs_new[:], "e_sca@g"], outs=[gam[:]], tag=f"{E}_gm")
                for pc in range(8):
                    p.emit("tensor", lambda e, pc=pc: e.matmul(
                        ps_big[:, pc * 256:pc * 256 + 128],
                        hb[:][:, pc * 128:(pc + 1) * 128], gam[:],
                        start=True, stop=True),
                        ins=[hb[:], gam[:]], outs=[f"e_psbig@{pc // 2}"], tag=f"{E}_gb{pc}")
                for pc in range(8):
                    p.emit("vector", lambda e, pc=pc: e.tensor_mul(
                        tmp[:, pc * 128:(pc + 1) * 128],
                        pcg[:, pc * 128:(pc + 1) * 128],
                        ps_big[:, pc * 256:pc * 256 + 128]),
                        ins=[pcg[:], f"e_psbig@{pc // 2}"], outs=["e_tmp"],
                        tag=f"{E}_gmul{pc}")
                p.emit("vector", lambda e: e.tensor_add(pcg[:], rcg[:], tmp[:]),
                       ins=[rcg[:], "e_tmp"],
                       outs=[pcg[:]], tag=f"{E}_pa2")
            if DBG and ch == 0:
                p.dma("pool", dbg_qs, xcg[:], ins_extra=[xcg[:]], tag="dbgqs")
                p.dma("pool", dbg_ap, tsq[:], ins_extra=["e_tsq"], tag="dbgap")
            EO = f"e{ch}o"
            p.emit("vector", lambda e: e.tensor_copy(p_bf[:], xcg[:]),
                   ins=[xcg[:]], outs=["e_qs"], tag=f"{EO}_qs")
            p.emit("vector", lambda e: e.tensor_mul(dp_bf[:], p_bf[:], dbl[:]),
                   ins=["e_qs"] + [f"e_dbl@{pc}" for pc in range(8)],
                   outs=["e_dqs"], tag=f"{EO}_dqs")
            for h in range(16):
                pc, hl = h // 2, h % 2
                p.emit("tensor", lambda e, pc=pc, hl=hl, h=h: e.matmul(
                    ps_big[:, h * 128:(h + 1) * 128],
                    k_l[64 * hl:64 * hl + 64, pc * 128:(pc + 1) * 128],
                    p_bf[64 * hl:64 * hl + 64, pc * 128:(pc + 1) * 128],
                    start=True, stop=True, tile_position=(64 * hl, 0)),
                    ins=[k_l[:], "e_qs"], outs=[f"e_psbig@{h // 4}"], tag=f"{EO}_S{h}")
            p.emit("vector", lambda e: e.tensor_mul(Wt[:], ps_big[:], Dt[:]),
                   ins=[Dt[:]] + [f"e_psbig@{q}" for q in range(4)], outs=[Wt[:]],
                   tag=f"{EO}_W")
            for h in range(16):
                pc, hl = h // 2, h % 2
                p.emit("tensor", lambda e, pc=pc, hl=hl, h=h: e.matmul(
                    ps_Ap[64 * hl:64 * hl + 64, pc * 128:(pc + 1) * 128],
                    Vbe[:, h * 64:(h + 1) * 64], Wt[:, h * 128:(h + 1) * 128],
                    start=True, stop=False, tile_position=(0, 64 * hl)),
                    ins=["e_Vbeta", Wt[:]], outs=[f"e_psAp@{h // 8}"], tag=f"{EO}_I{h}")
                p.emit("tensor", lambda e, pc=pc, hl=hl, h=h: e.matmul(
                    ps_Ap[64 * hl:64 * hl + 64, pc * 128:(pc + 1) * 128],
                    Hkvb[64 * hl:64 * hl + 64, h * 64:(h + 1) * 64],
                    dp_bf[64 * hl:64 * hl + 64, pc * 128:(pc + 1) * 128],
                    start=False, stop=True, tile_position=(64 * hl, 64 * hl)),
                    ins=[f"Hb1@{h}", "e_dqs"], outs=[f"e_psAp@{h // 8}"],
                    tag=f"{EO}_H{h}")
            p.emit("vector", lambda e: e.tensor_copy(osb[:], ps_Ap[:]),
                   ins=[f"e_psAp@{b}" for b in range(2)], outs=[osb[:]], tag=f"{EO}_oe")
            if DBG and ch == 0:
                p.dma("pool", dbg_o, osb[:], ins_extra=[osb[:]], tag="dbgo")
            for pc in range(8):
                p.emit("tensor", lambda e, pc=pc, tsl=tsl: e.matmul(
                    ps_sm[:, 0:128], gw2t[:, pc * 128:(pc + 1) * 128],
                    g1T[:, tsl], start=True, stop=True),
                    ins=[gw2t[:], f"d_g1T@{j512}"], outs=["e_pssm"],
                    tag=f"{EO}_g{pc}")
                p.emit("scalar", lambda e, pc=pc: e.activation(
                    tmp[:, pc * 128:(pc + 1) * 128], ps_sm[:, 0:128], AF.Sigmoid),
                    ins=["e_pssm"], outs=["e_tmp"], tag=f"{EO}_ge{pc}")
            p.emit("vector", lambda e: e.tensor_mul(osb[:], osb[:], tmp[:]),
                   ins=[osb[:], "e_tmp"], outs=[osb[:]], tag=f"{EO}_og")
            p.emit("scalar", lambda e: e.activation(tsq[:], osb[:], AF.Square),
                   ins=[osb[:]], outs=["e_tsq"], tag=f"{EO}_q")
            for hf in range(2):
                hsl = slice(hf * 512, (hf + 1) * 512)
                p.emit("tensor", lambda e, hsl=hsl: e.matmul(
                    ps_sm[0:2, 0:512], hs2[:], tsq[:, hsl],
                    start=True, stop=True),
                    ins=[hs2[:], "e_tsq"], outs=["e_pssm"], tag=f"{EO}_s{hf}")
                p.emit("scalar", lambda e: e.activation(sca[0:2, 0:512],
                                                        ps_sm[0:2, 0:512], AF.Sqrt,
                                                        scale=1.0 / 64.0, bias=1e-5),
                       ins=["e_pssm"], outs=["e_sca@og"], tag=f"{EO}_r{hf}")
                p.emit("vector", lambda e: e.reciprocal(sca[0:2, 0:512],
                                                        sca[0:2, 0:512]),
                       ins=["e_sca@og"], outs=["e_sca@og"], tag=f"{EO}_i{hf}")
                p.emit("tensor", lambda e: e.matmul(
                    ps_sm[:, 0:512], hb2[:], sca[0:2, 0:512],
                    start=True, stop=True),
                    ins=[hb2[:], "e_sca@og"], outs=["e_pssm"], tag=f"{EO}_b{hf}")
                p.emit("vector", lambda e, hsl=hsl: e.tensor_mul(
                    osb[:, hsl], osb[:, hsl], ps_sm[:, 0:512]),
                    ins=[osb[:], "e_pssm"], outs=[osb[:]], tag=f"{EO}_n{hf}")
            for pc in range(8):
                p.emit("vector", lambda e, pc=pc: e.tensor_scalar(
                    ogn[:, pc * 128:(pc + 1) * 128],
                    osb[:, pc * 128:(pc + 1) * 128], nwl[:, pc:pc + 1], None,
                    ALU.mult),
                    ins=[osb[:], nwl[:]], outs=[f"e_ogn@{pc}"], tag=f"{EO}_w{pc}")
            for dc in range(NDC):
                for pc in range(8):
                    p.emit("tensor", lambda e, pc=pc, dc=dc: e.matmul(
                        ps_sm[:, 128:256], wo_b[pc][:, dc * 128:(dc + 1) * 128],
                        ogn[:, pc * 128:(pc + 1) * 128],
                        start=(pc == 0), stop=(pc == 7)),
                        ins=[wo_b[pc][:], f"e_ogn@{pc}"], outs=["e_pssm"],
                        tag=f"{EO}_y{dc}_{pc}")
                p.dma("sync", xac[:], x_dram[dc * 128:(dc + 1) * 128, tsl],
                      tag=f"{EO}_lx{dc}")
                p.emit("vector", lambda e: e.tensor_add(xac[:], xac[:],
                                                        ps_sm[:, 128:256]),
                       ins=[xac[:], "e_pssm"], outs=[xac[:]], tag=f"{EO}_ya{dc}")
                p.dma("pool", xout_d[dc * 128:(dc + 1) * 128, tsl], xac[:],
                      tag=f"{EO}_st{dc}")
            ES = f"e{ch}s"
            for (kk, HT, HB, rhs_t, rhreg) in (
                    (0, Hkk, Hkkb, Ktok, None), (1, Hkv, Hkvb, Vtok, None)):
                for h in range(16):
                    p.emit("tensor", lambda e, h=h, rhs_t=rhs_t: e.matmul(
                        ps_sm[:, 0:64],
                        Ksc2[:, h * 128:(h + 1) * 128], rhs_t[:, h * 64:(h + 1) * 64],
                        start=True, stop=True),
                        ins=[Ksc2[:], rhs_t[:]], outs=["e_pssm"],
                        tag=f"{ES}_i{kk}_{h}")
                    p.emit("vector", lambda e, h=h, HT=HT: e.scalar_tensor_tensor(
                        HT[:, h * 64:(h + 1) * 64], HT[:, h * 64:(h + 1) * 64],
                        dec64[:, h:h + 1], ps_sm[:, 0:64], ALU.mult, ALU.add),
                        ins=[f"H{kk}@{h}", dec64[:], "e_pssm"],
                        outs=[f"H{kk}@{h}"], tag=f"{ES}_u{kk}_{h}")
                    p.emit("scalar", lambda e, h=h, HT=HT, HB=HB: e.copy(
                        HB[:, h * 64:(h + 1) * 64], HT[:, h * 64:(h + 1) * 64]),
                        ins=[f"H{kk}@{h}"], outs=[f"Hb{kk}@{h}"], tag=f"{ES}_c{kk}_{h}")
    p.fence()


from contextlib import ExitStack

CONST_ALL = ("c_ones", "c_onesr", "c_eye32", "c_eyebf", "c_hb", "c_hbbf",
             "c_hs2", "c_hb2", "c_hsg", "c_hsgbf", "c_eye2", "c_dup",
             "c_dup0", "c_dup1")
CONST_BLK = {"c_l0mask": ([128, 2048], BF16), "c_l0diag": ([128, 2048], BF16),
             "c_masklehb": ([128, 2048], BF16), "c_heye": ([16, 2048], F32)}


def build_fused():
    nc = bass.Bass("TRN2", target_bir_lowering=False, debug=False)
    x0 = nc.dram_tensor("x", [1024, 4096], F32, kind="ExternalInput").ap()
    xr1 = nc.dram_tensor("xr1", [1024, 4096], F32, kind="Internal").ap()
    xr2 = nc.dram_tensor("xr2", [1024, 4096], F32, kind="Internal").ap()
    xr3 = nc.dram_tensor("xr3", [1024, 4096], F32, kind="Internal").ap()
    xout = nc.dram_tensor("xout", [1024, 4096], F32, kind="ExternalOutput").ap()
    wk = nc.dram_tensor("wk", [1024, 1024], F32, kind="ExternalInput").ap()
    wv = nc.dram_tensor("wv", [1024, 1024], F32, kind="ExternalInput").ap()
    wq = nc.dram_tensor("wq", [1024, 1024], F32, kind="ExternalInput").ap()
    wbeta = nc.dram_tensor("wbeta", [1024, 16], F32, kind="ExternalInput").ap()
    wout = nc.dram_tensor("wout", [1024, 1024], F32, kind="ExternalInput").ap()
    q_dr = nc.dram_tensor("q_sc", [1024, 4096], BF16, kind="Internal").ap()
    k_dr = nc.dram_tensor("k_sc", [1024, 4096], BF16, kind="Internal").ap()
    v_dr = nc.dram_tensor("v_sc", [4096, 1024], BF16, kind="Internal").ap()
    cl_dr = nc.dram_tensor("cl_sc", [16, 4096], F32, kind="Internal").ap()
    bt_dr = nc.dram_tensor("bt_sc", [16, 4096], F32, kind="Internal").ap()
    rw = nc.dram_tensor("rw", [8, 1024], F32, kind="ExternalInput").ap()
    w13 = nc.dram_tensor("w13", [8, 1024, 4096], F32, kind="ExternalInput").ap()
    w2 = nc.dram_tensor("w2", [8, 2048, 1024], F32, kind="ExternalInput").ap()
    sc_dr = nc.dram_tensor("sc_sc", [8, 4096], F32, kind="Internal").ap()
    WD = {}
    for nm, shp in (("wq", [1024, 1024]), ("wk", [1024, 1024]), ("wv", [1024, 1024]),
                    ("wo", [1024, 1024]), ("wf", [1024, 16]), ("wbet", [1024, 16]),
                    ("delta", [16, 1]), ("lamb", [64, 16]), ("nw", [128, 8]),
                    ("gw1", [1024, 64]), ("gw2", [64, 1024])):
        WD[nm] = nc.dram_tensor("w_" + nm, shp, F32, kind="ExternalInput").ap()
    cft_dr = nc.dram_tensor("cft_sc", [4096, 16], F32, kind="Internal").ap()
    cf2_dr = nc.dram_tensor("cf2_sc", [16, 4096], F32, kind="Internal").ap()
    WD["ql_dr"] = nc.dram_tensor("dql_sc", [1024, 4096], BF16, kind="Internal").ap()
    WD["kl_dr"] = nc.dram_tensor("dkl_sc", [1024, 4096], BF16, kind="Internal").ap()
    WD["kt_dr"] = nc.dram_tensor("dkt_sc", [4096, 1024], BF16, kind="Internal").ap()
    WD["vt_dr"] = nc.dram_tensor("dvt_sc", [4096, 1024], BF16, kind="Internal").ap()
    hkk_dr = nc.dram_tensor("hkk_sc", [32, 128, 1024], BF16, kind="Internal").ap()
    hkv_dr = nc.dram_tensor("hkv_sc", [32, 128, 1024], BF16, kind="Internal").ap()
    FD = {}
    for f in "qkv":
        FD[f + "_w1cat"] = nc.dram_tensor(f + "_w1cat", [1024, 1024], F32,
                                          kind="ExternalInput").ap()
        FD[f + "_w2cat"] = nc.dram_tensor(f + "_w2cat", [1024, 1024], F32,
                                          kind="ExternalInput").ap()
        FD[f + "_b1cat"] = nc.dram_tensor(f + "_b1cat", [128, 8], F32,
                                          kind="ExternalInput").ap()
        FD[f + "_b2"] = nc.dram_tensor(f + "_b2", [16, 1024], F32,
                                       kind="ExternalInput").ap()
        FD[f + "_node_w"] = nc.dram_tensor(f + "_node_w", [15, 1024], F32,
                                           kind="ExternalInput").ap()
        FD[f + "_node_b"] = nc.dram_tensor(f + "_node_b", [15, 1], F32,
                                           kind="ExternalInput").ap()
    fq_dr = nc.dram_tensor("fq_sc", [1024, 4096], BF16, kind="Internal").ap()
    fk_dr = nc.dram_tensor("fk_sc", [1024, 4096], BF16, kind="Internal").ap()
    fv_dr = nc.dram_tensor("fv_sc", [4096, 1024], BF16, kind="Internal").ap()

    with ExitStack() as st:
        s = S(nc, st)
        p = Prog(nc)
        C = load_consts(p, s, nc, names=("c_ones", "c_onesr", "c_eye32"))
        pshapes = {"c_eyebf": ([128, 128], BF16), "c_hb": ([16, 1024], F32),
                   "c_hbbf": ([16, 1024], BF16), "c_hs2": ([128, 2], F32),
                   "c_hb2": ([2, 128], F32), "c_hsg": ([128, 128], F32),
                   "c_hsgbf": ([128, 128], BF16),
                   "c_eye2": ([128, 64], F32), "c_dup": ([64, 128], F32),
                   "c_dup0": ([64, 128], F32), "c_dup1": ([64, 128], F32),
                   "c_jgt": ([128, 64], F32)}
        pshapes.update(CONST_BLK)
        blkdr = {nm: nc.dram_tensor(nm, shp, dt, kind="ExternalInput").ap()
                 for nm, (shp, dt) in pshapes.items()}

        def rl(sx, *nms):
            for nm in nms:
                shp, dt = pshapes[nm]
                reload_const(p, sx, C, nm, blkdr[nm], shp, dt)

        with ExitStack() as st1:
            s1 = S(nc, st1)
            rl(s1, "c_hs2", "c_hb2", "c_l0mask", "c_l0diag")
            l0_attn(p, s1, C, nc, x0, wk, wv, wq, wbeta, wout, xr1,
                    q_dr, k_dr, v_dr, cl_dr, bt_dr)
        p.fence()
        with ExitStack() as st2:
            s2 = S(nc, st2)
            rl(s2, "c_jgt")
            xn = l0_ffn(p, s2, C, nc, xr1, rw, w13, w2, xr2, sc_dr)
            l0_ffn_experts(p, s2, C, nc, xn, w13, w2, xr1, xr2, sc_dr)
        p.fence()
        with ExitStack() as st3:
            s3 = S(nc, st3)
            rl(s3, "c_hb", "c_hs2", "c_hb2", "c_hsg", "c_masklehb", "c_heye",
               "c_eye2", "c_dup", "c_dup0", "c_dup1", "c_hbbf", "c_hsgbf")
            xn = l1_attn(p, s3, C, nc, xr2, WD, xr3, cft_dr, cf2_dr)
            tens = (_L1W["wq_b"], _L1W["wk_b"], _L1W["wv_b"], _L1W["wo_b"],
                    _L1W["cfT"], _L1W["btT"], _L1W["cf_tok"], _L1W["bt_tok"],
                    _L1W["g1T"], _L1W["dlam"], _L1W["nwl"], _L1W["Hkk"],
                    _L1W["Hkv"], _L1W["Hkkb"], _L1W["Hkvb"], _L1W["gw2t"])
            QD = {"ql": WD["ql_dr"], "kl": WD["kl_dr"], "kt": WD["kt_dr"],
                  "vt": WD["vt_dr"]}
            l1_attn_pass1(p, s3, C, nc, tens, QD, cft_dr, hkk_dr, hkv_dr)
            l1_attn_pass2v2(p, s3, C, nc, QD, tens, xr2, xr3, cft_dr, cf2_dr,
                            hkk_dr, hkv_dr)
        p.fence()
        with ExitStack() as st4:
            s4 = S(nc, st4)
            rl(s4, "c_eyebf", "c_hbbf")
            l1_ffn_block(p, s4, C, nc, xr3, FD, xout, fq_dr, fk_dr, fv_dr)
        p.finalize_and_emit()
    return nc


def kernel(**ins):
    ncores = 8
    cns = host_consts()
    cns.update(host_consts_ffn())
    xs = [np.ascontiguousarray(np.asarray(ins["L0_attn_x"][b]).T)
          for b in range(ncores)]
    sh = {k: cns[k] for k in CONST_ALL + ("c_jgt", "c_vals", "c_l0mask", "c_l0diag", "c_masklehb", "c_heye")}
    sh.update(wk=ins["L0_attn_Wk"], wv=ins["L0_attn_Wv"], wq=ins["L0_attn_Wq"],
              wbeta=ins["L0_attn_Wbeta"], wout=ins["L0_attn_Wout"],
              rw=ins["L0_ffn_router_w"], w13=ins["L0_ffn_w13"], w2=ins["L0_ffn_w2"],
              w_wq=ins["L1_attn_Wq"], w_wk=ins["L1_attn_Wk"], w_wv=ins["L1_attn_Wv"],
              w_wo=ins["L1_attn_Wo"], w_wf=ins["L1_attn_Wf"],
              w_wbet=ins["L1_attn_Wbet"],
              w_delta=np.ascontiguousarray(
                  np.asarray(ins["L1_attn_delta"]).reshape(16, 1)),
              w_lamb=np.ascontiguousarray(
                  np.asarray(ins["L1_attn_lamb_params"]).reshape(16, 64).T),
              w_nw=np.ascontiguousarray(
                  np.asarray(ins["L1_attn_norm_weight"]).reshape(8, 128).T),
              w_gw1=ins["L1_attn_gate_w1"], w_gw2=ins["L1_attn_gate_w2"])
    for f in "qkv":
        w1 = np.asarray(ins[f"L1_ffn_{f}_w1"])
        w2f = np.asarray(ins[f"L1_ffn_{f}_w2"])
        b1 = np.asarray(ins[f"L1_ffn_{f}_b1"])
        sh[f + "_w1cat"] = np.ascontiguousarray(
            w1.transpose(1, 0, 2).reshape(1024, 1024))
        sh[f + "_w2cat"] = np.ascontiguousarray(w2f.reshape(1024, 1024))
        sh[f + "_b1cat"] = np.ascontiguousarray(b1.reshape(-1).reshape(8, 128).T)
        sh[f + "_b2"] = ins[f"L1_ffn_{f}_b2"]
        sh[f + "_node_w"] = ins[f"L1_ffn_{f}_node_w"]
        sh[f + "_node_b"] = np.ascontiguousarray(
            np.asarray(ins[f"L1_ffn_{f}_node_b"]).reshape(15, 1))
    nc = build_fused()
    maps = [dict(sh, x=x) for x in xs]
    res = run_bass_kernel_spmd(nc, maps, core_ids=list(range(ncores)))
    return np.stack([r["xout"].T for r in res.results]).astype(np.float32)


def l1_attn_pass1(p, s, C, nc, tens, QD, cft_dr, hkk_dr, hkv_dr):
    from contextlib import ExitStack
    onesr = C["c_onesr"]
    (wq_b, wk_b, wv_b, wo_b, cfT, btT, cf_tok, bt_tok, g1T, dlam, nwl,
     Hkk, Hkv, Hkkb, Hkvb, gw2t) = tens
    HSEG = lambda t: t.rearrange("p (h j) -> p h j", j=64)
    with ExitStack() as st:
        s1 = S(nc, st)
        Hkk = s1.sb("d_Hkk", [128, 16 * 64])
        Hkv = s1.sb("d_Hkv", [128, 16 * 64])
        Hkkb = s1.sb("d_Hkkb", [128, 16 * 64], BF16)
        Hkvb = s1.sb("d_Hkvb", [128, 16 * 64], BF16)
        for kk, tt in ((0, Hkk), (1, Hkv)):
            p.emit("pool", lambda e, tt=tt: e.memset(tt[:], 0.0),
                   outs=[f"H{kk}@{h}" for h in range(16)], tag=f"d_zH{kk}")
        for kk, tt in ((0, Hkkb), (1, Hkvb)):
            p.emit("pool", lambda e, tt=tt: e.memset(tt[:], 0.0),
                   outs=[f"Hb{kk}@{h}" for h in range(16)], tag=f"d_zHb{kk}")
        Ktok = s1.sb("s1_Kt", [128, 1024], BF16)
        Vtok = s1.sb("s1_Vt", [128, 1024], BF16)
        Ksc = s1.sb("s1_Ks", [128, 1024], BF16)
        Ksc2 = s1.sb("s1_Ks2", [128, 2048], BF16)
        wex = s1.sb("s1_wex", [128, 16])
        cfr = s1.sb("s1_cfr", [1, 32])
        dec64 = s1.sb("s1_dec", [128, 16])
        ps_sm = s1.ps("s1_ps", [128, 512])
        for ch in range(NCH):
            tsl = slice(ch * 128, (ch + 1) * 128)
            b5 = p.open_batch("sync")
            p.dma("sync", Ktok[:], QD["kt"][tsl, :], tag=f"s1{ch}_lK", batch=b5)
            p.dma("sync", Vtok[:], QD["vt"][tsl, :], tag=f"s1{ch}_lV", batch=b5)
            p.dma("sync", cfr[:, 0:16], cft_dr[ch * 128 + 127: ch * 128 + 128, :],
                  tag=f"s1{ch}_lce")
            if ch > 0:
                p.dma("sync", cfr[:, 16:32], cft_dr[ch * 128 - 1: ch * 128, :],
                      tag=f"s1{ch}_lcs")
            else:
                p.emit("pool", lambda e: e.memset(cfr[:, 16:32], 0.0),
                       outs=[cfr[:]], tag=f"s1{ch}_zcs")
            p.dma("pool", hkk_dr[ch], Hkkb[:], ins_extra=[f"Hb0@{h}" for h in range(16)],
                  tag=f"s1{ch}_sK")
            p.dma("pool", hkv_dr[ch], Hkvb[:], ins_extra=[f"Hb1@{h}" for h in range(16)],
                  tag=f"s1{ch}_sV")
            p.emit("tensor", lambda e: e.matmul(ps_sm[:, 0:16], onesr[0:1, :],
                                                cfr[:, 0:16], start=True, stop=True),
                   ins=[onesr[:], cfr[:]], outs=["s1_ps"], tag=f"s1{ch}_cb")
            p.emit("vector", lambda e, ch=ch: e.tensor_sub(
                wex[:], ps_sm[:, 0:16], cf_tok[:, ch * 16:(ch + 1) * 16]),
                ins=["s1_ps", f"d_cftok@{ch}"], outs=["s1_wex"], tag=f"s1{ch}_wd")
            p.emit("scalar", lambda e: e.activation(wex[:], wex[:], AF.Exp),
                   ins=["s1_wex"], outs=["s1_wex"], tag=f"s1{ch}_we")
            p.emit("vector", lambda e, ch=ch: e.tensor_tensor(
                HSEG(Ksc[:]), HSEG(Ktok[:]),
                bt_tok[:, ch * 16:(ch + 1) * 16].unsqueeze(2)
                .broadcast_to([128, 16, 64]), ALU.mult),
                ins=[Ktok[:], f"d_bttok@{ch}"], outs=[Ksc[:]], tag=f"s1{ch}_Kb")
            p.emit("vector", lambda e: e.tensor_tensor(
                HSEG(Ksc[:]), HSEG(Ksc[:]),
                wex[:].unsqueeze(2).broadcast_to([128, 16, 64]), ALU.mult),
                ins=[Ksc[:], "s1_wex"], outs=[Ksc[:]], tag=f"s1{ch}_Kw")
            p.emit("vector", lambda e: e.tensor_copy(
                Ksc2[:].rearrange("p (h d j) -> p h d j", d=2, j=64),
                HSEG(Ksc[:]).unsqueeze(2).broadcast_to([128, 16, 2, 64])),
                ins=[Ksc[:]], outs=[Ksc2[:]], tag=f"s1{ch}_K2")
            p.emit("vector", lambda e: e.tensor_sub(cfr[:, 0:16], cfr[:, 0:16],
                                                    cfr[:, 16:32]),
                   ins=[cfr[:]], outs=[cfr[:]], tag=f"s1{ch}_dd")
            p.emit("scalar", lambda e: e.activation(cfr[:, 0:16], cfr[:, 0:16], AF.Exp),
                   ins=[cfr[:]], outs=[cfr[:]], tag=f"s1{ch}_de")
            p.emit("tensor", lambda e: e.matmul(ps_sm[:, 16:32], onesr[0:1, :],
                                                cfr[:, 0:16], start=True, stop=True),
                   ins=[onesr[:], cfr[:]], outs=["s1_ps"], tag=f"s1{ch}_db")
            p.emit("vector", lambda e: e.tensor_copy(dec64[:], ps_sm[:, 16:32]),
                   ins=["s1_ps"], outs=[dec64[:]], tag=f"s1{ch}_d2")
            for kk, HT, HB, rhs_t in ((0, Hkk, Hkkb, Ktok), (1, Hkv, Hkvb, Vtok)):
                for hh in range(0, 16, 2):
                    for h in (hh, hh + 1):
                        col = 0 if h % 2 == 0 else 64
                        p.emit("tensor", lambda e, h=h, rhs_t=rhs_t, col=col:
                               e.matmul(ps_sm[:, 64 + col:128 + col],
                                        Ksc2[:, h * 128:(h + 1) * 128],
                                        rhs_t[:, h * 64:(h + 1) * 64],
                                        start=True, stop=True),
                               ins=[Ksc2[:], rhs_t[:]], outs=["s1_ps"],
                               tag=f"s1{ch}_i{kk}_{h}")
                    for h in (hh, hh + 1):
                        col = 0 if h % 2 == 0 else 64
                        p.emit("vector", lambda e, h=h, HT=HT, col=col:
                               e.scalar_tensor_tensor(
                            HT[:, h * 64:(h + 1) * 64], HT[:, h * 64:(h + 1) * 64],
                            dec64[:, h:h + 1], ps_sm[:, 64 + col:128 + col],
                            ALU.mult, ALU.add),
                            ins=[f"H{kk}@{h}", dec64[:], "s1_ps"],
                            outs=[f"H{kk}@{h}"], tag=f"s1{ch}_u{kk}_{h}")
                        p.emit("scalar", lambda e, h=h, HT=HT, HB=HB: e.copy(
                            HB[:, h * 64:(h + 1) * 64], HT[:, h * 64:(h + 1) * 64]),
                            ins=[f"H{kk}@{h}"], outs=[f"Hb{kk}@{h}"],
                            tag=f"s1{ch}_c{kk}_{h}")
    p.fence()


def build_l0_attn():
    nc = bass.Bass("TRN2", target_bir_lowering=False, debug=False)
    x = nc.dram_tensor("x", [1024, 4096], F32, kind="ExternalInput").ap()
    wk = nc.dram_tensor("wk", [1024, 1024], F32, kind="ExternalInput").ap()
    wv = nc.dram_tensor("wv", [1024, 1024], F32, kind="ExternalInput").ap()
    wq = nc.dram_tensor("wq", [1024, 1024], F32, kind="ExternalInput").ap()
    wbeta = nc.dram_tensor("wbeta", [1024, 16], F32, kind="ExternalInput").ap()
    wout = nc.dram_tensor("wout", [1024, 1024], F32, kind="ExternalInput").ap()
    xout = nc.dram_tensor("xout", [1024, 4096], F32, kind="ExternalOutput").ap()
    q_dr = nc.dram_tensor("q_sc", [1024, 4096], BF16, kind="Internal").ap()
    k_dr = nc.dram_tensor("k_sc", [1024, 4096], BF16, kind="Internal").ap()
    v_dr = nc.dram_tensor("v_sc", [4096, 1024], BF16, kind="Internal").ap()
    cl_dr = nc.dram_tensor("cl_sc", [16, 4096], F32, kind="Internal").ap()
    bt_dr = nc.dram_tensor("bt_sc", [16, 4096], F32, kind="Internal").ap()
    with ExitStack() as st:
        s = S(nc, st)
        p = Prog(nc)
        C = load_consts(p, s, nc, names=("c_ones", "c_onesr", "c_eye32",
                                         "c_hs2", "c_hb2",
                                         "c_l0mask", "c_l0diag"))
        l0_attn(p, s, C, nc, x, wk, wv, wq, wbeta, wout, xout,
                q_dr, k_dr, v_dr, cl_dr, bt_dr)
        p.finalize_and_emit()
    return nc


def build_l0_ffn():
    nc = bass.Bass("TRN2", target_bir_lowering=False, debug=False)
    x = nc.dram_tensor("x", [1024, 4096], F32, kind="ExternalInput").ap()
    rw = nc.dram_tensor("rw", [8, 1024], F32, kind="ExternalInput").ap()
    w13 = nc.dram_tensor("w13", [8, 1024, 4096], F32, kind="ExternalInput").ap()
    w2 = nc.dram_tensor("w2", [8, 2048, 1024], F32, kind="ExternalInput").ap()
    xout = nc.dram_tensor("xout", [1024, 4096], F32, kind="ExternalOutput").ap()
    sc_dr = nc.dram_tensor("sc_sc", [8, 4096], F32, kind="Internal").ap()
    with ExitStack() as st:
        s = S(nc, st)
        p = Prog(nc)
        C = load_consts(p, s, nc, names=("c_ones", "c_onesr", "c_eye32"))
        C.update(load_extra_consts(p, s, nc, {"c_jgt": ([128, 64], F32)}))
        xn = l0_ffn(p, s, C, nc, x, rw, w13, w2, xout, sc_dr)
        l0_ffn_experts(p, s, C, nc, xn, w13, w2, x, xout, sc_dr)
        p.finalize_and_emit()
    return nc


def build_l1_attn():
    nc = bass.Bass("TRN2", target_bir_lowering=False, debug=False)
    x = nc.dram_tensor("x", [1024, 4096], F32, kind="ExternalInput").ap()
    WD = {}
    for nm, shp in (("wq", [1024, 1024]), ("wk", [1024, 1024]), ("wv", [1024, 1024]),
                    ("wo", [1024, 1024]), ("wf", [1024, 16]), ("wbet", [1024, 16]),
                    ("delta", [16, 1]), ("lamb", [64, 16]), ("nw", [128, 8]),
                    ("gw1", [1024, 64]), ("gw2", [64, 1024])):
        WD[nm] = nc.dram_tensor("w_" + nm, shp, F32, kind="ExternalInput").ap()
    xout = nc.dram_tensor("xout", [1024, 4096], F32, kind="ExternalOutput").ap()
    cft_dr = nc.dram_tensor("cft_sc", [4096, 16], F32, kind="Internal").ap()
    cf2_dr = nc.dram_tensor("cf2_sc", [16, 4096], F32, kind="Internal").ap()
    WD["ql_dr"] = nc.dram_tensor("dql_sc", [1024, 4096], BF16, kind="Internal").ap()
    WD["kl_dr"] = nc.dram_tensor("dkl_sc", [1024, 4096], BF16, kind="Internal").ap()
    WD["kt_dr"] = nc.dram_tensor("dkt_sc", [4096, 1024], BF16, kind="Internal").ap()
    WD["vt_dr"] = nc.dram_tensor("dvt_sc", [4096, 1024], BF16, kind="Internal").ap()
    with ExitStack() as st:
        s = S(nc, st)
        p = Prog(nc)
        C = load_consts(p, s, nc, names=("c_ones", "c_onesr", "c_eye32", "c_hb",
                                         "c_hs2", "c_hb2", "c_hsg", "c_masklehb",
                                         "c_heye", "c_eye2", "c_dup"))
        xn = l1_attn(p, s, C, nc, x, WD, xout, cft_dr, cf2_dr)
        tens = (_L1W["wq_b"], _L1W["wk_b"], _L1W["wv_b"], _L1W["wo_b"],
                _L1W["cfT"], _L1W["btT"], _L1W["cf_tok"], _L1W["bt_tok"],
                _L1W["g1T"], _L1W["dlam"], _L1W["nwl"], _L1W["Hkk"],
                _L1W["Hkv"], _L1W["Hkkb"], _L1W["Hkvb"], _L1W["gw2t"])
        QD = {"ql": WD["ql_dr"], "kl": WD["kl_dr"], "kt": WD["kt_dr"],
              "vt": WD["vt_dr"]}
        l1_attn_chunk_loop(p, s, C, nc, QD, tens, x, xout, cft_dr, cf2_dr)
        p.finalize_and_emit()
    return nc


def build_l1_ffn():
    nc = bass.Bass("TRN2", target_bir_lowering=False, debug=False)
    x = nc.dram_tensor("x", [1024, 4096], F32, kind="ExternalInput").ap()
    FD = {}
    for f in "qkv":
        FD[f + "_w1cat"] = nc.dram_tensor(f + "_w1cat", [1024, 1024], F32,
                                          kind="ExternalInput").ap()
        FD[f + "_w2cat"] = nc.dram_tensor(f + "_w2cat", [1024, 1024], F32,
                                          kind="ExternalInput").ap()
        FD[f + "_b1cat"] = nc.dram_tensor(f + "_b1cat", [128, 8], F32,
                                          kind="ExternalInput").ap()
        FD[f + "_b2"] = nc.dram_tensor(f + "_b2", [16, 1024], F32,
                                       kind="ExternalInput").ap()
        FD[f + "_node_w"] = nc.dram_tensor(f + "_node_w", [15, 1024], F32,
                                           kind="ExternalInput").ap()
        FD[f + "_node_b"] = nc.dram_tensor(f + "_node_b", [15, 1], F32,
                                           kind="ExternalInput").ap()
    xout = nc.dram_tensor("xout", [1024, 4096], F32, kind="ExternalOutput").ap()
    q_dr = nc.dram_tensor("fq_sc", [1024, 4096], BF16, kind="Internal").ap()
    k_dr = nc.dram_tensor("fk_sc", [1024, 4096], BF16, kind="Internal").ap()
    v_dr = nc.dram_tensor("fv_sc", [4096, 1024], BF16, kind="Internal").ap()
    with ExitStack() as st:
        s = S(nc, st)
        p = Prog(nc)
        C = load_consts(p, s, nc, names=("c_ones", "c_onesr", "c_eye32", "c_eyebf",
                                         "c_hbbf"))
        l1_ffn_block(p, s, C, nc, x, FD, xout, q_dr, k_dr, v_dr)
        p.finalize_and_emit()
    return nc


def kernel_unfused(**ins):
    ncores = 8
    cns = host_consts()
    cns.update(host_consts_ffn())
    xs = [np.ascontiguousarray(np.asarray(ins["L0_attn_x"][b]).T)
          for b in range(ncores)]

    def run(nc, per_core, shared):
        maps = [dict(shared, **pc) for pc in per_core]
        res = run_bass_kernel_spmd(nc, maps, core_ids=list(range(ncores)))
        return [r["xout"] for r in res.results]

    sh = {k: cns[k] for k in ("c_ones", "c_onesr", "c_eye32", "c_hs2", "c_hb2",
                              "c_l0mask", "c_l0diag", "c_vals")}
    sh.update(wk=ins["L0_attn_Wk"], wv=ins["L0_attn_Wv"], wq=ins["L0_attn_Wq"],
              wbeta=ins["L0_attn_Wbeta"], wout=ins["L0_attn_Wout"])
    xs = run(build_l0_attn(), [{"x": x} for x in xs], sh)

    sh = {k: cns[k] for k in ("c_ones", "c_onesr", "c_eye32", "c_jgt", "c_vals")}
    sh.update(rw=ins["L0_ffn_router_w"], w13=ins["L0_ffn_w13"], w2=ins["L0_ffn_w2"])
    xs = run(build_l0_ffn(), [{"x": x} for x in xs], sh)

    sh = {k: cns[k] for k in ("c_ones", "c_onesr", "c_eye32", "c_hb", "c_hs2",
                              "c_hb2", "c_hsg", "c_masklehb", "c_heye",
                              "c_eye2", "c_dup", "c_vals")}
    sh.update(w_wq=ins["L1_attn_Wq"], w_wk=ins["L1_attn_Wk"], w_wv=ins["L1_attn_Wv"],
              w_wo=ins["L1_attn_Wo"], w_wf=ins["L1_attn_Wf"],
              w_wbet=ins["L1_attn_Wbet"],
              w_delta=np.ascontiguousarray(
                  np.asarray(ins["L1_attn_delta"]).reshape(16, 1)),
              w_lamb=np.ascontiguousarray(
                  np.asarray(ins["L1_attn_lamb_params"]).reshape(16, 64).T),
              w_nw=np.ascontiguousarray(
                  np.asarray(ins["L1_attn_norm_weight"]).reshape(8, 128).T),
              w_gw1=ins["L1_attn_gate_w1"], w_gw2=ins["L1_attn_gate_w2"])
    xs = run(build_l1_attn(), [{"x": x} for x in xs], sh)

    sh = {k: cns[k] for k in ("c_ones", "c_onesr", "c_eye32", "c_eyebf",
                              "c_hbbf", "c_vals")}
    for f in "qkv":
        w1 = np.asarray(ins[f"L1_ffn_{f}_w1"])
        w2 = np.asarray(ins[f"L1_ffn_{f}_w2"])
        b1 = np.asarray(ins[f"L1_ffn_{f}_b1"])
        sh[f + "_w1cat"] = np.ascontiguousarray(
            w1.transpose(1, 0, 2).reshape(1024, 1024))
        sh[f + "_w2cat"] = np.ascontiguousarray(w2.reshape(1024, 1024))
        sh[f + "_b1cat"] = np.ascontiguousarray(b1.reshape(-1).reshape(8, 128).T)
        sh[f + "_b2"] = ins[f"L1_ffn_{f}_b2"]
        sh[f + "_node_w"] = ins[f"L1_ffn_{f}_node_w"]
        sh[f + "_node_b"] = np.ascontiguousarray(
            np.asarray(ins[f"L1_ffn_{f}_node_b"]).reshape(15, 1))
    xs = run(build_l1_ffn(), [{"x": x} for x in xs], sh)

    return np.stack([x.T for x in xs]).astype(np.float32)




def l1_attn_pass2(p, s, C, nc, QD, tens, x_dram, xout_d, cft_dr, cf2_dr,
                  hkk_dr, hkv_dr, lamb_b):
    from contextlib import ExitStack
    ones = C["c_ones"]; onesr = C["c_onesr"]
    hb = C["c_hb"]; hsg = C["c_hsg"]; mle = C["c_masklehb"]; heye = C["c_heye"]
    (wq_b, wk_b, wv_b, wo_b, cfT, btT, cf_tok, bt_tok, g1T, dlam, nwl,
     Hkk, Hkv, Hkkb, Hkvb, gw2t) = tens
    HSEG = lambda t: t.rearrange("p (h j) -> p h j", j=64)
    with ExitStack() as st:
        s3 = S(nc, st)
        d = {}
        d["ps_S"] = s3.ps("g0_psS", [128, 1024])
        d["ps_Ap"] = s3.ps("g0_psAp", [128, 1024])
        d["ps_sm"] = s3.ps("g0_pssm", [128, 512])
        for nm, shp, dt in (
                ("q_l", [128, 1024], BF16), ("k_l", [128, 1024], BF16),
                ("Kbe", [128, 1024], BF16), ("Vbe", [128, 1024], BF16),
                ("Hkkb", [128, 1024], BF16), ("Hkvb", [128, 1024], BF16),
                ("Dt", [128, 2048], BF16), ("dbl", [128, 1024], BF16),
                ("Wt", [128, 2048], BF16), ("x", [128, 1024], F32),
                ("r", [128, 1024], F32), ("pp", [128, 1024], F32),
                ("pbf", [128, 1024], BF16), ("dpbf", [128, 1024], BF16),
                ("Ap", [128, 1024], F32), ("tmp", [128, 1024], F32),
                ("tsq", [128, 1024], F32), ("rs0", [16, 128], F32),
                ("rs1", [16, 128], F32), ("alf", [16, 128], F32),
                ("gam", [16, 128], F32), ("sca", [16, 512], F32),
                ("wex", [128, 16], F32), ("cfr", [1, 32], F32),
                ("nst", [16, 128], F32), ("ncb", [128, 1], F32),
                ("cfc", [16, 129], F32), ("cfl", [1, 512], F32),
                ("osb", [128, 1024], F32), ("xac", [128, 128], F32)):
            d[nm] = s3.sb(f"g0_{nm}", shp, dt)

        _RMAP = {"tmpD": "tmp", "tmpP": "tmp", "tmpX": "tmp", "tmpG": "tmp",
                 "tmpgat": "tmp", "tmpL": "tmp", "tsqR": "tsq", "tsqS": "tsq",
                 "tsqO": "tsq", "scaA": "sca", "scaB": "sca", "scaO": "sca",
                 "Wt0": "Wt", "Wt1": "Wt", "Ap0": "Ap", "ApL": "Ap",
                 "os0": "osb", "ognB": "pbf", "cfc0": "cfc"}

        def R(dd, nm):
            return f"g0_{_RMAP.get(nm, nm)}"

        def ap_out(h):
            pc, hl = h // 2, h % 2
            c0 = hl * 512 + (pc % 4) * 128
            return (d["ps_Ap"][64 * hl:64 * hl + 64, c0:c0 + 128],
                    f"g0_psAp#{hl}")

        def emit_prep(ch):
            tsl = slice(ch * 128, (ch + 1) * 128)
            P = f"P0_{ch}"
            b5 = p.open_batch("sync")
            p.dma("sync", d["q_l"][:].rearrange("p (pc t) -> p pc t", t=128),
                  QD["ql"][:, tsl].rearrange("(pc p) t -> p pc t", p=128),
                  tag=f"{P}_lq", batch=b5)
            p.dma("sync", d["k_l"][:].rearrange("p (pc t) -> p pc t", t=128),
                  QD["kl"][:, tsl].rearrange("(pc p) t -> p pc t", p=128),
                  tag=f"{P}_lk", batch=b5)
            p.dma("sync", d["Kbe"][:], QD["kt"][tsl, :], tag=f"{P}_lKt", batch=b5)
            p.dma("sync", d["Vbe"][:], QD["vt"][tsl, :], tag=f"{P}_lVt", batch=b5)
            p.dma("sync", d["Hkkb"][:], hkk_dr[ch], tag=f"{P}_lHk", batch=b5)
            p.dma("sync", d["Hkvb"][:], hkv_dr[ch], tag=f"{P}_lHv", batch=b5)
            p.emit("vector", lambda e, ch=ch: e.tensor_tensor(
                HSEG(d["Kbe"][:]), HSEG(d["Kbe"][:]),
                bt_tok[:, ch * 16:(ch + 1) * 16].unsqueeze(2)
                .broadcast_to([128, 16, 64]), ALU.mult),
                ins=[d["Kbe"][:], f"d_bttok@{ch}"], outs=[d["Kbe"][:]],
                tag=f"{P}_Kb")
            p.emit("vector", lambda e, ch=ch: e.tensor_tensor(
                HSEG(d["Vbe"][:]), HSEG(d["Vbe"][:]),
                bt_tok[:, ch * 16:(ch + 1) * 16].unsqueeze(2)
                .broadcast_to([128, 16, 64]), ALU.mult),
                ins=[d["Vbe"][:], f"d_bttok@{ch}"], outs=[d["Vbe"][:]],
                tag=f"{P}_Vb")
            if ch > 0:
                p.dma("sync", d["cfc"][:], cf2_dr[:, ch * 128 - 1: ch * 128 + 128],
                      tag=f"{P}_lcfc")
            else:
                p.emit("pool", lambda e: e.memset(d["cfc"][:, 0:1], 0.0),
                       outs=[R(d, "cfc0")], tag=f"{P}_zc")
                p.dma("sync", d["cfc"][:, 1:129], cf2_dr[:, 0:128],
                      ins_extra=[R(d, "cfc0")], tag=f"{P}_lcfc")
            p.emit("vector", lambda e: e.tensor_scalar(
                d["nst"][:], d["cfc"][:, 1:129], -1.0, None, ALU.mult),
                ins=[d["cfc"][:]], outs=[d["nst"][:]], tag=f"{P}_nst")
            for q4 in range(4):
                qsl = slice((q4 % 2) * 512, (q4 % 2) * 512 + 512)
                dsl = slice(q4 * 512, (q4 + 1) * 512)
                p.dma("sync", d["cfl"][:].rearrange("p (x b) -> p x b", b=128),
                      cf2_dr[4 * q4:4 * q4 + 4,
                             ch * 128:(ch + 1) * 128].unsqueeze(0),
                      tag=f"{P}_lcfl{q4}")
                p.emit("tensor", lambda e, qsl=qsl: e.matmul(
                    d["ps_S"][:, qsl], onesr[0:1, :], d["cfl"][:],
                    start=True, stop=False),
                    ins=[onesr[:], d["cfl"][:]], outs=[f"g0_psS#{qsl.start}"],
                    tag=f"{P}_A{q4}")
                p.emit("tensor", lambda e, qsl=qsl, dsl=dsl: e.matmul(
                    d["ps_S"][:, qsl], d["nst"][:], heye[:][:, dsl],
                    start=False, stop=True),
                    ins=[d["nst"][:], heye[:]], outs=[f"g0_psS#{qsl.start}"],
                    tag=f"{P}_B{q4}")
                p.emit("vector", lambda e, qsl=qsl, dsl=dsl:
                       e.scalar_tensor_tensor(
                    d["tmp"][:, 0:512], d["ps_S"][:, qsl], 0.0, mle[:][:, dsl],
                    ALU.min, ALU.add),
                    ins=[mle[:], f"g0_psS#{qsl.start}"],
                    outs=[R(d, "tmpD")], tag=f"{P}_Dm{q4}")
                p.emit("scalar", lambda e, dsl=dsl: e.activation(
                    d["Dt"][:, dsl], d["tmp"][:, 0:512], AF.Exp),
                    ins=[R(d, "tmpD")], outs=[d["Dt"][:]], tag=f"{P}_De{q4}")
            for pc in range(8):
                p.emit("tensor", lambda e, pc=pc: e.matmul(
                    d["ps_sm"][:, 128:256], hb[:][:, pc * 128:(pc + 1) * 128],
                    d["cfc"][:, 1:129], start=True, stop=True),
                    ins=[hb[:], d["cfc"][:]], outs=["g0_pssm"], tag=f"{P}_db{pc}")
                p.emit("tensor", lambda e, pc=pc: e.matmul(
                    d["ps_sm"][:, 384:385], hb[:][:, pc * 128:(pc + 1) * 128],
                    d["cfc"][:, 0:1], start=True, stop=True),
                    ins=[hb[:], d["cfc"][:]], outs=["g0_pssm"], tag=f"{P}_ns{pc}")
                p.emit("vector", lambda e: e.tensor_scalar(
                    d["ncb"][:], d["ps_sm"][:, 384:385], -1.0, None, ALU.mult),
                    ins=["g0_pssm"], outs=[d["ncb"][:]], tag=f"{P}_nn{pc}")
                p.emit("scalar", lambda e, pc=pc: e.activation(
                    d["dbl"][:, pc * 128:(pc + 1) * 128], d["ps_sm"][:, 128:256],
                    AF.Exp, bias=d["ncb"][:]),
                    ins=["g0_pssm", d["ncb"][:]], outs=[d["dbl"][:]],
                    tag=f"{P}_dbe{pc}")
            p.emit("pool", lambda e: e.memset(d["x"][:], 0.0), outs=[d["x"][:]],
                   tag=f"{P}_x0")
            p.emit("vector", lambda e: e.tensor_copy(d["r"][:], d["q_l"][:]),
                   ins=[d["q_l"][:]], outs=[d["r"][:]], tag=f"{P}_r0")
            p.emit("vector", lambda e: e.tensor_copy(d["pp"][:], d["q_l"][:]),
                   ins=[d["q_l"][:]], outs=[d["pp"][:]], tag=f"{P}_p0")
            p.emit("scalar", lambda e: e.activation(d["tsq"][:], d["r"][:],
                                                    AF.Square),
                   ins=[d["r"][:]], outs=[R(d, "tsqS")], tag=f"{P}_sq0")
            for pc in range(8):
                p.emit("tensor", lambda e, pc=pc: e.matmul(
                    d["ps_sm"][0:16, 0:128], hsg[:][:, pc * 16:(pc + 1) * 16],
                    d["tsq"][:, pc * 128:(pc + 1) * 128],
                    start=(pc == 0), stop=(pc == 7)),
                    ins=[hsg[:], R(d, "tsqS")], outs=["g0_pssm"], tag=f"{P}_r0m{pc}")
            p.emit("vector", lambda e: e.tensor_copy(d["rs0"][:],
                                                     d["ps_sm"][0:16, 0:128]),
                   ins=["g0_pssm"], outs=[d["rs0"][:]], tag=f"{P}_rs0")

        def emit_SW(E, src_bf, src_rg):
            for half in range(2):
                for h in range(half * 8, half * 8 + 8):
                    pc, hl = h // 2, h % 2
                    scol = hl * 512 + ((h - half * 8) // 2) * 128
                    p.emit("tensor", lambda e, pc=pc, hl=hl, scol=scol,
                           src_bf=src_bf: e.matmul(
                        d["ps_S"][:, scol:scol + 128],
                        d["k_l"][64 * hl:64 * hl + 64, pc * 128:(pc + 1) * 128],
                        src_bf[64 * hl:64 * hl + 64, pc * 128:(pc + 1) * 128],
                        start=True, stop=True, tile_position=(64 * hl, 0)),
                        ins=[d["k_l"][:], src_rg],
                        outs=[f"g0_psS#{(scol // 512) * 512}"],
                        tag=f"{E}_S{h}")
                for b in range(2):
                    base_h = half * 8 + b
                    p.emit("vector", lambda e, half=half, b=b, base_h=base_h:
                           e.tensor_mul(
                        d["Wt"][:].rearrange("p (h t) -> p h t", t=128)
                        [:, base_h:half * 8 + 8:2, :],
                        d["ps_S"][:, b * 512:(b + 1) * 512]
                        .rearrange("p (h t) -> p h t", t=128),
                        d["Dt"][:].rearrange("p (h t) -> p h t", t=128)
                        [:, base_h:half * 8 + 8:2, :]),
                        ins=[d["Dt"][:], f"g0_psS#{b * 512}"],
                        outs=[R(d, f"Wt{half}")], tag=f"{E}_W{half}_{b}")

        def emit_ap(E, lhs2, rhs2_bf, hm_t, tagp):
            for half in range(2):
                for hh in range(half * 8, half * 8 + 8, 2):
                    for h in (hh, hh + 1):
                        o_ap, o_rg = ap_out(h)
                        p.emit("tensor", lambda e, h=h, o_ap=o_ap, lhs2=lhs2:
                               e.matmul(
                            o_ap, lhs2[:, h * 64:(h + 1) * 64],
                            d["Wt"][:, h * 128:(h + 1) * 128],
                            start=True, stop=False,
                            tile_position=(0, 64 * (h % 2))),
                            ins=[lhs2[:], R(d, f"Wt{half}")], outs=[o_rg],
                            tag=f"{E}_{tagp}I{h}")
                    for h in (hh, hh + 1):
                        pc, hl = h // 2, h % 2
                        o_ap, o_rg = ap_out(h)
                        p.emit("tensor", lambda e, h=h, hl=hl, pc=pc, o_ap=o_ap,
                               hm_t=hm_t, rhs2_bf=rhs2_bf: e.matmul(
                            o_ap, hm_t[64 * hl:64 * hl + 64, h * 64:(h + 1) * 64],
                            rhs2_bf[64 * hl:64 * hl + 64, pc * 128:(pc + 1) * 128],
                            start=False, stop=True,
                            tile_position=(64 * hl, 64 * hl)),
                            ins=[hm_t[:], d["dpbf"][:]], outs=[o_rg],
                            tag=f"{E}_{tagp}H{h}")
                hof = half * 512
                p.emit("vector", lambda e, hof=hof: e.tensor_copy(
                    d["Ap"][0:64, hof:hof + 512], d["ps_Ap"][0:64, 0:512]),
                    ins=["g0_psAp#0"], outs=[R(d, "Ap0")], tag=f"{E}_{tagp}e0{half}")
                p.emit("vector", lambda e, hof=hof: e.tensor_copy(
                    d["Ap"][64:128, hof:hof + 512], d["ps_Ap"][64:128, 512:1024]),
                    ins=["g0_psAp#1"], outs=[R(d, "Ap0")], tag=f"{E}_{tagp}e1{half}")

        def emit_iter(ch, it):
            E = f"I0_{ch}_{it}"
            rs_old = d["rs0"] if it % 2 == 0 else d["rs1"]
            rs_new = d["rs1"] if it % 2 == 0 else d["rs0"]
            p.emit("vector", lambda e: e.tensor_copy(d["pbf"][:], d["pp"][:]),
                   ins=[d["pp"][:]], outs=[d["pbf"][:]], tag=f"{E}_pb")
            p.emit("vector", lambda e: e.tensor_mul(d["dpbf"][:], d["pbf"][:],
                                                    d["dbl"][:]),
                   ins=[d["pbf"][:], d["dbl"][:]], outs=[d["dpbf"][:]],
                   tag=f"{E}_dp")
            emit_SW(E, d["pbf"], d["pbf"][:])
            emit_ap(E, d["Kbe"], d["dpbf"], d["Hkkb"], "c")
            p.emit("vector", lambda e: e.tensor_mul(d["tmp"][:], d["pp"][:],
                                                    lamb_b[:]),
                   ins=[d["pp"][:]] + [f"lambb@{pc}" for pc in range(8)],
                   outs=[R(d, "tmpL")], tag=f"{E}_lm")
            p.emit("pool", lambda e: e.tensor_add(d["Ap"][:], d["Ap"][:],
                                                  d["tmp"][:]),
                   ins=[R(d, "Ap0"), R(d, "tmpL")], outs=[R(d, "ApL")],
                   tag=f"{E}_la")
            p.emit("vector", lambda e: e.tensor_mul(d["tmp"][:], d["pp"][:],
                                                    d["Ap"][:]),
                   ins=[d["pp"][:], R(d, "ApL")], outs=[R(d, "tmpP")],
                   tag=f"{E}_pa")
            for pc in range(8):
                p.emit("tensor", lambda e, pc=pc: e.matmul(
                    d["ps_sm"][0:16, 0:128], hsg[:][:, pc * 16:(pc + 1) * 16],
                    d["tmp"][:, pc * 128:(pc + 1) * 128],
                    start=(pc == 0), stop=(pc == 7)),
                    ins=[hsg[:], R(d, "tmpP")], outs=["g0_pssm"], tag=f"{E}_pm{pc}")
            p.emit("vector", lambda e: e.tensor_scalar(
                d["sca"][:, 0:128], d["ps_sm"][0:16, 0:128], 1e-12, None, ALU.add),
                ins=["g0_pssm"], outs=[R(d, "scaA")], tag=f"{E}_ae")
            p.emit("vector", lambda e, rs_old=rs_old: e.tensor_scalar(
                d["sca"][:, 128:256], rs_old[:], 1e-12, None, ALU.add),
                ins=[rs_old[:]], outs=[R(d, "scaB")], tag=f"{E}_ge")
            p.emit("vector", lambda e: e.reciprocal(d["sca"][:, 0:256],
                                                    d["sca"][:, 0:256]),
                   ins=[R(d, "scaA"), R(d, "scaB")],
                   outs=[R(d, "scaA"), R(d, "scaB")], tag=f"{E}_rc")
            p.emit("vector", lambda e, rs_old=rs_old: e.tensor_mul(
                d["alf"][:], rs_old[:], d["sca"][:, 0:128]),
                ins=[rs_old[:], R(d, "scaA")], outs=[d["alf"][:]], tag=f"{E}_am")
            for pc in range(8):
                p.emit("tensor", lambda e, pc=pc: e.matmul(
                    d["ps_S"][:, pc * 128:pc * 128 + 128],
                    hb[:][:, pc * 128:(pc + 1) * 128], d["alf"][:],
                    start=True, stop=True),
                    ins=[hb[:], d["alf"][:]],
                    outs=[f"g0_psS#{(pc * 128 // 512) * 512}"], tag=f"{E}_ab{pc}")
            p.emit("vector", lambda e: e.tensor_mul(d["tmp"][:], d["pp"][:],
                                                    d["ps_S"][:, 0:1024]),
                   ins=[d["pp"][:], "g0_psS#0", "g0_psS#512"],
                   outs=[R(d, "tmpX")], tag=f"{E}_xm")
            p.emit("vector", lambda e: e.tensor_mul(d["tsq"][:], d["Ap"][:],
                                                    d["ps_S"][:, 0:1024]),
                   ins=[R(d, "ApL"), "g0_psS#0", "g0_psS#512"],
                   outs=[R(d, "tsqR")], tag=f"{E}_rm")
            p.emit("pool", lambda e: e.tensor_add(d["x"][:], d["x"][:],
                                                  d["tmp"][:]),
                   ins=[d["x"][:], R(d, "tmpX")], outs=[d["x"][:]], tag=f"{E}_xa")
            p.emit("pool", lambda e: e.tensor_sub(d["r"][:], d["r"][:],
                                                  d["tsq"][:]),
                   ins=[d["r"][:], R(d, "tsqR")], outs=[d["r"][:]], tag=f"{E}_ra")
            p.emit("scalar", lambda e: e.activation(d["tsq"][:], d["r"][:],
                                                    AF.Square),
                   ins=[d["r"][:]], outs=[R(d, "tsqS")], tag=f"{E}_rq")
            for pc in range(8):
                p.emit("tensor", lambda e, pc=pc: e.matmul(
                    d["ps_sm"][0:16, 128:256], hsg[:][:, pc * 16:(pc + 1) * 16],
                    d["tsq"][:, pc * 128:(pc + 1) * 128],
                    start=(pc == 0), stop=(pc == 7)),
                    ins=[hsg[:], R(d, "tsqS")], outs=["g0_pssm"], tag=f"{E}_rn{pc}")
            p.emit("vector", lambda e, rs_new=rs_new: e.tensor_copy(
                rs_new[:], d["ps_sm"][0:16, 128:256]),
                ins=["g0_pssm"], outs=[rs_new[:]], tag=f"{E}_rne")
            p.emit("vector", lambda e, rs_new=rs_new: e.tensor_mul(
                d["gam"][:], rs_new[:], d["sca"][:, 128:256]),
                ins=[rs_new[:], R(d, "scaB")], outs=[d["gam"][:]], tag=f"{E}_gm")
            for pc in range(8):
                p.emit("tensor", lambda e, pc=pc: e.matmul(
                    d["ps_S"][:, pc * 128:pc * 128 + 128],
                    hb[:][:, pc * 128:(pc + 1) * 128], d["gam"][:],
                    start=True, stop=True),
                    ins=[hb[:], d["gam"][:]],
                    outs=[f"g0_psS#{(pc * 128 // 512) * 512}"], tag=f"{E}_gb{pc}")
            p.emit("vector", lambda e: e.tensor_mul(d["tmp"][:], d["pp"][:],
                                                    d["ps_S"][:, 0:1024]),
                   ins=[d["pp"][:], "g0_psS#0", "g0_psS#512"],
                   outs=[R(d, "tmpG")], tag=f"{E}_g2")
            p.emit("pool", lambda e: e.tensor_add(d["pp"][:], d["r"][:],
                                                  d["tmp"][:]),
                   ins=[d["r"][:], R(d, "tmpG")], outs=[d["pp"][:]], tag=f"{E}_p2")

        def emit_out(ch):
            EO = f"O0_{ch}"
            tsl = slice(ch * 128, (ch + 1) * 128)
            j512 = ch // 4
            p.emit("vector", lambda e: e.tensor_copy(d["pbf"][:], d["x"][:]),
                   ins=[d["x"][:]], outs=[R(d, "ognB")], tag=f"{EO}_qs")
            p.emit("vector", lambda e: e.tensor_mul(d["dpbf"][:], d["pbf"][:],
                                                    d["dbl"][:]),
                   ins=[R(d, "ognB"), d["dbl"][:]], outs=[d["dpbf"][:]],
                   tag=f"{EO}_dq")
            emit_SW(EO, d["pbf"], R(d, "ognB"))
            emit_ap(EO, d["Vbe"], d["dpbf"], d["Hkvb"], "o")
            p.emit("vector", lambda e: e.tensor_copy(d["osb"][:], d["Ap"][:]),
                   ins=[R(d, "Ap0")], outs=[R(d, "os0")], tag=f"{EO}_oc")
            for pc in range(8):
                p.emit("tensor", lambda e, pc=pc, tsl=tsl: e.matmul(
                    d["ps_sm"][:, 0:128], gw2t[:, pc * 128:(pc + 1) * 128],
                    g1T[:, tsl], start=True, stop=True),
                    ins=[gw2t[:], f"d_g1T@{j512}"], outs=["g0_pssm"],
                    tag=f"{EO}_g{pc}")
                p.emit("scalar", lambda e, pc=pc: e.activation(
                    d["tmp"][:, pc * 128:(pc + 1) * 128], d["ps_sm"][:, 0:128],
                    AF.Sigmoid),
                    ins=["g0_pssm"], outs=[R(d, "tmpgat")], tag=f"{EO}_ge{pc}")
            p.emit("vector", lambda e: e.tensor_mul(d["osb"][:], d["osb"][:],
                                                    d["tmp"][:]),
                   ins=[R(d, "os0"), R(d, "tmpgat")], outs=[R(d, "os0")],
                   tag=f"{EO}_og")
            p.emit("scalar", lambda e: e.activation(d["tsq"][:], d["osb"][:],
                                                    AF.Square),
                   ins=[R(d, "os0")], outs=[R(d, "tsqO")], tag=f"{EO}_q")
            for hf in range(2):
                hsl = slice(hf * 512, (hf + 1) * 512)
                p.emit("tensor", lambda e, hsl=hsl: e.matmul(
                    d["ps_sm"][0:2, 0:512], C["c_hs2"][:], d["tsq"][:, hsl],
                    start=True, stop=True),
                    ins=[C["c_hs2"][:], R(d, "tsqO")], outs=["g0_pssm"],
                    tag=f"{EO}_s{hf}")
                p.emit("scalar", lambda e: e.activation(
                    d["sca"][0:2, 0:512], d["ps_sm"][0:2, 0:512], AF.Sqrt,
                    scale=1.0 / 64.0, bias=1e-5),
                    ins=["g0_pssm"], outs=[R(d, "scaO")], tag=f"{EO}_r{hf}")
                p.emit("vector", lambda e: e.reciprocal(d["sca"][0:2, 0:512],
                                                        d["sca"][0:2, 0:512]),
                       ins=[R(d, "scaO")], outs=[R(d, "scaO")], tag=f"{EO}_i{hf}")
                p.emit("tensor", lambda e: e.matmul(
                    d["ps_sm"][:, 0:512], C["c_hb2"][:], d["sca"][0:2, 0:512],
                    start=True, stop=True),
                    ins=[C["c_hb2"][:], R(d, "scaO")], outs=["g0_pssm"],
                    tag=f"{EO}_b{hf}")
                p.emit("vector", lambda e, hsl=hsl: e.tensor_mul(
                    d["osb"][:, hsl], d["osb"][:, hsl], d["ps_sm"][:, 0:512]),
                    ins=[R(d, "os0"), "g0_pssm"], outs=[R(d, "os0")],
                    tag=f"{EO}_n{hf}")
            for pc in range(8):
                p.emit("vector", lambda e, pc=pc: e.tensor_scalar(
                    d["pbf"][:, pc * 128:(pc + 1) * 128],
                    d["osb"][:, pc * 128:(pc + 1) * 128], nwl[:, pc:pc + 1],
                    None, ALU.mult),
                    ins=[R(d, "os0"), nwl[:]], outs=[R(d, "ognB")],
                    tag=f"{EO}_w{pc}")
            for dc in range(NDC):
                for pc in range(8):
                    p.emit("tensor", lambda e, pc=pc, dc=dc: e.matmul(
                        d["ps_sm"][:, 128:256],
                        wo_b[pc][:, dc * 128:(dc + 1) * 128],
                        d["pbf"][:, pc * 128:(pc + 1) * 128],
                        start=(pc == 0), stop=(pc == 7)),
                        ins=[wo_b[pc][:], R(d, "ognB")], outs=["g0_pssm"],
                        tag=f"{EO}_y{dc}_{pc}")
                p.dma("sync", d["xac"][:], x_dram[dc * 128:(dc + 1) * 128, tsl],
                      tag=f"{EO}_lx{dc}")
                p.emit("vector", lambda e: e.tensor_add(d["xac"][:], d["xac"][:],
                                                        d["ps_sm"][:, 128:256]),
                       ins=[d["xac"][:], "g0_pssm"], outs=[d["xac"][:]],
                       tag=f"{EO}_ya{dc}")
                p.dma("pool", xout_d[dc * 128:(dc + 1) * 128, tsl], d["xac"][:],
                      tag=f"{EO}_st{dc}")

        for ch in range(NCH):
            emit_prep(ch)
            for it in range(CG_ITERS):
                emit_iter(ch, it)
            emit_out(ch)
    p.fence()


NSLOT = 4


def l1_attn_pass2v2(p, s, C, nc, QD, tens, x_dram, xout_d, cft_dr, cf2_dr,
                    hkk_dr, hkv_dr):
    from contextlib import ExitStack
    ones = C["c_ones"]; onesr = C["c_onesr"]
    hb = C["c_hb"]; mle = C["c_masklehb"]; heye = C["c_heye"]
    hbbf = C["c_hbbf"]; hsgbf = C["c_hsgbf"]; eye32 = C["c_eye32"]
    (wq_b, wk_b, wv_b, wo_b, cfT, btT, cf_tok, bt_tok, g1T, dlam, nwl,
     _hkk, _hkv, _hkkb, _hkvb, gw2t) = tens
    HSEG = lambda t: t.rearrange("p (h j) -> p h j", j=64)
    with ExitStack() as st:
        s3 = S(nc, st)
        SL = []
        for sl in range(NSLOT):
            d = {}
            for nm, shp, dt in (
                    ("kl", [128, 1024], BF16), ("Kbe", [128, 1024], BF16),
                    ("Hkkb", [128, 1024], BF16), ("Dt", [128, 2048], BF16),
                    ("dbl", [128, 1024], BF16), ("x", [128, 1024], F32),
                    ("r", [128, 1024], BF16), ("pp", [128, 1024], BF16),
                    ("Apb", [128, 1024], BF16), ("rsb", [128, 32], F32),
                    ("sca", [128, 32], F32), ("alf", [128, 16], F32),
                    ("gam", [128, 16], F32), ("aTb", [16, 128], BF16),
                    ("gTb", [16, 128], BF16), ("cfc", [16, 129], F32),
                    ("nst", [16, 128], F32), ("cfl", [1, 512], F32),
                    ("ncb", [128, 1], F32)):
                d[nm] = s3.sb(f"w{sl}_{nm}", shp, dt)
            SL.append(d)
        PA = []
        for pr in range(2):
            e = {}
            for nm, shp, dt in (
                    ("dpbf", [128, 1024], BF16), ("Wt", [128, 2048], BF16),
                    ("tpa", [128, 1024], BF16), ("ab", [128, 1024], BF16),
                    ("gb", [128, 1024], BF16), ("tmul", [128, 1024], BF16),
                    ("tq", [128, 1024], BF16), ("txf", [128, 1024], F32),
                    ("tf32", [128, 512], F32)):
                e[nm] = s3.sb(f"pr{pr}_{nm}", shp, dt)
            e["psS"] = s3.ps(f"pr{pr}_psS", [128, 1024])
            e["psA"] = s3.ps(f"pr{pr}_psA", [128, 1024])
            PA.append(e)
        xac = s3.sb("o_xac", [128, 128], F32)

        def RS(pr, bank):
            return f"v2S{pr}#{bank}"

        def RA(pr, bank):
            return f"v2A{pr}#{bank}"

        def emit_S(E, sl, pr, src):
            d, e = SL[sl], PA[pr]
            for half in range(2):
                for h in range(half * 8, half * 8 + 8):
                    pc, hl = h // 2, h % 2
                    scol = hl * 512 + ((h - half * 8) // 2) * 128
                    p.emit("tensor", lambda ng, pc=pc, hl=hl, scol=scol,
                           src=src, d=d, e=e: ng.matmul(
                        e["psS"][:, scol:scol + 128],
                        d["kl"][64 * hl:64 * hl + 64, pc * 128:(pc + 1) * 128],
                        src[64 * hl:64 * hl + 64, pc * 128:(pc + 1) * 128],
                        start=True, stop=True, tile_position=(64 * hl, 0)),
                        ins=[d["kl"][:], src], outs=[RS(pr, (scol // 512) * 512)],
                        tag=f"{E}_S{h}")
                p.emit("vector", lambda ng, half=half, d=d, e=e: ng.tensor_mul(
                    e["Wt"][:, half * 1024:(half + 1) * 1024]
                    .rearrange("p (hp b t) -> p b hp t", b=2, t=128),
                    e["psS"][:].rearrange("p (b h t) -> p b h t", b=2, t=128),
                    d["Dt"][:, half * 1024:(half + 1) * 1024]
                    .rearrange("p (hp b t) -> p b hp t", b=2, t=128)),
                    ins=[d["Dt"][:], RS(pr, 0), RS(pr, 512)],
                    outs=[e["Wt"][:]], tag=f"{E}_W{half}")

        def emit_IH(E, sl, pr, lhs2, hm_t, use_lam, dst, dst_f32):
            d, e = SL[sl], PA[pr]
            for half in range(2):
                for hh in range(half * 8, half * 8 + 8, 2):
                    for h in (hh, hh + 1):
                        pc, hl = h // 2, h % 2
                        c0 = hl * 512 + (pc % 4) * 128
                        p.emit("tensor", lambda ng, h=h, hl=hl, c0=c0,
                               lhs2=lhs2, e=e: ng.matmul(
                            e["psA"][64 * hl:64 * hl + 64, c0:c0 + 128],
                            lhs2[:, h * 64:(h + 1) * 64],
                            e["Wt"][:, h * 128:(h + 1) * 128],
                            start=True, stop=False,
                            tile_position=(0, 64 * hl)),
                            ins=[lhs2[:], e["Wt"][:]], outs=[RA(pr, hl)],
                            tag=f"{E}_I{h}")
                    for h in (hh, hh + 1):
                        pc, hl = h // 2, h % 2
                        c0 = hl * 512 + (pc % 4) * 128
                        p.emit("tensor", lambda ng, h=h, hl=hl, pc=pc, c0=c0,
                               hm_t=hm_t, e=e: ng.matmul(
                            e["psA"][64 * hl:64 * hl + 64, c0:c0 + 128],
                            hm_t[64 * hl:64 * hl + 64, h * 64:(h + 1) * 64],
                            e["dpbf"][64 * hl:64 * hl + 64,
                                      pc * 128:(pc + 1) * 128],
                            start=False, stop=not use_lam,
                            tile_position=(64 * hl, 64 * hl)),
                            ins=[hm_t[:], e["dpbf"][:]], outs=[RA(pr, hl)],
                            tag=f"{E}_H{h}")
                    if use_lam:
                        for h in (hh, hh + 1):
                            pc, hl = h // 2, h % 2
                            c0 = hl * 512 + (pc % 4) * 128
                            p.emit("tensor", lambda ng, h=h, hl=hl, pc=pc,
                                   c0=c0, d=d, e=e: ng.matmul(
                                e["psA"][64 * hl:64 * hl + 64, c0:c0 + 128],
                                dlam[64 * hl:64 * hl + 64, h * 64:(h + 1) * 64],
                                d["pp"][64 * hl:64 * hl + 64,
                                        pc * 128:(pc + 1) * 128],
                                start=False, stop=True,
                                tile_position=(64 * hl, 64 * hl)),
                                ins=[dlam[:], d["pp"][:]], outs=[RA(pr, hl)],
                                tag=f"{E}_L{h}")
                hof = half * 512
                p.emit("scalar", lambda ng, hof=hof, e=e, dst=dst: ng.copy(
                    dst[0:64, hof:hof + 512], e["psA"][0:64, 0:512]),
                    ins=[RA(pr, 0)], outs=[dst[:]], tag=f"{E}_ae0{half}")
                p.emit("scalar", lambda ng, hof=hof, e=e, dst=dst: ng.copy(
                    dst[64:128, hof:hof + 512], e["psA"][64:128, 512:1024]),
                    ins=[RA(pr, 1)], outs=[dst[:]], tag=f"{E}_ae1{half}")

        def emit_prep(ch, sl, pr):
            d, e = SL[sl], PA[pr]
            tsl = slice(ch * 128, (ch + 1) * 128)
            P = f"V2P{ch}"
            b5 = p.open_batch("sync")
            p.dma("sync", d["r"][:].rearrange("p (pc t) -> p pc t", t=128),
                  QD["ql"][:, tsl].rearrange("(pc p) t -> p pc t", p=128),
                  tag=f"{P}_lq", batch=b5)
            p.dma("sync", d["kl"][:].rearrange("p (pc t) -> p pc t", t=128),
                  QD["kl"][:, tsl].rearrange("(pc p) t -> p pc t", p=128),
                  tag=f"{P}_lk", batch=b5)
            p.dma("sync", d["Kbe"][:], QD["kt"][tsl, :], tag=f"{P}_lKt", batch=b5)
            p.dma("sync", d["Hkkb"][:], hkk_dr[ch], tag=f"{P}_lHk", batch=b5)
            p.emit("vector", lambda ng, ch=ch, d=d: ng.tensor_tensor(
                HSEG(d["Kbe"][:]), HSEG(d["Kbe"][:]),
                bt_tok[:, ch * 16:(ch + 1) * 16].unsqueeze(2)
                .broadcast_to([128, 16, 64]), ALU.mult),
                ins=[d["Kbe"][:], f"d_bttok@{ch}"], outs=[d["Kbe"][:]],
                tag=f"{P}_Kb")
            if ch > 0:
                p.dma("sync", d["cfc"][:], cf2_dr[:, ch * 128 - 1: ch * 128 + 128],
                      tag=f"{P}_lcfc")
            else:
                p.emit("pool", lambda ng, d=d: ng.memset(d["cfc"][:, 0:1], 0.0),
                       outs=[d["cfc"][:]], tag=f"{P}_zc")
                p.dma("sync", d["cfc"][:, 1:129], cf2_dr[:, 0:128],
                      ins_extra=[d["cfc"][:]], tag=f"{P}_lcfc")
            p.emit("vector", lambda ng, d=d: ng.tensor_scalar(
                d["nst"][:], d["cfc"][:, 1:129], -1.0, None, ALU.mult),
                ins=[d["cfc"][:]], outs=[d["nst"][:]], tag=f"{P}_nst")
            for q4 in range(4):
                qs = (q4 % 2) * 512
                dsl = slice(q4 * 512, (q4 + 1) * 512)
                p.dma("sync", d["cfl"][:].rearrange("p (x b) -> p x b", b=128),
                      cf2_dr[4 * q4:4 * q4 + 4,
                             ch * 128:(ch + 1) * 128].unsqueeze(0),
                      tag=f"{P}_lcfl{q4}")
                p.emit("tensor", lambda ng, qs=qs, d=d, e=e: ng.matmul(
                    e["psS"][:, qs:qs + 512], onesr[0:1, :], d["cfl"][:],
                    start=True, stop=False),
                    ins=[onesr[:], d["cfl"][:]], outs=[RS(pr, qs)],
                    tag=f"{P}_A{q4}")
                p.emit("tensor", lambda ng, qs=qs, dsl=dsl, d=d, e=e: ng.matmul(
                    e["psS"][:, qs:qs + 512], d["nst"][:], heye[:][:, dsl],
                    start=False, stop=True),
                    ins=[d["nst"][:], heye[:]], outs=[RS(pr, qs)],
                    tag=f"{P}_B{q4}")
                p.emit("vector", lambda ng, qs=qs, dsl=dsl, e=e:
                       ng.scalar_tensor_tensor(
                    e["tf32"][:, 0:512], e["psS"][:, qs:qs + 512], 0.0,
                    mle[:][:, dsl], ALU.min, ALU.add),
                    ins=[mle[:], RS(pr, qs)], outs=[e["tf32"][:]],
                    tag=f"{P}_Dm{q4}")
                p.emit("scalar", lambda ng, dsl=dsl, d=d, e=e: ng.activation(
                    d["Dt"][:, dsl], e["tf32"][:, 0:512], AF.Exp),
                    ins=[e["tf32"][:]], outs=[d["Dt"][:]], tag=f"{P}_De{q4}")
            for pc in range(8):
                p.emit("tensor", lambda ng, pc=pc, d=d, e=e: ng.matmul(
                    e["psA"][:, 128:256], hb[:][:, pc * 128:(pc + 1) * 128],
                    d["cfc"][:, 1:129], start=True, stop=True),
                    ins=[hb[:], d["cfc"][:]], outs=[RA(pr, 0)], tag=f"{P}_db{pc}")
                p.emit("tensor", lambda ng, pc=pc, d=d, e=e: ng.matmul(
                    e["psA"][:, 384:385], hb[:][:, pc * 128:(pc + 1) * 128],
                    d["cfc"][:, 0:1], start=True, stop=True),
                    ins=[hb[:], d["cfc"][:]], outs=[RA(pr, 0)], tag=f"{P}_ns{pc}")
                p.emit("vector", lambda ng, d=d, e=e: ng.tensor_scalar(
                    d["ncb"][:], e["psA"][:, 384:385], -1.0, None, ALU.mult),
                    ins=[RA(pr, 0)], outs=[d["ncb"][:]], tag=f"{P}_nn{pc}")
                p.emit("scalar", lambda ng, pc=pc, d=d, e=e: ng.activation(
                    d["dbl"][:, pc * 128:(pc + 1) * 128], e["psA"][:, 128:256],
                    AF.Exp, bias=d["ncb"][:]),
                    ins=[RA(pr, 0), d["ncb"][:]], outs=[d["dbl"][:]],
                    tag=f"{P}_dbe{pc}")
            p.emit("pool", lambda ng, d=d: ng.memset(d["x"][:], 0.0),
                   outs=[d["x"][:]], tag=f"{P}_x0")
            p.emit("vector", lambda ng, d=d: ng.tensor_copy(d["pp"][:], d["r"][:]),
                   ins=[d["r"][:]], outs=[d["pp"][:]], tag=f"{P}_p0")
            p.emit("scalar", lambda ng, d=d, e=e: ng.activation(
                e["tq"][:], d["r"][:], AF.Square),
                ins=[d["r"][:]], outs=[e["tq"][:]], tag=f"{P}_sq0")
            for pc in range(8):
                p.emit("tensor", lambda ng, pc=pc, e=e: ng.matmul(
                    e["psA"][:, 128 + pc * 16:128 + (pc + 1) * 16],
                    e["tq"][:, pc * 128:(pc + 1) * 128],
                    hsgbf[:][:, pc * 16:(pc + 1) * 16],
                    start=True, stop=True),
                    ins=[e["tq"][:], hsgbf[:]], outs=[RA(pr, 0)],
                    tag=f"{P}_r0m{pc}")
            p.emit("vector", lambda ng, d=d, e=e: ng.tensor_reduce(
                d["rsb"][:, 0:16],
                e["psA"][:, 128:256].rearrange("p (a b) -> p b a", a=8),
                AX.X, ALU.add),
                ins=[RA(pr, 0)], outs=[d["rsb"][:]], tag=f"{P}_rs0")

        def emit_g1(E, sl, pr):
            d, e = SL[sl], PA[pr]
            p.emit("vector", lambda ng, d=d, e=e: ng.tensor_mul(
                e["dpbf"][:], d["pp"][:], d["dbl"][:]),
                ins=[d["pp"][:], d["dbl"][:]], outs=[e["dpbf"][:]],
                tag=f"{E}_dp")
            emit_S(E, sl, pr, SL[sl]["pp"][:])
            emit_IH(E, sl, pr, d["Kbe"][:], d["Hkkb"][:], True, d["Apb"], False)

        def emit_g2(E, sl, pr, it):
            d, e = SL[sl], PA[pr]
            ro = slice(16 * (it % 2), 16 * (it % 2) + 16)
            p.emit("vector", lambda ng, d=d, e=e: ng.tensor_mul(
                e["tpa"][:], d["pp"][:], d["Apb"][:]),
                ins=[d["pp"][:], d["Apb"][:]], outs=[e["tpa"][:]],
                tag=f"{E}_pa")
            for pc in range(8):
                p.emit("tensor", lambda ng, pc=pc, e=e: ng.matmul(
                    e["psA"][:, pc * 16:(pc + 1) * 16],
                    e["tpa"][:, pc * 128:(pc + 1) * 128],
                    hsgbf[:][:, pc * 16:(pc + 1) * 16],
                    start=True, stop=True),
                    ins=[e["tpa"][:], hsgbf[:]], outs=[RA(pr, 0)],
                    tag=f"{E}_pm{pc}")
            p.emit("vector", lambda ng, d=d, e=e: ng.tensor_reduce(
                d["sca"][:, 0:16],
                e["psA"][:, 0:128].rearrange("p (a b) -> p b a", a=8),
                AX.X, ALU.add),
                ins=[RA(pr, 0)], outs=[d["sca"][:]], tag=f"{E}_pr")
            p.emit("vector", lambda ng, d=d: ng.tensor_scalar(
                d["sca"][:, 0:16], d["sca"][:, 0:16], 1e-12, None, ALU.add),
                ins=[d["sca"][:]], outs=[d["sca"][:]], tag=f"{E}_ae")
            p.emit("vector", lambda ng, d=d, ro=ro: ng.tensor_scalar(
                d["sca"][:, 16:32], d["rsb"][:, ro], 1e-12, None, ALU.add),
                ins=[d["rsb"][:]], outs=[d["sca"][:]], tag=f"{E}_ge")
            p.emit("vector", lambda ng, d=d: ng.reciprocal(
                d["sca"][:, 0:32], d["sca"][:, 0:32]),
                ins=[d["sca"][:]], outs=[d["sca"][:]], tag=f"{E}_rc")
            p.emit("vector", lambda ng, d=d, ro=ro: ng.tensor_mul(
                d["alf"][:], d["rsb"][:, ro], d["sca"][:, 0:16]),
                ins=[d["rsb"][:], d["sca"][:]], outs=[d["alf"][:]],
                tag=f"{E}_am")
            p.emit("tensor", lambda ng, d=d, e=e: ng.transpose(
                e["psA"][0:16, 256:384], d["alf"][:], eye32[:]),
                ins=[d["alf"][:], eye32[:]], outs=[RA(pr, 0)], tag=f"{E}_aT")
            p.emit("scalar", lambda ng, d=d, e=e: ng.copy(
                d["aTb"][:], e["psA"][0:16, 256:384]),
                ins=[RA(pr, 0)], outs=[d["aTb"][:]], tag=f"{E}_aTe")

        def emit_g3(E, sl, pr):
            d, e = SL[sl], PA[pr]
            for pc in range(8):
                p.emit("tensor", lambda ng, pc=pc, d=d, e=e: ng.matmul(
                    e["psS"][:, pc * 128:(pc + 1) * 128],
                    hbbf[:][:, pc * 128:(pc + 1) * 128], d["aTb"][:],
                    start=True, stop=True),
                    ins=[hbbf[:], d["aTb"][:]],
                    outs=[RS(pr, (pc * 128 // 512) * 512)], tag=f"{E}_ab{pc}")
            p.emit("scalar", lambda ng, e=e: ng.copy(e["ab"][:], e["psS"][:]),
                   ins=[RS(pr, 0), RS(pr, 512)], outs=[e["ab"][:]],
                   tag=f"{E}_abe")
            p.emit("vector", lambda ng, d=d, e=e: ng.tensor_mul(
                e["txf"][:], e["ab"][:], d["pp"][:]),
                ins=[e["ab"][:], d["pp"][:]], outs=[e["txf"][:]], tag=f"{E}_xm")
            for bk in range(2):
                p.emit("vector", lambda ng, d=d, e=e, bk=bk: ng.tensor_mul(
                    e["tmul"][:, bk * 512:(bk + 1) * 512],
                    d["Apb"][:, bk * 512:(bk + 1) * 512],
                    e["psS"][:, bk * 512:(bk + 1) * 512]),
                    ins=[RS(pr, bk * 512), d["Apb"][:]], outs=[e["tmul"][:]],
                    tag=f"{E}_rm{bk}")
            p.emit("pool", lambda ng, d=d, e=e: ng.tensor_add(
                d["x"][:], d["x"][:], e["txf"][:]),
                ins=[d["x"][:], e["txf"][:]], outs=[d["x"][:]], tag=f"{E}_xa")
            p.emit("pool", lambda ng, d=d, e=e: ng.tensor_sub(
                d["r"][:], d["r"][:], e["tmul"][:]),
                ins=[d["r"][:], e["tmul"][:]], outs=[d["r"][:]], tag=f"{E}_ra")

        def emit_g4(E, sl, pr, it):
            d, e = SL[sl], PA[pr]
            rn = slice(16 * ((it + 1) % 2), 16 * ((it + 1) % 2) + 16)
            p.emit("scalar", lambda ng, d=d, e=e: ng.activation(
                e["tq"][:], d["r"][:], AF.Square),
                ins=[d["r"][:]], outs=[e["tq"][:]], tag=f"{E}_rq")
            for pc in range(8):
                p.emit("tensor", lambda ng, pc=pc, e=e: ng.matmul(
                    e["psA"][:, 128 + pc * 16:128 + (pc + 1) * 16],
                    e["tq"][:, pc * 128:(pc + 1) * 128],
                    hsgbf[:][:, pc * 16:(pc + 1) * 16],
                    start=True, stop=True),
                    ins=[e["tq"][:], hsgbf[:]], outs=[RA(pr, 0)],
                    tag=f"{E}_rn{pc}")
            p.emit("vector", lambda ng, d=d, e=e, rn=rn: ng.tensor_reduce(
                d["rsb"][:, rn],
                e["psA"][:, 128:256].rearrange("p (a b) -> p b a", a=8),
                AX.X, ALU.add),
                ins=[RA(pr, 0)], outs=[d["rsb"][:]], tag=f"{E}_rne")
            p.emit("vector", lambda ng, d=d, rn=rn: ng.tensor_mul(
                d["gam"][:], d["rsb"][:, rn], d["sca"][:, 16:32]),
                ins=[d["rsb"][:], d["sca"][:]], outs=[d["gam"][:]],
                tag=f"{E}_gm")
            p.emit("tensor", lambda ng, d=d, e=e: ng.transpose(
                e["psA"][0:16, 384:512], d["gam"][:], eye32[:]),
                ins=[d["gam"][:], eye32[:]], outs=[RA(pr, 0)], tag=f"{E}_gT")
            p.emit("scalar", lambda ng, d=d, e=e: ng.copy(
                d["gTb"][:], e["psA"][0:16, 384:512]),
                ins=[RA(pr, 0)], outs=[d["gTb"][:]], tag=f"{E}_gTe")

        def emit_g5(E, sl, pr):
            d, e = SL[sl], PA[pr]
            for pc in range(8):
                p.emit("tensor", lambda ng, pc=pc, d=d, e=e: ng.matmul(
                    e["psS"][:, pc * 128:(pc + 1) * 128],
                    hbbf[:][:, pc * 128:(pc + 1) * 128], d["gTb"][:],
                    start=True, stop=True),
                    ins=[hbbf[:], d["gTb"][:]],
                    outs=[RS(pr, (pc * 128 // 512) * 512)], tag=f"{E}_gb{pc}")
            for bk in range(2):
                p.emit("vector", lambda ng, d=d, e=e, bk=bk: ng.tensor_mul(
                    e["tmul"][:, bk * 512:(bk + 1) * 512],
                    d["pp"][:, bk * 512:(bk + 1) * 512],
                    e["psS"][:, bk * 512:(bk + 1) * 512]),
                    ins=[RS(pr, bk * 512), d["pp"][:]], outs=[e["tmul"][:]],
                    tag=f"{E}_p2{bk}")
            p.emit("vector", lambda ng, d=d, e=e: ng.tensor_add(
                d["pp"][:], d["r"][:], e["tmul"][:]),
                ins=[d["r"][:], e["tmul"][:]], outs=[d["pp"][:]],
                tag=f"{E}_pa2")

        def emit_out(ch, sl, pr):
            d, e = SL[sl], PA[pr]
            Vbe, Hvb, osb = d["Kbe"], d["Hkkb"], d["x"]
            rst = e["tf32"][0:2, 0:512]
            EO = f"V2O{ch}"
            tsl = slice(ch * 128, (ch + 1) * 128)
            j512 = ch // 4
            b6 = p.open_batch("sync")
            p.dma("sync", Vbe[:], QD["vt"][tsl, :], tag=f"{EO}_lVt", batch=b6)
            p.dma("sync", Hvb[:], hkv_dr[ch], tag=f"{EO}_lHv", batch=b6)
            p.emit("vector", lambda ng, ch=ch, Vbe=Vbe: ng.tensor_tensor(
                HSEG(Vbe[:]), HSEG(Vbe[:]),
                bt_tok[:, ch * 16:(ch + 1) * 16].unsqueeze(2)
                .broadcast_to([128, 16, 64]), ALU.mult),
                ins=[Vbe[:], f"d_bttok@{ch}"], outs=[Vbe[:]], tag=f"{EO}_Vb")
            p.emit("vector", lambda ng, d=d, e=e: ng.tensor_copy(
                e["tq"][:], d["x"][:]),
                ins=[d["x"][:]], outs=[e["tq"][:]], tag=f"{EO}_xq")
            p.emit("vector", lambda ng, d=d, e=e: ng.tensor_mul(
                e["dpbf"][:], e["tq"][:], d["dbl"][:]),
                ins=[e["tq"][:], d["dbl"][:]], outs=[e["dpbf"][:]],
                tag=f"{EO}_dq")
            emit_S(EO, sl, pr, PA[pr]["tq"][:])
            emit_IH(EO, sl, pr, Vbe[:], Hvb[:], False, osb, True)
            for pc in range(8):
                p.emit("tensor", lambda ng, pc=pc, e=e, tsl=tsl: ng.matmul(
                    e["psA"][:, 384:512], gw2t[:, pc * 128:(pc + 1) * 128],
                    g1T[:, tsl], start=True, stop=True),
                    ins=[gw2t[:], f"d_g1T@{j512}"], outs=[RA(pr, 0)],
                    tag=f"{EO}_g{pc}")
                p.emit("scalar", lambda ng, pc=pc, e=e: ng.activation(
                    e["txf"][:, pc * 128:(pc + 1) * 128], e["psA"][:, 384:512],
                    AF.Sigmoid),
                    ins=[RA(pr, 0)], outs=[e["txf"][:]], tag=f"{EO}_ge{pc}")
            p.emit("vector", lambda ng, e=e, osb=osb: ng.tensor_mul(
                osb[:], osb[:], e["txf"][:]),
                ins=[osb[:], e["txf"][:]], outs=[osb[:]], tag=f"{EO}_og")
            osq = PA[1 - pr]["txf"]
            p.emit("scalar", lambda ng, osq=osq, osb=osb: ng.activation(
                osq[:], osb[:], AF.Square),
                ins=[osb[:]], outs=[osq[:]], tag=f"{EO}_q")
            for hf in range(2):
                hsl = slice(hf * 512, (hf + 1) * 512)
                p.emit("tensor", lambda ng, hsl=hsl, e=e, osq=osq: ng.matmul(
                    e["psA"][0:2, 512:1024], C["c_hs2"][:], osq[:, hsl],
                    start=True, stop=True),
                    ins=[C["c_hs2"][:], osq[:]], outs=[RA(pr, 1)],
                    tag=f"{EO}_s{hf}")
                p.emit("scalar", lambda ng, e=e, rst=rst: ng.activation(
                    rst, e["psA"][0:2, 512:1024], AF.Sqrt,
                    scale=1.0 / 64.0, bias=1e-5),
                    ins=[RA(pr, 1)], outs=[e["tf32"][:]], tag=f"{EO}_r{hf}")
                p.emit("vector", lambda ng, rst=rst: ng.reciprocal(rst, rst),
                       ins=[e["tf32"][:]], outs=[e["tf32"][:]], tag=f"{EO}_i{hf}")
                p.emit("tensor", lambda ng, e=e, rst=rst: ng.matmul(
                    e["psA"][:, 512:1024], C["c_hb2"][:], rst,
                    start=True, stop=True),
                    ins=[C["c_hb2"][:], e["tf32"][:]], outs=[RA(pr, 1)],
                    tag=f"{EO}_b{hf}")
                p.emit("vector", lambda ng, hsl=hsl, e=e, osb=osb: ng.tensor_mul(
                    osb[:, hsl], osb[:, hsl], e["psA"][:, 512:1024]),
                    ins=[osb[:], RA(pr, 1)], outs=[osb[:]], tag=f"{EO}_n{hf}")
            for pc in range(8):
                p.emit("vector", lambda ng, pc=pc, e=e, osb=osb: ng.tensor_scalar(
                    e["tq"][:, pc * 128:(pc + 1) * 128],
                    osb[:, pc * 128:(pc + 1) * 128], nwl[:, pc:pc + 1],
                    None, ALU.mult),
                    ins=[osb[:], nwl[:]], outs=[e["tq"][:]], tag=f"{EO}_w{pc}")
            for dc in range(NDC):
                for pc in range(8):
                    p.emit("tensor", lambda ng, pc=pc, dc=dc, e=e: ng.matmul(
                        e["psA"][:, 384:512],
                        wo_b[pc][:, dc * 128:(dc + 1) * 128],
                        e["tq"][:, pc * 128:(pc + 1) * 128],
                        start=(pc == 0), stop=(pc == 7)),
                        ins=[wo_b[pc][:], e["tq"][:]], outs=[RA(pr, 0)],
                        tag=f"{EO}_y{dc}_{pc}")
                p.dma("sync", xac[:], x_dram[dc * 128:(dc + 1) * 128, tsl],
                      tag=f"{EO}_lx{dc}")
                p.emit("vector", lambda ng, e=e: ng.tensor_add(
                    xac[:], xac[:], e["psA"][:, 384:512]),
                    ins=[xac[:], RA(pr, 0)], outs=[xac[:]], tag=f"{EO}_ya{dc}")
                p.dma("pool", xout_d[dc * 128:(dc + 1) * 128, tsl], xac[:],
                      tag=f"{EO}_st{dc}")

        for w0 in range(0, NCH, NSLOT):
            W = min(NSLOT, NCH - w0)
            for sl in range(W):
                emit_prep(w0 + sl, sl, sl % 2)
            for it in range(CG_ITERS):
                for sl in range(W):
                    emit_g1(f"V2I{w0 + sl}_{it}", sl, sl % 2)
                for sl in range(W):
                    emit_g2(f"V2I{w0 + sl}_{it}", sl, sl % 2, it)
                for sl in range(W):
                    emit_g3(f"V2I{w0 + sl}_{it}", sl, sl % 2)
                for sl in range(W):
                    emit_g4(f"V2I{w0 + sl}_{it}", sl, sl % 2, it)
                for sl in range(W):
                    emit_g5(f"V2I{w0 + sl}_{it}", sl, sl % 2)
            for sl in range(W):
                emit_out(w0 + sl, sl, sl % 2)
    p.fence()


# version 15 (final)
# speedup vs baseline: 1.0080x; 1.0080x over previous
import math
import numpy as np
import concourse.bass as bass
from concourse import dve_ops as _dve_ops
import concourse.mybir as mybir
from concourse.alu_op_type import AluOpType as ALU
from concourse.bass_utils import run_bass_kernel_spmd

F32 = mybir.dt.float32
BF16 = mybir.dt.bfloat16
AF = mybir.ActivationFunctionType
AX = mybir.AxisListType

ENGINES = ("tensor", "vector", "scalar", "pool", "sync")

STRICT_SYNC = False
LAST_PROG = None
EXEMPT_ENGINES = frozenset(("tensor", "scalar", "pool", "sync"))
VEC_DIST = 2


NDMASEM = 8


class Prog:

    def __init__(self, nc):
        self.nc = nc
        self.ops = []
        self.eng_pos = {e: [] for e in ENGINES}
        self.last_w = {}
        self.readers = {}
        self.consumers = {}
        self.batches = {e: [] for e in ENGINES}
        self.slot_count = {e: [0] * NDMASEM for e in ENGINES}
        self.implicit_deps = []
        self.fence_pending = {}

    def _regions(self, aps):
        out = []
        for a in aps:
            if a is None:
                continue
            if isinstance(a, str):
                out.append(a)
            else:
                out.append(a.tensor.name)
        return out

    def emit(self, engine, fn, ins=(), outs=(), dma=False, tag="", extra_deps=()):
        idx = len(self.ops)
        rd = self._regions(ins)
        wr = self._regions(outs)
        deps = set(extra_deps)
        deps.update(self.implicit_deps)
        deps.update(self.fence_pending.pop(engine, ()))
        for r in rd:
            if r in self.last_w:
                deps.add(self.last_w[r])
        for r in wr:
            if r in self.last_w:
                deps.add(self.last_w[r])
            for ridx in self.readers.get(r, ()):
                deps.add(ridx)
        deps.discard(idx)
        op = dict(engine=engine, fn=fn, deps=deps, dma=dma, tag=tag,
                  idx=idx, batch=None, outr=set(wr))
        self.eng_pos[engine].append(idx)
        self.ops.append(op)
        for d in deps:
            self.consumers.setdefault(d, []).append(idx)
        for r in rd:
            self.readers.setdefault(r, []).append(idx)
        for r in wr:
            self.last_w[r] = idx
            self.readers[r] = []
        return idx

    def _open_batch(self, queue):
        bs = self.batches[queue]
        b = dict(slot=None, ops=[], value=None, bidx=len(bs))
        bs.append(b)
        return b

    def dma(self, engine, out, in_, ins_extra=(), tag="", batch=None, **kw):
        b = batch if batch is not None else self._open_batch(engine)
        idx = self.emit(engine,
                        lambda e, out=out, in_=in_, kw=kw: e.dma_start(out=out, in_=in_, **kw),
                        ins=[in_] + list(ins_extra), outs=[out], dma=True, tag=tag)
        op = self.ops[idx]
        inbatch = set(b["ops"]) & op["deps"]
        for d in inbatch:
            op["deps"].discard(d)
            self.consumers[d].remove(idx)
        op["batch"] = b
        b["ops"].append(idx)
        return idx

    def open_batch(self, queue):
        return self._open_batch(queue)

    def fence(self):
        targets = set()
        if not hasattr(self, "_fence_mark"):
            self._fence_mark = {q: 0 for q in ENGINES}
        for e in ENGINES:
            if self.eng_pos[e]:
                targets.add(self.eng_pos[e][-1])
            for b in self.batches[e][self._fence_mark[e]:]:
                if b["ops"]:
                    targets.add(b["ops"][-1])
            self._fence_mark[e] = len(self.batches[e])
        for e in ENGINES:
            self.emit(e, lambda eng: eng.nop(), extra_deps=targets,
                      tag=f"fence_{e}_{len(self.ops)}")

    def finalize_and_emit(self):
        nc = self.nc
        n = len(self.ops)
        for q in ENGINES:
            slot_last = [None] * NDMASEM
            slot_cum = [0] * NDMASEM
            bi = 0
            for b in self.batches[q]:
                if not b["ops"]:
                    continue
                first = b["ops"][0]
                cons = []
                for oi in b["ops"]:
                    cons += self.consumers.get(oi, [])
                if cons:
                    assert max(b["ops"]) < min(cons), \
                        f"dma batch on {q} consumed before fully recorded"
                sl = bi % NDMASEM
                bi += 1
                prev = slot_last[sl]
                b["slot"] = sl
                slot_last[sl] = b
                slot_cum[sl] += 16 * len(b["ops"])
                b["value"] = slot_cum[sl]
                if prev is not None:
                    x = prev["ops"][-1]
                    if x not in self.ops[first]["deps"]:
                        self.ops[first]["deps"].add(x)
                        self.consumers.setdefault(x, []).append(first)
        tpos = {}
        for e2 in ENGINES:
            for k2, i2 in enumerate(self.eng_pos[e2]):
                tpos[i2] = k2

        PE_PIPELINE = getattr(self, "pe_pipeline", False)

        def pe_chain(dep, op):
            return (PE_PIPELINE
                    and dep["engine"] == "tensor" and op["engine"] == "tensor"
                    and dep["outr"] & op["outr"]
                    and tpos[dep["idx"]] == tpos[op["idx"]] - 1)
        strict = STRICT_SYNC
        need_inc = [False] * n

        def same_ok(dep, op):
            if dep["engine"] != op["engine"] or dep["dma"]:
                return False
            if dep["engine"] in EXEMPT_ENGINES:
                return True
            if dep["engine"] == "vector":
                return tpos[op["idx"]] - tpos[dep["idx"]] > VEC_DIST
            return False
        for op in self.ops:
            for d in op["deps"]:
                dep = self.ops[d]
                if dep["dma"]:
                    continue
                if same_ok(dep, op) and not strict:
                    continue
                need_inc[d] = True
        inc_count = [0] * n
        eng_counter = {e: 0 for e in ENGINES}
        for op in self.ops:
            i = op["idx"]
            if op["dma"]:
                continue
            if need_inc[i]:
                eng_counter[op["engine"]] += 1
                inc_count[i] = eng_counter[op["engine"]]

        def dep_sem_val(d):
            dep = self.ops[d]
            if dep["dma"]:
                b = dep["batch"]
                return (f"q_{dep['engine']}_{b['slot']}", b["value"])
            return ("p_" + dep["engine"], inc_count[d])

        knowledge = {e: {} for e in ENGINES}
        op_know = {}
        waits = [[] for _ in range(n)]
        cur_inc = {e: 0 for e in ENGINES}
        for op in self.ops:
            i, e = op["idx"], op["engine"]
            myknow = knowledge[e]
            needed = {}
            merges = []
            for d in sorted(op["deps"]):
                dep = self.ops[d]
                if (not strict) and same_ok(dep, op):
                    merges.append(d)
                    continue
                sem, val = dep_sem_val(d)
                if myknow.get(sem, 0) >= val:
                    merges.append(d)
                    continue
                if needed.get(sem, 0) < val:
                    needed[sem] = val
                merges.append(d)
            for sem, val in needed.items():
                waits[i].append((sem, val))
                myknow[sem] = val
            for d in merges:
                for s2, v2 in op_know.get(d, {}).items():
                    if myknow.get(s2, 0) < v2:
                        myknow[s2] = v2
            if op["dma"]:
                op_know[i] = dict(myknow)
            elif need_inc[i]:
                cur_inc[e] = inc_count[i]
                kn = dict(myknow)
                kn["p_" + e] = inc_count[i]
                op_know[i] = kn
            else:
                op_know[i] = dict(myknow)
        self._waits = waits
        sem_handles = {}
        sem_ctxs = []

        def mksem(name):
            ctx = nc.semaphore(name)
            sem_handles[name] = ctx.__enter__()
            sem_ctxs.append(ctx)
        for e in ENGINES:
            mksem("p_" + e)
            nslots = min(NDMASEM, len(self.batches[e])) if self.batches[e] else 0
            for sidx in range(nslots):
                mksem(f"q_{e}_{sidx}")
        blk_ctx = nc.Block()
        blk = blk_ctx.__enter__()
        ops = self.ops
        final_q = {}
        for q in ENGINES:
            slot_cum = {}
            for b in self.batches[q]:
                slot_cum[b["slot"]] = b["value"]
            final_q[q] = slot_cum

        def mk_section(engine):
            def body(eng):
                for i in self.eng_pos[engine]:
                    op = ops[i]
                    for sem, val in waits[i]:
                        eng.wait_ge(sem_handles[sem], val)
                    inst = op["fn"](eng)
                    if op["dma"]:
                        b = op["batch"]
                        inst.then_inc(sem_handles[f"q_{engine}_{b['slot']}"], 16)
                    elif need_inc[i]:
                        if inst is None:
                            raise RuntimeError(f"op {op['tag']} returned no inst")
                        inst.then_inc(sem_handles["p_" + engine], 1)
                for sidx, val in final_q[engine].items():
                    eng.wait_ge(sem_handles[f"q_{engine}_{sidx}"], val)
            return body

        for e in ENGINES:
            sect = getattr(blk, "gpsimd" if e == "pool" else e)
            sect(mk_section(e))
        global LAST_PROG
        LAST_PROG = self
        blk_ctx.__exit__(None, None, None)
        for ctx in sem_ctxs:
            ctx.__exit__(None, None, None)


D = 1024
T = 4096
H = 16
DM = 64
NDC = D // 128
NTB = T // 512
NSB = T // 128


def host_consts():
    import ml_dtypes
    c = {}
    c["c_vals"] = np.tile(np.array([CONST_VALS], np.float32), (128, 1))
    c["c_ones"] = np.ones((128, 1), np.float32)
    c["c_onesr"] = np.ones((1, 128), np.float32)
    eye = np.eye(128, dtype=np.float32)
    c["c_eye32"] = eye
    import ml_dtypes
    c["c_eyebf"] = eye.astype(ml_dtypes.bfloat16)
    hb = np.zeros((16, 1024), np.float32)
    for pc in range(8):
        for j in range(128):
            hb[(pc * 128 + j) // 64, pc * 128 + j] = 1.0
    c["c_hb"] = hb
    import ml_dtypes
    c["c_hbbf"] = hb.astype(ml_dtypes.bfloat16)
    hs2 = np.zeros((128, 2), np.float32)
    hs2[np.arange(128), np.arange(128) // 64] = 1.0
    c["c_hs2"] = hs2
    c["c_hb2"] = hs2.T.copy()
    hsg = np.zeros((128, 8 * 16), np.float32)
    for pc in range(8):
        for j in range(128):
            hsg[j, 16 * pc + (pc * 128 + j) // 64] = 1.0
    c["c_hsg"] = hsg
    c["c_hsgbf"] = hsg.astype(ml_dtypes.bfloat16)
    m = np.zeros((128, 4 * 512), np.float32)
    dg = np.zeros((128, 4 * 512), np.float32)
    for r in range(4):
        s = np.arange(128)[:, None]
        t = np.arange(512)[None, :]
        m[:, r * 512:(r + 1) * 512] = (128 * r + s >= t)
        dg[:, r * 512:(r + 1) * 512] = (128 * r + s == t)
    c["c_l0mask"] = m.astype(ml_dtypes.bfloat16)
    c["c_l0diag"] = dg.astype(ml_dtypes.bfloat16)
    ml = np.where(np.arange(128)[:, None] <= np.arange(128)[None, :], 0.0, -30000.0)
    c["c_masklehb"] = np.tile(ml, (1, 16)).astype(ml_dtypes.bfloat16)
    e64 = np.eye(64, dtype=np.float32)
    c["c_eye2"] = np.vstack([e64, e64])
    dup = np.zeros((64, 128), np.float32)
    dup[np.arange(128) % 64, np.arange(128)] = 1.0
    c["c_dup"] = dup
    d0 = np.zeros((64, 128), np.float32); d0[np.arange(64), np.arange(64)] = 1.0
    d1 = np.zeros((64, 128), np.float32); d1[np.arange(64), 64 + np.arange(64)] = 1.0
    c["c_dup0"] = d0
    c["c_dup1"] = d1
    he = np.zeros((16, 2048), np.float32)
    for h in range(16):
        he[h, h * 128:(h + 1) * 128] = 1.0
    c["c_heye"] = he
    return c


class S:

    def __init__(self, nc, stack):
        self.nc = nc
        self.stack = stack
        self.n = 0

    def sb(self, name, shape, dt=F32):
        return self.stack.enter_context(self.nc.sbuf_tensor(name, shape, dt))

    def ps(self, name, shape, dt=F32):
        return self.stack.enter_context(self.nc.psum_tensor(name, shape, dt))

    def dram(self, name, shape, dt=F32, kind="Internal"):
        return self.nc.dram_tensor(name, shape, dt, kind=kind).ap()


CONST_VALS = (1e-5, 1e-6, 1e-12, 1e-8, 0.25, -0.5, 0.5, 2.0, -1.0, 1.0, 0.0)


def load_consts(p, s, nc, names=("c_ones", "c_onesr", "c_eye32", "c_eyebf",
                                "c_hb", "c_hs2", "c_hb2", "c_hsg",
                                "c_l0mask", "c_l0diag")):
    shapes = {"c_ones": ([128, 1], F32), "c_onesr": ([1, 128], F32),
              "c_eye32": ([128, 128], F32), "c_eyebf": ([128, 128], BF16),
              "c_hb": ([16, 1024], F32), "c_hbbf": ([16, 1024], BF16),
              "c_hs2": ([128, 2], F32),
              "c_hb2": ([2, 128], F32), "c_hsg": ([128, 128], F32),
              "c_hsgbf": ([128, 128], BF16),
              "c_l0mask": ([128, 2048], BF16), "c_l0diag": ([128, 2048], BF16),
              "c_masklehb": ([128, 2048], BF16), "c_heye": ([16, 2048], F32),
              "c_eye2": ([128, 64], F32), "c_dup": ([64, 128], F32),
              "c_dup0": ([64, 128], F32), "c_dup1": ([64, 128], F32)}
    tiles = {}
    b = p.open_batch("sync")
    for nm in names:
        shp, dt = shapes[nm]
        dr = nc.dram_tensor(nm, shp, dt, kind="ExternalInput").ap()
        tl = s.sb(nm + "_t", shp, dt)
        p.dma("sync", tl[:], dr, tag="ld_" + nm, batch=b)
        tiles[nm] = tl
    cv = nc.dram_tensor("c_vals", [128, len(CONST_VALS)], F32, kind="ExternalInput").ap()
    cvt = s.sb("c_vals_t", [128, len(CONST_VALS)], F32)
    cv_idx = p.dma("sync", cvt[:], cv, tag="ld_cvals", batch=b)
    p.implicit_deps.append(cv_idx)
    for i, v in enumerate(CONST_VALS):
        nc.const_aps.register(F32, v, cvt[:, i:i + 1])
    tiles["c_vals"] = cvt
    return tiles


def reload_const(p, s_local, C, nm, dram_ap, shp, dt):
    tl = s_local.sb(nm + "_t" + str(len(p.ops)), shp, dt)
    p.dma("sync", tl[:], dram_ap, tag="rlc_" + nm)
    C[nm] = tl
    return tl


def load_extra_consts(p, s, nc, shapes):
    tiles = {}
    b = p.open_batch("sync")
    for nm, (shp, dt) in shapes.items():
        dr = nc.dram_tensor(nm, shp, dt, kind="ExternalInput").ap()
        tl = s.sb(nm + "_t", shp, dt)
        p.dma("sync", tl[:], dr, tag="ld_" + nm, batch=b)
        tiles[nm] = tl
    return tiles


def ln_fm(p, s, C, x_dram, xn_out, xn_lo=None, tag="ln"):
    xbuf = [s.sb(f"{tag}_xb{c}", [128, 512]) for c in range(NDC)]
    sq = s.sb(f"{tag}_sq", [128, 512])
    ps_s = s.ps(f"{tag}_ps_s", [1, 512])
    ps_q = s.ps(f"{tag}_ps_q", [1, 512])
    stat = s.sb(f"{tag}_stat", [1, 2048])
    ps_ab = s.ps(f"{tag}_ps_ab", [128, 1024])
    xn32 = s.sb(f"{tag}_xn32", [128, 512])
    ones = C["c_ones"]
    onesr = C["c_onesr"]
    for j in range(NTB):
        cols = slice(j * 512, (j + 1) * 512)
        bb = p.open_batch("sync")
        for c in range(NDC):
            p.dma("sync", xbuf[c][:], x_dram[c * 128:(c + 1) * 128, cols],
                  tag=f"{tag}_ldx{c}_{j}", batch=bb)
        for c in range(NDC):
            p.emit("tensor", lambda e, c=c: e.matmul(ps_s[:], ones[:, 0:1], xbuf[c][:],
                                                     start=(c == 0), stop=(c == NDC - 1)),
                   ins=[ones[:], xbuf[c][:]], outs=[ps_s[:]], tag=f"{tag}_mmS{c}_{j}")
        for c in range(NDC):
            p.emit("scalar", lambda e, c=c: e.activation(sq[:], xbuf[c][:], AF.Square),
                   ins=[xbuf[c][:]], outs=[sq[:]], tag=f"{tag}_sq{c}_{j}")
            p.emit("tensor", lambda e, c=c: e.matmul(ps_q[:], ones[:, 0:1], sq[:],
                                                     start=(c == 0), stop=(c == NDC - 1)),
                   ins=[ones[:], sq[:]], outs=[ps_q[:]], tag=f"{tag}_mmQ{c}_{j}")
        p.emit("vector", lambda e: e.tensor_scalar(stat[:, 0:512], ps_s[:], 1.0 / D, None,
                                                   ALU.mult),
               ins=[ps_s[:]], outs=["stat_m"], tag=f"{tag}_m_{j}")
        p.emit("vector", lambda e: e.tensor_scalar(stat[:, 512:1024], ps_q[:], 1.0 / D, None,
                                                   ALU.mult),
               ins=[ps_q[:]], outs=["stat_v"], tag=f"{tag}_q_{j}")
        p.emit("vector", lambda e: e.tensor_mul(sq[0:1, :], stat[:, 0:512], stat[:, 0:512]),
               ins=["stat_m"], outs=["stat_m2"], tag=f"{tag}_m2_{j}")
        p.emit("vector", lambda e: e.tensor_sub(stat[:, 512:1024], stat[:, 512:1024], sq[0:1, :]),
               ins=["stat_v", "stat_m2"], outs=["stat_v"], tag=f"{tag}_v2_{j}")
        p.emit("scalar", lambda e: e.activation(stat[:, 1024:1536], stat[:, 512:1024], AF.Sqrt,
                                                bias=1e-5),
               ins=["stat_v"], outs=["stat_sd"], tag=f"{tag}_sd_{j}")
        p.emit("vector", lambda e: e.reciprocal(stat[:, 1024:1536], stat[:, 1024:1536]),
               ins=["stat_sd"], outs=["stat_r"], tag=f"{tag}_r_{j}")
        p.emit("vector", lambda e: e.scalar_tensor_tensor(stat[:, 1536:2048], stat[:, 0:512], -1.0,
                                                          stat[:, 1024:1536], ALU.mult, ALU.mult),
               ins=["stat_m", "stat_r"], outs=["stat_b"], tag=f"{tag}_b_{j}")
        p.emit("tensor", lambda e: e.matmul(ps_ab[:, 0:512], onesr[0:1, :], stat[:, 1024:1536],
                                            start=True, stop=True),
               ins=[onesr[:], "stat_r"], outs=["ps_a"], tag=f"{tag}_bA_{j}")
        p.emit("tensor", lambda e: e.matmul(ps_ab[:, 512:1024], onesr[0:1, :], stat[:, 1536:2048],
                                            start=True, stop=True),
               ins=[onesr[:], "stat_b"], outs=["ps_b"], tag=f"{tag}_bB_{j}")
        for c in range(NDC):
            p.emit("vector", lambda e, c=c: e.tensor_mul(xn32[:], xbuf[c][:],
                                                         ps_ab[:, 0:512]),
                   ins=[xbuf[c][:], "ps_a"], outs=[xn32[:]], tag=f"{tag}_sc{c}_{j}")
            if xn_lo is None:
                p.emit("vector", lambda e, c=c, cols=cols: e.tensor_add(
                    xn_out[c][:, cols], xn32[:], ps_ab[:, 512:1024]),
                    ins=[xn32[:], "ps_b"], outs=[f"{xn_out[c].name}@{j}"],
                    tag=f"{tag}_o{c}_{j}")
            else:
                p.emit("vector", lambda e, c=c: e.tensor_add(xn32[:], xn32[:],
                                                             ps_ab[:, 512:1024]),
                       ins=[xn32[:], "ps_b"], outs=[xn32[:]], tag=f"{tag}_o32{c}_{j}")
                p.emit("scalar", lambda e, c=c, cols=cols: e.copy(xn_out[c][:, cols], xn32[:]),
                       ins=[xn32[:]], outs=[f"{xn_out[c].name}@{j}"], tag=f"{tag}_hi{c}_{j}")
                p.emit("vector", lambda e, c=c, cols=cols: e.tensor_sub(
                    xn_lo[c][:, cols], xn32[:], xn_out[c][:, cols]),
                    ins=[xn32[:], f"{xn_out[c].name}@{j}"],
                    outs=[f"{xn_lo[c].name}@{j}"], tag=f"{tag}_lo{c}_{j}")


def load_w_bf16(p, s, C, w_dram, name, cols, stage, batch_q="sync", tiles=None):
    if tiles is None:
        tiles = [s.sb(f"{name}_b{c}", [128, cols], BF16) for c in range(NDC)]
    for c in range(NDC):
        p.dma(batch_q, stage[:], w_dram[c * 128:(c + 1) * 128, :], tag=f"ldw_{name}{c}")
        p.emit("vector", lambda e, c=c: e.tensor_copy(tiles[c][:], stage[:]),
               ins=[stage[:]], outs=[tiles[c][:]], tag=f"cw_{name}{c}")
    return tiles


def l0_attn(p, s, C, nc, x_dram, wk_d, wv_d, wq_d, wbeta_d, wout_d, xout_d,
            q_dr, k_dr, v_dr, cl_dr, bt_dr):
    from contextlib import ExitStack
    ones = C["c_ones"]; onesr = C["c_onesr"]; eye32 = C["c_eye32"]
    hs2 = C["c_hs2"]; hb2 = C["c_hb2"]
    clt_tok = s.sb("clt_tok", [128, NSB * 16])
    bt_tok = s.sb("bt_tok", [128, NSB * 16])
    oT = [s.sb(f"oT{pc}", [128, T], BF16) for pc in range(8)]

    stA_outer = ExitStack()
    sAo = S(nc, stA_outer)
    xn = [sAo.sb(f"xn{c}", [128, T], BF16) for c in range(NDC)]
    with ExitStack() as stLN:
        ln_fm(p, S(nc, stLN), C, x_dram, xn, tag="ln0a")
    p.fence()
    with ExitStack() as stA:
        sA = S(nc, stA)
        wbeta_st = sA.sb("wbeta_st", [128, H])
        wbeta_b = load_w_bf16(p, sA, C, wbeta_d, "wbeta", H, wbeta_st)
        ps_be = sA.ps("ps_be", [16, 512])
        betaT = sA.sb("betaT", [16, T])
        clT = sA.sb("clT", [16, T])
        lrT = clT
        for j in range(NTB):
            cols = slice(j * 512, (j + 1) * 512)
            for c in range(NDC):
                p.emit("tensor", lambda e, c=c, cols=cols: e.matmul(
                    ps_be[:], wbeta_b[c][:], xn[c][:, cols],
                    start=(c == 0), stop=(c == NDC - 1)),
                    ins=[wbeta_b[c][:], f"xn{c}@{j}"], outs=[ps_be[:]], tag=f"bmm{c}_{j}")
            p.emit("scalar", lambda e, cols=cols: e.activation(betaT[:, cols], ps_be[:],
                                                               AF.Sigmoid),
                   ins=[ps_be[:]], outs=[f"betaT@{j}"], tag=f"bsig_{j}")
            p.emit("vector", lambda e, cols=cols: e.tensor_scalar(
                lrT[:, cols], betaT[:, cols], -1.0, 1.0 + 1e-8, ALU.mult, ALU.add),
                ins=[f"betaT@{j}"], outs=[f"lrT@{j}"], tag=f"blr_{j}")
            p.emit("scalar", lambda e, cols=cols: e.activation(lrT[:, cols], lrT[:, cols],
                                                               AF.Ln),
                   ins=[f"lrT@{j}"], outs=[f"lrT@{j}"], tag=f"bln_{j}")
        p.emit("vector", lambda e: e.tensor_tensor_scan(clT[:], lrT[:], lrT[:], 0.0,
                                                        ALU.add, ALU.bypass),
               ins=[f"lrT@{j}" for j in range(NTB)], outs=[clT[:]], tag="bscan")
        p.dma("pool", cl_dr, clT[:], tag="st_cl")
        bb = p.open_batch("pool")
        for j in range(NTB):
            p.dma("pool", bt_dr[:, j * 512:(j + 1) * 512],
                  betaT[:, j * 512:(j + 1) * 512],
                  ins_extra=[f"betaT@{j}"], tag=f"st_bt{j}", batch=bb)
        ps_tr = sA.ps("ps_tr", [128, 16])
        for sb in range(NSB):
            sl = slice(sb * 128, (sb + 1) * 128)
            dst = slice(sb * 16, (sb + 1) * 16)
            p.emit("tensor", lambda e, sl=sl: e.transpose(ps_tr[:], clT[:, sl],
                                                          eye32[:][0:16, 0:16]),
                   ins=[clT[:], eye32[:]], outs=[ps_tr[:]], tag=f"trc{sb}")
            p.emit("vector", lambda e, dst=dst: e.tensor_copy(clt_tok[:, dst], ps_tr[:]),
                   ins=[ps_tr[:]], outs=[f"clt_tok@{sb}"], tag=f"trce{sb}")
            p.emit("tensor", lambda e, sl=sl: e.transpose(ps_tr[:], betaT[:, sl],
                                                          eye32[:][0:16, 0:16]),
                   ins=[f"betaT@{sb // 4}", eye32[:]], outs=[ps_tr[:]], tag=f"trb{sb}")
            p.emit("vector", lambda e, dst=dst: e.tensor_copy(bt_tok[:, dst], ps_tr[:]),
                   ins=[ps_tr[:]], outs=[f"bt_tok@{sb}"], tag=f"trbe{sb}")

    p.fence()
    with ExitStack() as stA:
        sA = S(nc, stA)
        wv_st = sA.sb("wv_st", [128, D])
        wv_b = load_w_bf16(p, sA, C, wv_d, "wv", D, wv_st)
        ps_v = [sA.ps(f"ps_v{i}", [128, 512]) for i in range(2)]
        vrow = sA.sb("vrow", [128, D], BF16)
        for sb in range(NSB):
            j = sb // 4
            tsl = slice(sb * 128, (sb + 1) * 128)
            for half in range(2):
                esl = slice(half * 512, (half + 1) * 512)
                for c in range(NDC):
                    p.emit("tensor", lambda e, c=c, tsl=tsl, esl=esl, half=half: e.matmul(
                        ps_v[half][:], xn[c][:, tsl], wv_b[c][:, esl],
                        start=(c == 0), stop=(c == NDC - 1)),
                        ins=[f"xn{c}@{j}", wv_b[c][:]], outs=[ps_v[half][:]],
                        tag=f"vmm{sb}_{half}_{c}")
                p.emit("vector", lambda e, esl=esl, half=half: e.tensor_copy(
                    vrow[:, esl], ps_v[half][:]),
                    ins=[ps_v[half][:]], outs=[f"vrow@{half}"], tag=f"vev{sb}_{half}")
            p.dma("pool", v_dr[tsl, :], vrow[:],
                  ins_extra=["vrow@0", "vrow@1"], tag=f"st_v{sb}")

    p.fence()
    with ExitStack() as stA:
        sA = S(nc, stA)
        w_st = sA.sb("w_st", [128, 128])
        wp_b = sA.sb("wp_b", [128, 2048], BF16)
        qk_t = sA.sb("qk_t", [128, T], BF16)
        qf = sA.sb("qf", [128, 512])
        qsq = sA.sb("qsq", [128, 512])
        ps_p = sA.ps("ps_p", [128, 512])
        ps_ss = sA.ps("ps_ss", [2, 512])
        ps_bb = sA.ps("ps_bb", [128, 512])
        rst = sA.sb("rst", [2, 512])
        for pc in range(8):
            csl = slice(pc * 128, (pc + 1) * 128)
            for wi, (which, wd, dr) in enumerate((("q", wq_d, q_dr), ("k", wk_d, k_dr))):
                for c in range(NDC):
                    p.dma("sync", w_st[:], wd[c * 128:(c + 1) * 128, csl],
                          tag=f"ld{which}{pc}_{c}")
                    p.emit("vector",
                           lambda e, c=c, wi=wi: e.tensor_copy(
                               wp_b[:, wi * 1024 + c * 128: wi * 1024 + (c + 1) * 128],
                               w_st[:]),
                           ins=[w_st[:]], outs=[f"wp_{wi}_{c}"], tag=f"cw{which}{pc}_{c}")
                for j in range(NTB):
                    cols = slice(j * 512, (j + 1) * 512)
                    for c in range(NDC):
                        p.emit("tensor", lambda e, c=c, cols=cols, wi=wi: e.matmul(
                            ps_p[:],
                            wp_b[:, wi * 1024 + c * 128: wi * 1024 + (c + 1) * 128],
                            xn[c][:, cols], start=(c == 0), stop=(c == NDC - 1)),
                            ins=[f"wp_{wi}_{c}", f"xn{c}@{j}"], outs=[ps_p[:]],
                            tag=f"p{which}{pc}_{j}_{c}")
                    p.emit("vector", lambda e: e.tensor_copy(qf[:], ps_p[:]),
                           ins=[ps_p[:]], outs=[qf[:]], tag=f"pe{which}{pc}_{j}")
                    p.emit("scalar", lambda e: e.activation(qsq[:], qf[:], AF.Square),
                           ins=[qf[:]], outs=[qsq[:]], tag=f"psq{which}{pc}_{j}")
                    p.emit("tensor", lambda e: e.matmul(ps_ss[:], hs2[:], qsq[:],
                                                        start=True, stop=True),
                           ins=[hs2[:], qsq[:]], outs=[ps_ss[:]], tag=f"pss{which}{pc}_{j}")
                    p.emit("scalar", lambda e: e.activation(rst[:], ps_ss[:], AF.Sqrt),
                           ins=[ps_ss[:]], outs=[rst[:]], tag=f"psd{which}{pc}_{j}")
                    p.emit("vector", lambda e: e.tensor_scalar(rst[:], rst[:], 1e-12,
                                                               None, ALU.max),
                           ins=[rst[:]], outs=[rst[:]], tag=f"pmx{which}{pc}_{j}")
                    p.emit("vector", lambda e: e.reciprocal(rst[:], rst[:]),
                           ins=[rst[:]], outs=[rst[:]], tag=f"prc{which}{pc}_{j}")
                    if which == "q":
                        p.emit("vector", lambda e: e.tensor_scalar(rst[:], rst[:], 0.125,
                                                                   None, ALU.mult),
                               ins=[rst[:]], outs=[rst[:]], tag=f"pq8{pc}_{j}")
                    p.emit("tensor", lambda e: e.matmul(ps_bb[:], hb2[:], rst[:],
                                                        start=True, stop=True),
                           ins=[hb2[:], rst[:]], outs=[ps_bb[:]], tag=f"pbb{which}{pc}_{j}")
                    p.emit("vector", lambda e, cols=cols: e.tensor_mul(
                        qk_t[:, cols], qf[:], ps_bb[:]),
                        ins=[qf[:], ps_bb[:]], outs=[f"qk_t@{j}"], tag=f"pn{which}{pc}_{j}")
                bq = p.open_batch("pool")
                for j in range(NTB):
                    p.dma("pool", dr[csl, j * 512:(j + 1) * 512],
                          qk_t[:, j * 512:(j + 1) * 512], ins_extra=[f"qk_t@{j}"],
                          tag=f"st{which}{pc}_{j}", batch=bq)
    stA_outer.close()
    p.fence()

    l0mask = C["c_l0mask"]; l0diag = C["c_l0diag"]
    with ExitStack() as stB:
        sB = S(nc, stB)
        kp = sB.sb("kp", [128, T], BF16)
        qp = sB.sb("qp", [128, T], BF16)
        vp = sB.sb("vp", [128, T], BF16)
        clr = [sB.sb(f"clr{h}", [1, 512]) for h in range(2)]
        clb = [sB.sb(f"clb{h}", [128, 512]) for h in range(2)]
        Et = [sB.sb(f"Et{h}", [128, 512]) for h in range(2)]
        Wt = [sB.sb(f"Wt{h}", [128, 512], BF16) for h in range(2)]
        ps_S = [sB.ps(f"ps_S{h}", [128, 512]) for h in range(2)]
        ps_o = [sB.ps(f"ps_o{h}", [128, 512]) for h in range(2)]
        ps_cb = sB.ps("ps_cb", [128, 512])
        for pc in range(8):
            rsl = slice(pc * 128, (pc + 1) * 128)
            bqk = p.open_batch("sync")
            p.dma("sync", kp[:], k_dr[rsl, :], tag=f"B_ldk{pc}", batch=bqk)
            p.dma("sync", qp[:], q_dr[rsl, :], tag=f"B_ldq{pc}", batch=bqk)
            p.dma("sync", vp[:].rearrange("p (sb j) -> p sb j", j=128),
                  v_dr[:, rsl].rearrange("(sb p) j -> p sb j", p=128),
                  tag=f"B_ldv{pc}", batch=bqk)
            for tb in range(NTB):
                tsl = slice(tb * 512, (tb + 1) * 512)
                for h in range(2):
                    p.dma("sync", clr[h][:], cl_dr[2 * pc + h: 2 * pc + h + 1, tsl],
                          tag=f"B_ldcl{pc}_{tb}_{h}")
                    p.emit("tensor", lambda e, h=h: e.matmul(
                        ps_cb[:], onesr[0:1, :], clr[h][:], start=True, stop=True),
                        ins=[onesr[:], clr[h][:]], outs=[ps_cb[:]], tag=f"B_cb{pc}_{tb}_{h}")
                    p.emit("vector", lambda e, h=h: e.tensor_copy(clb[h][:], ps_cb[:]),
                           ins=[ps_cb[:]], outs=[clb[h][:]], tag=f"B_cbe{pc}_{tb}_{h}")
                for sb in range(4 * tb, NSB):
                    r = sb - 4 * tb
                    for h in range(2):
                        hof = 64 * h
                        gh = 2 * pc + h
                        p.emit("tensor", lambda e, h=h, hof=hof, sb=sb, tsl=tsl: e.matmul(
                            ps_S[h][:], kp[hof:hof + 64, sb * 128:(sb + 1) * 128],
                            qp[hof:hof + 64, tsl], start=True, stop=True,
                            tile_position=(hof, 0)),
                            ins=[kp[:], qp[:]], outs=[ps_S[h][:]],
                            tag=f"B_S{pc}_{tb}_{sb}_{h}")
                        p.emit("vector", lambda e, h=h, sb=sb, gh=gh: e.tensor_scalar(
                            Et[h][:], clb[h][:],
                            clt_tok[:, sb * 16 + gh: sb * 16 + gh + 1], 0.0,
                            ALU.subtract, ALU.min),
                            ins=[clb[h][:], f"clt_tok@{sb}"], outs=[Et[h][:]],
                            tag=f"B_E{pc}_{tb}_{sb}_{h}")
                        p.emit("scalar", lambda e, h=h: e.activation(Et[h][:], Et[h][:],
                                                                     AF.Exp),
                               ins=[Et[h][:]], outs=[Et[h][:]], tag=f"B_X{pc}_{tb}_{sb}_{h}")
                        if r < 4:
                            p.emit("vector", lambda e, h=h, r=r: e.tensor_mul(
                                Et[h][:], Et[h][:], l0mask[:][:, r * 512:(r + 1) * 512]),
                                ins=[Et[h][:], l0mask[:]], outs=[Et[h][:]],
                                tag=f"B_M{pc}_{tb}_{sb}_{h}")
                        p.emit("vector", lambda e, h=h: e.tensor_mul(
                            Wt[h][:], ps_S[h][:], Et[h][:]),
                            ins=[ps_S[h][:], Et[h][:]], outs=[Wt[h][:]],
                            tag=f"B_W{pc}_{tb}_{sb}_{h}")
                        if r < 4:
                            p.emit("vector", lambda e, h=h, r=r, sb=sb, gh=gh:
                                   e.scalar_tensor_tensor(
                                Wt[h][:], l0diag[:][:, r * 512:(r + 1) * 512],
                                bt_tok[:, sb * 16 + gh: sb * 16 + gh + 1], Wt[h][:],
                                ALU.mult, ALU.add),
                                ins=[l0diag[:], f"bt_tok@{sb}", Wt[h][:]],
                                outs=[Wt[h][:]], tag=f"B_D{pc}_{tb}_{sb}_{h}")
                        p.emit("tensor", lambda e, h=h, hof=hof, sb=sb, tb=tb: e.matmul(
                            ps_o[h][hof:hof + 64, :],
                            vp[:, sb * 128 + hof: sb * 128 + hof + 64], Wt[h][:],
                            start=(sb == 4 * tb), stop=(sb == NSB - 1),
                            tile_position=(0, hof)),
                            ins=[vp[:], Wt[h][:]], outs=[ps_o[h][:]],
                            tag=f"B_O{pc}_{tb}_{sb}_{h}")
                for h in range(2):
                    hof = 64 * h
                    p.emit("vector", lambda e, h=h, hof=hof, tsl=tsl, pc=pc:
                           e.tensor_copy(oT[pc][hof:hof + 64, tsl],
                                         ps_o[h][hof:hof + 64, :]),
                        ins=[ps_o[h][:]], outs=[f"oT{pc}@{tb}_{h}"],
                        tag=f"B_oe{pc}_{tb}_{h}")
    p.fence()

    with ExitStack() as stC:
        sC = S(nc, stC)
        wo_st = sC.sb("wo_st", [128, D])
        wo_b = load_w_bf16(p, sC, C, wout_d, "wo", D, wo_st)
        xbuf = sC.sb("C_xb", [128, 512])
        xnew = sC.sb("C_xn", [128, 512])
        ps_y = [sC.ps(f"ps_y{i}", [128, 512]) for i in range(2)]
        for tb in range(NTB):
            tsl = slice(tb * 512, (tb + 1) * 512)
            for dc in range(NDC):
                pb = ps_y[dc % 2]
                for pc in range(8):
                    p.emit("tensor", lambda e, pc=pc, dc=dc, tsl=tsl, pb=pb: e.matmul(
                        pb[:], wo_b[pc][:, dc * 128:(dc + 1) * 128], oT[pc][:, tsl],
                        start=(pc == 0), stop=(pc == 7)),
                        ins=[wo_b[pc][:]] + [f"oT{pc}@{tb}_{h}" for h in range(2)],
                        outs=[pb[:]], tag=f"C_mm{tb}_{dc}_{pc}")
                p.dma("sync", xbuf[:], x_dram[dc * 128:(dc + 1) * 128, tsl],
                      tag=f"C_ldx{tb}_{dc}")
                p.emit("vector", lambda e, pb=pb: e.tensor_add(xnew[:], xbuf[:], pb[:]),
                       ins=[xbuf[:], pb[:]], outs=[xnew[:]], tag=f"C_add{tb}_{dc}")
                p.dma("pool", xout_d[dc * 128:(dc + 1) * 128, tsl], xnew[:],
                      ins_extra=[xnew[:]], tag=f"C_st{tb}_{dc}")
    p.fence()


def host_consts_ffn():
    c = {}
    jg = np.zeros((128, 64), np.float32)
    for jj in range(8):
        jg[:, jj * 8:(jj + 1) * 8] = (np.arange(8)[None, :] > jj)
    c["c_jgt"] = jg
    return c


def l0_ffn(p, s, C, nc, x_dram, rw_d, w13_d, w2_d, xout_d, sc_dr):
    from contextlib import ExitStack
    ones = C["c_ones"]; onesr = C["c_onesr"]; eye32 = C["c_eye32"]
    jgt = C["c_jgt"]
    xn = [s.sb(f"fxn{c}", [128, T], BF16) for c in range(NDC)]
    scale_dr = sc_dr

    with ExitStack() as st1:
        s1 = S(nc, st1)
        xlo = [s1.sb(f"fxl{c}", [128, T], BF16) for c in range(NDC)]
        with ExitStack() as stLN:
            ln_fm(p, S(nc, stLN), C, x_dram, xn, xn_lo=xlo, tag="ln0f")
        p.fence()
        with ExitStack() as st2:
            s2 = S(nc, st2)
            rw_st = s2.sb("rw_st", [8, D])
            p.dma("sync", rw_st[:], rw_d, tag="ld_rw")
            ps_rt = s2.ps("ps_rt", [128, 8])
            rw_hi = s2.sb("rw_hi", [128, 8 * NDC], BF16)
            rw_lo = s2.sb("rw_lo", [128, 8 * NDC], BF16)
            rw_f = s2.sb("rw_f", [128, 8])
            for c in range(NDC):
                dst = slice(c * 8, (c + 1) * 8)
                p.emit("tensor", lambda e, c=c: e.transpose(
                    ps_rt[:], rw_st[:, c * 128:(c + 1) * 128], eye32[:][0:8, 0:8]),
                    ins=[rw_st[:], eye32[:]], outs=[ps_rt[:]], tag=f"rtr{c}")
                p.emit("vector", lambda e: e.tensor_copy(rw_f[:], ps_rt[:]),
                       ins=[ps_rt[:]], outs=[rw_f[:]], tag=f"rtre{c}")
                p.emit("scalar", lambda e, dst=dst: e.copy(rw_hi[:, dst], rw_f[:]),
                       ins=[rw_f[:]], outs=[f"rw_hi@{c}"], tag=f"rthi{c}")
                p.emit("vector", lambda e, dst=dst: e.tensor_sub(
                    rw_lo[:, dst], rw_f[:], rw_hi[:, dst]),
                    ins=[rw_f[:], f"rw_hi@{c}"], outs=[f"rw_lo@{c}"], tag=f"rtlo{c}")
            ps_lg = s2.ps("ps_lg", [8, 512])
            lgT = s2.sb("lgT", [8, T])
            for j in range(NTB):
                cols = slice(j * 512, (j + 1) * 512)
                nmm = 3 * NDC
                i = 0
                for c in range(NDC):
                    for (wa, xb) in ((rw_hi, xn), (rw_hi, xlo), (rw_lo, xn)):
                        wreg = "rw_hi" if wa is rw_hi else "rw_lo"
                        xreg = "fxn" if xb is xn else "fxl"
                        p.emit("tensor", lambda e, c=c, cols=cols, wa=wa, xb=xb,
                               i=i, nmm=nmm: e.matmul(
                            ps_lg[:], wa[:, c * 8:(c + 1) * 8], xb[c][:, cols],
                            start=(i == 0), stop=(i == nmm - 1)),
                            ins=[f"{wreg}@{c}", f"{xreg}{c}@{j}"], outs=[ps_lg[:]],
                            tag=f"rl{j}_{c}_{i}")
                        i += 1
                p.emit("vector", lambda e, cols=cols: e.tensor_copy(lgT[:, cols],
                                                                    ps_lg[:]),
                       ins=[ps_lg[:]], outs=[f"lgT@{j}"], tag=f"rle{j}")
            ps_tt = s2.ps("ps_tt", [128, 8])
            lg = s2.sb("lg", [128, 8])
            rmx = s2.sb("rmx", [128, 2])
            es = s2.sb("es", [128, 8])
            rank = s2.sb("rank", [128, 8])
            eqt = s2.sb("eqt", [128, 8])
            sel = s2.sb("sel", [128, 8])
            scl = s2.sb("scl", [128, 8])
            sct = s2.sb("sct", [8, 128])
            ps_sc = s2.ps("ps_sc", [8, 128])
            for sb in range(NSB):
                sl = slice(sb * 128, (sb + 1) * 128)
                p.emit("tensor", lambda e, sl=sl: e.transpose(
                    ps_tt[:], lgT[:, sl], eye32[:][0:8, 0:8]),
                    ins=[f"lgT@{sb // 4}", eye32[:]], outs=[ps_tt[:]], tag=f"qt{sb}")
                p.emit("vector", lambda e: e.tensor_copy(lg[:], ps_tt[:]),
                       ins=[ps_tt[:]], outs=[lg[:]], tag=f"qte{sb}")
                p.emit("vector", lambda e: e.tensor_reduce(rmx[:, 0:1], lg[:], AX.X,
                                                           ALU.max),
                       ins=[lg[:]], outs=["rmx_m"], tag=f"qmx{sb}")
                p.emit("vector", lambda e: e.tensor_scalar(rmx[:, 1:2], rmx[:, 0:1],
                                                           -1.0, None, ALU.mult),
                       ins=["rmx_m"], outs=["rmx_n"], tag=f"qng{sb}")
                p.emit("scalar", lambda e: e.activation(es[:], lg[:], AF.Exp,
                                                        bias=rmx[:, 1:2]),
                       ins=[lg[:], "rmx_n"], outs=[es[:]], tag=f"qex{sb}")
                p.emit("vector", lambda e: e.tensor_reduce(rmx[:, 0:1], es[:], AX.X,
                                                           ALU.add),
                       ins=[es[:]], outs=["rmx_s"], tag=f"qsm{sb}")
                p.emit("vector", lambda e: e.reciprocal(rmx[:, 0:1], rmx[:, 0:1]),
                       ins=["rmx_s"], outs=["rmx_r"], tag=f"qrc{sb}")
                p.emit("vector", lambda e: e.tensor_scalar(es[:], es[:], rmx[:, 0:1],
                                                           None, ALU.mult),
                       ins=[es[:], "rmx_r"], outs=[es[:]], tag=f"qpr{sb}")
                p.emit("pool", lambda e: e.memset(rank[:], 0.0),
                       outs=[rank[:]], tag=f"qr0{sb}")
                for jj in range(8):
                    p.emit("vector", lambda e, jj=jj: e.scalar_tensor_tensor(
                        rank[:], es[:], es[:, jj:jj + 1], rank[:], ALU.is_lt, ALU.add),
                        ins=[es[:], rank[:]], outs=[rank[:]], tag=f"qlt{sb}_{jj}")
                    p.emit("vector", lambda e, jj=jj: e.tensor_scalar(
                        eqt[:], es[:], es[:, jj:jj + 1], None, ALU.is_equal),
                        ins=[es[:]], outs=[eqt[:]], tag=f"qeq{sb}_{jj}")
                    p.emit("pool", lambda e, jj=jj: e.tensor_mul(
                        eqt[:], eqt[:], jgt[:][:, jj * 8:(jj + 1) * 8]),
                        ins=[eqt[:], jgt[:]], outs=[eqt[:]], tag=f"qm2{sb}_{jj}")
                    p.emit("vector", lambda e: e.tensor_add(rank[:], rank[:], eqt[:]),
                           ins=[rank[:], eqt[:]], outs=[rank[:]], tag=f"qad{sb}_{jj}")
                p.emit("vector", lambda e: e.tensor_scalar(sel[:], rank[:], 1.5, None,
                                                           ALU.is_le),
                       ins=[rank[:]], outs=[sel[:]], tag=f"qsl{sb}")
                p.emit("vector", lambda e: e.tensor_mul(sel[:], sel[:], es[:]),
                       ins=[sel[:], es[:]], outs=[sel[:]], tag=f"qw{sb}")
                p.emit("vector", lambda e: e.tensor_reduce(rmx[:, 0:1], sel[:], AX.X,
                                                           ALU.add),
                       ins=[sel[:]], outs=["rmx_s2"], tag=f"qws{sb}")
                p.emit("vector", lambda e: e.tensor_scalar(rmx[:, 0:1], rmx[:, 0:1],
                                                           1e-6, None, ALU.add),
                       ins=["rmx_s2"], outs=["rmx_s2"], tag=f"qwe{sb}")
                p.emit("vector", lambda e: e.reciprocal(rmx[:, 0:1], rmx[:, 0:1]),
                       ins=["rmx_s2"], outs=["rmx_r2"], tag=f"qwr{sb}")
                p.emit("vector", lambda e: e.tensor_scalar(scl[:], sel[:], rmx[:, 0:1],
                                                           None, ALU.mult),
                       ins=[sel[:], "rmx_r2"], outs=[scl[:]], tag=f"qsc{sb}")
                p.emit("tensor", lambda e: e.transpose(ps_sc[:], scl[:],
                                                       eye32[:]),
                       ins=[scl[:], eye32[:]], outs=[ps_sc[:]], tag=f"qtr{sb}")
                p.emit("vector", lambda e: e.tensor_copy(sct[:], ps_sc[0:8, :]),
                       ins=[ps_sc[:]], outs=[sct[:]], tag=f"qse{sb}")
                p.dma("pool", scale_dr[:, sb * 128:(sb + 1) * 128], sct[:],
                      tag=f"qst{sb}")
    p.fence()
    return xn


def l0_ffn_experts(p, s, C, nc, xn, w13_d, w2_d, x_dram, xout_d, sc_dr):
    from contextlib import ExitStack
    onesr = C["c_onesr"]
    with ExitStack() as st:
        s3 = S(nc, st)
        w13s = s3.sb("w13s", [128, 1024])
        w13b = [s3.sb(f"w13b{c}", [128, 4096], BF16) for c in range(NDC)]
        w2s = s3.sb("w2s", [128, D])
        w2b = [s3.sb(f"w2b{i}", [128, D], BF16) for i in range(16)]
        xnu = [s3.sb(f"xnu{c}", [128, 512], BF16) for c in range(NDC)]
        scr = s3.sb("scr", [1, 512])
        hsil = s3.sb("hsil", [128, 512])
        hbuf = [s3.sb(f"hbuf{i}", [128, 512], BF16) for i in range(16)]
        xacc = s3.sb("xacc", [128, 512])
        ps_g = s3.ps("ps_g", [128, 512])
        ps_u = s3.ps("ps_u", [128, 512])
        ps_sb = s3.ps("ps_sb", [128, 512])
        ps_y = [s3.ps(f"ps_yy{i}", [128, 512]) for i in range(2)]
        for ei in range(8):
            for c in range(NDC):
                for hf in range(4):
                    p.dma("sync", w13s[:],
                          w13_d[ei, c * 128:(c + 1) * 128,
                                hf * 1024:(hf + 1) * 1024],
                          tag=f"E{ei}_ldw13_{c}_{hf}")
                    p.emit("vector", lambda e, c=c, hf=hf: e.tensor_copy(
                        w13b[c][:, hf * 1024:(hf + 1) * 1024], w13s[:]),
                        ins=[w13s[:]], outs=[f"w13b{c}@{hf // 2}"],
                        tag=f"E{ei}_cw13_{c}_{hf}")
            for i in range(16):
                p.dma("sync", w2s[:], w2_d[ei, i * 128:(i + 1) * 128, :],
                      tag=f"E{ei}_ldw2_{i}")
                p.emit("scalar", lambda e, i=i: e.copy(w2b[i][:], w2s[:]),
                       ins=[w2s[:]], outs=[w2b[i][:]], tag=f"E{ei}_cw2_{i}")
            for j in range(NTB):
                cols = slice(j * 512, (j + 1) * 512)
                p.dma("sync", scr[:], sc_dr[ei:ei + 1, cols], tag=f"E{ei}_ldsc_{j}")
                p.emit("tensor", lambda e: e.matmul(ps_sb[:], onesr[0:1, :], scr[:],
                                                    start=True, stop=True),
                       ins=[onesr[:], scr[:]], outs=[ps_sb[:]], tag=f"E{ei}_scb_{j}")
                for c in range(NDC):
                    p.emit("vector", lambda e, c=c, cols=cols: e.tensor_mul(
                        xnu[c][:], xn[c][:, cols], ps_sb[:]),
                        ins=[f"fxn{c}@{j}", ps_sb[:]], outs=[xnu[c][:]],
                        tag=f"E{ei}_xu_{j}_{c}")
                for ip in range(16):
                    for c in range(NDC):
                        p.emit("tensor", lambda e, c=c, ip=ip, cols=cols: e.matmul(
                            ps_g[:], w13b[c][:, ip * 128:(ip + 1) * 128],
                            xn[c][:, cols], start=(c == 0), stop=(c == NDC - 1)),
                            ins=[f"w13b{c}@0", f"fxn{c}@{j}"], outs=[ps_g[:]],
                            tag=f"E{ei}_g_{j}_{ip}_{c}")
                    for c in range(NDC):
                        p.emit("tensor", lambda e, c=c, ip=ip: e.matmul(
                            ps_u[:], w13b[c][:, 2048 + ip * 128: 2048 + (ip + 1) * 128],
                            xnu[c][:], start=(c == 0), stop=(c == NDC - 1)),
                            ins=[f"w13b{c}@1", xnu[c][:]], outs=[ps_u[:]],
                            tag=f"E{ei}_u_{j}_{ip}_{c}")
                    p.emit("scalar", lambda e: e.activation(hsil[:], ps_g[:], AF.Silu),
                           ins=[ps_g[:]], outs=[hsil[:]], tag=f"E{ei}_sil_{j}_{ip}")
                    p.emit("vector", lambda e, ip=ip: e.tensor_mul(hbuf[ip][:], hsil[:],
                                                                   ps_u[:]),
                           ins=[hsil[:], ps_u[:]], outs=[hbuf[ip][:]],
                           tag=f"E{ei}_h_{j}_{ip}")
                for hc in range(NDC):
                    pb = ps_y[hc % 2]
                    for i in range(16):
                        p.emit("tensor", lambda e, i=i, hc=hc, pb=pb: e.matmul(
                            pb[:], w2b[i][:, hc * 128:(hc + 1) * 128], hbuf[i][:],
                            start=(i == 0), stop=(i == 15)),
                            ins=[w2b[i][:], hbuf[i][:]], outs=[pb[:]],
                            tag=f"E{ei}_d_{j}_{hc}_{i}")
                    src_d = x_dram if ei == 0 else xout_d
                    p.dma("sync", xacc[:], src_d[hc * 128:(hc + 1) * 128, cols],
                          tag=f"E{ei}_lda_{j}_{hc}")
                    p.emit("vector", lambda e, pb=pb: e.tensor_add(xacc[:], xacc[:],
                                                                   pb[:]),
                           ins=[xacc[:], pb[:]], outs=[xacc[:]], tag=f"E{ei}_acc_{j}_{hc}")
                    p.dma("pool", xout_d[hc * 128:(hc + 1) * 128, cols], xacc[:],
                          tag=f"E{ei}_st_{j}_{hc}")
    p.fence()


def l1_ffn_block(p, s, C, nc, x_dram, FD, xout_d, q_dr, k_dr, v_dr):
    from contextlib import ExitStack
    with ExitStack() as st:
        sF = S(nc, st)
        xn = l1_ffn(p, sF, C, nc, x_dram, FD, xout_d, q_dr, k_dr, v_dr)
        selT = _SELT
        l1_ffn_fff(p, sF, C, nc, xn, selT["q"][:], "q", FD, q_dr, False, 1.0 / 32.0)
        l1_ffn_fff(p, sF, C, nc, xn, selT["k"][:], "k", FD, k_dr, False, 1.0)
        l1_ffn_fff(p, sF, C, nc, xn, selT["v"][:], "v", FD, v_dr, True, 1.0)
    p.fence()
    l1_ffn_attn(p, s, C, nc, x_dram, xout_d, q_dr, k_dr, v_dr)


def l1_ffn(p, s, C, nc, x_dram, FD, xout_d, q_dr, k_dr, v_dr):
    from contextlib import ExitStack
    ones = C["c_ones"]; onesr = C["c_onesr"]; eye32 = C["c_eye32"]
    eyebf = C["c_eyebf"]; hbbf = C["c_hbbf"]
    xn = [s.sb(f"uxn{c}", [128, T], BF16) for c in range(NDC)]
    selT = {f: s.sb(f"selT_{f}", [16, T], BF16) for f in "qkv"}
    global _SELT
    _SELT = selT

    with ExitStack() as st1:
        s1 = S(nc, st1)
        xlo = [s1.sb(f"uxl{c}", [128, T], BF16) for c in range(NDC)]
        with ExitStack() as stLN:
            ln_fm(p, S(nc, stLN), C, x_dram, xn, xn_lo=xlo, tag="ln1f")
        p.fence()
        with ExitStack() as st2:
            s2 = S(nc, st2)
            nw_st = s2.sb("nw_st", [15, D])
            nb_st = s2.sb("nb_st", [15, 1])
            ps_nt = s2.ps("ps_nt", [128, 15])
            nw_hi = s2.sb("nw_hi", [128, 15 * NDC], BF16)
            nw_lo = s2.sb("nw_lo", [128, 15 * NDC], BF16)
            nw_f = s2.sb("nw_f", [128, 15])
            ps_ns = s2.ps("ps_ns", [15, 512])
            nsT = s2.sb("nsT", [15, T])
            ps_st = s2.ps("ps_st", [128, 15])
            sco = s2.sb("sco", [128, 15])
            pos = s2.sb("pos", [128, 15])
            neg = s2.sb("neg", [128, 15])
            Qb = s2.sb("Qb", [128, 16])
            Qc = s2.sb("Qc", [128, 16])
            ps_sel = s2.ps("ps_sel", [16, 128])
            for f in "qkv":
                b = p.open_batch("sync")
                p.dma("sync", nw_st[:], FD[f + "_node_w"], tag=f"N{f}_ldw", batch=b)
                p.dma("sync", nb_st[:], FD[f + "_node_b"], tag=f"N{f}_ldb", batch=b)
                for c in range(NDC):
                    dst = slice(c * 15, (c + 1) * 15)
                    p.emit("tensor", lambda e, c=c: e.transpose(
                        ps_nt[:], nw_st[:, c * 128:(c + 1) * 128], eye32[:][0:15, 0:15]),
                        ins=[nw_st[:], eye32[:]], outs=[ps_nt[:]], tag=f"N{f}tr{c}")
                    p.emit("vector", lambda e: e.tensor_copy(nw_f[:], ps_nt[:]),
                           ins=[ps_nt[:]], outs=[nw_f[:]], tag=f"N{f}te{c}")
                    p.emit("scalar", lambda e, dst=dst: e.copy(nw_hi[:, dst], nw_f[:]),
                           ins=[nw_f[:]], outs=[f"nw_hi@{c}"], tag=f"N{f}hi{c}")
                    p.emit("vector", lambda e, dst=dst: e.tensor_sub(
                        nw_lo[:, dst], nw_f[:], nw_hi[:, dst]),
                        ins=[nw_f[:], f"nw_hi@{c}"], outs=[f"nw_lo@{c}"], tag=f"N{f}lo{c}")
                for j in range(NTB):
                    cols = slice(j * 512, (j + 1) * 512)
                    i = 0
                    for c in range(NDC):
                        for (wa, xb, wreg, xreg) in ((nw_hi, xn, "nw_hi", "uxn"),
                                                     (nw_hi, xlo, "nw_hi", "uxl"),
                                                     (nw_lo, xn, "nw_lo", "uxn")):
                            p.emit("tensor", lambda e, c=c, cols=cols, wa=wa, xb=xb,
                                   i=i: e.matmul(
                                ps_ns[:], wa[:, c * 15:(c + 1) * 15], xb[c][:, cols],
                                start=(i == 0), stop=(i == 3 * NDC - 1)),
                                ins=[f"{wreg}@{c}", f"{xreg}{c}@{j}"], outs=[ps_ns[:]],
                                tag=f"N{f}mm{j}_{c}_{i}")
                            i += 1
                    p.emit("vector", lambda e, cols=cols: e.tensor_scalar(
                        nsT[:, cols], ps_ns[:], nb_st[:], None, ALU.add),
                        ins=[ps_ns[:], nb_st[:]], outs=[f"nsT@{j}"], tag=f"N{f}b{j}")
                for sb in range(NSB):
                    sl = slice(sb * 128, (sb + 1) * 128)
                    p.emit("tensor", lambda e, sl=sl: e.transpose(
                        ps_st[:], nsT[:, sl], eye32[:][0:15, 0:15]),
                        ins=[f"nsT@{sb // 4}", eye32[:]], outs=[ps_st[:]],
                        tag=f"N{f}tt{sb}")
                    p.emit("vector", lambda e: e.tensor_copy(sco[:], ps_st[:]),
                           ins=[ps_st[:]], outs=[sco[:]], tag=f"N{f}se{sb}")
                    p.emit("vector", lambda e: e.tensor_scalar(pos[:], sco[:], 0.0,
                                                               None, ALU.is_ge),
                           ins=[sco[:]], outs=[pos[:]], tag=f"N{f}pg{sb}")
                    p.emit("vector", lambda e: e.tensor_scalar(neg[:], sco[:], 0.0,
                                                               None, ALU.is_lt),
                           ins=[sco[:]], outs=[neg[:]], tag=f"N{f}ng{sb}")
                    p.emit("vector", lambda e: e.tensor_copy(Qb[:, 0:1], neg[:, 0:1]),
                           ins=[neg[:]], outs=["Qb"], tag=f"N{f}q0{sb}")
                    p.emit("vector", lambda e: e.tensor_copy(Qb[:, 1:2], pos[:, 0:1]),
                           ins=[pos[:], "Qb"], outs=["Qb"], tag=f"N{f}q1{sb}")
                    cur, nxt = Qb, Qc
                    base = 1
                    for lev in range(1, 4):
                        n = 2 ** lev
                        nodes = slice(base, base + n)
                        p.emit("vector", lambda e, cur=cur, nxt=nxt, n=n, nodes=nodes:
                               e.tensor_mul(
                            nxt[:].rearrange("p (a b) -> p a b", b=2)[:, 0:n, 0:1],
                            cur[:, 0:n].unsqueeze(2), neg[:, nodes].unsqueeze(2)),
                            ins=["Qb" if cur is Qb else "Qc", neg[:]],
                            outs=["halfA"], tag=f"N{f}e{sb}_{lev}")
                        p.emit("vector", lambda e, cur=cur, nxt=nxt, n=n, nodes=nodes:
                               e.tensor_mul(
                            nxt[:].rearrange("p (a b) -> p a b", b=2)[:, 0:n, 1:2],
                            cur[:, 0:n].unsqueeze(2), pos[:, nodes].unsqueeze(2)),
                            ins=["Qb" if cur is Qb else "Qc", pos[:], "halfA"],
                            outs=["Qc" if nxt is Qc else "Qb"], tag=f"N{f}o{sb}_{lev}")
                        cur, nxt = nxt, cur
                        base += n
                    p.emit("tensor", lambda e, cur=cur: e.transpose(
                        ps_sel[:], cur[:, 0:16].bitcast(F32) if False else cur[:, 0:16],
                        eye32[:]),
                        ins=["Qc" if cur is Qc else "Qb", eye32[:]], outs=[ps_sel[:]],
                        tag=f"N{f}st{sb}")
                    p.emit("vector", lambda e, f=f, sb=sb: e.tensor_copy(
                        selT[f][:, sb * 128:(sb + 1) * 128], ps_sel[:]),
                        ins=[ps_sel[:]], outs=[f"selT_{f}@{sb}"], tag=f"N{f}sv{sb}")
    p.fence()
    return xn


def l1_ffn_fff(p, s, C, nc, xn, selT, f, FD, out_dr, token_major, qscale):
    from contextlib import ExitStack
    hbbf = C["c_hbbf"]
    with ExitStack() as st:
        s3 = S(nc, st)
        w1s = s3.sb(f"{f}_w1s", [128, 2048])
        w1b = [s3.sb(f"{f}_w1b{c}", [128, D], BF16) for c in range(NDC)]
        w2b = [s3.sb(f"{f}_w2b{c}", [128, D], BF16) for c in range(NDC)]
        b1t = s3.sb(f"{f}_b1t", [128, NDC])
        b2b = s3.sb(f"{f}_b2b", [16, D], BF16)
        b2s = s3.sb(f"{f}_b2s", [16, D])
        hrel = s3.sb(f"{f}_hrel", [128, 512])
        hm = [s3.sb(f"{f}_hm{cc}", [128, 512], BF16) for cc in range(NDC)]
        orow = s3.sb(f"{f}_orow", [128, 512], BF16)
        ps_h = s3.ps(f"{f}_ps_h", [128, 512])
        ps_sb2 = s3.ps(f"{f}_ps_sb2", [128, 512])
        ps_d = [s3.ps(f"{f}_ps_d{i}", [128, 512]) for i in range(2)]
        for c in range(NDC):
            p.dma("sync", w1s[:, 0:1024], FD[f + "_w1cat"][c * 128:(c + 1) * 128, :],
                  tag=f"F{f}_lw1_{c}")
            p.emit("vector", lambda e, c=c: e.tensor_copy(w1b[c][:], w1s[:, 0:1024]),
                   ins=[w1s[:]], outs=[w1b[c][:]], tag=f"F{f}_cw1_{c}")
            p.dma("sync", w1s[:, 1024:2048], FD[f + "_w2cat"][c * 128:(c + 1) * 128, :],
                  tag=f"F{f}_lw2_{c}")
            p.emit("scalar", lambda e, c=c: e.copy(w2b[c][:], w1s[:, 1024:2048]),
                   ins=[w1s[:]], outs=[w2b[c][:]], tag=f"F{f}_cw2_{c}")
        b0 = p.open_batch("sync")
        p.dma("sync", b1t[:], FD[f + "_b1cat"], tag=f"F{f}_lb1", batch=b0)
        p.dma("sync", b2s[:], FD[f + "_b2"], tag=f"F{f}_lb2", batch=b0)
        p.emit("vector", lambda e: e.tensor_copy(b2b[:], b2s[:]),
               ins=[b2s[:]], outs=[b2b[:]], tag=f"F{f}_cb2")
        for j in range(NTB):
            cols = slice(j * 512, (j + 1) * 512)
            for cc in range(NDC):
                for c in range(NDC):
                    p.emit("tensor", lambda e, c=c, cc=cc, cols=cols: e.matmul(
                        ps_h[:], w1b[c][:, cc * 128:(cc + 1) * 128], xn[c][:, cols],
                        start=(c == 0), stop=(c == NDC - 1)),
                        ins=[w1b[c][:], f"uxn{c}@{j}"], outs=[ps_h[:]],
                        tag=f"F{f}_h_{j}_{cc}_{c}")
                p.emit("scalar", lambda e, cc=cc: e.activation(
                    hrel[:], ps_h[:], AF.Relu, bias=b1t[:, cc:cc + 1]),
                    ins=[ps_h[:], b1t[:]], outs=[hrel[:]], tag=f"F{f}_r_{j}_{cc}")
                p.emit("tensor", lambda e, cc=cc, cols=cols: e.matmul(
                    ps_sb2[:], hbbf[:][:, cc * 128:(cc + 1) * 128], selT[:, cols],
                    start=True, stop=True),
                    ins=[hbbf[:]] + [f"selT_{f}@{sb}" for sb in range(4 * j, 4 * j + 4)],
                    outs=[ps_sb2[:]], tag=f"F{f}_sb_{j}_{cc}")
                p.emit("vector", lambda e, cc=cc: e.tensor_mul(hm[cc][:], hrel[:],
                                                               ps_sb2[:]),
                       ins=[hrel[:], ps_sb2[:]], outs=[hm[cc][:]],
                       tag=f"F{f}_m_{j}_{cc}")
            if not token_major:
                for oc in range(NDC):
                    pb = ps_d[oc % 2]
                    for cc in range(NDC):
                        p.emit("tensor", lambda e, cc=cc, oc=oc, pb=pb: e.matmul(
                            pb[:], w2b[cc][:, oc * 128:(oc + 1) * 128], hm[cc][:],
                            start=(cc == 0), stop=False),
                            ins=[w2b[cc][:], hm[cc][:]], outs=[pb[:]],
                            tag=f"F{f}_d_{j}_{oc}_{cc}")
                    p.emit("tensor", lambda e, oc=oc, cols=cols, pb=pb: e.matmul(
                        pb[:], b2b[:, oc * 128:(oc + 1) * 128], selT[:, cols],
                        start=False, stop=True),
                        ins=[b2b[:]] + [f"selT_{f}@{sb}" for sb in range(4 * j, 4 * j + 4)],
                        outs=[pb[:]], tag=f"F{f}_db_{j}_{oc}")
                    p.emit("vector", lambda e, pb=pb: e.tensor_scalar(
                        orow[:], pb[:], qscale, None, ALU.mult),
                        ins=[pb[:]], outs=[orow[:]], tag=f"F{f}_oe_{j}_{oc}")
                    p.dma("pool", out_dr[oc * 128:(oc + 1) * 128, cols], orow[:],
                          tag=f"F{f}_os_{j}_{oc}")
            else:
                for tq in range(4):
                    trow = j * 4 + tq
                    tsl = slice(tq * 128, (tq + 1) * 128)
                    for oc in range(2):
                        pb = ps_d[oc]
                        osl = slice(oc * 512, (oc + 1) * 512)
                        for cc in range(NDC):
                            p.emit("tensor", lambda e, cc=cc, tsl=tsl, osl=osl, pb=pb:
                                   e.matmul(
                                pb[:], hm[cc][:, tsl], w2b[cc][:, osl],
                                start=(cc == 0), stop=False),
                                ins=[w2b[cc][:], hm[cc][:]], outs=[pb[:]],
                                tag=f"F{f}_vd_{trow}_{oc}_{cc}")
                        p.emit("tensor", lambda e, trow=trow, osl=osl, pb=pb: e.matmul(
                            pb[:], selT[:, trow * 128:(trow + 1) * 128], b2b[:, osl],
                            start=False, stop=True),
                            ins=[b2b[:], f"selT_{f}@{trow}"], outs=[pb[:]],
                            tag=f"F{f}_vb_{trow}_{oc}")
                        p.emit("vector", lambda e, pb=pb: e.tensor_copy(orow[:], pb[:]),
                               ins=[pb[:]], outs=[orow[:]], tag=f"F{f}_ve_{trow}_{oc}")
                        p.dma("pool", out_dr[trow * 128:(trow + 1) * 128, osl], orow[:],
                              tag=f"F{f}_vs_{trow}_{oc}")
    p.fence()


def l1_ffn_attn(p, s, C, nc, x_dram, xout_d, q_dr, k_dr, v_dr):
    from contextlib import ExitStack
    eyebf = C["c_eyebf"]
    with ExitStack() as st:
        s4 = S(nc, st)
        kT = [s4.sb(f"akT{c}", [128, T], BF16) for c in range(NDC)]
        vtk = s4.sb("avtk", [128, NSB * 1024], BF16)
        qb = s4.sb("aqb", [128, NDC * 128], BF16)
        Ab = s4.sb("aAb", [128, T])
        At = s4.sb("aAt", [128, T], BF16)
        At2 = s4.sb("aAt2", [128, T], BF16)
        rmx = s4.sb("armx", [128, 2])
        xb2 = s4.sb("axb", [128, 128])
        yout = s4.sb("ayout", [128, 128])
        ps_A = s4.ps("ps_A", [128, 512])
        ps_T = s4.ps("ps_T", [128, 512], BF16)
        ps_O = [s4.ps(f"ps_O{i}", [128, 128]) for i in range(2)]
        bk = p.open_batch("sync")
        for c in range(NDC):
            p.dma("sync", kT[c][:], k_dr[c * 128:(c + 1) * 128, :], tag=f"A_lk{c}",
                  batch=bk)
        p.dma("sync", vtk[:].rearrange("p (sb o) -> p sb o", o=1024),
              v_dr.rearrange("(sb p) o -> p sb o", p=128), tag="A_lv", batch=bk)
        for qbi in range(NSB):
            p.dma("sync", qb[:].rearrange("p (c q) -> p c q", q=128),
                  q_dr[:, qbi * 128:(qbi + 1) * 128].rearrange(
                      "(c p) q -> p c q", p=128), tag=f"A_lq{qbi}")
            for sc in range(NTB):
                ssl = slice(sc * 512, (sc + 1) * 512)
                for c in range(NDC):
                    p.emit("tensor", lambda e, c=c, ssl=ssl: e.matmul(
                        ps_A[:], qb[:, c * 128:(c + 1) * 128], kT[c][:, ssl],
                        start=(c == 0), stop=(c == NDC - 1)),
                        ins=[qb[:], kT[c][:]], outs=[ps_A[:]], tag=f"A_s{qbi}_{sc}_{c}")
                p.emit("vector", lambda e, ssl=ssl: e.tensor_copy(Ab[:, ssl], ps_A[:]),
                       ins=[ps_A[:]], outs=[f"Ab@{sc}"], tag=f"A_se{qbi}_{sc}")
            p.emit("vector", lambda e: e.tensor_reduce(rmx[:, 0:1], Ab[:], AX.X,
                                                       ALU.max),
                   ins=[f"Ab@{sc}" for sc in range(NTB)], outs=["armx_m"],
                   tag=f"A_mx{qbi}")
            p.emit("vector", lambda e: e.tensor_scalar(rmx[:, 1:2], rmx[:, 0:1], -1.0,
                                                       None, ALU.mult),
                   ins=["armx_m"], outs=["armx_n"], tag=f"A_ng{qbi}")
            p.emit("scalar", lambda e: e.activation(Ab[:], Ab[:], AF.Exp,
                                                    bias=rmx[:, 1:2],
                                                    accum_out=rmx[:, 0:1]),
                   ins=["armx_n"] + [f"Ab@{sc}" for sc in range(NTB)],
                   outs=["armx_s"] + [f"Ab@{sc}" for sc in range(NTB)],
                   tag=f"A_ex{qbi}")
            p.emit("vector", lambda e: e.reciprocal(rmx[:, 0:1], rmx[:, 0:1]),
                   ins=["armx_s"], outs=["armx_r"], tag=f"A_rc{qbi}")
            p.emit("vector", lambda e: e.tensor_scalar(At[:], Ab[:], rmx[:, 0:1],
                                                       None, ALU.mult),
                   ins=["armx_r"] + [f"Ab@{sc}" for sc in range(NTB)],
                   outs=["At_pre"], tag=f"A_pr{qbi}")
            for g in range(NDC):
                for t4 in range(4):
                    sb = g * 4 + t4
                    p.emit("tensor", lambda e, sb=sb, t4=t4: e.transpose(
                        ps_T[:, t4 * 128:(t4 + 1) * 128],
                        At[:, sb * 128:(sb + 1) * 128], eyebf[:]),
                        ins=["At_pre", eyebf[:]], outs=[f"ps_T@{t4}"],
                        tag=f"A_t{qbi}_{sb}")
                p.emit("vector", lambda e, g=g: e.tensor_copy(
                    At2[:, g * 512:(g + 1) * 512], ps_T[:]),
                    ins=[f"ps_T@{t4}" for t4 in range(4)], outs=[f"AtT@{g}"],
                    tag=f"A_te{qbi}_{g}")
            for oc in range(NDC):
                pb = ps_O[oc % 2]
                for sb in range(NSB):
                    p.emit("tensor", lambda e, sb=sb, oc=oc, pb=pb: e.matmul(
                        pb[:], vtk[:, sb * 1024 + oc * 128: sb * 1024 + (oc + 1) * 128],
                        At2[:, sb * 128:(sb + 1) * 128],
                        start=(sb == 0), stop=(sb == NSB - 1)),
                        ins=[vtk[:], f"AtT@{sb // 4}"], outs=[pb[:]],
                        tag=f"A_o{qbi}_{oc}_{sb}")
                p.dma("sync", xb2[:], x_dram[oc * 128:(oc + 1) * 128,
                                             qbi * 128:(qbi + 1) * 128],
                      tag=f"A_lx{qbi}_{oc}")
                p.emit("vector", lambda e, pb=pb: e.tensor_add(yout[:], xb2[:], pb[:]),
                       ins=[xb2[:], pb[:]], outs=[yout[:]], tag=f"A_ad{qbi}_{oc}")
                p.dma("pool", xout_d[oc * 128:(oc + 1) * 128,
                                     qbi * 128:(qbi + 1) * 128], yout[:],
                      tag=f"A_st{qbi}_{oc}")
    p.fence()


NCH = 32
CG_ITERS = 30


def l1_attn(p, s, C, nc, x_dram, WD, xout_d, cft_dr, cf2_dr):
    from contextlib import ExitStack
    ones = C["c_ones"]; onesr = C["c_onesr"]; eye32 = C["c_eye32"]
    hb = C["c_hb"]; hsg = C["c_hsg"]; hs2 = C["c_hs2"]; hb2 = C["c_hb2"]
    mle = C["c_masklehb"]

    cf_tok = s.sb("d_cftok", [128, (T // 128) * 16])
    bt_tok = s.sb("d_bttok", [128, (T // 128) * 16])
    g1T = s.sb("d_g1T", [64, T], BF16)
    lamt = s.sb("d_lamt", [64, 16])
    lamt2 = s.sb("d_lamt2", [128, 16])
    lamt3 = s.sb("d_lamt3", [128, 8])
    lamb_b = s.sb("d_lambb", [128, 1024])
    dlam = s.sb("d_dlam", [128, 16 * 64], BF16)
    nwl = s.sb("d_nwl", [128, 8])
    Hkk = Hkv = Hkkb = Hkvb = None
    gw2t = s.sb("d_gw2t", [64, D], BF16)
    wo_b = [s.sb(f"dwo_b{c}", [128, D], BF16) for c in range(NDC)]
    stQ = ExitStack()
    sQ = S(nc, stQ)
    xn = [sQ.sb(f"dxn{c}", [128, T], BF16) for c in range(NDC)]
    with ExitStack() as stLN:
        ln_fm(p, S(nc, stLN), C, x_dram, xn, tag="ln1a")
    p.fence()

    wq_b = [sQ.sb(f"dwq_b{c}", [128, D], BF16) for c in range(NDC)]
    wk_b = [sQ.sb(f"dwk_b{c}", [128, D], BF16) for c in range(NDC)]
    wv_b = [sQ.sb(f"dwv_b{c}", [128, D], BF16) for c in range(NDC)]
    with ExitStack() as stw:
        wst = S(nc, stw).sb("d_wst", [128, D])
        load_w_bf16(p, s, C, WD["wq"], "dwq", D, wst, tiles=wq_b)
        load_w_bf16(p, s, C, WD["wk"], "dwk", D, wst, tiles=wk_b)
        load_w_bf16(p, s, C, WD["wv"], "dwv", D, wst, tiles=wv_b)
    p.fence()

    with ExitStack() as st2:
        s2 = S(nc, st2)
        lam_s = s2.sb("d_lams", [64, 16])
        p.dma("sync", lam_s[:], WD["lamb"], tag="d_ldlam")
        t1 = s2.sb("d_t1", [64, 16])
        p.emit("scalar", lambda e: e.activation(t1[:], lam_s[:], AF.Abs),
               ins=[lam_s[:]], outs=[t1[:]], tag="d_labs")
        p.emit("scalar", lambda e: e.activation(t1[:], t1[:], AF.Exp, scale=-1.0),
               ins=[t1[:]], outs=[t1[:]], tag="d_lexp")
        p.emit("scalar", lambda e: e.activation(t1[:], t1[:], AF.Ln, bias=1.0),
               ins=[t1[:]], outs=[t1[:]], tag="d_lln")
        p.emit("vector", lambda e: e.tensor_scalar(lamt[:], lam_s[:], 0.0, None,
                                                   ALU.max),
               ins=[lam_s[:]], outs=[lamt[:]], tag="d_lmax")
        p.emit("vector", lambda e: e.tensor_add(lamt[:], lamt[:], t1[:]),
               ins=[lamt[:], t1[:]], outs=[lamt[:]], tag="d_ladd")
        p.emit("vector", lambda e: e.tensor_scalar(lamt[:], lamt[:], 0.25, None,
                                                   ALU.add),
               ins=[lamt[:]], outs=[lamt[:]], tag="d_l25")
        ps_l2 = s2.ps("d_psl2", [128, 16])
        p.emit("tensor", lambda e: e.matmul(ps_l2[:], C["c_dup"][:], lamt[:],
                                            start=True, stop=True),
               ins=[C["c_dup"][:], lamt[:]], outs=[ps_l2[:]], tag="d_l2m")
        p.emit("vector", lambda e: e.tensor_copy(lamt2[:], ps_l2[:]),
               ins=[ps_l2[:]], outs=[lamt2[:]], tag="d_l2e")
        for h in range(16):
            p.emit("vector", lambda e, h=h: e.tensor_scalar(
                dlam[:, h * 64:(h + 1) * 64], C["c_eye2"][:],
                lamt2[:, h:h + 1], None, ALU.mult),
                ins=[C["c_eye2"][:], lamt2[:]], outs=[f"dlam@{h}"], tag=f"d_dg{h}")
        p.emit("tensor", lambda e: e.matmul(ps_l2[:, 0:8], C["c_dup0"][:],
                                            lamt[:, 0:16:2], start=True, stop=False),
               ins=[C["c_dup0"][:], lamt[:]], outs=[ps_l2[:]], tag="d_l3a")
        p.emit("tensor", lambda e: e.matmul(ps_l2[:, 0:8], C["c_dup1"][:],
                                            lamt[:, 1:16:2], start=False, stop=True),
               ins=[C["c_dup1"][:], lamt[:]], outs=[ps_l2[:]], tag="d_l3b")
        p.emit("vector", lambda e: e.tensor_copy(lamt3[:], ps_l2[:, 0:8]),
               ins=[ps_l2[:]], outs=[lamt3[:]], tag="d_l3e")
        p.emit("pool", lambda e: e.memset(lamb_b[:], 1.0), outs=["lambb_init"],
               tag="d_lbz")
        for pc in range(8):
            p.emit("scalar", lambda e, pc=pc: e.mul(
                lamb_b[:, pc * 128:(pc + 1) * 128],
                lamb_b[:, pc * 128:(pc + 1) * 128], lamt3[:, pc:pc + 1]),
                ins=["lambb_init", lamt3[:]], outs=[f"lambb@{pc}"], tag=f"d_lb{pc}")
        p.dma("sync", nwl[:], WD["nw"], tag="d_ldnw")
        dlt = s2.sb("d_dlt", [16, 1])
        p.dma("sync", dlt[:], WD["delta"], tag="d_lddlt")
        st2b = ExitStack()
        s2b = S(nc, st2b)
        cfT = s2b.sb("d_cfT", [16, T])
        btt = s2b.sb("d_btt", [16, 512])
        wf_st = s2b.sb("d_wfst", [128, 16])
        wf_b = load_w_bf16(p, s2b, C, WD["wf"], "dwf", 16, wf_st)
        wb_b = load_w_bf16(p, s2b, C, WD["wbet"], "dwb", 16, wf_st)
        ps_f = s2b.ps("d_psf", [16, 512])
        ps_bt = s2b.ps("d_psbt", [128, 16])
        zf = s2b.sb("d_zf", [16, 512])
        za = s2b.sb("d_za", [16, 512])
        for j in range(NTB):
            cols = slice(j * 512, (j + 1) * 512)
            for c in range(NDC):
                p.emit("tensor", lambda e, c=c, cols=cols: e.matmul(
                    ps_f[:], wf_b[c][:], xn[c][:, cols],
                    start=(c == 0), stop=(c == NDC - 1)),
                    ins=[wf_b[c][:], f"dxn{c}@{j}"], outs=[ps_f[:]], tag=f"d_f{j}_{c}")
            p.emit("vector", lambda e: e.tensor_scalar(zf[:], ps_f[:], dlt[:], None,
                                                       ALU.add),
                   ins=[ps_f[:], dlt[:]], outs=[zf[:]], tag=f"d_fd{j}")
            p.emit("scalar", lambda e: e.activation(za[:], zf[:], AF.Abs),
                   ins=[zf[:]], outs=[za[:]], tag=f"d_fa{j}")
            p.emit("scalar", lambda e: e.activation(za[:], za[:], AF.Exp, scale=-1.0),
                   ins=[za[:]], outs=[za[:]], tag=f"d_fe{j}")
            p.emit("scalar", lambda e: e.activation(za[:], za[:], AF.Ln, bias=1.0),
                   ins=[za[:]], outs=[za[:]], tag=f"d_fl{j}")
            p.emit("vector", lambda e, cols=cols: e.tensor_scalar(
                cfT[:, cols], zf[:], 0.0, None, ALU.min),
                ins=[zf[:]], outs=[f"d_cfT@{j}"], tag=f"d_fm{j}")
            p.emit("vector", lambda e, cols=cols: e.tensor_sub(
                cfT[:, cols], cfT[:, cols], za[:]),
                ins=[f"d_cfT@{j}", za[:]], outs=[f"d_cfT@{j}"], tag=f"d_fs{j}")
            for c in range(NDC):
                p.emit("tensor", lambda e, c=c, cols=cols: e.matmul(
                    ps_f[:], wb_b[c][:], xn[c][:, cols],
                    start=(c == 0), stop=(c == NDC - 1)),
                    ins=[wb_b[c][:], f"dxn{c}@{j}"], outs=[ps_f[:]], tag=f"d_b{j}_{c}")
            p.emit("scalar", lambda e: e.activation(btt[:], ps_f[:], AF.Sigmoid),
                   ins=[ps_f[:]], outs=[btt[:]], tag=f"d_bs{j}")
            for q4 in range(4):
                sb = 4 * j + q4
                p.emit("tensor", lambda e, q4=q4: e.transpose(
                    ps_f[:, 0:16] if False else ps_bt[:],
                    btt[:, q4 * 128:(q4 + 1) * 128], eye32[:][0:16, 0:16]),
                    ins=[btt[:], eye32[:]], outs=[ps_bt[:]], tag=f"d_tb{sb}")
                p.emit("vector", lambda e, sb=sb: e.tensor_copy(
                    bt_tok[:, sb * 16:(sb + 1) * 16], ps_bt[:]),
                    ins=[ps_bt[:]], outs=[f"d_bttok@{sb}"], tag=f"d_tbe{sb}")
        p.emit("vector", lambda e: e.tensor_tensor_scan(cfT[:], cfT[:], cfT[:], 0.0,
                                                        ALU.add, ALU.bypass),
               ins=[f"d_cfT@{j}" for j in range(NTB)], outs=["d_cf_done"], tag="d_scan")
        p.dma("pool", cf2_dr, cfT[:], ins_extra=["d_cf_done"], tag="d_stcf2")
        ps_tr2 = s2b.ps("d_pstr", [128, 16])
        for sb in range(T // 128):
            sl = slice(sb * 128, (sb + 1) * 128)
            dst = slice(sb * 16, (sb + 1) * 16)
            p.emit("tensor", lambda e, sl=sl: e.transpose(ps_tr2[:], cfT[:, sl],
                                                          eye32[:][0:16, 0:16]),
                   ins=["d_cf_done", eye32[:]], outs=[ps_tr2[:]], tag=f"d_tc{sb}")
            p.emit("vector", lambda e, dst=dst: e.tensor_copy(cf_tok[:, dst], ps_tr2[:]),
                   ins=[ps_tr2[:]], outs=[f"d_cftok@{sb}"], tag=f"d_tce{sb}")
            p.dma("pool", cft_dr[sl, :], cf_tok[:, dst],
                  ins_extra=[f"d_cftok@{sb}"], tag=f"d_tcs{sb}")

        st2b.close()
        p.fence()
        gw1_st = s2.sb("d_g1st", [128, 64])
        gw1_b = load_w_bf16(p, s2, C, WD["gw1"], "dg1", 64, gw1_st)
        ps_g1 = s2.ps("d_psg1", [64, 512])
        for j in range(NTB):
            cols = slice(j * 512, (j + 1) * 512)
            for c in range(NDC):
                p.emit("tensor", lambda e, c=c, cols=cols: e.matmul(
                    ps_g1[:], gw1_b[c][:], xn[c][:, cols],
                    start=(c == 0), stop=(c == NDC - 1)),
                    ins=[gw1_b[c][:], f"dxn{c}@{j}"], outs=[ps_g1[:]], tag=f"d_g1{j}_{c}")
            p.emit("vector", lambda e, cols=cols: e.tensor_copy(g1T[:, cols], ps_g1[:]),
                   ins=[ps_g1[:]], outs=[f"d_g1T@{j}"], tag=f"d_g1e{j}")
        gw2s = s2.sb("d_g2s", [64, D])
        p.dma("sync", gw2s[:], WD["gw2"], tag="d_ldg2")
        p.emit("vector", lambda e: e.tensor_copy(gw2t[:], gw2s[:]),
               ins=[gw2s[:]], outs=[gw2t[:]], tag="d_cg2")
    p.fence()
    ql_dr, kl_dr, kt_dr, vt_dr = WD["ql_dr"], WD["kl_dr"], WD["kt_dr"], WD["vt_dr"]
    with ExitStack() as stP:
        sP = S(nc, stP)
        hs2 = C["c_hs2"]; hb2 = C["c_hb2"]
        ps_p = sP.ps("y_psp", [128, 512])
        ps_s2 = sP.ps("y_pss", [128, 512])
        qf = sP.sb("y_qf", [128, 512])
        qq = sP.sb("y_qq", [128, 512])
        rst = sP.sb("y_rst", [2, 512])
        qrow = sP.sb("y_qrow", [128, T], BF16)
        vrow = sP.sb("y_vrow", [128, 1024], BF16)
        wexp = sP.sb("y_wex", [128, 16])
        for (wb, dr, nm) in ((wq_b, ql_dr, "q"), (wk_b, kl_dr, "k")):
            for pc in range(8):
                csl = slice(pc * 128, (pc + 1) * 128)
                for j in range(NTB):
                    cols = slice(j * 512, (j + 1) * 512)
                    for c in range(NDC):
                        p.emit("tensor", lambda e, c=c, csl=csl, wb=wb, cols=cols:
                               e.matmul(ps_p[:], wb[c][:, csl], xn[c][:, cols],
                                        start=(c == 0), stop=(c == NDC - 1)),
                               ins=[wb[c][:], f"dxn{c}@{j}"], outs=[ps_p[:]],
                               tag=f"y{nm}{pc}_{j}_mm{c}")
                    p.emit("vector", lambda e: e.tensor_copy(qf[:], ps_p[:]),
                           ins=[ps_p[:]], outs=[qf[:]], tag=f"y{nm}{pc}_{j}_e")
                    p.emit("scalar", lambda e: e.activation(qq[:], qf[:], AF.Square),
                           ins=[qf[:]], outs=[qq[:]], tag=f"y{nm}{pc}_{j}_q")
                    p.emit("tensor", lambda e: e.matmul(ps_s2[0:2, 0:512], hs2[:],
                                                        qq[:], start=True, stop=True),
                           ins=[hs2[:], qq[:]], outs=["y_pss"], tag=f"y{nm}{pc}_{j}_s")
                    p.emit("scalar", lambda e: e.activation(rst[:], ps_s2[0:2, 0:512],
                                                            AF.Sqrt, bias=1e-6),
                           ins=["y_pss"], outs=[rst[:]], tag=f"y{nm}{pc}_{j}_r")
                    p.emit("vector", lambda e: e.reciprocal(rst[:], rst[:]),
                           ins=[rst[:]], outs=[rst[:]], tag=f"y{nm}{pc}_{j}_i")
                    p.emit("tensor", lambda e: e.matmul(ps_s2[:, 0:512], hb2[:],
                                                        rst[:], start=True, stop=True),
                           ins=[hb2[:], rst[:]], outs=["y_pss"], tag=f"y{nm}{pc}_{j}_b")
                    p.emit("vector", lambda e, cols=cols: e.tensor_mul(
                        qrow[:, cols], qf[:], ps_s2[:, 0:512]),
                        ins=[qf[:], "y_pss"], outs=[f"y_qrow@{j}"],
                        tag=f"y{nm}{pc}_{j}_n")
                bq = p.open_batch("pool")
                for j in range(NTB):
                    p.dma("pool", dr[csl, j * 512:(j + 1) * 512],
                          qrow[:, j * 512:(j + 1) * 512],
                          ins_extra=[f"y_qrow@{j}"], tag=f"y{nm}{pc}_st{j}", batch=bq)
        for sb in range(NCH):
            tsl2 = slice(sb * 128, (sb + 1) * 128)
            jj = sb // 4
            for (wb, dr, nm) in ((wk_b, kt_dr, "K"), (wv_b, vt_dr, "V")):
                for half in range(2):
                    esl = slice(half * 512, (half + 1) * 512)
                    for c in range(NDC):
                        p.emit("tensor", lambda e, c=c, esl=esl, wb=wb, tsl2=tsl2:
                               e.matmul(ps_p[:], xn[c][:, tsl2], wb[c][:, esl],
                                        start=(c == 0), stop=(c == NDC - 1)),
                               ins=[wb[c][:], f"dxn{c}@{jj}"], outs=[ps_p[:]],
                               tag=f"yt{nm}{sb}_{half}_mm{c}")
                    p.emit("vector", lambda e, esl=esl: e.tensor_copy(vrow[:, esl],
                                                                      ps_p[:]),
                           ins=[ps_p[:]], outs=[f"y_vrow@{half}"],
                           tag=f"yt{nm}{sb}_{half}_e")
                if nm == "K":
                    p.emit("scalar", lambda e: e.activation(qf[:], vrow[:, 0:512],
                                                            AF.Square,
                                                            accum_out=None),
                           ins=["y_vrow@0"], outs=[qf[:]], tag=f"ytK{sb}_sqa")
                    p.emit("vector", lambda e: e.tensor_reduce(
                        wexp[:, 0:8], qf[:].rearrange("p (h j) -> p h j", j=64),
                        AX.X, ALU.add),
                        ins=[qf[:]], outs=["y_wexp@a"], tag=f"ytK{sb}_ra")
                    p.emit("scalar", lambda e: e.activation(qf[:], vrow[:, 512:1024],
                                                            AF.Square),
                           ins=["y_vrow@1"], outs=[qf[:]], tag=f"ytK{sb}_sqb")
                    p.emit("vector", lambda e: e.tensor_reduce(
                        wexp[:, 8:16], qf[:].rearrange("p (h j) -> p h j", j=64),
                        AX.X, ALU.add),
                        ins=[qf[:]], outs=["y_wexp@b"], tag=f"ytK{sb}_rb")
                    p.emit("scalar", lambda e: e.activation(wexp[:], wexp[:], AF.Sqrt,
                                                            bias=1e-6),
                           ins=["y_wexp@a", "y_wexp@b"], outs=["y_wexp@s"],
                           tag=f"ytK{sb}_sq")
                    p.emit("vector", lambda e: e.reciprocal(wexp[:], wexp[:]),
                           ins=["y_wexp@s"], outs=["y_wexp@r"], tag=f"ytK{sb}_rc")
                    p.emit("vector", lambda e: e.tensor_tensor(
                        vrow[:].rearrange("p (h j) -> p h j", j=64),
                        vrow[:].rearrange("p (h j) -> p h j", j=64),
                        wexp[:].unsqueeze(2).broadcast_to([128, 16, 64]), ALU.mult),
                        ins=["y_vrow@0", "y_vrow@1", "y_wexp@r"],
                        outs=["y_vrow@0", "y_vrow@1"], tag=f"ytK{sb}_n")
                    p.dma("pool", dr[tsl2, :], vrow[:],
                          ins_extra=["y_vrow@0", "y_vrow@1"], tag=f"yt{nm}{sb}_st")
                else:
                    p.dma("pool", dr[tsl2, :], vrow[:],
                          ins_extra=["y_vrow@0", "y_vrow@1"], tag=f"yt{nm}{sb}_st")
    stQ.close()
    p.fence()
    with ExitStack() as stw2:
        wst2 = S(nc, stw2).sb("d_wst2", [128, D])
        load_w_bf16(p, s, C, WD["wo"], "dwo", D, wst2, tiles=wo_b)
    p.fence()
    global _L1W
    _L1W = dict(wq_b=wq_b, wk_b=wk_b, wv_b=wv_b, wo_b=wo_b, cfT=None, btT=None,
                cf_tok=cf_tok, bt_tok=bt_tok, g1T=g1T, dlam=dlam, nwl=nwl,
                Hkk=Hkk, Hkv=Hkv, Hkkb=Hkkb, Hkvb=Hkvb, gw2t=gw2t,
                lamb_b=lamb_b)
    return xn




DBG = False


def l1_attn_chunk_loop(p, s, C, nc, QD, tens, x_dram, xout_d, cft_dr, cf2_dr):
    from contextlib import ExitStack
    if DBG:
        dbg_dt = nc.dram_tensor("dbg_Dt", [128, 2048], BF16, kind="Internal").ap()
        dbg_db = nc.dram_tensor("dbg_dbl", [128, 1024], BF16, kind="Internal").ap()
        dbg_qs = nc.dram_tensor("dbg_qs", [128, 1024], F32, kind="Internal").ap()
        dbg_o = nc.dram_tensor("dbg_o", [128, 1024], F32, kind="Internal").ap()
        dbg_ap = nc.dram_tensor("dbg_ap", [128, 1024], F32, kind="Internal").ap()
    ones = C["c_ones"]; onesr = C["c_onesr"]; eye32 = C["c_eye32"]
    hb = C["c_hb"]; hsg = C["c_hsg"]; mle = C["c_masklehb"]
    hs2 = C["c_hs2"]; hb2 = C["c_hb2"]; heye = C["c_heye"]
    (wq_b, wk_b, wv_b, wo_b, cfT, btT, cf_tok, bt_tok, g1T, dlam, nwl,
     Hkk, Hkv, Hkkb, Hkvb, gw2t) = tens
    HSEG = lambda t: t.rearrange("p (h j) -> p h j", j=64)
    with ExitStack() as st:
        s3 = S(nc, st)
        ps_big = s3.ps("e_psbig", [128, 2048])
        ps_Ap = s3.ps("e_psAp", [128, 1024])
        ps_sm = s3.ps("e_pssm", [128, 512])
        q_l = s3.sb("e_ql", [128, 1024], BF16)
        k_l = s3.sb("e_kl", [128, 1024], BF16)
        Ktok = s3.sb("e_Ktok", [128, 1024], BF16)
        Kbe = s3.sb("e_Kbe", [128, 1024], BF16)
        Ksc = s3.sb("e_Ksc", [128, 1024], BF16)
        Ksc2 = s3.sb("e_Ksc2", [128, 2048], BF16)
        Vtok = s3.sb("e_Vtok", [128, 1024], BF16)
        Vbe = s3.sb("e_Vbe", [128, 1024], BF16)
        Dt = s3.sb("e_Dt", [128, 2048], BF16)
        dbl = s3.sb("e_dbl", [128, 1024], BF16)
        Wt = s3.sb("e_Wt", [128, 2048], BF16)
        xcg = s3.sb("e_xq", [128, 1024])
        rcg = s3.sb("e_rr", [128, 1024])
        pcg = s3.sb("e_pp", [128, 1024])
        p_bf = s3.sb("e_pbf", [128, 1024], BF16)
        dp_bf = s3.sb("e_dpbf", [128, 1024], BF16)
        tmp = s3.sb("e_tmp", [128, 1024])
        tsq = s3.sb("e_tsq", [128, 1024])
        rs_t = [s3.sb(f"e_rs{i}", [16, 128]) for i in range(2)]
        alf = s3.sb("e_alf", [16, 256])
        gam = s3.sb("e_gam", [16, 128])
        sca = s3.sb("e_sca", [16, 512])
        wex = s3.sb("e_wex", [128, 16])
        cfr = s3.sb("e_cfr", [1, 32])
        nst = s3.sb("e_nst", [16, 128])
        cfc = s3.sb("e_cfc", [16, 129])
        ncb = s3.sb("e_ncb", [128, 1])
        dec64 = s3.sb("e_dec64", [128, 16])
        cfl = s3.sb("e_cfl", [1, 512])
        ogn = s3.sb("e_ogn", [128, 1024], BF16)
        osb = s3.sb("e_osb", [128, 1024])
        xac = s3.sb("e_xac", [128, 128])

        def seg_bcast_mul(out_t, in_t, col_ap, ins, outs, tag):
            p.emit("vector", lambda e: e.tensor_tensor(
                HSEG(out_t[:]), HSEG(in_t[:]),
                col_ap.unsqueeze(2).broadcast_to([128, 16, 64]), ALU.mult),
                ins=ins, outs=outs, tag=tag)

        for ch in range(NCH):
            tsl = slice(ch * 128, (ch + 1) * 128)
            j512 = ch // 4
            bt5 = p.open_batch("sync")
            p.dma("sync", q_l[:].rearrange("p (pc t) -> p pc t", t=128),
                  QD["ql"][:, tsl].rearrange("(pc p) t -> p pc t", p=128),
                  tag=f"e{ch}_lq", batch=bt5)
            p.dma("sync", k_l[:].rearrange("p (pc t) -> p pc t", t=128),
                  QD["kl"][:, tsl].rearrange("(pc p) t -> p pc t", p=128),
                  tag=f"e{ch}_lk", batch=bt5)
            p.dma("sync", Ktok[:], QD["kt"][tsl, :], tag=f"e{ch}_lKt", batch=bt5)
            p.dma("sync", Vtok[:], QD["vt"][tsl, :], tag=f"e{ch}_lVt", batch=bt5)
            seg_bcast_mul(Kbe, Ktok, bt_tok[:, ch * 16:(ch + 1) * 16],
                          [Ktok[:], f"d_bttok@{ch}"], [Kbe[:]], f"e{ch}_Kb")
            seg_bcast_mul(Vbe, Vtok, bt_tok[:, ch * 16:(ch + 1) * 16],
                          [Vtok[:], f"d_bttok@{ch}"], ["e_Vbeta"],
                          f"e{ch}_Vb")
            p.dma("sync", cfr[:, 0:16], cft_dr[ch * 128 + 127: ch * 128 + 128, :],
                  tag=f"e{ch}_lcfe")
            if ch > 0:
                p.dma("sync", cfr[:, 16:32], cft_dr[ch * 128 - 1: ch * 128, :],
                      tag=f"e{ch}_lcfs")
            else:
                p.emit("pool", lambda e: e.memset(cfr[:, 16:32], 0.0),
                       outs=[cfr[:]], tag=f"e{ch}_zcfs")
            p.emit("tensor", lambda e: e.matmul(ps_sm[:, 0:16], onesr[0:1, :],
                                                cfr[:, 0:16], start=True, stop=True),
                   ins=[onesr[:], cfr[:]], outs=["e_pssm"], tag=f"e{ch}_cfeb")
            p.emit("vector", lambda e, ch=ch: e.tensor_sub(
                wex[:], ps_sm[:, 0:16], cf_tok[:, ch * 16:(ch + 1) * 16]),
                ins=["e_pssm", f"d_cftok@{ch}"], outs=["e_wex@d"], tag=f"e{ch}_wxd")
            p.emit("scalar", lambda e: e.activation(wex[:], wex[:], AF.Exp),
                   ins=["e_wex@d"], outs=["e_wex@e"], tag=f"e{ch}_wxe")
            seg_bcast_mul(Ksc, Kbe, wex[:], [Kbe[:], "e_wex@e"], [Ksc[:]],
                          f"e{ch}_Kw")
            p.emit("vector", lambda e: e.tensor_sub(cfr[:, 0:16], cfr[:, 0:16],
                                                    cfr[:, 16:32]),
                   ins=[cfr[:]], outs=[cfr[:]], tag=f"e{ch}_dcd")
            p.emit("scalar", lambda e: e.activation(cfr[:, 0:16], cfr[:, 0:16],
                                                    AF.Exp),
                   ins=[cfr[:]], outs=[cfr[:]], tag=f"e{ch}_dce")
            p.emit("tensor", lambda e: e.matmul(ps_sm[:, 16:32], onesr[0:1, :],
                                                cfr[:, 0:16], start=True, stop=True),
                   ins=[onesr[:], cfr[:]], outs=["e_pssm"], tag=f"e{ch}_dcb")
            p.emit("vector", lambda e: e.tensor_copy(dec64[:], ps_sm[:, 16:32]),
                   ins=["e_pssm"], outs=[dec64[:]], tag=f"e{ch}_dce2")
            p.emit("vector", lambda e: e.tensor_copy(
                Ksc2[:].rearrange("p (h d j) -> p h d j", d=2, j=64),
                Ksc[:].rearrange("p (h j) -> p h j", j=64).unsqueeze(2)
                .broadcast_to([128, 16, 2, 64])),
                ins=[Ksc[:]], outs=[Ksc2[:]], tag=f"e{ch}_K2")
            if ch > 0:
                p.dma("sync", cfc[:], cf2_dr[:, ch * 128 - 1: ch * 128 + 128],
                      tag=f"e{ch}_lcfc")
            else:
                p.emit("pool", lambda e: e.memset(cfc[:, 0:1], 0.0),
                       outs=["e_cfc0"], tag=f"e{ch}_zcfc")
                p.dma("sync", cfc[:, 1:129], cf2_dr[:, 0:128],
                      ins_extra=["e_cfc0"], tag=f"e{ch}_lcfc")
            p.emit("vector", lambda e: e.tensor_scalar(
                nst[:], cfc[:, 1:129], -1.0, None, ALU.mult),
                ins=[cfc[:]], outs=[nst[:]], tag=f"e{ch}_nst")
            for q4 in range(4):
                qsl = slice(q4 * 512, (q4 + 1) * 512)
                p.dma("sync", cfl[:].rearrange("p (a b) -> p a b", b=128),
                      cf2_dr[4 * q4:4 * q4 + 4,
                             ch * 128:(ch + 1) * 128].unsqueeze(0),
                      tag=f"e{ch}_lcfl{q4}")
                p.emit("tensor", lambda e, qsl=qsl: e.matmul(
                    ps_big[:, qsl], onesr[0:1, :], cfl[:],
                    start=True, stop=False),
                    ins=[onesr[:], cfl[:]], outs=[f"e_psbig@{q4}"], tag=f"e{ch}_A{q4}")
                p.emit("tensor", lambda e, qsl=qsl: e.matmul(
                    ps_big[:, qsl], nst[:], heye[:][:, qsl],
                    start=False, stop=True),
                    ins=[nst[:], heye[:]], outs=[f"e_psbig@{q4}"], tag=f"e{ch}_B{q4}")
            p.emit("vector", lambda e: e.scalar_tensor_tensor(
                ps_big[:], ps_big[:], 0.0, mle[:], ALU.min, ALU.add),
                ins=[mle[:]] + [f"e_psbig@{q4}" for q4 in range(4)],
                outs=[f"e_psbig@{q4}" for q4 in range(4)], tag=f"e{ch}_Dmn")
            p.emit("scalar", lambda e: e.activation(Dt[:], ps_big[:], AF.Exp),
                   ins=[f"e_psbig@{q4}" for q4 in range(4)], outs=[Dt[:]],
                   tag=f"e{ch}_Dex")
            for pc in range(8):
                p.emit("tensor", lambda e, pc=pc: e.matmul(
                    ps_sm[:, 128:256], hb[:][:, pc * 128:(pc + 1) * 128],
                    cfc[:, 1:129], start=True, stop=True),
                    ins=[hb[:], cfc[:]], outs=["e_pssm"], tag=f"e{ch}_db{pc}")
                p.emit("tensor", lambda e, pc=pc: e.matmul(
                    ps_sm[:, 384:385], hb[:][:, pc * 128:(pc + 1) * 128],
                    cfc[:, 0:1], start=True, stop=True),
                    ins=[hb[:], cfc[:]], outs=["e_pssm"],
                    tag=f"e{ch}_ns{pc}")
                p.emit("vector", lambda e: e.tensor_scalar(
                    ncb[:], ps_sm[:, 384:385], -1.0, None, ALU.mult),
                    ins=["e_pssm"], outs=["e_ncb"], tag=f"e{ch}_nsn{pc}")
                p.emit("scalar", lambda e, pc=pc: e.activation(
                    dbl[:, pc * 128:(pc + 1) * 128], ps_sm[:, 128:256], AF.Exp,
                    bias=ncb[:]),
                    ins=["e_pssm", "e_ncb"], outs=[f"e_dbl@{pc}"],
                    tag=f"e{ch}_dbe{pc}")
            if DBG and ch == 0:
                p.dma("pool", dbg_dt, Dt[:], ins_extra=[Dt[:]], tag="dbgdt")
                p.dma("pool", dbg_db, dbl[:],
                      ins_extra=[f"e_dbl@{pc}" for pc in range(8)], tag="dbgdb")
            p.emit("pool", lambda e: e.memset(xcg[:], 0.0), outs=[xcg[:]],
                   tag=f"e{ch}_cgx0")
            qreg = [q_l[:]]
            p.emit("vector", lambda e: e.tensor_copy(rcg[:], q_l[:]),
                   ins=qreg, outs=[rcg[:]], tag=f"e{ch}_cgr0")
            p.emit("vector", lambda e: e.tensor_copy(pcg[:], q_l[:]),
                   ins=qreg, outs=[pcg[:]], tag=f"e{ch}_cgp0")
            p.emit("scalar", lambda e: e.activation(tsq[:], rcg[:], AF.Square),
                   ins=[rcg[:]], outs=["e_tsq"], tag=f"e{ch}_cgsq0")
            for pc in range(8):
                p.emit("tensor", lambda e, pc=pc: e.matmul(
                    ps_sm[0:16, 0:128], hsg[:][:, pc * 16:(pc + 1) * 16],
                    tsq[:, pc * 128:(pc + 1) * 128],
                    start=(pc == 0), stop=(pc == 7)),
                    ins=[hsg[:], "e_tsq"], outs=["e_pssm"], tag=f"e{ch}_cgr0m{pc}")
            p.emit("vector", lambda e: e.tensor_copy(rs_t[0][:], ps_sm[0:16, 0:128]),
                   ins=["e_pssm"], outs=[rs_t[0][:]], tag=f"e{ch}_cgrs0")
            for it in range(CG_ITERS):
                rs_old = rs_t[it % 2]
                rs_new = rs_t[(it + 1) % 2]
                E = f"e{ch}i{it}"
                p.emit("vector", lambda e: e.tensor_copy(p_bf[:], pcg[:]),
                       ins=[pcg[:]], outs=[p_bf[:]], tag=f"{E}_pbf")
                p.emit("vector", lambda e: e.tensor_mul(dp_bf[:], p_bf[:], dbl[:]),
                       ins=[p_bf[:]] + [f"e_dbl@{pc}" for pc in range(8)],
                       outs=[dp_bf[:]], tag=f"{E}_dp")
                for h in range(16):
                    pc, hl = h // 2, h % 2
                    p.emit("tensor", lambda e, pc=pc, hl=hl, h=h: e.matmul(
                        ps_big[:, h * 128:(h + 1) * 128],
                        k_l[64 * hl:64 * hl + 64, pc * 128:(pc + 1) * 128],
                        p_bf[64 * hl:64 * hl + 64, pc * 128:(pc + 1) * 128],
                        start=True, stop=True, tile_position=(64 * hl, 0)),
                        ins=[k_l[:], p_bf[:]], outs=[f"e_psbig@{h // 4}"],
                        tag=f"{E}_S{h}")
                p.emit("vector", lambda e: e.tensor_mul(Wt[:], ps_big[:], Dt[:]),
                       ins=[Dt[:]] + [f"e_psbig@{q}" for q in range(4)], outs=[Wt[:]],
                       tag=f"{E}_W")
                for h in range(16):
                    pc, hl = h // 2, h % 2
                    p.emit("tensor", lambda e, pc=pc, hl=hl, h=h: e.matmul(
                        ps_Ap[64 * hl:64 * hl + 64, pc * 128:(pc + 1) * 128],
                        Kbe[:, h * 64:(h + 1) * 64], Wt[:, h * 128:(h + 1) * 128],
                        start=True, stop=False, tile_position=(0, 64 * hl)),
                        ins=[Kbe[:], Wt[:]], outs=[f"e_psAp@{h // 8}"], tag=f"{E}_I{h}")
                    p.emit("tensor", lambda e, pc=pc, hl=hl, h=h: e.matmul(
                        ps_Ap[64 * hl:64 * hl + 64, pc * 128:(pc + 1) * 128],
                        Hkkb[64 * hl:64 * hl + 64, h * 64:(h + 1) * 64],
                        dp_bf[64 * hl:64 * hl + 64, pc * 128:(pc + 1) * 128],
                        start=False, stop=False, tile_position=(64 * hl, 64 * hl)),
                        ins=[f"Hb0@{h}", dp_bf[:]], outs=[f"e_psAp@{h // 8}"],
                        tag=f"{E}_H{h}")
                    p.emit("tensor", lambda e, pc=pc, hl=hl, h=h: e.matmul(
                        ps_Ap[64 * hl:64 * hl + 64, pc * 128:(pc + 1) * 128],
                        dlam[64 * hl:64 * hl + 64, h * 64:(h + 1) * 64],
                        p_bf[64 * hl:64 * hl + 64, pc * 128:(pc + 1) * 128],
                        start=False, stop=True, tile_position=(64 * hl, 64 * hl)),
                        ins=[f"dlam@{h}", p_bf[:]], outs=[f"e_psAp@{h // 8}"],
                        tag=f"{E}_L{h}")
                p.emit("vector", lambda e: e.tensor_copy(tsq[:], ps_Ap[:]),
                       ins=[f"e_psAp@{b}" for b in range(2)], outs=["e_tsq"],
                       tag=f"{E}_Ae")
                p.emit("vector", lambda e: e.tensor_mul(tmp[:], pcg[:], tsq[:]),
                       ins=[pcg[:], "e_tsq"], outs=["e_tmp"], tag=f"{E}_pa")
                for pc in range(8):
                    p.emit("tensor", lambda e, pc=pc: e.matmul(
                        ps_sm[0:16, 0:128], hsg[:][:, pc * 16:(pc + 1) * 16],
                        tmp[:, pc * 128:(pc + 1) * 128],
                        start=(pc == 0), stop=(pc == 7)),
                        ins=[hsg[:], "e_tmp"], outs=["e_pssm"],
                        tag=f"{E}_pm{pc}")
                p.emit("vector", lambda e: e.tensor_scalar(
                    sca[:, 0:128], ps_sm[0:16, 0:128], 1e-12, None, ALU.add),
                    ins=["e_pssm"], outs=["e_sca@pap"], tag=f"{E}_ae")
                p.emit("vector", lambda e: e.reciprocal(sca[:, 0:128], sca[:, 0:128]),
                       ins=["e_sca@pap"], outs=["e_sca@pap"], tag=f"{E}_ar")
                p.emit("vector", lambda e, rs_old=rs_old: e.tensor_mul(
                    alf[:, 0:128], rs_old[:], sca[:, 0:128]),
                    ins=[rs_old[:], "e_sca@pap"], outs=["e_alf@a"], tag=f"{E}_am")
                p.emit("vector", lambda e: e.tensor_scalar(
                    alf[:, 128:256], alf[:, 0:128], -1.0, None, ALU.mult),
                    ins=["e_alf@a"], outs=["e_alf@n"], tag=f"{E}_an")
                for pc in range(8):
                    p.emit("tensor", lambda e, pc=pc: e.matmul(
                        ps_big[:, pc * 256:pc * 256 + 256],
                        hb[:][:, pc * 128:(pc + 1) * 128], alf[:],
                        start=True, stop=True),
                        ins=[hb[:], "e_alf@a", "e_alf@n"], outs=[f"e_psbig@{pc // 2}"],
                        tag=f"{E}_ab{pc}")
                for pc in range(8):
                    p.emit("vector", lambda e, pc=pc: e.tensor_mul(
                        tmp[:, pc * 128:(pc + 1) * 128],
                        pcg[:, pc * 128:(pc + 1) * 128],
                        ps_big[:, pc * 256:pc * 256 + 128]),
                        ins=[pcg[:], f"e_psbig@{pc // 2}"], outs=["e_tmp"],
                        tag=f"{E}_xm{pc}")
                    p.emit("vector", lambda e, pc=pc: e.tensor_mul(
                        tsq[:, pc * 128:(pc + 1) * 128],
                        tsq[:, pc * 128:(pc + 1) * 128],
                        ps_big[:, pc * 256 + 128:pc * 256 + 256]),
                        ins=["e_tsq", f"e_psbig@{pc // 2}"], outs=["e_tsq"],
                        tag=f"{E}_rm{pc}")
                p.emit("vector", lambda e: e.tensor_add(xcg[:], xcg[:], tmp[:]),
                       ins=[xcg[:], "e_tmp"],
                       outs=[xcg[:]], tag=f"{E}_xa")
                p.emit("vector", lambda e: e.tensor_add(rcg[:], rcg[:], tsq[:]),
                       ins=[rcg[:], "e_tsq"],
                       outs=[rcg[:]], tag=f"{E}_ra")
                p.emit("scalar", lambda e: e.activation(tsq[:], rcg[:], AF.Square),
                       ins=[rcg[:]], outs=["e_tsq"], tag=f"{E}_rsq")
                for pc in range(8):
                    p.emit("tensor", lambda e, pc=pc: e.matmul(
                        ps_sm[0:16, 128:256], hsg[:][:, pc * 16:(pc + 1) * 16],
                        tsq[:, pc * 128:(pc + 1) * 128],
                        start=(pc == 0), stop=(pc == 7)),
                        ins=[hsg[:], "e_tsq"], outs=["e_pssm"],
                        tag=f"{E}_rn{pc}")
                p.emit("vector", lambda e, rs_new=rs_new: e.tensor_copy(
                    rs_new[:], ps_sm[0:16, 128:256]),
                    ins=["e_pssm"], outs=[rs_new[:]], tag=f"{E}_rne")
                p.emit("vector", lambda e, rs_old=rs_old: e.tensor_scalar(
                    sca[:, 128:256], rs_old[:], 1e-12, None, ALU.add),
                    ins=[rs_old[:]], outs=["e_sca@g"], tag=f"{E}_ge")
                p.emit("vector", lambda e: e.reciprocal(sca[:, 128:256],
                                                        sca[:, 128:256]),
                       ins=["e_sca@g"], outs=["e_sca@g"], tag=f"{E}_gr")
                p.emit("vector", lambda e, rs_new=rs_new: e.tensor_mul(
                    gam[:], rs_new[:], sca[:, 128:256]),
                    ins=[rs_new[:], "e_sca@g"], outs=[gam[:]], tag=f"{E}_gm")
                for pc in range(8):
                    p.emit("tensor", lambda e, pc=pc: e.matmul(
                        ps_big[:, pc * 256:pc * 256 + 128],
                        hb[:][:, pc * 128:(pc + 1) * 128], gam[:],
                        start=True, stop=True),
                        ins=[hb[:], gam[:]], outs=[f"e_psbig@{pc // 2}"], tag=f"{E}_gb{pc}")
                for pc in range(8):
                    p.emit("vector", lambda e, pc=pc: e.tensor_mul(
                        tmp[:, pc * 128:(pc + 1) * 128],
                        pcg[:, pc * 128:(pc + 1) * 128],
                        ps_big[:, pc * 256:pc * 256 + 128]),
                        ins=[pcg[:], f"e_psbig@{pc // 2}"], outs=["e_tmp"],
                        tag=f"{E}_gmul{pc}")
                p.emit("vector", lambda e: e.tensor_add(pcg[:], rcg[:], tmp[:]),
                       ins=[rcg[:], "e_tmp"],
                       outs=[pcg[:]], tag=f"{E}_pa2")
            if DBG and ch == 0:
                p.dma("pool", dbg_qs, xcg[:], ins_extra=[xcg[:]], tag="dbgqs")
                p.dma("pool", dbg_ap, tsq[:], ins_extra=["e_tsq"], tag="dbgap")
            EO = f"e{ch}o"
            p.emit("vector", lambda e: e.tensor_copy(p_bf[:], xcg[:]),
                   ins=[xcg[:]], outs=["e_qs"], tag=f"{EO}_qs")
            p.emit("vector", lambda e: e.tensor_mul(dp_bf[:], p_bf[:], dbl[:]),
                   ins=["e_qs"] + [f"e_dbl@{pc}" for pc in range(8)],
                   outs=["e_dqs"], tag=f"{EO}_dqs")
            for h in range(16):
                pc, hl = h // 2, h % 2
                p.emit("tensor", lambda e, pc=pc, hl=hl, h=h: e.matmul(
                    ps_big[:, h * 128:(h + 1) * 128],
                    k_l[64 * hl:64 * hl + 64, pc * 128:(pc + 1) * 128],
                    p_bf[64 * hl:64 * hl + 64, pc * 128:(pc + 1) * 128],
                    start=True, stop=True, tile_position=(64 * hl, 0)),
                    ins=[k_l[:], "e_qs"], outs=[f"e_psbig@{h // 4}"], tag=f"{EO}_S{h}")
            p.emit("vector", lambda e: e.tensor_mul(Wt[:], ps_big[:], Dt[:]),
                   ins=[Dt[:]] + [f"e_psbig@{q}" for q in range(4)], outs=[Wt[:]],
                   tag=f"{EO}_W")
            for h in range(16):
                pc, hl = h // 2, h % 2
                p.emit("tensor", lambda e, pc=pc, hl=hl, h=h: e.matmul(
                    ps_Ap[64 * hl:64 * hl + 64, pc * 128:(pc + 1) * 128],
                    Vbe[:, h * 64:(h + 1) * 64], Wt[:, h * 128:(h + 1) * 128],
                    start=True, stop=False, tile_position=(0, 64 * hl)),
                    ins=["e_Vbeta", Wt[:]], outs=[f"e_psAp@{h // 8}"], tag=f"{EO}_I{h}")
                p.emit("tensor", lambda e, pc=pc, hl=hl, h=h: e.matmul(
                    ps_Ap[64 * hl:64 * hl + 64, pc * 128:(pc + 1) * 128],
                    Hkvb[64 * hl:64 * hl + 64, h * 64:(h + 1) * 64],
                    dp_bf[64 * hl:64 * hl + 64, pc * 128:(pc + 1) * 128],
                    start=False, stop=True, tile_position=(64 * hl, 64 * hl)),
                    ins=[f"Hb1@{h}", "e_dqs"], outs=[f"e_psAp@{h // 8}"],
                    tag=f"{EO}_H{h}")
            p.emit("vector", lambda e: e.tensor_copy(osb[:], ps_Ap[:]),
                   ins=[f"e_psAp@{b}" for b in range(2)], outs=[osb[:]], tag=f"{EO}_oe")
            if DBG and ch == 0:
                p.dma("pool", dbg_o, osb[:], ins_extra=[osb[:]], tag="dbgo")
            for pc in range(8):
                p.emit("tensor", lambda e, pc=pc, tsl=tsl: e.matmul(
                    ps_sm[:, 0:128], gw2t[:, pc * 128:(pc + 1) * 128],
                    g1T[:, tsl], start=True, stop=True),
                    ins=[gw2t[:], f"d_g1T@{j512}"], outs=["e_pssm"],
                    tag=f"{EO}_g{pc}")
                p.emit("scalar", lambda e, pc=pc: e.activation(
                    tmp[:, pc * 128:(pc + 1) * 128], ps_sm[:, 0:128], AF.Sigmoid),
                    ins=["e_pssm"], outs=["e_tmp"], tag=f"{EO}_ge{pc}")
            p.emit("vector", lambda e: e.tensor_mul(osb[:], osb[:], tmp[:]),
                   ins=[osb[:], "e_tmp"], outs=[osb[:]], tag=f"{EO}_og")
            p.emit("scalar", lambda e: e.activation(tsq[:], osb[:], AF.Square),
                   ins=[osb[:]], outs=["e_tsq"], tag=f"{EO}_q")
            for hf in range(2):
                hsl = slice(hf * 512, (hf + 1) * 512)
                p.emit("tensor", lambda e, hsl=hsl: e.matmul(
                    ps_sm[0:2, 0:512], hs2[:], tsq[:, hsl],
                    start=True, stop=True),
                    ins=[hs2[:], "e_tsq"], outs=["e_pssm"], tag=f"{EO}_s{hf}")
                p.emit("scalar", lambda e: e.activation(sca[0:2, 0:512],
                                                        ps_sm[0:2, 0:512], AF.Sqrt,
                                                        scale=1.0 / 64.0, bias=1e-5),
                       ins=["e_pssm"], outs=["e_sca@og"], tag=f"{EO}_r{hf}")
                p.emit("vector", lambda e: e.reciprocal(sca[0:2, 0:512],
                                                        sca[0:2, 0:512]),
                       ins=["e_sca@og"], outs=["e_sca@og"], tag=f"{EO}_i{hf}")
                p.emit("tensor", lambda e: e.matmul(
                    ps_sm[:, 0:512], hb2[:], sca[0:2, 0:512],
                    start=True, stop=True),
                    ins=[hb2[:], "e_sca@og"], outs=["e_pssm"], tag=f"{EO}_b{hf}")
                p.emit("vector", lambda e, hsl=hsl: e.tensor_mul(
                    osb[:, hsl], osb[:, hsl], ps_sm[:, 0:512]),
                    ins=[osb[:], "e_pssm"], outs=[osb[:]], tag=f"{EO}_n{hf}")
            for pc in range(8):
                p.emit("vector", lambda e, pc=pc: e.tensor_scalar(
                    ogn[:, pc * 128:(pc + 1) * 128],
                    osb[:, pc * 128:(pc + 1) * 128], nwl[:, pc:pc + 1], None,
                    ALU.mult),
                    ins=[osb[:], nwl[:]], outs=[f"e_ogn@{pc}"], tag=f"{EO}_w{pc}")
            for dc in range(NDC):
                for pc in range(8):
                    p.emit("tensor", lambda e, pc=pc, dc=dc: e.matmul(
                        ps_sm[:, 128:256], wo_b[pc][:, dc * 128:(dc + 1) * 128],
                        ogn[:, pc * 128:(pc + 1) * 128],
                        start=(pc == 0), stop=(pc == 7)),
                        ins=[wo_b[pc][:], f"e_ogn@{pc}"], outs=["e_pssm"],
                        tag=f"{EO}_y{dc}_{pc}")
                p.dma("sync", xac[:], x_dram[dc * 128:(dc + 1) * 128, tsl],
                      tag=f"{EO}_lx{dc}")
                p.emit("vector", lambda e: e.tensor_add(xac[:], xac[:],
                                                        ps_sm[:, 128:256]),
                       ins=[xac[:], "e_pssm"], outs=[xac[:]], tag=f"{EO}_ya{dc}")
                p.dma("pool", xout_d[dc * 128:(dc + 1) * 128, tsl], xac[:],
                      tag=f"{EO}_st{dc}")
            ES = f"e{ch}s"
            for (kk, HT, HB, rhs_t, rhreg) in (
                    (0, Hkk, Hkkb, Ktok, None), (1, Hkv, Hkvb, Vtok, None)):
                for h in range(16):
                    p.emit("tensor", lambda e, h=h, rhs_t=rhs_t: e.matmul(
                        ps_sm[:, 0:64],
                        Ksc2[:, h * 128:(h + 1) * 128], rhs_t[:, h * 64:(h + 1) * 64],
                        start=True, stop=True),
                        ins=[Ksc2[:], rhs_t[:]], outs=["e_pssm"],
                        tag=f"{ES}_i{kk}_{h}")
                    p.emit("vector", lambda e, h=h, HT=HT: e.scalar_tensor_tensor(
                        HT[:, h * 64:(h + 1) * 64], HT[:, h * 64:(h + 1) * 64],
                        dec64[:, h:h + 1], ps_sm[:, 0:64], ALU.mult, ALU.add),
                        ins=[f"H{kk}@{h}", dec64[:], "e_pssm"],
                        outs=[f"H{kk}@{h}"], tag=f"{ES}_u{kk}_{h}")
                    p.emit("scalar", lambda e, h=h, HT=HT, HB=HB: e.copy(
                        HB[:, h * 64:(h + 1) * 64], HT[:, h * 64:(h + 1) * 64]),
                        ins=[f"H{kk}@{h}"], outs=[f"Hb{kk}@{h}"], tag=f"{ES}_c{kk}_{h}")
    p.fence()


from contextlib import ExitStack

CONST_ALL = ("c_ones", "c_onesr", "c_eye32", "c_eyebf", "c_hb", "c_hbbf",
             "c_hs2", "c_hb2", "c_hsg", "c_hsgbf", "c_eye2", "c_dup",
             "c_dup0", "c_dup1")
CONST_BLK = {"c_l0mask": ([128, 2048], BF16), "c_l0diag": ([128, 2048], BF16),
             "c_masklehb": ([128, 2048], BF16), "c_heye": ([16, 2048], F32)}


def build_fused():
    nc = bass.Bass("TRN2", target_bir_lowering=False, debug=False)
    x0 = nc.dram_tensor("x", [1024, 4096], F32, kind="ExternalInput").ap()
    xr1 = nc.dram_tensor("xr1", [1024, 4096], F32, kind="Internal").ap()
    xr2 = nc.dram_tensor("xr2", [1024, 4096], F32, kind="Internal").ap()
    xr3 = nc.dram_tensor("xr3", [1024, 4096], F32, kind="Internal").ap()
    xout = nc.dram_tensor("xout", [1024, 4096], F32, kind="ExternalOutput").ap()
    wk = nc.dram_tensor("wk", [1024, 1024], F32, kind="ExternalInput").ap()
    wv = nc.dram_tensor("wv", [1024, 1024], F32, kind="ExternalInput").ap()
    wq = nc.dram_tensor("wq", [1024, 1024], F32, kind="ExternalInput").ap()
    wbeta = nc.dram_tensor("wbeta", [1024, 16], F32, kind="ExternalInput").ap()
    wout = nc.dram_tensor("wout", [1024, 1024], F32, kind="ExternalInput").ap()
    q_dr = nc.dram_tensor("q_sc", [1024, 4096], BF16, kind="Internal").ap()
    k_dr = nc.dram_tensor("k_sc", [1024, 4096], BF16, kind="Internal").ap()
    v_dr = nc.dram_tensor("v_sc", [4096, 1024], BF16, kind="Internal").ap()
    cl_dr = nc.dram_tensor("cl_sc", [16, 4096], F32, kind="Internal").ap()
    bt_dr = nc.dram_tensor("bt_sc", [16, 4096], F32, kind="Internal").ap()
    rw = nc.dram_tensor("rw", [8, 1024], F32, kind="ExternalInput").ap()
    w13 = nc.dram_tensor("w13", [8, 1024, 4096], F32, kind="ExternalInput").ap()
    w2 = nc.dram_tensor("w2", [8, 2048, 1024], F32, kind="ExternalInput").ap()
    sc_dr = nc.dram_tensor("sc_sc", [8, 4096], F32, kind="Internal").ap()
    WD = {}
    for nm, shp in (("wq", [1024, 1024]), ("wk", [1024, 1024]), ("wv", [1024, 1024]),
                    ("wo", [1024, 1024]), ("wf", [1024, 16]), ("wbet", [1024, 16]),
                    ("delta", [16, 1]), ("lamb", [64, 16]), ("nw", [128, 8]),
                    ("gw1", [1024, 64]), ("gw2", [64, 1024])):
        WD[nm] = nc.dram_tensor("w_" + nm, shp, F32, kind="ExternalInput").ap()
    cft_dr = nc.dram_tensor("cft_sc", [4096, 16], F32, kind="Internal").ap()
    cf2_dr = nc.dram_tensor("cf2_sc", [16, 4096], F32, kind="Internal").ap()
    WD["ql_dr"] = nc.dram_tensor("dql_sc", [1024, 4096], BF16, kind="Internal").ap()
    WD["kl_dr"] = nc.dram_tensor("dkl_sc", [1024, 4096], BF16, kind="Internal").ap()
    WD["kt_dr"] = nc.dram_tensor("dkt_sc", [4096, 1024], BF16, kind="Internal").ap()
    WD["vt_dr"] = nc.dram_tensor("dvt_sc", [4096, 1024], BF16, kind="Internal").ap()
    hkk_dr = nc.dram_tensor("hkk_sc", [32, 128, 1024], BF16, kind="Internal").ap()
    hkv_dr = nc.dram_tensor("hkv_sc", [32, 128, 1024], BF16, kind="Internal").ap()
    FD = {}
    for f in "qkv":
        FD[f + "_w1cat"] = nc.dram_tensor(f + "_w1cat", [1024, 1024], F32,
                                          kind="ExternalInput").ap()
        FD[f + "_w2cat"] = nc.dram_tensor(f + "_w2cat", [1024, 1024], F32,
                                          kind="ExternalInput").ap()
        FD[f + "_b1cat"] = nc.dram_tensor(f + "_b1cat", [128, 8], F32,
                                          kind="ExternalInput").ap()
        FD[f + "_b2"] = nc.dram_tensor(f + "_b2", [16, 1024], F32,
                                       kind="ExternalInput").ap()
        FD[f + "_node_w"] = nc.dram_tensor(f + "_node_w", [15, 1024], F32,
                                           kind="ExternalInput").ap()
        FD[f + "_node_b"] = nc.dram_tensor(f + "_node_b", [15, 1], F32,
                                           kind="ExternalInput").ap()
    fq_dr = nc.dram_tensor("fq_sc", [1024, 4096], BF16, kind="Internal").ap()
    fk_dr = nc.dram_tensor("fk_sc", [1024, 4096], BF16, kind="Internal").ap()
    fv_dr = nc.dram_tensor("fv_sc", [4096, 1024], BF16, kind="Internal").ap()

    with ExitStack() as st:
        s = S(nc, st)
        p = Prog(nc)
        C = load_consts(p, s, nc, names=("c_ones", "c_onesr", "c_eye32"))
        pshapes = {"c_eyebf": ([128, 128], BF16), "c_hb": ([16, 1024], F32),
                   "c_hbbf": ([16, 1024], BF16), "c_hs2": ([128, 2], F32),
                   "c_hb2": ([2, 128], F32), "c_hsg": ([128, 128], F32),
                   "c_hsgbf": ([128, 128], BF16),
                   "c_eye2": ([128, 64], F32), "c_dup": ([64, 128], F32),
                   "c_dup0": ([64, 128], F32), "c_dup1": ([64, 128], F32),
                   "c_jgt": ([128, 64], F32)}
        pshapes.update(CONST_BLK)
        blkdr = {nm: nc.dram_tensor(nm, shp, dt, kind="ExternalInput").ap()
                 for nm, (shp, dt) in pshapes.items()}

        def rl(sx, *nms):
            for nm in nms:
                shp, dt = pshapes[nm]
                reload_const(p, sx, C, nm, blkdr[nm], shp, dt)

        with ExitStack() as st1:
            s1 = S(nc, st1)
            rl(s1, "c_hs2", "c_hb2", "c_l0mask", "c_l0diag")
            l0_attn(p, s1, C, nc, x0, wk, wv, wq, wbeta, wout, xr1,
                    q_dr, k_dr, v_dr, cl_dr, bt_dr)
        p.fence()
        with ExitStack() as st2:
            s2 = S(nc, st2)
            rl(s2, "c_jgt")
            xn = l0_ffn(p, s2, C, nc, xr1, rw, w13, w2, xr2, sc_dr)
            l0_ffn_experts(p, s2, C, nc, xn, w13, w2, xr1, xr2, sc_dr)
        p.fence()
        with ExitStack() as st3:
            s3 = S(nc, st3)
            rl(s3, "c_hb", "c_hs2", "c_hb2", "c_hsg", "c_masklehb", "c_heye",
               "c_eye2", "c_dup", "c_dup0", "c_dup1", "c_hbbf", "c_hsgbf")
            xn = l1_attn(p, s3, C, nc, xr2, WD, xr3, cft_dr, cf2_dr)
            tens = (_L1W["wq_b"], _L1W["wk_b"], _L1W["wv_b"], _L1W["wo_b"],
                    _L1W["cfT"], _L1W["btT"], _L1W["cf_tok"], _L1W["bt_tok"],
                    _L1W["g1T"], _L1W["dlam"], _L1W["nwl"], _L1W["Hkk"],
                    _L1W["Hkv"], _L1W["Hkkb"], _L1W["Hkvb"], _L1W["gw2t"])
            QD = {"ql": WD["ql_dr"], "kl": WD["kl_dr"], "kt": WD["kt_dr"],
                  "vt": WD["vt_dr"]}
            l1_attn_pass1(p, s3, C, nc, tens, QD, cft_dr, hkk_dr, hkv_dr)
            l1_attn_pass2v2(p, s3, C, nc, QD, tens, xr2, xr3, cft_dr, cf2_dr,
                            hkk_dr, hkv_dr)
        p.fence()
        with ExitStack() as st4:
            s4 = S(nc, st4)
            rl(s4, "c_eyebf", "c_hbbf")
            l1_ffn_block(p, s4, C, nc, xr3, FD, xout, fq_dr, fk_dr, fv_dr)
        p.finalize_and_emit()
    return nc


def kernel(**ins):
    ncores = 8
    cns = host_consts()
    cns.update(host_consts_ffn())
    xs = [np.ascontiguousarray(np.asarray(ins["L0_attn_x"][b]).T)
          for b in range(ncores)]
    sh = {k: cns[k] for k in CONST_ALL + ("c_jgt", "c_vals", "c_l0mask", "c_l0diag", "c_masklehb", "c_heye")}
    sh.update(wk=ins["L0_attn_Wk"], wv=ins["L0_attn_Wv"], wq=ins["L0_attn_Wq"],
              wbeta=ins["L0_attn_Wbeta"], wout=ins["L0_attn_Wout"],
              rw=ins["L0_ffn_router_w"], w13=ins["L0_ffn_w13"], w2=ins["L0_ffn_w2"],
              w_wq=ins["L1_attn_Wq"], w_wk=ins["L1_attn_Wk"], w_wv=ins["L1_attn_Wv"],
              w_wo=ins["L1_attn_Wo"], w_wf=ins["L1_attn_Wf"],
              w_wbet=ins["L1_attn_Wbet"],
              w_delta=np.ascontiguousarray(
                  np.asarray(ins["L1_attn_delta"]).reshape(16, 1)),
              w_lamb=np.ascontiguousarray(
                  np.asarray(ins["L1_attn_lamb_params"]).reshape(16, 64).T),
              w_nw=np.ascontiguousarray(
                  np.asarray(ins["L1_attn_norm_weight"]).reshape(8, 128).T),
              w_gw1=ins["L1_attn_gate_w1"], w_gw2=ins["L1_attn_gate_w2"])
    for f in "qkv":
        w1 = np.asarray(ins[f"L1_ffn_{f}_w1"])
        w2f = np.asarray(ins[f"L1_ffn_{f}_w2"])
        b1 = np.asarray(ins[f"L1_ffn_{f}_b1"])
        sh[f + "_w1cat"] = np.ascontiguousarray(
            w1.transpose(1, 0, 2).reshape(1024, 1024))
        sh[f + "_w2cat"] = np.ascontiguousarray(w2f.reshape(1024, 1024))
        sh[f + "_b1cat"] = np.ascontiguousarray(b1.reshape(-1).reshape(8, 128).T)
        sh[f + "_b2"] = ins[f"L1_ffn_{f}_b2"]
        sh[f + "_node_w"] = ins[f"L1_ffn_{f}_node_w"]
        sh[f + "_node_b"] = np.ascontiguousarray(
            np.asarray(ins[f"L1_ffn_{f}_node_b"]).reshape(15, 1))
    nc = build_fused()
    maps = [dict(sh, x=x) for x in xs]
    res = run_bass_kernel_spmd(nc, maps, core_ids=list(range(ncores)))
    return np.stack([r["xout"].T for r in res.results]).astype(np.float32)


def l1_attn_pass1(p, s, C, nc, tens, QD, cft_dr, hkk_dr, hkv_dr):
    from contextlib import ExitStack
    onesr = C["c_onesr"]
    (wq_b, wk_b, wv_b, wo_b, cfT, btT, cf_tok, bt_tok, g1T, dlam, nwl,
     Hkk, Hkv, Hkkb, Hkvb, gw2t) = tens
    HSEG = lambda t: t.rearrange("p (h j) -> p h j", j=64)
    with ExitStack() as st:
        s1 = S(nc, st)
        Hkk = s1.sb("d_Hkk", [128, 16 * 64])
        Hkv = s1.sb("d_Hkv", [128, 16 * 64])
        Hkkb = s1.sb("d_Hkkb", [128, 16 * 64], BF16)
        Hkvb = s1.sb("d_Hkvb", [128, 16 * 64], BF16)
        for kk, tt in ((0, Hkk), (1, Hkv)):
            p.emit("pool", lambda e, tt=tt: e.memset(tt[:], 0.0),
                   outs=[f"H{kk}@{h}" for h in range(16)], tag=f"d_zH{kk}")
        for kk, tt in ((0, Hkkb), (1, Hkvb)):
            p.emit("pool", lambda e, tt=tt: e.memset(tt[:], 0.0),
                   outs=[f"Hb{kk}@{h}" for h in range(16)], tag=f"d_zHb{kk}")
        Ktok = s1.sb("s1_Kt", [128, 1024], BF16)
        Vtok = s1.sb("s1_Vt", [128, 1024], BF16)
        Ksc = s1.sb("s1_Ks", [128, 1024], BF16)
        Ksc2 = s1.sb("s1_Ks2", [128, 2048], BF16)
        wex = s1.sb("s1_wex", [128, 16])
        cfr = s1.sb("s1_cfr", [1, 32])
        dec64 = s1.sb("s1_dec", [128, 16])
        ps_sm = s1.ps("s1_ps", [128, 512])
        for ch in range(NCH):
            tsl = slice(ch * 128, (ch + 1) * 128)
            b5 = p.open_batch("sync")
            p.dma("sync", Ktok[:], QD["kt"][tsl, :], tag=f"s1{ch}_lK", batch=b5)
            p.dma("sync", Vtok[:], QD["vt"][tsl, :], tag=f"s1{ch}_lV", batch=b5)
            p.dma("sync", cfr[:, 0:16], cft_dr[ch * 128 + 127: ch * 128 + 128, :],
                  tag=f"s1{ch}_lce")
            if ch > 0:
                p.dma("sync", cfr[:, 16:32], cft_dr[ch * 128 - 1: ch * 128, :],
                      tag=f"s1{ch}_lcs")
            else:
                p.emit("pool", lambda e: e.memset(cfr[:, 16:32], 0.0),
                       outs=[cfr[:]], tag=f"s1{ch}_zcs")
            p.dma("pool", hkk_dr[ch], Hkkb[:], ins_extra=[f"Hb0@{h}" for h in range(16)],
                  tag=f"s1{ch}_sK")
            p.dma("pool", hkv_dr[ch], Hkvb[:], ins_extra=[f"Hb1@{h}" for h in range(16)],
                  tag=f"s1{ch}_sV")
            p.emit("tensor", lambda e: e.matmul(ps_sm[:, 0:16], onesr[0:1, :],
                                                cfr[:, 0:16], start=True, stop=True),
                   ins=[onesr[:], cfr[:]], outs=["s1_ps"], tag=f"s1{ch}_cb")
            p.emit("vector", lambda e, ch=ch: e.tensor_sub(
                wex[:], ps_sm[:, 0:16], cf_tok[:, ch * 16:(ch + 1) * 16]),
                ins=["s1_ps", f"d_cftok@{ch}"], outs=["s1_wex"], tag=f"s1{ch}_wd")
            p.emit("scalar", lambda e: e.activation(wex[:], wex[:], AF.Exp),
                   ins=["s1_wex"], outs=["s1_wex"], tag=f"s1{ch}_we")
            p.emit("vector", lambda e, ch=ch: e.tensor_tensor(
                HSEG(Ksc[:]), HSEG(Ktok[:]),
                bt_tok[:, ch * 16:(ch + 1) * 16].unsqueeze(2)
                .broadcast_to([128, 16, 64]), ALU.mult),
                ins=[Ktok[:], f"d_bttok@{ch}"], outs=[Ksc[:]], tag=f"s1{ch}_Kb")
            p.emit("vector", lambda e: e.tensor_tensor(
                HSEG(Ksc[:]), HSEG(Ksc[:]),
                wex[:].unsqueeze(2).broadcast_to([128, 16, 64]), ALU.mult),
                ins=[Ksc[:], "s1_wex"], outs=[Ksc[:]], tag=f"s1{ch}_Kw")
            p.emit("vector", lambda e: e.tensor_copy(
                Ksc2[:].rearrange("p (h d j) -> p h d j", d=2, j=64),
                HSEG(Ksc[:]).unsqueeze(2).broadcast_to([128, 16, 2, 64])),
                ins=[Ksc[:]], outs=[Ksc2[:]], tag=f"s1{ch}_K2")
            p.emit("vector", lambda e: e.tensor_sub(cfr[:, 0:16], cfr[:, 0:16],
                                                    cfr[:, 16:32]),
                   ins=[cfr[:]], outs=[cfr[:]], tag=f"s1{ch}_dd")
            p.emit("scalar", lambda e: e.activation(cfr[:, 0:16], cfr[:, 0:16], AF.Exp),
                   ins=[cfr[:]], outs=[cfr[:]], tag=f"s1{ch}_de")
            p.emit("tensor", lambda e: e.matmul(ps_sm[:, 16:32], onesr[0:1, :],
                                                cfr[:, 0:16], start=True, stop=True),
                   ins=[onesr[:], cfr[:]], outs=["s1_ps"], tag=f"s1{ch}_db")
            p.emit("vector", lambda e: e.tensor_copy(dec64[:], ps_sm[:, 16:32]),
                   ins=["s1_ps"], outs=[dec64[:]], tag=f"s1{ch}_d2")
            for kk, HT, HB, rhs_t in ((0, Hkk, Hkkb, Ktok), (1, Hkv, Hkvb, Vtok)):
                for hh in range(0, 16, 2):
                    for h in (hh, hh + 1):
                        col = 0 if h % 2 == 0 else 64
                        p.emit("tensor", lambda e, h=h, rhs_t=rhs_t, col=col:
                               e.matmul(ps_sm[:, 64 + col:128 + col],
                                        Ksc2[:, h * 128:(h + 1) * 128],
                                        rhs_t[:, h * 64:(h + 1) * 64],
                                        start=True, stop=True),
                               ins=[Ksc2[:], rhs_t[:]], outs=["s1_ps"],
                               tag=f"s1{ch}_i{kk}_{h}")
                    for h in (hh, hh + 1):
                        col = 0 if h % 2 == 0 else 64
                        p.emit("vector", lambda e, h=h, HT=HT, col=col:
                               e.scalar_tensor_tensor(
                            HT[:, h * 64:(h + 1) * 64], HT[:, h * 64:(h + 1) * 64],
                            dec64[:, h:h + 1], ps_sm[:, 64 + col:128 + col],
                            ALU.mult, ALU.add),
                            ins=[f"H{kk}@{h}", dec64[:], "s1_ps"],
                            outs=[f"H{kk}@{h}"], tag=f"s1{ch}_u{kk}_{h}")
                        p.emit("scalar", lambda e, h=h, HT=HT, HB=HB: e.copy(
                            HB[:, h * 64:(h + 1) * 64], HT[:, h * 64:(h + 1) * 64]),
                            ins=[f"H{kk}@{h}"], outs=[f"Hb{kk}@{h}"],
                            tag=f"s1{ch}_c{kk}_{h}")
    p.fence()


def build_l0_attn():
    nc = bass.Bass("TRN2", target_bir_lowering=False, debug=False)
    x = nc.dram_tensor("x", [1024, 4096], F32, kind="ExternalInput").ap()
    wk = nc.dram_tensor("wk", [1024, 1024], F32, kind="ExternalInput").ap()
    wv = nc.dram_tensor("wv", [1024, 1024], F32, kind="ExternalInput").ap()
    wq = nc.dram_tensor("wq", [1024, 1024], F32, kind="ExternalInput").ap()
    wbeta = nc.dram_tensor("wbeta", [1024, 16], F32, kind="ExternalInput").ap()
    wout = nc.dram_tensor("wout", [1024, 1024], F32, kind="ExternalInput").ap()
    xout = nc.dram_tensor("xout", [1024, 4096], F32, kind="ExternalOutput").ap()
    q_dr = nc.dram_tensor("q_sc", [1024, 4096], BF16, kind="Internal").ap()
    k_dr = nc.dram_tensor("k_sc", [1024, 4096], BF16, kind="Internal").ap()
    v_dr = nc.dram_tensor("v_sc", [4096, 1024], BF16, kind="Internal").ap()
    cl_dr = nc.dram_tensor("cl_sc", [16, 4096], F32, kind="Internal").ap()
    bt_dr = nc.dram_tensor("bt_sc", [16, 4096], F32, kind="Internal").ap()
    with ExitStack() as st:
        s = S(nc, st)
        p = Prog(nc)
        C = load_consts(p, s, nc, names=("c_ones", "c_onesr", "c_eye32",
                                         "c_hs2", "c_hb2",
                                         "c_l0mask", "c_l0diag"))
        l0_attn(p, s, C, nc, x, wk, wv, wq, wbeta, wout, xout,
                q_dr, k_dr, v_dr, cl_dr, bt_dr)
        p.finalize_and_emit()
    return nc


def build_l0_ffn():
    nc = bass.Bass("TRN2", target_bir_lowering=False, debug=False)
    x = nc.dram_tensor("x", [1024, 4096], F32, kind="ExternalInput").ap()
    rw = nc.dram_tensor("rw", [8, 1024], F32, kind="ExternalInput").ap()
    w13 = nc.dram_tensor("w13", [8, 1024, 4096], F32, kind="ExternalInput").ap()
    w2 = nc.dram_tensor("w2", [8, 2048, 1024], F32, kind="ExternalInput").ap()
    xout = nc.dram_tensor("xout", [1024, 4096], F32, kind="ExternalOutput").ap()
    sc_dr = nc.dram_tensor("sc_sc", [8, 4096], F32, kind="Internal").ap()
    with ExitStack() as st:
        s = S(nc, st)
        p = Prog(nc)
        C = load_consts(p, s, nc, names=("c_ones", "c_onesr", "c_eye32"))
        C.update(load_extra_consts(p, s, nc, {"c_jgt": ([128, 64], F32)}))
        xn = l0_ffn(p, s, C, nc, x, rw, w13, w2, xout, sc_dr)
        l0_ffn_experts(p, s, C, nc, xn, w13, w2, x, xout, sc_dr)
        p.finalize_and_emit()
    return nc


def build_l1_attn():
    nc = bass.Bass("TRN2", target_bir_lowering=False, debug=False)
    x = nc.dram_tensor("x", [1024, 4096], F32, kind="ExternalInput").ap()
    WD = {}
    for nm, shp in (("wq", [1024, 1024]), ("wk", [1024, 1024]), ("wv", [1024, 1024]),
                    ("wo", [1024, 1024]), ("wf", [1024, 16]), ("wbet", [1024, 16]),
                    ("delta", [16, 1]), ("lamb", [64, 16]), ("nw", [128, 8]),
                    ("gw1", [1024, 64]), ("gw2", [64, 1024])):
        WD[nm] = nc.dram_tensor("w_" + nm, shp, F32, kind="ExternalInput").ap()
    xout = nc.dram_tensor("xout", [1024, 4096], F32, kind="ExternalOutput").ap()
    cft_dr = nc.dram_tensor("cft_sc", [4096, 16], F32, kind="Internal").ap()
    cf2_dr = nc.dram_tensor("cf2_sc", [16, 4096], F32, kind="Internal").ap()
    WD["ql_dr"] = nc.dram_tensor("dql_sc", [1024, 4096], BF16, kind="Internal").ap()
    WD["kl_dr"] = nc.dram_tensor("dkl_sc", [1024, 4096], BF16, kind="Internal").ap()
    WD["kt_dr"] = nc.dram_tensor("dkt_sc", [4096, 1024], BF16, kind="Internal").ap()
    WD["vt_dr"] = nc.dram_tensor("dvt_sc", [4096, 1024], BF16, kind="Internal").ap()
    with ExitStack() as st:
        s = S(nc, st)
        p = Prog(nc)
        C = load_consts(p, s, nc, names=("c_ones", "c_onesr", "c_eye32", "c_hb",
                                         "c_hs2", "c_hb2", "c_hsg", "c_masklehb",
                                         "c_heye", "c_eye2", "c_dup"))
        xn = l1_attn(p, s, C, nc, x, WD, xout, cft_dr, cf2_dr)
        tens = (_L1W["wq_b"], _L1W["wk_b"], _L1W["wv_b"], _L1W["wo_b"],
                _L1W["cfT"], _L1W["btT"], _L1W["cf_tok"], _L1W["bt_tok"],
                _L1W["g1T"], _L1W["dlam"], _L1W["nwl"], _L1W["Hkk"],
                _L1W["Hkv"], _L1W["Hkkb"], _L1W["Hkvb"], _L1W["gw2t"])
        QD = {"ql": WD["ql_dr"], "kl": WD["kl_dr"], "kt": WD["kt_dr"],
              "vt": WD["vt_dr"]}
        l1_attn_chunk_loop(p, s, C, nc, QD, tens, x, xout, cft_dr, cf2_dr)
        p.finalize_and_emit()
    return nc


def build_l1_ffn():
    nc = bass.Bass("TRN2", target_bir_lowering=False, debug=False)
    x = nc.dram_tensor("x", [1024, 4096], F32, kind="ExternalInput").ap()
    FD = {}
    for f in "qkv":
        FD[f + "_w1cat"] = nc.dram_tensor(f + "_w1cat", [1024, 1024], F32,
                                          kind="ExternalInput").ap()
        FD[f + "_w2cat"] = nc.dram_tensor(f + "_w2cat", [1024, 1024], F32,
                                          kind="ExternalInput").ap()
        FD[f + "_b1cat"] = nc.dram_tensor(f + "_b1cat", [128, 8], F32,
                                          kind="ExternalInput").ap()
        FD[f + "_b2"] = nc.dram_tensor(f + "_b2", [16, 1024], F32,
                                       kind="ExternalInput").ap()
        FD[f + "_node_w"] = nc.dram_tensor(f + "_node_w", [15, 1024], F32,
                                           kind="ExternalInput").ap()
        FD[f + "_node_b"] = nc.dram_tensor(f + "_node_b", [15, 1], F32,
                                           kind="ExternalInput").ap()
    xout = nc.dram_tensor("xout", [1024, 4096], F32, kind="ExternalOutput").ap()
    q_dr = nc.dram_tensor("fq_sc", [1024, 4096], BF16, kind="Internal").ap()
    k_dr = nc.dram_tensor("fk_sc", [1024, 4096], BF16, kind="Internal").ap()
    v_dr = nc.dram_tensor("fv_sc", [4096, 1024], BF16, kind="Internal").ap()
    with ExitStack() as st:
        s = S(nc, st)
        p = Prog(nc)
        C = load_consts(p, s, nc, names=("c_ones", "c_onesr", "c_eye32", "c_eyebf",
                                         "c_hbbf"))
        l1_ffn_block(p, s, C, nc, x, FD, xout, q_dr, k_dr, v_dr)
        p.finalize_and_emit()
    return nc


def kernel_unfused(**ins):
    ncores = 8
    cns = host_consts()
    cns.update(host_consts_ffn())
    xs = [np.ascontiguousarray(np.asarray(ins["L0_attn_x"][b]).T)
          for b in range(ncores)]

    def run(nc, per_core, shared):
        maps = [dict(shared, **pc) for pc in per_core]
        res = run_bass_kernel_spmd(nc, maps, core_ids=list(range(ncores)))
        return [r["xout"] for r in res.results]

    sh = {k: cns[k] for k in ("c_ones", "c_onesr", "c_eye32", "c_hs2", "c_hb2",
                              "c_l0mask", "c_l0diag", "c_vals")}
    sh.update(wk=ins["L0_attn_Wk"], wv=ins["L0_attn_Wv"], wq=ins["L0_attn_Wq"],
              wbeta=ins["L0_attn_Wbeta"], wout=ins["L0_attn_Wout"])
    xs = run(build_l0_attn(), [{"x": x} for x in xs], sh)

    sh = {k: cns[k] for k in ("c_ones", "c_onesr", "c_eye32", "c_jgt", "c_vals")}
    sh.update(rw=ins["L0_ffn_router_w"], w13=ins["L0_ffn_w13"], w2=ins["L0_ffn_w2"])
    xs = run(build_l0_ffn(), [{"x": x} for x in xs], sh)

    sh = {k: cns[k] for k in ("c_ones", "c_onesr", "c_eye32", "c_hb", "c_hs2",
                              "c_hb2", "c_hsg", "c_masklehb", "c_heye",
                              "c_eye2", "c_dup", "c_vals")}
    sh.update(w_wq=ins["L1_attn_Wq"], w_wk=ins["L1_attn_Wk"], w_wv=ins["L1_attn_Wv"],
              w_wo=ins["L1_attn_Wo"], w_wf=ins["L1_attn_Wf"],
              w_wbet=ins["L1_attn_Wbet"],
              w_delta=np.ascontiguousarray(
                  np.asarray(ins["L1_attn_delta"]).reshape(16, 1)),
              w_lamb=np.ascontiguousarray(
                  np.asarray(ins["L1_attn_lamb_params"]).reshape(16, 64).T),
              w_nw=np.ascontiguousarray(
                  np.asarray(ins["L1_attn_norm_weight"]).reshape(8, 128).T),
              w_gw1=ins["L1_attn_gate_w1"], w_gw2=ins["L1_attn_gate_w2"])
    xs = run(build_l1_attn(), [{"x": x} for x in xs], sh)

    sh = {k: cns[k] for k in ("c_ones", "c_onesr", "c_eye32", "c_eyebf",
                              "c_hbbf", "c_vals")}
    for f in "qkv":
        w1 = np.asarray(ins[f"L1_ffn_{f}_w1"])
        w2 = np.asarray(ins[f"L1_ffn_{f}_w2"])
        b1 = np.asarray(ins[f"L1_ffn_{f}_b1"])
        sh[f + "_w1cat"] = np.ascontiguousarray(
            w1.transpose(1, 0, 2).reshape(1024, 1024))
        sh[f + "_w2cat"] = np.ascontiguousarray(w2.reshape(1024, 1024))
        sh[f + "_b1cat"] = np.ascontiguousarray(b1.reshape(-1).reshape(8, 128).T)
        sh[f + "_b2"] = ins[f"L1_ffn_{f}_b2"]
        sh[f + "_node_w"] = ins[f"L1_ffn_{f}_node_w"]
        sh[f + "_node_b"] = np.ascontiguousarray(
            np.asarray(ins[f"L1_ffn_{f}_node_b"]).reshape(15, 1))
    xs = run(build_l1_ffn(), [{"x": x} for x in xs], sh)

    return np.stack([x.T for x in xs]).astype(np.float32)




def l1_attn_pass2(p, s, C, nc, QD, tens, x_dram, xout_d, cft_dr, cf2_dr,
                  hkk_dr, hkv_dr, lamb_b):
    from contextlib import ExitStack
    ones = C["c_ones"]; onesr = C["c_onesr"]
    hb = C["c_hb"]; hsg = C["c_hsg"]; mle = C["c_masklehb"]; heye = C["c_heye"]
    (wq_b, wk_b, wv_b, wo_b, cfT, btT, cf_tok, bt_tok, g1T, dlam, nwl,
     Hkk, Hkv, Hkkb, Hkvb, gw2t) = tens
    HSEG = lambda t: t.rearrange("p (h j) -> p h j", j=64)
    with ExitStack() as st:
        s3 = S(nc, st)
        d = {}
        d["ps_S"] = s3.ps("g0_psS", [128, 1024])
        d["ps_Ap"] = s3.ps("g0_psAp", [128, 1024])
        d["ps_sm"] = s3.ps("g0_pssm", [128, 512])
        for nm, shp, dt in (
                ("q_l", [128, 1024], BF16), ("k_l", [128, 1024], BF16),
                ("Kbe", [128, 1024], BF16), ("Vbe", [128, 1024], BF16),
                ("Hkkb", [128, 1024], BF16), ("Hkvb", [128, 1024], BF16),
                ("Dt", [128, 2048], BF16), ("dbl", [128, 1024], BF16),
                ("Wt", [128, 2048], BF16), ("x", [128, 1024], F32),
                ("r", [128, 1024], F32), ("pp", [128, 1024], F32),
                ("pbf", [128, 1024], BF16), ("dpbf", [128, 1024], BF16),
                ("Ap", [128, 1024], F32), ("tmp", [128, 1024], F32),
                ("tsq", [128, 1024], F32), ("rs0", [16, 128], F32),
                ("rs1", [16, 128], F32), ("alf", [16, 128], F32),
                ("gam", [16, 128], F32), ("sca", [16, 512], F32),
                ("wex", [128, 16], F32), ("cfr", [1, 32], F32),
                ("nst", [16, 128], F32), ("ncb", [128, 1], F32),
                ("cfc", [16, 129], F32), ("cfl", [1, 512], F32),
                ("osb", [128, 1024], F32), ("xac", [128, 128], F32)):
            d[nm] = s3.sb(f"g0_{nm}", shp, dt)

        _RMAP = {"tmpD": "tmp", "tmpP": "tmp", "tmpX": "tmp", "tmpG": "tmp",
                 "tmpgat": "tmp", "tmpL": "tmp", "tsqR": "tsq", "tsqS": "tsq",
                 "tsqO": "tsq", "scaA": "sca", "scaB": "sca", "scaO": "sca",
                 "Wt0": "Wt", "Wt1": "Wt", "Ap0": "Ap", "ApL": "Ap",
                 "os0": "osb", "ognB": "pbf", "cfc0": "cfc"}

        def R(dd, nm):
            return f"g0_{_RMAP.get(nm, nm)}"

        def ap_out(h):
            pc, hl = h // 2, h % 2
            c0 = hl * 512 + (pc % 4) * 128
            return (d["ps_Ap"][64 * hl:64 * hl + 64, c0:c0 + 128],
                    f"g0_psAp#{hl}")

        def emit_prep(ch):
            tsl = slice(ch * 128, (ch + 1) * 128)
            P = f"P0_{ch}"
            b5 = p.open_batch("sync")
            p.dma("sync", d["q_l"][:].rearrange("p (pc t) -> p pc t", t=128),
                  QD["ql"][:, tsl].rearrange("(pc p) t -> p pc t", p=128),
                  tag=f"{P}_lq", batch=b5)
            p.dma("sync", d["k_l"][:].rearrange("p (pc t) -> p pc t", t=128),
                  QD["kl"][:, tsl].rearrange("(pc p) t -> p pc t", p=128),
                  tag=f"{P}_lk", batch=b5)
            p.dma("sync", d["Kbe"][:], QD["kt"][tsl, :], tag=f"{P}_lKt", batch=b5)
            p.dma("sync", d["Vbe"][:], QD["vt"][tsl, :], tag=f"{P}_lVt", batch=b5)
            p.dma("sync", d["Hkkb"][:], hkk_dr[ch], tag=f"{P}_lHk", batch=b5)
            p.dma("sync", d["Hkvb"][:], hkv_dr[ch], tag=f"{P}_lHv", batch=b5)
            p.emit("vector", lambda e, ch=ch: e.tensor_tensor(
                HSEG(d["Kbe"][:]), HSEG(d["Kbe"][:]),
                bt_tok[:, ch * 16:(ch + 1) * 16].unsqueeze(2)
                .broadcast_to([128, 16, 64]), ALU.mult),
                ins=[d["Kbe"][:], f"d_bttok@{ch}"], outs=[d["Kbe"][:]],
                tag=f"{P}_Kb")
            p.emit("vector", lambda e, ch=ch: e.tensor_tensor(
                HSEG(d["Vbe"][:]), HSEG(d["Vbe"][:]),
                bt_tok[:, ch * 16:(ch + 1) * 16].unsqueeze(2)
                .broadcast_to([128, 16, 64]), ALU.mult),
                ins=[d["Vbe"][:], f"d_bttok@{ch}"], outs=[d["Vbe"][:]],
                tag=f"{P}_Vb")
            if ch > 0:
                p.dma("sync", d["cfc"][:], cf2_dr[:, ch * 128 - 1: ch * 128 + 128],
                      tag=f"{P}_lcfc")
            else:
                p.emit("pool", lambda e: e.memset(d["cfc"][:, 0:1], 0.0),
                       outs=[R(d, "cfc0")], tag=f"{P}_zc")
                p.dma("sync", d["cfc"][:, 1:129], cf2_dr[:, 0:128],
                      ins_extra=[R(d, "cfc0")], tag=f"{P}_lcfc")
            p.emit("vector", lambda e: e.tensor_scalar(
                d["nst"][:], d["cfc"][:, 1:129], -1.0, None, ALU.mult),
                ins=[d["cfc"][:]], outs=[d["nst"][:]], tag=f"{P}_nst")
            for q4 in range(4):
                qsl = slice((q4 % 2) * 512, (q4 % 2) * 512 + 512)
                dsl = slice(q4 * 512, (q4 + 1) * 512)
                p.dma("sync", d["cfl"][:].rearrange("p (x b) -> p x b", b=128),
                      cf2_dr[4 * q4:4 * q4 + 4,
                             ch * 128:(ch + 1) * 128].unsqueeze(0),
                      tag=f"{P}_lcfl{q4}")
                p.emit("tensor", lambda e, qsl=qsl: e.matmul(
                    d["ps_S"][:, qsl], onesr[0:1, :], d["cfl"][:],
                    start=True, stop=False),
                    ins=[onesr[:], d["cfl"][:]], outs=[f"g0_psS#{qsl.start}"],
                    tag=f"{P}_A{q4}")
                p.emit("tensor", lambda e, qsl=qsl, dsl=dsl: e.matmul(
                    d["ps_S"][:, qsl], d["nst"][:], heye[:][:, dsl],
                    start=False, stop=True),
                    ins=[d["nst"][:], heye[:]], outs=[f"g0_psS#{qsl.start}"],
                    tag=f"{P}_B{q4}")
                p.emit("vector", lambda e, qsl=qsl, dsl=dsl:
                       e.scalar_tensor_tensor(
                    d["tmp"][:, 0:512], d["ps_S"][:, qsl], 0.0, mle[:][:, dsl],
                    ALU.min, ALU.add),
                    ins=[mle[:], f"g0_psS#{qsl.start}"],
                    outs=[R(d, "tmpD")], tag=f"{P}_Dm{q4}")
                p.emit("scalar", lambda e, dsl=dsl: e.activation(
                    d["Dt"][:, dsl], d["tmp"][:, 0:512], AF.Exp),
                    ins=[R(d, "tmpD")], outs=[d["Dt"][:]], tag=f"{P}_De{q4}")
            for pc in range(8):
                p.emit("tensor", lambda e, pc=pc: e.matmul(
                    d["ps_sm"][:, 128:256], hb[:][:, pc * 128:(pc + 1) * 128],
                    d["cfc"][:, 1:129], start=True, stop=True),
                    ins=[hb[:], d["cfc"][:]], outs=["g0_pssm"], tag=f"{P}_db{pc}")
                p.emit("tensor", lambda e, pc=pc: e.matmul(
                    d["ps_sm"][:, 384:385], hb[:][:, pc * 128:(pc + 1) * 128],
                    d["cfc"][:, 0:1], start=True, stop=True),
                    ins=[hb[:], d["cfc"][:]], outs=["g0_pssm"], tag=f"{P}_ns{pc}")
                p.emit("vector", lambda e: e.tensor_scalar(
                    d["ncb"][:], d["ps_sm"][:, 384:385], -1.0, None, ALU.mult),
                    ins=["g0_pssm"], outs=[d["ncb"][:]], tag=f"{P}_nn{pc}")
                p.emit("scalar", lambda e, pc=pc: e.activation(
                    d["dbl"][:, pc * 128:(pc + 1) * 128], d["ps_sm"][:, 128:256],
                    AF.Exp, bias=d["ncb"][:]),
                    ins=["g0_pssm", d["ncb"][:]], outs=[d["dbl"][:]],
                    tag=f"{P}_dbe{pc}")
            p.emit("pool", lambda e: e.memset(d["x"][:], 0.0), outs=[d["x"][:]],
                   tag=f"{P}_x0")
            p.emit("vector", lambda e: e.tensor_copy(d["r"][:], d["q_l"][:]),
                   ins=[d["q_l"][:]], outs=[d["r"][:]], tag=f"{P}_r0")
            p.emit("vector", lambda e: e.tensor_copy(d["pp"][:], d["q_l"][:]),
                   ins=[d["q_l"][:]], outs=[d["pp"][:]], tag=f"{P}_p0")
            p.emit("scalar", lambda e: e.activation(d["tsq"][:], d["r"][:],
                                                    AF.Square),
                   ins=[d["r"][:]], outs=[R(d, "tsqS")], tag=f"{P}_sq0")
            for pc in range(8):
                p.emit("tensor", lambda e, pc=pc: e.matmul(
                    d["ps_sm"][0:16, 0:128], hsg[:][:, pc * 16:(pc + 1) * 16],
                    d["tsq"][:, pc * 128:(pc + 1) * 128],
                    start=(pc == 0), stop=(pc == 7)),
                    ins=[hsg[:], R(d, "tsqS")], outs=["g0_pssm"], tag=f"{P}_r0m{pc}")
            p.emit("vector", lambda e: e.tensor_copy(d["rs0"][:],
                                                     d["ps_sm"][0:16, 0:128]),
                   ins=["g0_pssm"], outs=[d["rs0"][:]], tag=f"{P}_rs0")

        def emit_SW(E, src_bf, src_rg):
            for half in range(2):
                for h in range(half * 8, half * 8 + 8):
                    pc, hl = h // 2, h % 2
                    scol = hl * 512 + ((h - half * 8) // 2) * 128
                    p.emit("tensor", lambda e, pc=pc, hl=hl, scol=scol,
                           src_bf=src_bf: e.matmul(
                        d["ps_S"][:, scol:scol + 128],
                        d["k_l"][64 * hl:64 * hl + 64, pc * 128:(pc + 1) * 128],
                        src_bf[64 * hl:64 * hl + 64, pc * 128:(pc + 1) * 128],
                        start=True, stop=True, tile_position=(64 * hl, 0)),
                        ins=[d["k_l"][:], src_rg],
                        outs=[f"g0_psS#{(scol // 512) * 512}"],
                        tag=f"{E}_S{h}")
                for b in range(2):
                    base_h = half * 8 + b
                    p.emit("vector", lambda e, half=half, b=b, base_h=base_h:
                           e.tensor_mul(
                        d["Wt"][:].rearrange("p (h t) -> p h t", t=128)
                        [:, base_h:half * 8 + 8:2, :],
                        d["ps_S"][:, b * 512:(b + 1) * 512]
                        .rearrange("p (h t) -> p h t", t=128),
                        d["Dt"][:].rearrange("p (h t) -> p h t", t=128)
                        [:, base_h:half * 8 + 8:2, :]),
                        ins=[d["Dt"][:], f"g0_psS#{b * 512}"],
                        outs=[R(d, f"Wt{half}")], tag=f"{E}_W{half}_{b}")

        def emit_ap(E, lhs2, rhs2_bf, hm_t, tagp):
            for half in range(2):
                for hh in range(half * 8, half * 8 + 8, 2):
                    for h in (hh, hh + 1):
                        o_ap, o_rg = ap_out(h)
                        p.emit("tensor", lambda e, h=h, o_ap=o_ap, lhs2=lhs2:
                               e.matmul(
                            o_ap, lhs2[:, h * 64:(h + 1) * 64],
                            d["Wt"][:, h * 128:(h + 1) * 128],
                            start=True, stop=False,
                            tile_position=(0, 64 * (h % 2))),
                            ins=[lhs2[:], R(d, f"Wt{half}")], outs=[o_rg],
                            tag=f"{E}_{tagp}I{h}")
                    for h in (hh, hh + 1):
                        pc, hl = h // 2, h % 2
                        o_ap, o_rg = ap_out(h)
                        p.emit("tensor", lambda e, h=h, hl=hl, pc=pc, o_ap=o_ap,
                               hm_t=hm_t, rhs2_bf=rhs2_bf: e.matmul(
                            o_ap, hm_t[64 * hl:64 * hl + 64, h * 64:(h + 1) * 64],
                            rhs2_bf[64 * hl:64 * hl + 64, pc * 128:(pc + 1) * 128],
                            start=False, stop=True,
                            tile_position=(64 * hl, 64 * hl)),
                            ins=[hm_t[:], d["dpbf"][:]], outs=[o_rg],
                            tag=f"{E}_{tagp}H{h}")
                hof = half * 512
                p.emit("vector", lambda e, hof=hof: e.tensor_copy(
                    d["Ap"][0:64, hof:hof + 512], d["ps_Ap"][0:64, 0:512]),
                    ins=["g0_psAp#0"], outs=[R(d, "Ap0")], tag=f"{E}_{tagp}e0{half}")
                p.emit("vector", lambda e, hof=hof: e.tensor_copy(
                    d["Ap"][64:128, hof:hof + 512], d["ps_Ap"][64:128, 512:1024]),
                    ins=["g0_psAp#1"], outs=[R(d, "Ap0")], tag=f"{E}_{tagp}e1{half}")

        def emit_iter(ch, it):
            E = f"I0_{ch}_{it}"
            rs_old = d["rs0"] if it % 2 == 0 else d["rs1"]
            rs_new = d["rs1"] if it % 2 == 0 else d["rs0"]
            p.emit("vector", lambda e: e.tensor_copy(d["pbf"][:], d["pp"][:]),
                   ins=[d["pp"][:]], outs=[d["pbf"][:]], tag=f"{E}_pb")
            p.emit("vector", lambda e: e.tensor_mul(d["dpbf"][:], d["pbf"][:],
                                                    d["dbl"][:]),
                   ins=[d["pbf"][:], d["dbl"][:]], outs=[d["dpbf"][:]],
                   tag=f"{E}_dp")
            emit_SW(E, d["pbf"], d["pbf"][:])
            emit_ap(E, d["Kbe"], d["dpbf"], d["Hkkb"], "c")
            p.emit("vector", lambda e: e.tensor_mul(d["tmp"][:], d["pp"][:],
                                                    lamb_b[:]),
                   ins=[d["pp"][:]] + [f"lambb@{pc}" for pc in range(8)],
                   outs=[R(d, "tmpL")], tag=f"{E}_lm")
            p.emit("pool", lambda e: e.tensor_add(d["Ap"][:], d["Ap"][:],
                                                  d["tmp"][:]),
                   ins=[R(d, "Ap0"), R(d, "tmpL")], outs=[R(d, "ApL")],
                   tag=f"{E}_la")
            p.emit("vector", lambda e: e.tensor_mul(d["tmp"][:], d["pp"][:],
                                                    d["Ap"][:]),
                   ins=[d["pp"][:], R(d, "ApL")], outs=[R(d, "tmpP")],
                   tag=f"{E}_pa")
            for pc in range(8):
                p.emit("tensor", lambda e, pc=pc: e.matmul(
                    d["ps_sm"][0:16, 0:128], hsg[:][:, pc * 16:(pc + 1) * 16],
                    d["tmp"][:, pc * 128:(pc + 1) * 128],
                    start=(pc == 0), stop=(pc == 7)),
                    ins=[hsg[:], R(d, "tmpP")], outs=["g0_pssm"], tag=f"{E}_pm{pc}")
            p.emit("vector", lambda e: e.tensor_scalar(
                d["sca"][:, 0:128], d["ps_sm"][0:16, 0:128], 1e-12, None, ALU.add),
                ins=["g0_pssm"], outs=[R(d, "scaA")], tag=f"{E}_ae")
            p.emit("vector", lambda e, rs_old=rs_old: e.tensor_scalar(
                d["sca"][:, 128:256], rs_old[:], 1e-12, None, ALU.add),
                ins=[rs_old[:]], outs=[R(d, "scaB")], tag=f"{E}_ge")
            p.emit("vector", lambda e: e.reciprocal(d["sca"][:, 0:256],
                                                    d["sca"][:, 0:256]),
                   ins=[R(d, "scaA"), R(d, "scaB")],
                   outs=[R(d, "scaA"), R(d, "scaB")], tag=f"{E}_rc")
            p.emit("vector", lambda e, rs_old=rs_old: e.tensor_mul(
                d["alf"][:], rs_old[:], d["sca"][:, 0:128]),
                ins=[rs_old[:], R(d, "scaA")], outs=[d["alf"][:]], tag=f"{E}_am")
            for pc in range(8):
                p.emit("tensor", lambda e, pc=pc: e.matmul(
                    d["ps_S"][:, pc * 128:pc * 128 + 128],
                    hb[:][:, pc * 128:(pc + 1) * 128], d["alf"][:],
                    start=True, stop=True),
                    ins=[hb[:], d["alf"][:]],
                    outs=[f"g0_psS#{(pc * 128 // 512) * 512}"], tag=f"{E}_ab{pc}")
            p.emit("vector", lambda e: e.tensor_mul(d["tmp"][:], d["pp"][:],
                                                    d["ps_S"][:, 0:1024]),
                   ins=[d["pp"][:], "g0_psS#0", "g0_psS#512"],
                   outs=[R(d, "tmpX")], tag=f"{E}_xm")
            p.emit("vector", lambda e: e.tensor_mul(d["tsq"][:], d["Ap"][:],
                                                    d["ps_S"][:, 0:1024]),
                   ins=[R(d, "ApL"), "g0_psS#0", "g0_psS#512"],
                   outs=[R(d, "tsqR")], tag=f"{E}_rm")
            p.emit("pool", lambda e: e.tensor_add(d["x"][:], d["x"][:],
                                                  d["tmp"][:]),
                   ins=[d["x"][:], R(d, "tmpX")], outs=[d["x"][:]], tag=f"{E}_xa")
            p.emit("pool", lambda e: e.tensor_sub(d["r"][:], d["r"][:],
                                                  d["tsq"][:]),
                   ins=[d["r"][:], R(d, "tsqR")], outs=[d["r"][:]], tag=f"{E}_ra")
            p.emit("scalar", lambda e: e.activation(d["tsq"][:], d["r"][:],
                                                    AF.Square),
                   ins=[d["r"][:]], outs=[R(d, "tsqS")], tag=f"{E}_rq")
            for pc in range(8):
                p.emit("tensor", lambda e, pc=pc: e.matmul(
                    d["ps_sm"][0:16, 128:256], hsg[:][:, pc * 16:(pc + 1) * 16],
                    d["tsq"][:, pc * 128:(pc + 1) * 128],
                    start=(pc == 0), stop=(pc == 7)),
                    ins=[hsg[:], R(d, "tsqS")], outs=["g0_pssm"], tag=f"{E}_rn{pc}")
            p.emit("vector", lambda e, rs_new=rs_new: e.tensor_copy(
                rs_new[:], d["ps_sm"][0:16, 128:256]),
                ins=["g0_pssm"], outs=[rs_new[:]], tag=f"{E}_rne")
            p.emit("vector", lambda e, rs_new=rs_new: e.tensor_mul(
                d["gam"][:], rs_new[:], d["sca"][:, 128:256]),
                ins=[rs_new[:], R(d, "scaB")], outs=[d["gam"][:]], tag=f"{E}_gm")
            for pc in range(8):
                p.emit("tensor", lambda e, pc=pc: e.matmul(
                    d["ps_S"][:, pc * 128:pc * 128 + 128],
                    hb[:][:, pc * 128:(pc + 1) * 128], d["gam"][:],
                    start=True, stop=True),
                    ins=[hb[:], d["gam"][:]],
                    outs=[f"g0_psS#{(pc * 128 // 512) * 512}"], tag=f"{E}_gb{pc}")
            p.emit("vector", lambda e: e.tensor_mul(d["tmp"][:], d["pp"][:],
                                                    d["ps_S"][:, 0:1024]),
                   ins=[d["pp"][:], "g0_psS#0", "g0_psS#512"],
                   outs=[R(d, "tmpG")], tag=f"{E}_g2")
            p.emit("pool", lambda e: e.tensor_add(d["pp"][:], d["r"][:],
                                                  d["tmp"][:]),
                   ins=[d["r"][:], R(d, "tmpG")], outs=[d["pp"][:]], tag=f"{E}_p2")

        def emit_out(ch):
            EO = f"O0_{ch}"
            tsl = slice(ch * 128, (ch + 1) * 128)
            j512 = ch // 4
            p.emit("vector", lambda e: e.tensor_copy(d["pbf"][:], d["x"][:]),
                   ins=[d["x"][:]], outs=[R(d, "ognB")], tag=f"{EO}_qs")
            p.emit("vector", lambda e: e.tensor_mul(d["dpbf"][:], d["pbf"][:],
                                                    d["dbl"][:]),
                   ins=[R(d, "ognB"), d["dbl"][:]], outs=[d["dpbf"][:]],
                   tag=f"{EO}_dq")
            emit_SW(EO, d["pbf"], R(d, "ognB"))
            emit_ap(EO, d["Vbe"], d["dpbf"], d["Hkvb"], "o")
            p.emit("vector", lambda e: e.tensor_copy(d["osb"][:], d["Ap"][:]),
                   ins=[R(d, "Ap0")], outs=[R(d, "os0")], tag=f"{EO}_oc")
            for pc in range(8):
                p.emit("tensor", lambda e, pc=pc, tsl=tsl: e.matmul(
                    d["ps_sm"][:, 0:128], gw2t[:, pc * 128:(pc + 1) * 128],
                    g1T[:, tsl], start=True, stop=True),
                    ins=[gw2t[:], f"d_g1T@{j512}"], outs=["g0_pssm"],
                    tag=f"{EO}_g{pc}")
                p.emit("scalar", lambda e, pc=pc: e.activation(
                    d["tmp"][:, pc * 128:(pc + 1) * 128], d["ps_sm"][:, 0:128],
                    AF.Sigmoid),
                    ins=["g0_pssm"], outs=[R(d, "tmpgat")], tag=f"{EO}_ge{pc}")
            p.emit("vector", lambda e: e.tensor_mul(d["osb"][:], d["osb"][:],
                                                    d["tmp"][:]),
                   ins=[R(d, "os0"), R(d, "tmpgat")], outs=[R(d, "os0")],
                   tag=f"{EO}_og")
            p.emit("scalar", lambda e: e.activation(d["tsq"][:], d["osb"][:],
                                                    AF.Square),
                   ins=[R(d, "os0")], outs=[R(d, "tsqO")], tag=f"{EO}_q")
            for hf in range(2):
                hsl = slice(hf * 512, (hf + 1) * 512)
                p.emit("tensor", lambda e, hsl=hsl: e.matmul(
                    d["ps_sm"][0:2, 0:512], C["c_hs2"][:], d["tsq"][:, hsl],
                    start=True, stop=True),
                    ins=[C["c_hs2"][:], R(d, "tsqO")], outs=["g0_pssm"],
                    tag=f"{EO}_s{hf}")
                p.emit("scalar", lambda e: e.activation(
                    d["sca"][0:2, 0:512], d["ps_sm"][0:2, 0:512], AF.Sqrt,
                    scale=1.0 / 64.0, bias=1e-5),
                    ins=["g0_pssm"], outs=[R(d, "scaO")], tag=f"{EO}_r{hf}")
                p.emit("vector", lambda e: e.reciprocal(d["sca"][0:2, 0:512],
                                                        d["sca"][0:2, 0:512]),
                       ins=[R(d, "scaO")], outs=[R(d, "scaO")], tag=f"{EO}_i{hf}")
                p.emit("tensor", lambda e: e.matmul(
                    d["ps_sm"][:, 0:512], C["c_hb2"][:], d["sca"][0:2, 0:512],
                    start=True, stop=True),
                    ins=[C["c_hb2"][:], R(d, "scaO")], outs=["g0_pssm"],
                    tag=f"{EO}_b{hf}")
                p.emit("vector", lambda e, hsl=hsl: e.tensor_mul(
                    d["osb"][:, hsl], d["osb"][:, hsl], d["ps_sm"][:, 0:512]),
                    ins=[R(d, "os0"), "g0_pssm"], outs=[R(d, "os0")],
                    tag=f"{EO}_n{hf}")
            for pc in range(8):
                p.emit("vector", lambda e, pc=pc: e.tensor_scalar(
                    d["pbf"][:, pc * 128:(pc + 1) * 128],
                    d["osb"][:, pc * 128:(pc + 1) * 128], nwl[:, pc:pc + 1],
                    None, ALU.mult),
                    ins=[R(d, "os0"), nwl[:]], outs=[R(d, "ognB")],
                    tag=f"{EO}_w{pc}")
            for dc in range(NDC):
                for pc in range(8):
                    p.emit("tensor", lambda e, pc=pc, dc=dc: e.matmul(
                        d["ps_sm"][:, 128:256],
                        wo_b[pc][:, dc * 128:(dc + 1) * 128],
                        d["pbf"][:, pc * 128:(pc + 1) * 128],
                        start=(pc == 0), stop=(pc == 7)),
                        ins=[wo_b[pc][:], R(d, "ognB")], outs=["g0_pssm"],
                        tag=f"{EO}_y{dc}_{pc}")
                p.dma("sync", d["xac"][:], x_dram[dc * 128:(dc + 1) * 128, tsl],
                      tag=f"{EO}_lx{dc}")
                p.emit("vector", lambda e: e.tensor_add(d["xac"][:], d["xac"][:],
                                                        d["ps_sm"][:, 128:256]),
                       ins=[d["xac"][:], "g0_pssm"], outs=[d["xac"][:]],
                       tag=f"{EO}_ya{dc}")
                p.dma("pool", xout_d[dc * 128:(dc + 1) * 128, tsl], d["xac"][:],
                      tag=f"{EO}_st{dc}")

        for ch in range(NCH):
            emit_prep(ch)
            for it in range(CG_ITERS):
                emit_iter(ch, it)
            emit_out(ch)
    p.fence()


NSLOT = 4


def l1_attn_pass2v2(p, s, C, nc, QD, tens, x_dram, xout_d, cft_dr, cf2_dr,
                    hkk_dr, hkv_dr):
    from contextlib import ExitStack
    ones = C["c_ones"]; onesr = C["c_onesr"]
    hb = C["c_hb"]; mle = C["c_masklehb"]; heye = C["c_heye"]
    hbbf = C["c_hbbf"]; hsgbf = C["c_hsgbf"]; eye32 = C["c_eye32"]
    (wq_b, wk_b, wv_b, wo_b, cfT, btT, cf_tok, bt_tok, g1T, dlam, nwl,
     _hkk, _hkv, _hkkb, _hkvb, gw2t) = tens
    HSEG = lambda t: t.rearrange("p (h j) -> p h j", j=64)
    with ExitStack() as st:
        s3 = S(nc, st)
        SL = []
        for sl in range(NSLOT):
            d = {}
            for nm, shp, dt in (
                    ("kl", [128, 1024], BF16), ("Kbe", [128, 1024], BF16),
                    ("Hkkb", [128, 1024], BF16), ("Dt", [128, 2048], BF16),
                    ("dbl", [128, 1024], BF16), ("x", [128, 1024], F32),
                    ("r", [128, 1024], BF16), ("pp", [128, 1024], BF16),
                    ("Apb", [128, 1024], BF16), ("rsb", [128, 32], F32),
                    ("sca", [128, 32], F32), ("alf", [128, 16], F32),
                    ("gam", [128, 16], F32), ("aTb", [16, 128], BF16),
                    ("gTb", [16, 128], BF16), ("cfc", [16, 129], F32),
                    ("nst", [16, 128], F32), ("cfl", [1, 512], F32),
                    ("ncb", [128, 1], F32)):
                d[nm] = s3.sb(f"w{sl}_{nm}", shp, dt)
            SL.append(d)
        PA = []
        for pr in range(2):
            e = {}
            for nm, shp, dt in (
                    ("dpbf", [128, 1024], BF16), ("Wt", [128, 2048], BF16),
                    ("tpa", [128, 1024], BF16), ("ab", [128, 1024], BF16),
                    ("gb", [128, 1024], BF16), ("tmul", [128, 1024], BF16),
                    ("tq", [128, 1024], BF16), ("txf", [128, 1024], F32),
                    ("tf32", [128, 512], F32)):
                e[nm] = s3.sb(f"pr{pr}_{nm}", shp, dt)
            e["psS"] = s3.ps(f"pr{pr}_psS", [128, 1024])
            e["psA"] = s3.ps(f"pr{pr}_psA", [128, 1024])
            PA.append(e)
        xac = s3.sb("o_xac", [128, 128], F32)
        xac2 = s3.sb("o_xac2", [128, 128], F32)

        def RS(pr, bank):
            return f"v2S{pr}#{bank}"

        def RA(pr, bank):
            return f"v2A{pr}#{bank}"

        def emit_S(E, sl, pr, src):
            d, e = SL[sl], PA[pr]
            for half in range(2):
                for h in range(half * 8, half * 8 + 8):
                    pc, hl = h // 2, h % 2
                    scol = hl * 512 + ((h - half * 8) // 2) * 128
                    p.emit("tensor", lambda ng, pc=pc, hl=hl, scol=scol,
                           src=src, d=d, e=e: ng.matmul(
                        e["psS"][:, scol:scol + 128],
                        d["kl"][64 * hl:64 * hl + 64, pc * 128:(pc + 1) * 128],
                        src[64 * hl:64 * hl + 64, pc * 128:(pc + 1) * 128],
                        start=True, stop=True, tile_position=(64 * hl, 0)),
                        ins=[d["kl"][:], src], outs=[RS(pr, (scol // 512) * 512)],
                        tag=f"{E}_S{h}")
                p.emit("vector", lambda ng, half=half, d=d, e=e: ng.tensor_mul(
                    e["Wt"][:, half * 1024:(half + 1) * 1024]
                    .rearrange("p (hp b t) -> p b hp t", b=2, t=128),
                    e["psS"][:].rearrange("p (b h t) -> p b h t", b=2, t=128),
                    d["Dt"][:, half * 1024:(half + 1) * 1024]
                    .rearrange("p (hp b t) -> p b hp t", b=2, t=128)),
                    ins=[d["Dt"][:], RS(pr, 0), RS(pr, 512)],
                    outs=[e["Wt"][:]], tag=f"{E}_W{half}")

        def emit_IH(E, sl, pr, lhs2, hm_t, use_lam, dst, dst_f32):
            d, e = SL[sl], PA[pr]
            for half in range(2):
                for hh in range(half * 8, half * 8 + 8, 2):
                    for h in (hh, hh + 1):
                        pc, hl = h // 2, h % 2
                        c0 = hl * 512 + (pc % 4) * 128
                        p.emit("tensor", lambda ng, h=h, hl=hl, c0=c0,
                               lhs2=lhs2, e=e: ng.matmul(
                            e["psA"][64 * hl:64 * hl + 64, c0:c0 + 128],
                            lhs2[:, h * 64:(h + 1) * 64],
                            e["Wt"][:, h * 128:(h + 1) * 128],
                            start=True, stop=False,
                            tile_position=(0, 64 * hl)),
                            ins=[lhs2[:], e["Wt"][:]], outs=[RA(pr, hl)],
                            tag=f"{E}_I{h}")
                    for h in (hh, hh + 1):
                        pc, hl = h // 2, h % 2
                        c0 = hl * 512 + (pc % 4) * 128
                        p.emit("tensor", lambda ng, h=h, hl=hl, pc=pc, c0=c0,
                               hm_t=hm_t, e=e: ng.matmul(
                            e["psA"][64 * hl:64 * hl + 64, c0:c0 + 128],
                            hm_t[64 * hl:64 * hl + 64, h * 64:(h + 1) * 64],
                            e["dpbf"][64 * hl:64 * hl + 64,
                                      pc * 128:(pc + 1) * 128],
                            start=False, stop=not use_lam,
                            tile_position=(64 * hl, 64 * hl)),
                            ins=[hm_t[:], e["dpbf"][:]], outs=[RA(pr, hl)],
                            tag=f"{E}_H{h}")
                    if use_lam:
                        for h in (hh, hh + 1):
                            pc, hl = h // 2, h % 2
                            c0 = hl * 512 + (pc % 4) * 128
                            p.emit("tensor", lambda ng, h=h, hl=hl, pc=pc,
                                   c0=c0, d=d, e=e: ng.matmul(
                                e["psA"][64 * hl:64 * hl + 64, c0:c0 + 128],
                                dlam[64 * hl:64 * hl + 64, h * 64:(h + 1) * 64],
                                d["pp"][64 * hl:64 * hl + 64,
                                        pc * 128:(pc + 1) * 128],
                                start=False, stop=True,
                                tile_position=(64 * hl, 64 * hl)),
                                ins=[dlam[:], d["pp"][:]], outs=[RA(pr, hl)],
                                tag=f"{E}_L{h}")
                hof = half * 512
                p.emit("scalar", lambda ng, hof=hof, e=e, dst=dst: ng.copy(
                    dst[0:64, hof:hof + 512], e["psA"][0:64, 0:512]),
                    ins=[RA(pr, 0)], outs=[dst[:]], tag=f"{E}_ae0{half}")
                p.emit("scalar", lambda ng, hof=hof, e=e, dst=dst: ng.copy(
                    dst[64:128, hof:hof + 512], e["psA"][64:128, 512:1024]),
                    ins=[RA(pr, 1)], outs=[dst[:]], tag=f"{E}_ae1{half}")

        def emit_prep(ch, sl, pr):
            d, e = SL[sl], PA[pr]
            tsl = slice(ch * 128, (ch + 1) * 128)
            P = f"V2P{ch}"
            b5 = p.open_batch("sync")
            p.dma("sync", d["r"][:].rearrange("p (pc t) -> p pc t", t=128),
                  QD["ql"][:, tsl].rearrange("(pc p) t -> p pc t", p=128),
                  tag=f"{P}_lq", batch=b5)
            p.dma("sync", d["kl"][:].rearrange("p (pc t) -> p pc t", t=128),
                  QD["kl"][:, tsl].rearrange("(pc p) t -> p pc t", p=128),
                  tag=f"{P}_lk", batch=b5)
            p.dma("sync", d["Kbe"][:], QD["kt"][tsl, :], tag=f"{P}_lKt", batch=b5)
            p.dma("sync", d["Hkkb"][:], hkk_dr[ch], tag=f"{P}_lHk", batch=b5)
            p.emit("vector", lambda ng, ch=ch, d=d: ng.tensor_tensor(
                HSEG(d["Kbe"][:]), HSEG(d["Kbe"][:]),
                bt_tok[:, ch * 16:(ch + 1) * 16].unsqueeze(2)
                .broadcast_to([128, 16, 64]), ALU.mult),
                ins=[d["Kbe"][:], f"d_bttok@{ch}"], outs=[d["Kbe"][:]],
                tag=f"{P}_Kb")
            if ch > 0:
                p.dma("sync", d["cfc"][:], cf2_dr[:, ch * 128 - 1: ch * 128 + 128],
                      tag=f"{P}_lcfc")
            else:
                p.emit("pool", lambda ng, d=d: ng.memset(d["cfc"][:, 0:1], 0.0),
                       outs=[d["cfc"][:]], tag=f"{P}_zc")
                p.dma("sync", d["cfc"][:, 1:129], cf2_dr[:, 0:128],
                      ins_extra=[d["cfc"][:]], tag=f"{P}_lcfc")
            p.emit("vector", lambda ng, d=d: ng.tensor_scalar(
                d["nst"][:], d["cfc"][:, 1:129], -1.0, None, ALU.mult),
                ins=[d["cfc"][:]], outs=[d["nst"][:]], tag=f"{P}_nst")
            for q4 in range(4):
                qs = (q4 % 2) * 512
                dsl = slice(q4 * 512, (q4 + 1) * 512)
                p.dma("sync", d["cfl"][:].rearrange("p (x b) -> p x b", b=128),
                      cf2_dr[4 * q4:4 * q4 + 4,
                             ch * 128:(ch + 1) * 128].unsqueeze(0),
                      tag=f"{P}_lcfl{q4}")
                p.emit("tensor", lambda ng, qs=qs, d=d, e=e: ng.matmul(
                    e["psS"][:, qs:qs + 512], onesr[0:1, :], d["cfl"][:],
                    start=True, stop=False),
                    ins=[onesr[:], d["cfl"][:]], outs=[RS(pr, qs)],
                    tag=f"{P}_A{q4}")
                p.emit("tensor", lambda ng, qs=qs, dsl=dsl, d=d, e=e: ng.matmul(
                    e["psS"][:, qs:qs + 512], d["nst"][:], heye[:][:, dsl],
                    start=False, stop=True),
                    ins=[d["nst"][:], heye[:]], outs=[RS(pr, qs)],
                    tag=f"{P}_B{q4}")
                p.emit("vector", lambda ng, qs=qs, dsl=dsl, e=e:
                       ng.scalar_tensor_tensor(
                    e["tf32"][:, 0:512], e["psS"][:, qs:qs + 512], 0.0,
                    mle[:][:, dsl], ALU.min, ALU.add),
                    ins=[mle[:], RS(pr, qs)], outs=[e["tf32"][:]],
                    tag=f"{P}_Dm{q4}")
                p.emit("scalar", lambda ng, dsl=dsl, d=d, e=e: ng.activation(
                    d["Dt"][:, dsl], e["tf32"][:, 0:512], AF.Exp),
                    ins=[e["tf32"][:]], outs=[d["Dt"][:]], tag=f"{P}_De{q4}")
            for pc in range(8):
                p.emit("tensor", lambda ng, pc=pc, d=d, e=e: ng.matmul(
                    e["psA"][:, 128:256], hb[:][:, pc * 128:(pc + 1) * 128],
                    d["cfc"][:, 1:129], start=True, stop=True),
                    ins=[hb[:], d["cfc"][:]], outs=[RA(pr, 0)], tag=f"{P}_db{pc}")
                p.emit("tensor", lambda ng, pc=pc, d=d, e=e: ng.matmul(
                    e["psA"][:, 384:385], hb[:][:, pc * 128:(pc + 1) * 128],
                    d["cfc"][:, 0:1], start=True, stop=True),
                    ins=[hb[:], d["cfc"][:]], outs=[RA(pr, 0)], tag=f"{P}_ns{pc}")
                p.emit("vector", lambda ng, d=d, e=e: ng.tensor_scalar(
                    d["ncb"][:], e["psA"][:, 384:385], -1.0, None, ALU.mult),
                    ins=[RA(pr, 0)], outs=[d["ncb"][:]], tag=f"{P}_nn{pc}")
                p.emit("scalar", lambda ng, pc=pc, d=d, e=e: ng.activation(
                    d["dbl"][:, pc * 128:(pc + 1) * 128], e["psA"][:, 128:256],
                    AF.Exp, bias=d["ncb"][:]),
                    ins=[RA(pr, 0), d["ncb"][:]], outs=[d["dbl"][:]],
                    tag=f"{P}_dbe{pc}")
            p.emit("pool", lambda ng, d=d: ng.memset(d["x"][:], 0.0),
                   outs=[d["x"][:]], tag=f"{P}_x0")
            p.emit("vector", lambda ng, d=d: ng.tensor_copy(d["pp"][:], d["r"][:]),
                   ins=[d["r"][:]], outs=[d["pp"][:]], tag=f"{P}_p0")
            p.emit("scalar", lambda ng, d=d, e=e: ng.activation(
                e["tq"][:], d["r"][:], AF.Square),
                ins=[d["r"][:]], outs=[e["tq"][:]], tag=f"{P}_sq0")
            for pc in range(8):
                p.emit("tensor", lambda ng, pc=pc, e=e: ng.matmul(
                    e["psA"][:, 128 + pc * 16:128 + (pc + 1) * 16],
                    e["tq"][:, pc * 128:(pc + 1) * 128],
                    hsgbf[:][:, pc * 16:(pc + 1) * 16],
                    start=True, stop=True),
                    ins=[e["tq"][:], hsgbf[:]], outs=[RA(pr, 0)],
                    tag=f"{P}_r0m{pc}")
            p.emit("vector", lambda ng, d=d, e=e: ng.tensor_reduce(
                d["rsb"][:, 0:16],
                e["psA"][:, 128:256].rearrange("p (a b) -> p b a", a=8),
                AX.X, ALU.add),
                ins=[RA(pr, 0)], outs=[d["rsb"][:]], tag=f"{P}_rs0")

        def emit_g1(E, sl, pr):
            d, e = SL[sl], PA[pr]
            p.emit("vector", lambda ng, d=d, e=e: ng.tensor_mul(
                e["dpbf"][:], d["pp"][:], d["dbl"][:]),
                ins=[d["pp"][:], d["dbl"][:]], outs=[e["dpbf"][:]],
                tag=f"{E}_dp")
            emit_S(E, sl, pr, SL[sl]["pp"][:])
            emit_IH(E, sl, pr, d["Kbe"][:], d["Hkkb"][:], True, d["Apb"], False)

        def emit_g2(E, sl, pr, it):
            d, e = SL[sl], PA[pr]
            ro = slice(16 * (it % 2), 16 * (it % 2) + 16)
            p.emit("vector", lambda ng, d=d, e=e: ng.tensor_mul(
                e["tpa"][:], d["pp"][:], d["Apb"][:]),
                ins=[d["pp"][:], d["Apb"][:]], outs=[e["tpa"][:]],
                tag=f"{E}_pa")
            for pc in range(8):
                p.emit("tensor", lambda ng, pc=pc, e=e: ng.matmul(
                    e["psA"][:, pc * 16:(pc + 1) * 16],
                    e["tpa"][:, pc * 128:(pc + 1) * 128],
                    hsgbf[:][:, pc * 16:(pc + 1) * 16],
                    start=True, stop=True),
                    ins=[e["tpa"][:], hsgbf[:]], outs=[RA(pr, 0)],
                    tag=f"{E}_pm{pc}")
            p.emit("vector", lambda ng, d=d, e=e: ng.tensor_reduce(
                d["sca"][:, 0:16],
                e["psA"][:, 0:128].rearrange("p (a b) -> p b a", a=8),
                AX.X, ALU.add),
                ins=[RA(pr, 0)], outs=[d["sca"][:]], tag=f"{E}_pr")
            p.emit("vector", lambda ng, d=d: ng.tensor_scalar(
                d["sca"][:, 0:16], d["sca"][:, 0:16], 1e-12, None, ALU.add),
                ins=[d["sca"][:]], outs=[d["sca"][:]], tag=f"{E}_ae")
            p.emit("vector", lambda ng, d=d, ro=ro: ng.tensor_scalar(
                d["sca"][:, 16:32], d["rsb"][:, ro], 1e-12, None, ALU.add),
                ins=[d["rsb"][:]], outs=[d["sca"][:]], tag=f"{E}_ge")
            p.emit("vector", lambda ng, d=d: ng.reciprocal(
                d["sca"][:, 0:32], d["sca"][:, 0:32]),
                ins=[d["sca"][:]], outs=[d["sca"][:]], tag=f"{E}_rc")
            p.emit("vector", lambda ng, d=d, ro=ro: ng.tensor_mul(
                d["alf"][:], d["rsb"][:, ro], d["sca"][:, 0:16]),
                ins=[d["rsb"][:], d["sca"][:]], outs=[d["alf"][:]],
                tag=f"{E}_am")
            p.emit("tensor", lambda ng, d=d, e=e: ng.transpose(
                e["psA"][0:16, 256:384], d["alf"][:], eye32[:]),
                ins=[d["alf"][:], eye32[:]], outs=[RA(pr, 0)], tag=f"{E}_aT")
            p.emit("scalar", lambda ng, d=d, e=e: ng.copy(
                d["aTb"][:], e["psA"][0:16, 256:384]),
                ins=[RA(pr, 0)], outs=[d["aTb"][:]], tag=f"{E}_aTe")

        def emit_g3(E, sl, pr):
            d, e = SL[sl], PA[pr]
            for pc in range(8):
                p.emit("tensor", lambda ng, pc=pc, d=d, e=e: ng.matmul(
                    e["psS"][:, pc * 128:(pc + 1) * 128],
                    hbbf[:][:, pc * 128:(pc + 1) * 128], d["aTb"][:],
                    start=True, stop=True),
                    ins=[hbbf[:], d["aTb"][:]],
                    outs=[RS(pr, (pc * 128 // 512) * 512)], tag=f"{E}_ab{pc}")
            p.emit("scalar", lambda ng, e=e: ng.copy(e["ab"][:], e["psS"][:]),
                   ins=[RS(pr, 0), RS(pr, 512)], outs=[e["ab"][:]],
                   tag=f"{E}_abe")
            p.emit("vector", lambda ng, d=d, e=e: ng.tensor_mul(
                e["txf"][:], e["ab"][:], d["pp"][:]),
                ins=[e["ab"][:], d["pp"][:]], outs=[e["txf"][:]], tag=f"{E}_xm")
            for bk in range(2):
                p.emit("vector", lambda ng, d=d, e=e, bk=bk: ng.tensor_mul(
                    e["tmul"][:, bk * 512:(bk + 1) * 512],
                    d["Apb"][:, bk * 512:(bk + 1) * 512],
                    e["psS"][:, bk * 512:(bk + 1) * 512]),
                    ins=[RS(pr, bk * 512), d["Apb"][:]], outs=[e["tmul"][:]],
                    tag=f"{E}_rm{bk}")
            p.emit("pool", lambda ng, d=d, e=e: ng.tensor_add(
                d["x"][:], d["x"][:], e["txf"][:]),
                ins=[d["x"][:], e["txf"][:]], outs=[d["x"][:]], tag=f"{E}_xa")
            p.emit("pool", lambda ng, d=d, e=e: ng.tensor_sub(
                d["r"][:], d["r"][:], e["tmul"][:]),
                ins=[d["r"][:], e["tmul"][:]], outs=[d["r"][:]], tag=f"{E}_ra")

        def emit_g4(E, sl, pr, it):
            d, e = SL[sl], PA[pr]
            rn = slice(16 * ((it + 1) % 2), 16 * ((it + 1) % 2) + 16)
            p.emit("scalar", lambda ng, d=d, e=e: ng.activation(
                e["tq"][:], d["r"][:], AF.Square),
                ins=[d["r"][:]], outs=[e["tq"][:]], tag=f"{E}_rq")
            for pc in range(8):
                p.emit("tensor", lambda ng, pc=pc, e=e: ng.matmul(
                    e["psA"][:, 128 + pc * 16:128 + (pc + 1) * 16],
                    e["tq"][:, pc * 128:(pc + 1) * 128],
                    hsgbf[:][:, pc * 16:(pc + 1) * 16],
                    start=True, stop=True),
                    ins=[e["tq"][:], hsgbf[:]], outs=[RA(pr, 0)],
                    tag=f"{E}_rn{pc}")
            p.emit("vector", lambda ng, d=d, e=e, rn=rn: ng.tensor_reduce(
                d["rsb"][:, rn],
                e["psA"][:, 128:256].rearrange("p (a b) -> p b a", a=8),
                AX.X, ALU.add),
                ins=[RA(pr, 0)], outs=[d["rsb"][:]], tag=f"{E}_rne")
            p.emit("vector", lambda ng, d=d, rn=rn: ng.tensor_mul(
                d["gam"][:], d["rsb"][:, rn], d["sca"][:, 16:32]),
                ins=[d["rsb"][:], d["sca"][:]], outs=[d["gam"][:]],
                tag=f"{E}_gm")
            p.emit("tensor", lambda ng, d=d, e=e: ng.transpose(
                e["psA"][0:16, 384:512], d["gam"][:], eye32[:]),
                ins=[d["gam"][:], eye32[:]], outs=[RA(pr, 0)], tag=f"{E}_gT")
            p.emit("scalar", lambda ng, d=d, e=e: ng.copy(
                d["gTb"][:], e["psA"][0:16, 384:512]),
                ins=[RA(pr, 0)], outs=[d["gTb"][:]], tag=f"{E}_gTe")

        def emit_g5(E, sl, pr):
            d, e = SL[sl], PA[pr]
            for pc in range(8):
                p.emit("tensor", lambda ng, pc=pc, d=d, e=e: ng.matmul(
                    e["psS"][:, pc * 128:(pc + 1) * 128],
                    hbbf[:][:, pc * 128:(pc + 1) * 128], d["gTb"][:],
                    start=True, stop=True),
                    ins=[hbbf[:], d["gTb"][:]],
                    outs=[RS(pr, (pc * 128 // 512) * 512)], tag=f"{E}_gb{pc}")
            for bk in range(2):
                p.emit("vector", lambda ng, d=d, e=e, bk=bk: ng.tensor_mul(
                    e["tmul"][:, bk * 512:(bk + 1) * 512],
                    d["pp"][:, bk * 512:(bk + 1) * 512],
                    e["psS"][:, bk * 512:(bk + 1) * 512]),
                    ins=[RS(pr, bk * 512), d["pp"][:]], outs=[e["tmul"][:]],
                    tag=f"{E}_p2{bk}")
            p.emit("vector", lambda ng, d=d, e=e: ng.tensor_add(
                d["pp"][:], d["r"][:], e["tmul"][:]),
                ins=[d["r"][:], e["tmul"][:]], outs=[d["pp"][:]],
                tag=f"{E}_pa2")

        def emit_out(ch, sl, pr):
            d, e = SL[sl], PA[pr]
            Vbe, Hvb, osb = d["Kbe"], d["Hkkb"], d["x"]
            rst = e["tf32"][0:2, 0:512]
            EO = f"V2O{ch}"
            tsl = slice(ch * 128, (ch + 1) * 128)
            j512 = ch // 4
            b6 = p.open_batch("sync")
            p.dma("sync", Vbe[:], QD["vt"][tsl, :], tag=f"{EO}_lVt", batch=b6)
            p.dma("sync", Hvb[:], hkv_dr[ch], tag=f"{EO}_lHv", batch=b6)
            p.emit("vector", lambda ng, ch=ch, Vbe=Vbe: ng.tensor_tensor(
                HSEG(Vbe[:]), HSEG(Vbe[:]),
                bt_tok[:, ch * 16:(ch + 1) * 16].unsqueeze(2)
                .broadcast_to([128, 16, 64]), ALU.mult),
                ins=[Vbe[:], f"d_bttok@{ch}"], outs=[Vbe[:]], tag=f"{EO}_Vb")
            p.emit("vector", lambda ng, d=d, e=e: ng.tensor_copy(
                e["tq"][:], d["x"][:]),
                ins=[d["x"][:]], outs=[e["tq"][:]], tag=f"{EO}_xq")
            p.emit("vector", lambda ng, d=d, e=e: ng.tensor_mul(
                e["dpbf"][:], e["tq"][:], d["dbl"][:]),
                ins=[e["tq"][:], d["dbl"][:]], outs=[e["dpbf"][:]],
                tag=f"{EO}_dq")
            emit_S(EO, sl, pr, PA[pr]["tq"][:])
            emit_IH(EO, sl, pr, Vbe[:], Hvb[:], False, osb, True)
            for pc in range(8):
                p.emit("tensor", lambda ng, pc=pc, e=e, tsl=tsl: ng.matmul(
                    e["psA"][:, 384:512], gw2t[:, pc * 128:(pc + 1) * 128],
                    g1T[:, tsl], start=True, stop=True),
                    ins=[gw2t[:], f"d_g1T@{j512}"], outs=[RA(pr, 0)],
                    tag=f"{EO}_g{pc}")
                p.emit("scalar", lambda ng, pc=pc, e=e: ng.activation(
                    e["txf"][:, pc * 128:(pc + 1) * 128], e["psA"][:, 384:512],
                    AF.Sigmoid),
                    ins=[RA(pr, 0)], outs=[e["txf"][:]], tag=f"{EO}_ge{pc}")
            p.emit("vector", lambda ng, e=e, osb=osb: ng.tensor_mul(
                osb[:], osb[:], e["txf"][:]),
                ins=[osb[:], e["txf"][:]], outs=[osb[:]], tag=f"{EO}_og")
            osq = PA[1 - pr]["txf"]
            p.emit("scalar", lambda ng, osq=osq, osb=osb: ng.activation(
                osq[:], osb[:], AF.Square),
                ins=[osb[:]], outs=[osq[:]], tag=f"{EO}_q")
            for hf in range(2):
                hsl = slice(hf * 512, (hf + 1) * 512)
                p.emit("tensor", lambda ng, hsl=hsl, e=e, osq=osq: ng.matmul(
                    e["psA"][0:2, 512:1024], C["c_hs2"][:], osq[:, hsl],
                    start=True, stop=True),
                    ins=[C["c_hs2"][:], osq[:]], outs=[RA(pr, 1)],
                    tag=f"{EO}_s{hf}")
                p.emit("scalar", lambda ng, e=e, rst=rst: ng.activation(
                    rst, e["psA"][0:2, 512:1024], AF.Sqrt,
                    scale=1.0 / 64.0, bias=1e-5),
                    ins=[RA(pr, 1)], outs=[e["tf32"][:]], tag=f"{EO}_r{hf}")
                p.emit("vector", lambda ng, rst=rst: ng.reciprocal(rst, rst),
                       ins=[e["tf32"][:]], outs=[e["tf32"][:]], tag=f"{EO}_i{hf}")
                p.emit("tensor", lambda ng, e=e, rst=rst: ng.matmul(
                    e["psA"][:, 512:1024], C["c_hb2"][:], rst,
                    start=True, stop=True),
                    ins=[C["c_hb2"][:], e["tf32"][:]], outs=[RA(pr, 1)],
                    tag=f"{EO}_b{hf}")
                p.emit("vector", lambda ng, hsl=hsl, e=e, osb=osb: ng.tensor_mul(
                    osb[:, hsl], osb[:, hsl], e["psA"][:, 512:1024]),
                    ins=[osb[:], RA(pr, 1)], outs=[osb[:]], tag=f"{EO}_n{hf}")
            for pc in range(8):
                p.emit("vector", lambda ng, pc=pc, e=e, osb=osb: ng.tensor_scalar(
                    e["tq"][:, pc * 128:(pc + 1) * 128],
                    osb[:, pc * 128:(pc + 1) * 128], nwl[:, pc:pc + 1],
                    None, ALU.mult),
                    ins=[osb[:], nwl[:]], outs=[e["tq"][:]], tag=f"{EO}_w{pc}")
            for dc in range(NDC):
                wcol = 384 if dc % 2 == 0 else 512
                wbk = 0 if dc % 2 == 0 else 1
                xb = xac if dc % 2 == 0 else xac2
                for pc in range(8):
                    p.emit("tensor", lambda ng, pc=pc, dc=dc, e=e, wcol=wcol:
                           ng.matmul(
                        e["psA"][:, wcol:wcol + 128],
                        wo_b[pc][:, dc * 128:(dc + 1) * 128],
                        e["tq"][:, pc * 128:(pc + 1) * 128],
                        start=(pc == 0), stop=(pc == 7)),
                        ins=[wo_b[pc][:], e["tq"][:]], outs=[RA(pr, wbk)],
                        tag=f"{EO}_y{dc}_{pc}")
                p.dma("sync", xb[:], x_dram[dc * 128:(dc + 1) * 128, tsl],
                      tag=f"{EO}_lx{dc}")
                p.emit("vector", lambda ng, e=e, xb=xb, wcol=wcol: ng.tensor_add(
                    xb[:], xb[:], e["psA"][:, wcol:wcol + 128]),
                    ins=[xb[:], RA(pr, wbk)], outs=[xb[:]], tag=f"{EO}_ya{dc}")
                p.dma("pool", xout_d[dc * 128:(dc + 1) * 128, tsl], xb[:],
                      tag=f"{EO}_st{dc}")

        for w0 in range(0, NCH, NSLOT):
            W = min(NSLOT, NCH - w0)
            for sl in range(W):
                emit_prep(w0 + sl, sl, sl % 2)
            for it in range(CG_ITERS):
                for sl in range(W):
                    emit_g1(f"V2I{w0 + sl}_{it}", sl, sl % 2)
                for sl in range(W):
                    emit_g2(f"V2I{w0 + sl}_{it}", sl, sl % 2, it)
                for sl in range(W):
                    emit_g3(f"V2I{w0 + sl}_{it}", sl, sl % 2)
                for sl in range(W):
                    emit_g4(f"V2I{w0 + sl}_{it}", sl, sl % 2, it)
                for sl in range(W):
                    emit_g5(f"V2I{w0 + sl}_{it}", sl, sl % 2)
            for sl in range(W):
                emit_out(w0 + sl, sl, sl % 2)
    p.fence()
